# version 06
# baseline (speedup 1.0000x reference)
import math
from contextlib import ExitStack

import numpy as np
import concourse.bass as bass
import concourse.mybir as mybir
from concourse.bass_utils import run_bass_kernel_spmd

F32 = mybir.dt.float32
BF16 = mybir.dt.bfloat16
I32 = mybir.dt.int32
AF = mybir.ActivationFunctionType
ALU = mybir.AluOpType

B, S, E, H, KV, D = 2, 4096, 4096, 32, 8, 128
I = 11008
EC = E // 128
IC = I // 128
GW, DW = I // 8, E // 8
TOWN = 1024
NG = S // 512
NQ = TOWN // 512
TWO_PI = float(2 * math.pi)
AWQ_INV = [0, 2, 4, 6, 1, 3, 5, 7]


def _perm128():
    P = np.zeros(128, np.int64)
    for s_ in range(128):
        j, wl = s_ // 16, s_ % 16
        P[s_] = 8 * wl + AWQ_INV[j]
    return P


class Sem:
    def __init__(self, es, nc, name):
        self.h = es.enter_context(nc.semaphore(name))
        self.n = 0

    def bump(self, k):
        self.n += k
        return self.n


def build_nc(dbg=False):
    nc = bass.Bass("TRN2", target_bir_lowering=False)

    xT = nc.dram_tensor("xT", [E, S], F32, kind="ExternalInput").ap()
    xq = nc.dram_tensor("xq", [E, TOWN], F32, kind="ExternalInput").ap()
    wq_d = nc.dram_tensor("wq", [E, E], F32, kind="ExternalInput").ap()
    wk_d = nc.dram_tensor("wk", [E, KV * D], F32, kind="ExternalInput").ap()
    wv_d = nc.dram_tensor("wv", [E, KV * D], F32, kind="ExternalInput").ap()
    bq_d = nc.dram_tensor("bq", [1, E], F32, kind="ExternalInput").ap()
    bk_d = nc.dram_tensor("bk", [1, KV * D], F32, kind="ExternalInput").ap()
    bv_d = nc.dram_tensor("bv", [1, KV * D], F32, kind="ExternalInput").ap()
    gq_d = nc.dram_tensor("gq", [E, GW], I32, kind="ExternalInput").ap()
    uq_d = nc.dram_tensor("uq", [E, GW], I32, kind="ExternalInput").ap()
    dq_d = nc.dram_tensor("dq", [I, DW], I32, kind="ExternalInput").ap()
    gs_d = nc.dram_tensor("gs", [32, I], F32, kind="ExternalInput").ap()
    us_d = nc.dram_tensor("us", [32, I], F32, kind="ExternalInput").ap()
    ds_d = nc.dram_tensor("ds", [86, E], F32, kind="ExternalInput").ap()
    gz_d = nc.dram_tensor("gz", [32, GW], I32, kind="ExternalInput").ap()
    uz_d = nc.dram_tensor("uz", [32, GW], I32, kind="ExternalInput").ap()
    dz_d = nc.dram_tensor("dz", [86, DW], I32, kind="ExternalInput").ap()
    pos0_d = nc.dram_tensor("pos0", [64, 1], F32, kind="ExternalInput").ap()
    out_d = nc.dram_tensor("out", [E, TOWN], F32, kind="ExternalOutput").ap()

    wg_scr = nc.dram_tensor("wg_scr", [E, I], BF16, kind="Internal").ap()
    wu_scr = nc.dram_tensor("wu_scr", [E, I], BF16, kind="Internal").ap()
    wd_scr = nc.dram_tensor("wd_scr", [I, E], BF16, kind="Internal").ap()
    k_scr = nc.dram_tensor("k_scr", [KV * D, S], BF16, kind="Internal").ap()
    v_scr = nc.dram_tensor("v_scr", [S, KV * D], BF16, kind="Internal").ap()
    y_scr_t = nc.dram_tensor("y_scr", [E, TOWN], F32, kind="Internal")
    y_scr = y_scr_t.ap()
    ln2_scr = nc.dram_tensor("ln2_scr", [E, TOWN], BF16, kind="Internal").ap()

    es = ExitStack()
    ones_col = es.enter_context(nc.sbuf_tensor("ones_col", [128, 1], BF16))
    ones2 = es.enter_context(nc.sbuf_tensor("ones2", [2, 128], BF16))
    onesN = es.enter_context(nc.sbuf_tensor("onesN", [2, 512], BF16))
    eps_col = es.enter_context(nc.sbuf_tensor("eps_col", [128, 1], F32))
    barrier_tile = es.enter_context(nc.sbuf_tensor("barrier_tile", [1, 2], F32))

    mT = Sem(es, nc, "mT")
    mV = Sem(es, nc, "mV")
    mS = Sem(es, nc, "mS")
    mG = Sem(es, nc, "mG")
    ALLSEMS = [mT, mV, mS, mG]

    def mksem(name):
        s_ = Sem(es, nc, name)
        ALLSEMS.append(s_)
        return s_

    def mkslots(name, k):
        return [mksem(f"{name}{i}") for i in range(k)]

    sl_ld = mkslots("sl_ld", 2)
    sl_st = mkslots("sl_st", 2)
    ps_sl = mksem("ps_sl")
    br_sl = mksem("br_sl")
    bs_sl = mksem("bs_sl")
    xsl = mkslots("xsl", 4)
    ysl = mkslots("ysl", 4)
    la_sl = mksem("la_sl")
    lb_sl = mksem("lb_sl")
    wk_sl = mkslots("wk_sl", 2)
    wv_sl = mksem("wv_sl")
    ks_sl = mksem("ks_sl")
    vs_sl = mksem("vs_sl")
    qw_sl = mkslots("qw_sl", 2)
    kv_sl = mkslots("kv_sl", 2)
    rh_sl = mksem("rh_sl")
    xq_sl = mkslots("xq_sl", 2)
    ys_sl = mkslots("ys_sl", 2)
    l2y_sl = mkslots("l2y_sl", 2)
    ls_sl = mkslots("ls_sl", 2)
    l2_sl = mksem("l2_sl")
    wg_sl = mkslots("wg_sl", 2)
    wu_sl = mkslots("wu_sl", 2)
    wd_sl = mkslots("wd_sl", 2)
    yr_sl = mkslots("yr_sl", 2)
    os_sl = mkslots("os_sl", 2)

    pes = ExitStack()
    pA = pes.enter_context(nc.psum_tensor("pA", [128, 512], F32))
    pB = pes.enter_context(nc.psum_tensor("pB", [128, 512], F32))
    pS_ = pes.enter_context(nc.psum_tensor("pS", [1, 512], F32))
    pC = pes.enter_context(nc.psum_tensor("pC", [128, 512], F32))
    pD = pes.enter_context(nc.psum_tensor("pD", [128, 512], F32))
    pE0 = pes.enter_context(nc.psum_tensor("pE0", [128, 512], F32))
    pE1 = pes.enter_context(nc.psum_tensor("pE1", [128, 512], F32))
    pF0 = pes.enter_context(nc.psum_tensor("pF0", [128, 512], F32))

    def hard_barrier():
        snap = [(s_.h, s_.n) for s_ in ALLSEMS]
        bm = mG.n + 1
        with nc.Block() as bl:
            @bl.gpsimd
            def _(g):
                for hh, vv in snap:
                    g.wait_ge(hh, vv)
                g.memset(barrier_tile[:, :], 0.0).then_inc(mG.h, 1)

            @bl.tensor
            def _(te):
                te.wait_ge(mG.h, bm)

            @bl.vector
            def _(ve):
                ve.wait_ge(mG.h, bm)

            @bl.scalar
            def _(sc):
                sc.wait_ge(mG.h, bm)

            @bl.sync
            def _(sy):
                sy.wait_ge(mG.h, bm)
        mG.bump(1)

    with nc.Block() as bl:
        @bl.gpsimd
        def _(g):
            g.memset(ones_col[:, :], 1.0)
            g.memset(ones2[:, :], 1.0)
            g.memset(onesN[:, :], 1.0)
            g.memset(eps_col[:, :], 1e-5).then_inc(mG.h, 1)
    mG.bump(1)

    deq = ExitStack()
    NSH = I // 2
    WSH = GW // 2
    q_sb = [deq.enter_context(nc.sbuf_tensor(f"q_sb{i}", [128, WSH], I32)) for i in range(2)]
    z_sb = [deq.enter_context(nc.sbuf_tensor(f"z_sb{i}", [128, WSH], I32)) for i in range(2)]
    s_sb = [deq.enter_context(nc.sbuf_tensor(f"s_sb{i}", [128, NSH], F32)) for i in range(2)]
    wn_sb = deq.enter_context(nc.sbuf_tensor("wn_sb", [128, NSH], I32))
    zn_sb = deq.enter_context(nc.sbuf_tensor("zn_sb", [128, NSH], I32))
    db_sb = deq.enter_context(nc.sbuf_tensor("db_sb", [128, NSH], BF16))
    wo_sb = [deq.enter_context(nc.sbuf_tensor(f"wo_sb{i}", [128, NSH], BF16)) for i in range(2)]

    slices = []
    for c in range(EC):
        for hf in range(2):
            slices.append((gq_d, gz_d, gs_d, wg_scr, c, hf * WSH, hf * NSH, WSH))
    for c in range(EC):
        for hf in range(2):
            slices.append((uq_d, uz_d, us_d, wu_scr, c, hf * WSH, hf * NSH, WSH))
    for c in range(IC):
        slices.append((dq_d, dz_d, ds_d, wd_scr, c, 0, 0, DW))
    nsl = len(slices)

    ld_v = [sl_ld[i % 2].bump(48) for i in range(nsl)]
    dv_v = [mV.bump(1) for i in range(nsl)]
    st_v = [sl_st[i % 2].bump(16) for i in range(nsl)]

    with nc.Block() as bl:
        @bl.sync
        def _(sy):
            for i, (qsrc, zsrc, ssrc, dst, c, woff, noff, nw) in enumerate(slices):
                bi = i % 2
                nnib = nw * 8
                if i >= 2:
                    sy.wait_ge(mV.h, dv_v[i - 2])
                sy.dma_start(out=q_sb[bi][:, 0:nw],
                             in_=qsrc[c * 128:(c + 1) * 128, woff:woff + nw]
                             ).then_inc(sl_ld[bi].h, 16)
                sy.dma_start(out=z_sb[bi][:, 0:nw],
                             in_=zsrc[c:c + 1, woff:woff + nw].broadcast_to([128, nw])
                             ).then_inc(sl_ld[bi].h, 16)
                sy.dma_start(out=s_sb[bi][:, 0:nnib],
                             in_=ssrc[c:c + 1, noff:noff + nnib].broadcast_to([128, nnib])
                             ).then_inc(sl_ld[bi].h, 16)

        @bl.vector
        def _(ve):
            for i, (qsrc, zsrc, ssrc, dst, c, woff, noff, nw) in enumerate(slices):
                bi = i % 2
                nnib = nw * 8
                ve.wait_ge(sl_ld[bi].h, ld_v[i])
                if i >= 2:
                    ve.wait_ge(sl_st[bi].h, st_v[i - 2])
                wv3 = wn_sb[:, 0:nnib].rearrange("p (q j w) -> p q j w", j=8, w=16)
                zv3 = zn_sb[:, 0:nnib].rearrange("p (q j w) -> p q j w", j=8, w=16)
                qv3 = q_sb[bi][:, 0:nw].rearrange("p (q w) -> p q w", w=16)
                zq3 = z_sb[bi][:, 0:nw].rearrange("p (q w) -> p q w", w=16)
                for j in range(8):
                    ve.tensor_scalar(out=wv3[:, :, j, :], in0=qv3, scalar1=4 * j,
                                     scalar2=15, op0=ALU.logical_shift_right,
                                     op1=ALU.bitwise_and)
                    ve.tensor_scalar(out=zv3[:, :, j, :], in0=zq3, scalar1=4 * j,
                                     scalar2=15, op0=ALU.logical_shift_right,
                                     op1=ALU.bitwise_and)
                ve.tensor_tensor(out=db_sb[:, 0:nnib], in0=wn_sb[:, 0:nnib],
                                 in1=zn_sb[:, 0:nnib], op=ALU.subtract)
                ve.tensor_tensor(out=wo_sb[bi][:, 0:nnib], in0=db_sb[:, 0:nnib],
                                 in1=s_sb[bi][:, 0:nnib], op=ALU.mult).then_inc(mV.h, 1)

        @bl.gpsimd
        def _(g):
            for i, (qsrc, zsrc, ssrc, dst, c, woff, noff, nw) in enumerate(slices):
                bi = i % 2
                nnib = nw * 8
                g.wait_ge(mV.h, dv_v[i])
                g.dma_start(out=dst[c * 128:(c + 1) * 128, noff:noff + nnib],
                            in_=wo_sb[bi][:, 0:nnib]).then_inc(sl_st[bi].h, 16)
    deq.close()

    hard_barrier()
    ap_es = ExitStack()
    kcos = ap_es.enter_context(nc.sbuf_tensor("kcos", [64, S], F32))
    ksin = ap_es.enter_context(nc.sbuf_tensor("ksin", [64, S], F32))
    qcos = ap_es.enter_context(nc.sbuf_tensor("qcos", [64, TOWN], F32))
    qsin = ap_es.enter_context(nc.sbuf_tensor("qsin", [64, TOWN], F32))
    bqhl = ap_es.enter_context(nc.sbuf_tensor("bqhl", [2, E], BF16))
    bkhl = ap_es.enter_context(nc.sbuf_tensor("bkhl", [2, KV * D], BF16))
    bvhl = ap_es.enter_context(nc.sbuf_tensor("bvhl", [2, KV * D], BF16))
    m_r = ap_es.enter_context(nc.sbuf_tensor("m_r", [1, 512], F32))
    msq_r = ap_es.enter_context(nc.sbuf_tensor("msq_r", [1, 512], F32))
    mm_r = ap_es.enter_context(nc.sbuf_tensor("mm_r", [1, 512], F32))
    rstd_r = ap_es.enter_context(nc.sbuf_tensor("rstd_r", [1, 512], F32))
    negmr_r = ap_es.enter_context(nc.sbuf_tensor("negmr_r", [1, 512], F32))
    hi_r = ap_es.enter_context(nc.sbuf_tensor("hi_r", [1, 512], BF16))
    lo_r = ap_es.enter_context(nc.sbuf_tensor("lo_r", [1, 512], BF16))
    rhl_a = ap_es.enter_context(nc.sbuf_tensor("rhl_a", [2, 512], BF16))
    rhl_b = ap_es.enter_context(nc.sbuf_tensor("rhl_b", [2, 512], BF16))

    tb_es = ExitStack()
    t_pos = tb_es.enter_context(nc.sbuf_tensor("t_pos", [64, S], F32))
    t_ang = tb_es.enter_context(nc.sbuf_tensor("t_ang", [64, S], F32))
    t_k = tb_es.enter_context(nc.sbuf_tensor("t_k", [64, S], F32))
    t_ki = tb_es.enter_context(nc.sbuf_tensor("t_ki", [64, S], I32))
    t_pidx = tb_es.enter_context(nc.sbuf_tensor("t_pidx", [64, 1], F32))
    t_invf = tb_es.enter_context(nc.sbuf_tensor("t_invf", [64, 1], F32))
    t_pos0 = tb_es.enter_context(nc.sbuf_tensor("t_pos0", [64, 1], F32))
    t_brow = tb_es.enter_context(nc.sbuf_tensor("t_brow", [1, E], F32))
    t_bhi = tb_es.enter_context(nc.sbuf_tensor("t_bhi", [1, E], BF16))
    t_blo = tb_es.enter_context(nc.sbuf_tensor("t_blo", [1, E], BF16))

    IOTA_G = mG.n + 1
    POS0_V = ps_sl.n + 16
    INVF_S = mS.n + 1
    with nc.Block() as bl:
        @bl.gpsimd
        def _(g):
            g.iota(t_pos[:, :], [[1, S]], channel_multiplier=0,
                   allow_small_or_imprecise_dtypes=True)
            g.iota(t_pidx[:, :], [[0, 1]], channel_multiplier=1,
                   allow_small_or_imprecise_dtypes=True).then_inc(mG.h, 1)

        @bl.sync
        def _(sy):
            sy.dma_start(out=t_pos0[:, :], in_=pos0_d[:, :]).then_inc(ps_sl.h, 16)

        @bl.scalar
        def _(sc):
            sc.wait_ge(mG.h, IOTA_G)
            sc.activation(out=t_invf[:, :], in_=t_pidx[:, :], func=AF.Exp,
                          scale=float(-math.log(10000.0) / 64.0)).then_inc(mS.h, 1)
    mG.bump(1)
    ps_sl.bump(16)
    mS.bump(1)

    tables = [(ksin[:, :], S, 0.0, False), (kcos[:, :], S, math.pi / 2, False),
              (qsin[:, :], TOWN, 0.0, True), (qcos[:, :], TOWN, math.pi / 2, True)]
    tb_v = [mV.bump(1) for _ in range(4)]
    with nc.Block() as bl:
        @bl.vector
        def _(ve):
            ve.wait_ge(mS.h, INVF_S)
            ve.wait_ge(ps_sl.h, POS0_V)
            for ti, (dst, width, shift, ap0) in enumerate(tables):
                if ap0:
                    ve.tensor_scalar(out=t_ang[:, 0:width], in0=t_pos[:, 0:width],
                                     scalar1=t_pos0[:, :], scalar2=None, op0=ALU.add)
                    ve.tensor_scalar(out=t_ang[:, 0:width], in0=t_ang[:, 0:width],
                                     scalar1=t_invf[:, :], scalar2=float(shift),
                                     op0=ALU.mult, op1=ALU.add)
                else:
                    ve.tensor_scalar(out=t_ang[:, 0:width], in0=t_pos[:, 0:width],
                                     scalar1=t_invf[:, :], scalar2=float(shift),
                                     op0=ALU.mult, op1=ALU.add)
                ve.tensor_scalar(out=t_k[:, 0:width], in0=t_ang[:, 0:width],
                                 scalar1=float(1.0 / TWO_PI), scalar2=None, op0=ALU.mult)
                ve.tensor_copy(out=t_ki[:, 0:width], in_=t_k[:, 0:width])
                ve.tensor_copy(out=t_k[:, 0:width], in_=t_ki[:, 0:width])
                ve.scalar_tensor_tensor(out=dst, in0=t_k[:, 0:width],
                                        scalar=float(-TWO_PI), in1=t_ang[:, 0:width],
                                        op0=ALU.mult, op1=ALU.add).then_inc(mV.h, 1)

        @bl.scalar
        def _(sc):
            for ti, (dst, width, shift, ap0) in enumerate(tables):
                sc.wait_ge(mV.h, tb_v[ti])
                sc.activation(out=dst, in_=dst, func=AF.Sin).then_inc(mS.h, 1)
    mS.bump(4)

    bias_specs = ((bq_d, bqhl, E), (bk_d, bkhl, KV * D), (bv_d, bvhl, KV * D))
    brow_v = [br_sl.bump(16) for _ in range(3)]
    bsplit_v = [mV.bump(1) for _ in range(3)]
    bpl_v = [bs_sl.bump(32) for _ in range(3)]
    with nc.Block() as bl:
        @bl.sync
        def _(sy):
            for i, (src, dsttile, wdt) in enumerate(bias_specs):
                if i > 0:
                    sy.wait_ge(bs_sl.h, bpl_v[i - 1])
                sy.dma_start(out=t_brow[:, 0:wdt], in_=src[:, :]).then_inc(br_sl.h, 16)

        @bl.vector
        def _(ve):
            for i, (src, dsttile, wdt) in enumerate(bias_specs):
                ve.wait_ge(br_sl.h, brow_v[i])
                if i > 0:
                    ve.wait_ge(bs_sl.h, bpl_v[i - 1])
                ve.tensor_copy(out=t_bhi[:, 0:wdt], in_=t_brow[:, 0:wdt])
                ve.tensor_tensor(out=t_blo[:, 0:wdt], in0=t_brow[:, 0:wdt],
                                 in1=t_bhi[:, 0:wdt], op=ALU.subtract).then_inc(mV.h, 1)

        @bl.gpsimd
        def _(g):
            for i, (src, dsttile, wdt) in enumerate(bias_specs):
                g.wait_ge(mV.h, bsplit_v[i])
                g.dma_start(out=dsttile[0:1, 0:wdt], in_=t_bhi[:, 0:wdt]).then_inc(bs_sl.h, 16)
                g.dma_start(out=dsttile[1:2, 0:wdt], in_=t_blo[:, 0:wdt]).then_inc(bs_sl.h, 16)
    tb_es.close()
    hard_barrier()

    def ln_rows_block(stats_ms):
        Va = mV.bump(1)
        Sa = mS.bump(1)
        Vb = mV.bump(1)
        Ga = la_sl.bump(32)
        Vc = mV.bump(1)
        Gb = lb_sl.bump(32)
        Ta = mT.bump(1)
        with nc.Block() as bl:
            @bl.vector
            def _(ve):
                ve.wait_ge(mT.h, stats_ms)
                ve.tensor_scalar(out=m_r[:, :], in0=pA[0:1, :], scalar1=float(1.0 / 4096.0),
                                 scalar2=None, op0=ALU.mult)
                ve.tensor_scalar(out=msq_r[:, :], in0=pB[0:1, :], scalar1=float(1.0 / 4096.0),
                                 scalar2=None, op0=ALU.mult)
                ve.tensor_tensor(out=mm_r[:, :], in0=m_r[:, :], in1=m_r[:, :], op=ALU.mult)
                ve.tensor_tensor(out=msq_r[:, :], in0=msq_r[:, :], in1=mm_r[:, :],
                                 op=ALU.subtract).then_inc(mV.h, 1)
                ve.wait_ge(mS.h, Sa)
                ve.reciprocal(out=rstd_r[:, :], in_=rstd_r[:, :])
                ve.scalar_tensor_tensor(out=negmr_r[:, :], in0=m_r[:, :], scalar=-1.0,
                                        in1=rstd_r[:, :], op0=ALU.mult, op1=ALU.mult)
                ve.tensor_copy(out=hi_r[:, :], in_=rstd_r[:, :])
                ve.tensor_tensor(out=lo_r[:, :], in0=rstd_r[:, :], in1=hi_r[:, :],
                                 op=ALU.subtract).then_inc(mV.h, 1)
                ve.wait_ge(la_sl.h, Ga)
                ve.tensor_copy(out=hi_r[:, :], in_=negmr_r[:, :])
                ve.tensor_tensor(out=lo_r[:, :], in0=negmr_r[:, :], in1=hi_r[:, :],
                                 op=ALU.subtract).then_inc(mV.h, 1)

            @bl.scalar
            def _(sc):
                sc.wait_ge(mV.h, Va)
                sc.activation(out=rstd_r[:, :], in_=msq_r[:, :], func=AF.Sqrt,
                              bias=eps_col[0:1, :]).then_inc(mS.h, 1)

            @bl.gpsimd
            def _(g):
                g.wait_ge(mV.h, Vb)
                g.dma_start(out=rhl_a[0:1, :], in_=hi_r[:, :]).then_inc(la_sl.h, 16)
                g.dma_start(out=rhl_a[1:2, :], in_=lo_r[:, :]).then_inc(la_sl.h, 16)
                g.wait_ge(mV.h, Vc)
                g.dma_start(out=rhl_b[0:1, :], in_=hi_r[:, :]).then_inc(lb_sl.h, 16)
                g.dma_start(out=rhl_b[1:2, :], in_=lo_r[:, :]).then_inc(lb_sl.h, 16)

            @bl.tensor
            def _(te):
                te.wait_ge(la_sl.h, Ga)
                te.wait_ge(lb_sl.h, Gb)
                te.matmul(pC[:, :], ones2[:, :], rhl_a[:, :], start=True, stop=True,
                          skip_group_check=True)
                te.matmul(pD[:, :], ones2[:, :], rhl_b[:, :], start=True, stop=True,
                          skip_group_check=True).then_inc(mT.h, 1)
        return Ta

    def stats_wave(src_ap, col0, xbf_grp, x2_t):
        xld = [xsl[c % 4].bump(16) for c in range(EC)]
        sqv = [mS.bump(1) for c in range(EC)]
        stv = [mT.bump(1) for c in range(EC)]
        with nc.Block() as bl:
            @bl.gpsimd
            def _(g):
                for c in range(EC):
                    if c >= 4:
                        g.wait_ge(mS.h, sqv[c - 4])
                    g.dma_start(out=xbf_grp[:, c * 512:(c + 1) * 512],
                                in_=src_ap[c * 128:(c + 1) * 128, col0:col0 + 512]
                                ).then_inc(xsl[c % 4].h, 16)

            @bl.scalar
            def _(sc):
                for c in range(EC):
                    sc.wait_ge(xsl[c % 4].h, xld[c])
                    if c >= 4:
                        sc.wait_ge(mT.h, stv[c - 4])
                    sc.activation(out=x2_t[:, (c % 4) * 512:(c % 4 + 1) * 512],
                                  in_=xbf_grp[:, c * 512:(c + 1) * 512],
                                  func=AF.Square).then_inc(mS.h, 1)

            @bl.tensor
            def _(te):
                for c in range(EC):
                    te.wait_ge(mS.h, sqv[c])
                    te.matmul(pA[0:1, :], ones_col[:, :],
                              xbf_grp[:, c * 512:(c + 1) * 512],
                              start=(c == 0), stop=(c == EC - 1), skip_group_check=True)
                    te.matmul(pB[0:1, :], ones_col[:, :],
                              x2_t[:, (c % 4) * 512:(c % 4 + 1) * 512],
                              start=(c == 0), stop=(c == EC - 1),
                              skip_group_check=True).then_inc(mT.h, 1)
        return stv[-1]

    def norm_wave(xbf_grp, dst, bcast_ms, tn_t):
        nv = [mV.bump(1) for c in range(EC)]
        with nc.Block() as bl:
            @bl.vector
            def _(ve):
                ve.wait_ge(mT.h, bcast_ms)
                for c in range(EC):
                    ve.tensor_tensor(out=tn_t[:, :],
                                     in0=xbf_grp[:, c * 512:(c + 1) * 512],
                                     in1=pC[:, :], op=ALU.mult)
                    ve.tensor_tensor(out=dst[:, c * 512:(c + 1) * 512], in0=tn_t[:, :],
                                     in1=pD[:, :], op=ALU.add).then_inc(mV.h, 1)
        return nv[-1]

    def rope_ops(ve, psrc, cos_t, sin_t, col0, dst, dcol, rp):
        ve.tensor_tensor(out=rp[0][:, :], in0=psrc[0:64, :],
                         in1=cos_t[:, col0:col0 + 512], op=ALU.mult)
        ve.tensor_tensor(out=rp[1][:, :], in0=psrc[64:128, :],
                         in1=sin_t[:, col0:col0 + 512], op=ALU.mult)
        ve.tensor_tensor(out=dst[0:64, dcol:dcol + 512], in0=rp[0][:, :],
                         in1=rp[1][:, :], op=ALU.subtract)
        ve.tensor_tensor(out=rp[0][:, :], in0=psrc[0:64, :],
                         in1=sin_t[:, col0:col0 + 512], op=ALU.mult)
        ve.tensor_tensor(out=rp[1][:, :], in0=psrc[64:128, :],
                         in1=cos_t[:, col0:col0 + 512], op=ALU.mult)
        ve.tensor_tensor(out=dst[64:128, dcol:dcol + 512], in0=rp[0][:, :],
                         in1=rp[1][:, :], op=ALU.add).then_inc(mV.h, 1)

    a1 = ExitStack()
    xln = a1.enter_context(nc.sbuf_tensor("xln", [128, EC * 512], BF16))
    xbf_grp = a1.enter_context(nc.sbuf_tensor("xbf_grp", [128, EC * 512], BF16))
    x2_rot = a1.enter_context(nc.sbuf_tensor("x2_rot", [128, 4 * 512], BF16))
    tnorm = a1.enter_context(nc.sbuf_tensor("tnorm", [128, 512], F32))
    wvb = a1.enter_context(nc.sbuf_tensor("wvb", [128, EC * 512], BF16))
    wkb = a1.enter_context(nc.sbuf_tensor("wkb", [128, 2 * EC * 128], BF16))
    kst = a1.enter_context(nc.sbuf_tensor("kst", [128, 8 * 512], BF16))
    vst = a1.enter_context(nc.sbuf_tensor("vst", [128, 4 * 1024], BF16))
    rp1 = [a1.enter_context(nc.sbuf_tensor(f"rp1_{i}", [64, 512], F32)) for i in range(2)]

    ks_done = 0
    vs_done = 0
    for g_i in range(NG):
        hard_barrier()
        col0 = g_i * 512
        stats_ms = stats_wave(xT, col0, xbf_grp, x2_rot)
        bcast_ms = ln_rows_block(stats_ms)
        norm_ms = norm_wave(xbf_grp, xln, bcast_ms, tnorm)

        prev_ks = ks_done
        wkld = [wk_sl[h % 2].bump(16) for h in range(KV)]
        kmm = [mT.bump(1) for h in range(KV)]
        krp = [mV.bump(1) for h in range(KV)]
        with nc.Block() as bl:
            @bl.gpsimd
            def _(g):
                for h in range(KV):
                    if h >= 2:
                        g.wait_ge(mT.h, kmm[h - 2])
                    g.dma_start(
                        out=wkb[:, (h % 2) * EC * 128:(h % 2 + 1) * EC * 128].rearrange(
                            "p (c n) -> p c n", n=128),
                        in_=wk_d[:, h * 128:(h + 1) * 128].rearrange(
                            "(c p) n -> p c n", p=128)).then_inc(wk_sl[h % 2].h, 16)

            @bl.tensor
            def _(te):
                te.wait_ge(mV.h, norm_ms)
                for h in range(KV):
                    te.wait_ge(wk_sl[h % 2].h, wkld[h])
                    if h >= 2:
                        te.wait_ge(mV.h, krp[h - 2])
                    pk = pE0 if h % 2 == 0 else pE1
                    base = (h % 2) * EC * 128
                    for c in range(EC):
                        te.matmul(pk[:, :], wkb[:, base + c * 128:base + (c + 1) * 128],
                                  xln[:, c * 512:(c + 1) * 512],
                                  start=(c == 0), stop=False, skip_group_check=True)
                    te.matmul(pk[:, :], bkhl[:, h * 128:(h + 1) * 128], onesN[:, :],
                              start=False, stop=True,
                              skip_group_check=True).then_inc(mT.h, 1)

            @bl.vector
            def _(ve):
                for h in range(KV):
                    ve.wait_ge(mT.h, kmm[h])
                    if h == 0 and prev_ks:
                        ve.wait_ge(ks_sl.h, prev_ks)
                    pk = pE0 if h % 2 == 0 else pE1
                    rope_ops(ve, pk, kcos, ksin, col0, kst, h * 512, rp1)

            @bl.sync
            def _(sy):
                for h in range(KV):
                    sy.wait_ge(mV.h, krp[h])
                    sy.dma_start(out=k_scr[h * 128:(h + 1) * 128, col0:col0 + 512],
                                 in_=kst[:, h * 512:(h + 1) * 512]).then_inc(ks_sl.h, 16)
        ks_done = ks_sl.bump(KV * 16)
        ROPES_V = mV.n

        prev_vs = vs_done
        for half in range(2):
            PRE_T = mT.n
            PRE_S = mS.n
            wvld = wv_sl.bump(16)
            vmm = [mT.bump(1) for tt in range(4)]
            vev = [mS.bump(1) for tt in range(4)]
            with nc.Block() as bl:
                @bl.gpsimd
                def _(g):
                    g.wait_ge(mT.h, PRE_T)
                    g.dma_start(
                        out=wvb[:, :].rearrange("p (c n) -> p c n", n=512),
                        in_=wv_d[:, half * 512:(half + 1) * 512].rearrange(
                            "(c p) n -> p c n", p=128)).then_inc(wv_sl.h, 16)

                @bl.tensor
                def _(te):
                    te.wait_ge(wv_sl.h, wvld)
                    te.wait_ge(mS.h, PRE_S)
                    if half == 0:
                        te.wait_ge(mV.h, ROPES_V)
                    for tt in range(4):
                        pv = pF0 if tt % 2 == 0 else pE0
                        if tt >= 2:
                            te.wait_ge(mS.h, vev[tt - 2])
                        for c in range(EC):
                            te.matmul(pv[:, :],
                                      xln[:, c * 512 + tt * 128:c * 512 + tt * 128 + 128],
                                      wvb[:, c * 512:(c + 1) * 512],
                                      start=(c == 0), stop=False, skip_group_check=True)
                        te.matmul(pv[:, :], ones2[:, :],
                                  bvhl[:, half * 512:(half + 1) * 512],
                                  start=False, stop=True,
                                  skip_group_check=True).then_inc(mT.h, 1)

                @bl.scalar
                def _(sc):
                    for tt in range(4):
                        pv = pF0 if tt % 2 == 0 else pE0
                        sc.wait_ge(mT.h, vmm[tt])
                        if tt == 0 and half == 0 and prev_vs:
                            sc.wait_ge(vs_sl.h, prev_vs)
                        sc.activation(out=vst[:, tt * 1024 + half * 512:
                                              tt * 1024 + half * 512 + 512],
                                      in_=pv[:, :], func=AF.Copy).then_inc(mS.h, 1)
        VS_ALL = mS.n
        with nc.Block() as bl:
            @bl.sync
            def _(sy):
                sy.wait_ge(mS.h, VS_ALL)
                for tt in range(4):
                    sy.dma_start(out=v_scr[col0 + tt * 128:col0 + tt * 128 + 128, :],
                                 in_=vst[:, tt * 1024:(tt + 1) * 1024]).then_inc(vs_sl.h, 16)
        vs_done = vs_sl.bump(4 * 16)
    A1_KS = ks_sl.n
    A1_VS = vs_sl.n
    a1.close()

    a2p = ExitStack()
    q_tile = a2p.enter_context(nc.sbuf_tensor("q_tile", [128, EC * 512], BF16))
    for qg in range(NQ):
        qcol = qg * 512
        hard_barrier()
        p1 = ExitStack()
        xlnq = p1.enter_context(nc.sbuf_tensor(f"xlnq{qg}", [128, EC * 512], BF16))
        wqb = p1.enter_context(nc.sbuf_tensor(f"wqb{qg}", [128, 2 * EC * 128], BF16))
        xbf2 = p1.enter_context(nc.sbuf_tensor(f"xbf2{qg}", [128, EC * 512], BF16))
        x22 = p1.enter_context(nc.sbuf_tensor(f"x22{qg}", [128, 4 * 512], BF16))
        tn2 = p1.enter_context(nc.sbuf_tensor(f"tn2{qg}", [128, 512], F32))
        rp2 = [p1.enter_context(nc.sbuf_tensor(f"rp2_{qg}_{i}", [64, 512], F32))
               for i in range(2)]

        stats2 = stats_wave(xq, qcol, xbf2, x22)
        bcast2 = ln_rows_block(stats2)
        normq = norm_wave(xbf2, xlnq, bcast2, tn2)

        qwld = [qw_sl[m % 2].bump(16) for m in range(EC)]
        qmm = [mT.bump(1) for m in range(EC)]
        qrp = [mV.bump(1) for m in range(EC)]
        with nc.Block() as bl:
            @bl.gpsimd
            def _(g):
                for m in range(EC):
                    if m >= 2:
                        g.wait_ge(mT.h, qmm[m - 2])
                    g.dma_start(
                        out=wqb[:, (m % 2) * EC * 128:(m % 2 + 1) * EC * 128].rearrange(
                            "p (c n) -> p c n", n=128),
                        in_=wq_d[:, m * 128:(m + 1) * 128].rearrange(
                            "(c p) n -> p c n", p=128)).then_inc(qw_sl[m % 2].h, 16)

            @bl.tensor
            def _(te):
                te.wait_ge(mV.h, normq)
                for m in range(EC):
                    te.wait_ge(qw_sl[m % 2].h, qwld[m])
                    if m >= 2:
                        te.wait_ge(mV.h, qrp[m - 2])
                    pq = pE0 if m % 2 == 0 else pE1
                    base = (m % 2) * EC * 128
                    for c in range(EC):
                        te.matmul(pq[:, :], wqb[:, base + c * 128:base + (c + 1) * 128],
                                  xlnq[:, c * 512:(c + 1) * 512],
                                  start=(c == 0), stop=False, skip_group_check=True)
                    te.matmul(pq[:, :], bqhl[:, m * 128:(m + 1) * 128], onesN[:, :],
                              start=False, stop=True,
                              skip_group_check=True).then_inc(mT.h, 1)

            @bl.vector
            def _(ve):
                for m in range(EC):
                    ve.wait_ge(mT.h, qmm[m])
                    pq = pE0 if m % 2 == 0 else pE1
                    rope_ops(ve, pq, qcos, qsin, qcol, q_tile, m * 512, rp2)
        p1.close()
        hard_barrier()

        p2 = ExitStack()
        khb = p2.enter_context(nc.sbuf_tensor(f"khb{qg}", [128, 2 * S], BF16))
        vhb = p2.enter_context(nc.sbuf_tensor(f"vhb{qg}", [128, 2 * EC * 128], BF16))
        expb = p2.enter_context(nc.sbuf_tensor(f"expb{qg}", [128, 4 * 512], BF16))
        srecip = p2.enter_context(nc.sbuf_tensor(f"srecip{qg}", [128, 512], F32))
        attn_t = p2.enter_context(nc.sbuf_tensor(f"attn_t{qg}", [128, 512], F32))
        yst = p2.enter_context(nc.sbuf_tensor(f"yst{qg}", [128, 2 * 512], F32))
        ybf_grp = p2.enter_context(nc.sbuf_tensor(f"ybfg{qg}", [128, EC * 512], BF16))
        y22_rot = p2.enter_context(nc.sbuf_tensor(f"y22r{qg}", [128, 4 * 512], BF16))
        sacc = p2.enter_context(nc.sbuf_tensor(f"sacc{qg}", [128, 512], F32))
        saccbf = p2.enter_context(nc.sbuf_tensor(f"saccbf{qg}", [128, 512], BF16))
        xqr = p2.enter_context(nc.sbuf_tensor(f"xqr{qg}", [128, 2 * 512], F32))
        ln2st = p2.enter_context(nc.sbuf_tensor(f"ln2st{qg}", [128, 2 * 512], BF16))
        rrow = p2.enter_context(nc.sbuf_tensor(f"rrow{qg}", [1, 512], F32))
        rhi = p2.enter_context(nc.sbuf_tensor(f"rhi{qg}", [1, 512], BF16))
        rlo = p2.enter_context(nc.sbuf_tensor(f"rlo{qg}", [1, 512], BF16))
        rhl = p2.enter_context(nc.sbuf_tensor(f"rhl{qg}", [2, 512], BF16))

        evac_ms = {}
        ystore_v = {}
        h_mm_done = {}
        last_exv = None
        prev_Vr = None
        A2S_T = mT.n
        A2S_V = mV.n
        QRP_LAST = qrp[-1]
        for h in range(KV):
            kvld = kv_sl[h % 2].bump(32)
            with nc.Block() as bl:
                @bl.sync
                def _(sy):
                    if h == 0:
                        sy.wait_ge(mT.h, A2S_T)
                        sy.wait_ge(mV.h, A2S_V)
                        if qg == 0:
                            sy.wait_ge(ks_sl.h, A1_KS)
                            sy.wait_ge(vs_sl.h, A1_VS)
                    if h >= 2:
                        sy.wait_ge(mT.h, h_mm_done[h - 2])
                    sy.dma_start(out=khb[:, (h % 2) * S:(h % 2 + 1) * S],
                                 in_=k_scr[h * 128:(h + 1) * 128, :]
                                 ).then_inc(kv_sl[h % 2].h, 16)
                    sy.dma_start(
                        out=vhb[:, (h % 2) * EC * 128:(h % 2 + 1) * EC * 128].rearrange(
                            "p (kc n) -> p kc n", n=128),
                        in_=v_scr[:, h * 128:(h + 1) * 128].rearrange(
                            "(kc kp) n -> kp kc n", kp=128)).then_inc(kv_sl[h % 2].h, 16)
            kb = (h % 2) * S
            vb = (h % 2) * EC * 128

            for sub in range(4):
                qh = h * 4 + sub
                pAT = pF0 if qh % 2 == 0 else pD
                SCB = [pE0, pE1, pA, pB]
                exv = [mS.n + kc + 1 for kc in range(32)]
                svv = [mV.n + kc + 1 for kc in range(32)]
                scv = {}
                pvv = {}
                t_ = mT.n
                for kc in range(4):
                    t_ += 1
                    scv[kc] = t_
                for kc in range(32):
                    t_ += 1
                    pvv[kc] = t_
                    if kc + 4 < 32:
                        t_ += 1
                        scv[kc + 4] = t_
                mT.n = t_

                with nc.Block() as bl:
                    @bl.tensor
                    def _(te):
                        te.wait_ge(kv_sl[h % 2].h, kvld)
                        if qh == 0:
                            te.wait_ge(mV.h, QRP_LAST)

                        def sc_mm(kc):
                            psc = SCB[kc % 4]
                            if kc >= 4:
                                te.wait_ge(mS.h, exv[kc - 4])
                            elif last_exv is not None:
                                te.wait_ge(mS.h, last_exv[28 + kc])
                            te.matmul(psc[:, :],
                                      khb[:, kb + kc * 128:kb + (kc + 1) * 128],
                                      q_tile[:, qh * 512:(qh + 1) * 512],
                                      start=True, stop=True,
                                      skip_group_check=True).then_inc(mT.h, 1)
                        for kc in range(4):
                            sc_mm(kc)
                        for kc in range(32):
                            te.wait_ge(mS.h, exv[kc])
                            if kc == 0 and qh >= 2:
                                te.wait_ge(mV.h, evac_ms[qh - 2])
                            esl = slice((kc % 4) * 512, (kc % 4 + 1) * 512)
                            te.matmul(pAT[:, :],
                                      vhb[:, vb + kc * 128:vb + (kc + 1) * 128],
                                      expb[:, esl], start=(kc == 0), stop=(kc == 31),
                                      skip_group_check=True).then_inc(mT.h, 1)
                            if kc + 4 < 32:
                                sc_mm(kc + 4)

                    @bl.scalar
                    def _(sc):
                        for kc in range(32):
                            psc = SCB[kc % 4]
                            sc.wait_ge(mT.h, scv[kc])
                            if kc >= 4:
                                sc.wait_ge(mT.h, pvv[kc - 4])
                                sc.wait_ge(mV.h, svv[kc - 4])
                            sc.activation(out=expb[:, (kc % 4) * 512:(kc % 4 + 1) * 512],
                                          in_=psc[:, :], func=AF.Exp,
                                          scale=float(1.0 / math.sqrt(D))
                                          ).then_inc(mS.h, 1)

                    @bl.vector
                    def _(ve):
                        for kc in range(32):
                            ve.wait_ge(mS.h, exv[kc])
                            esl = slice((kc % 4) * 512, (kc % 4 + 1) * 512)
                            if kc == 0:
                                ve.tensor_copy(out=sacc[:, :],
                                               in_=expb[:, esl]).then_inc(mV.h, 1)
                            else:
                                ve.tensor_tensor(out=sacc[:, :], in0=sacc[:, :],
                                                 in1=expb[:, esl],
                                                 op=ALU.add).then_inc(mV.h, 1)
                        ve.tensor_copy(out=saccbf[:, :], in_=sacc[:, :]).then_inc(mV.h, 1)
                mS.bump(32)
                mV.bump(32)
                Vcast = mV.bump(1)

                Tsm = mT.bump(1)
                Vr = mV.bump(1)
                Ghl = rh_sl.bump(32)
                Tb = mT.bump(1)
                Lq = xq_sl[qh % 2].bump(16)
                Ve = mV.bump(1)
                Qs = ys_sl[qh % 2].bump(16)
                with nc.Block() as bl:
                    @bl.tensor
                    def _(te):
                        te.wait_ge(mV.h, Vcast)
                        if prev_Vr is not None:
                            te.wait_ge(mV.h, prev_Vr)
                        te.matmul(pS_[:, :], ones_col[:, :], saccbf[:, :],
                                  start=True, stop=True,
                                  skip_group_check=True).then_inc(mT.h, 1)
                        te.wait_ge(rh_sl.h, Ghl)
                        te.matmul(pC[:, :], ones2[:, :], rhl[:, :], start=True,
                                  stop=True, skip_group_check=True).then_inc(mT.h, 1)

                    @bl.vector
                    def _(ve):
                        ve.wait_ge(mT.h, Tsm)
                        ve.reciprocal(out=rrow[:, :], in_=pS_[:, :])
                        ve.tensor_copy(out=rhi[:, :], in_=rrow[:, :])
                        ve.tensor_tensor(out=rlo[:, :], in0=rrow[:, :], in1=rhi[:, :],
                                         op=ALU.subtract).then_inc(mV.h, 1)
                        ve.wait_ge(mT.h, Tb)
                        ve.tensor_copy(out=srecip[:, :], in_=pC[:, :])
                        ve.wait_ge(xq_sl[qh % 2].h, Lq)
                        if qh >= 2:
                            ve.wait_ge(ys_sl[qh % 2].h, ystore_v[qh - 2])
                        ve.tensor_tensor(out=attn_t[:, :], in0=pAT[:, :],
                                         in1=srecip[:, :], op=ALU.mult)
                        ve.tensor_tensor(out=yst[:, (qh % 2) * 512:(qh % 2 + 1) * 512],
                                         in0=attn_t[:, :],
                                         in1=xqr[:, (qh % 2) * 512:(qh % 2 + 1) * 512],
                                         op=ALU.add).then_inc(mV.h, 1)

                    @bl.gpsimd
                    def _(g):
                        g.wait_ge(mV.h, Vr)
                        g.dma_start(out=rhl[0:1, :], in_=rhi[:, :]).then_inc(rh_sl.h, 16)
                        g.dma_start(out=rhl[1:2, :], in_=rlo[:, :]).then_inc(rh_sl.h, 16)

                    @bl.sync
                    def _(sy):
                        if qh >= 2:
                            sy.wait_ge(mV.h, evac_ms[qh - 2])
                        sy.dma_start(out=xqr[:, (qh % 2) * 512:(qh % 2 + 1) * 512],
                                     in_=xq[qh * 128:(qh + 1) * 128, qcol:qcol + 512]
                                     ).then_inc(xq_sl[qh % 2].h, 16)
                        sy.wait_ge(mV.h, Ve)
                        sy.dma_start(out=y_scr[qh * 128:(qh + 1) * 128, qcol:qcol + 512],
                                     in_=yst[:, (qh % 2) * 512:(qh % 2 + 1) * 512]
                                     ).then_inc(ys_sl[qh % 2].h, 16)
                evac_ms[qh] = Ve
                ystore_v[qh] = Qs
                last_exv = exv
                prev_Vr = Vr
            h_mm_done[h] = mT.n

        YALL0 = [ys_sl[0].n, ys_sl[1].n]
        with nc.Block() as bl:
            @bl.gpsimd
            def _(g):
                g.wait_ge(ys_sl[0].h, YALL0[0])
                g.wait_ge(ys_sl[1].h, YALL0[1])
                g.memset(barrier_tile[:, :], 0.0).then_inc(mG.h, 1)
        YB = mG.bump(1)
        with nc.Block() as bl:
            @bl.gpsimd
            def _(g):
                g.wait_ge(mG.h, YB)
        stats3 = stats_wave(y_scr, qcol, ybf_grp, y22_rot)
        bcast3 = ln_rows_block(stats3)

        l2v = [mV.bump(1) for c in range(EC)]
        l2s = [ls_sl[c % 2].bump(16) for c in range(EC)]
        with nc.Block() as bl:
            @bl.vector
            def _(ve):
                ve.wait_ge(mT.h, bcast3)
                for c in range(EC):
                    if c >= 2:
                        ve.wait_ge(ls_sl[c % 2].h, l2s[c - 2])
                    ve.tensor_tensor(out=attn_t[:, :],
                                     in0=ybf_grp[:, c * 512:(c + 1) * 512],
                                     in1=pC[:, :], op=ALU.mult)
                    ve.tensor_tensor(out=ln2st[:, (c % 2) * 512:(c % 2 + 1) * 512],
                                     in0=attn_t[:, :], in1=pD[:, :],
                                     op=ALU.add).then_inc(mV.h, 1)

            @bl.gpsimd
            def _(g):
                for c in range(EC):
                    g.wait_ge(mV.h, l2v[c])
                    g.dma_start(out=ln2_scr[c * 128:(c + 1) * 128, qcol:qcol + 512],
                                in_=ln2st[:, (c % 2) * 512:(c % 2 + 1) * 512]
                                ).then_inc(ls_sl[c % 2].h, 16)
        p2.close()
    a2p.close()
    ap_es.close()

    hard_barrier()
    ffn = ExitStack()
    ln2b = ffn.enter_context(nc.sbuf_tensor("ln2b", [128, EC * 512], BF16))
    h_full = ffn.enter_context(nc.sbuf_tensor("h_full", [128, IC * 512], BF16))
    wgb = ffn.enter_context(nc.sbuf_tensor("wgb", [128, 2 * EC * 128], BF16))
    wub = ffn.enter_context(nc.sbuf_tensor("wub", [128, 2 * EC * 128], BF16))
    silu_t = ffn.enter_context(nc.sbuf_tensor("silu_t", [128, 512], F32))
    wdb = ffn.enter_context(nc.sbuf_tensor("wdb", [128, 2 * IC * 128], BF16))
    yres = ffn.enter_context(nc.sbuf_tensor("yres", [128, 2 * 512], F32))
    ost = ffn.enter_context(nc.sbuf_tensor("ost", [128, 2 * 512], F32))

    for hf in range(2):
        tcol = hf * 512
        PRE_T = mT.n
        l2bv = l2_sl.bump(16)
        with nc.Block() as bl:
            @bl.sync
            def _(sy):
                sy.wait_ge(mT.h, PRE_T)
                sy.dma_start(out=ln2b[:, :].rearrange("p (c t) -> p c t", t=512),
                             in_=ln2_scr[:, tcol:tcol + 512].rearrange(
                                 "(c p) t -> p c t", p=128)).then_inc(l2_sl.h, 16)

        wgld = [wg_sl[p % 2].bump(16) for p in range(IC)]
        wuld = [wu_sl[p % 2].bump(16) for p in range(IC)]
        gmm = [mT.bump(2) for p in range(IC)]
        sqv = [mS.bump(1) for p in range(IC)]
        hv = [mV.bump(1) for p in range(IC)]
        with nc.Block() as bl:
            @bl.gpsimd
            def _(g):
                for p in range(IC):
                    if p == 0 and hf == 1:
                        g.wait_ge(mT.h, PRE_T)
                    if p >= 2:
                        g.wait_ge(mT.h, gmm[p - 2])
                    g.dma_start(
                        out=wgb[:, (p % 2) * EC * 128:(p % 2 + 1) * EC * 128].rearrange(
                            "p (c n) -> p c n", n=128),
                        in_=wg_scr[:, p * 128:(p + 1) * 128].rearrange(
                            "(c kp) n -> kp c n", kp=128)).then_inc(wg_sl[p % 2].h, 16)
                    g.dma_start(
                        out=wub[:, (p % 2) * EC * 128:(p % 2 + 1) * EC * 128].rearrange(
                            "p (c n) -> p c n", n=128),
                        in_=wu_scr[:, p * 128:(p + 1) * 128].rearrange(
                            "(c kp) n -> kp c n", kp=128)).then_inc(wu_sl[p % 2].h, 16)

            @bl.tensor
            def _(te):
                te.wait_ge(l2_sl.h, l2bv)
                for p in range(IC):
                    te.wait_ge(wg_sl[p % 2].h, wgld[p])
                    te.wait_ge(wu_sl[p % 2].h, wuld[p])
                    if p >= 2:
                        te.wait_ge(mS.h, sqv[p - 2])
                        te.wait_ge(mV.h, hv[p - 2])
                    pg = pE0 if p % 2 == 0 else pE1
                    pu = pC if p % 2 == 0 else pD
                    base = (p % 2) * EC * 128
                    for c in range(EC):
                        mm = te.matmul(pg[:, :],
                                       wgb[:, base + c * 128:base + (c + 1) * 128],
                                       ln2b[:, c * 512:(c + 1) * 512],
                                       start=(c == 0), stop=(c == EC - 1),
                                       skip_group_check=True)
                        if c == EC - 1:
                            mm.then_inc(mT.h, 1)
                    for c in range(EC):
                        mm = te.matmul(pu[:, :],
                                       wub[:, base + c * 128:base + (c + 1) * 128],
                                       ln2b[:, c * 512:(c + 1) * 512],
                                       start=(c == 0), stop=(c == EC - 1),
                                       skip_group_check=True)
                        if c == EC - 1:
                            mm.then_inc(mT.h, 1)

            @bl.scalar
            def _(sc):
                for p in range(IC):
                    pg = pE0 if p % 2 == 0 else pE1
                    sc.wait_ge(mT.h, gmm[p] - 1)
                    sc.activation(out=silu_t[:, :], in_=pg[:, :],
                                  func=AF.Silu).then_inc(mS.h, 1)

            @bl.vector
            def _(ve):
                for p in range(IC):
                    pu = pC if p % 2 == 0 else pD
                    ve.wait_ge(mS.h, sqv[p])
                    ve.wait_ge(mT.h, gmm[p])
                    ve.tensor_tensor(out=h_full[:, p * 512:(p + 1) * 512],
                                     in0=silu_t[:, :], in1=pu[:, :],
                                     op=ALU.mult).then_inc(mV.h, 1)

        HIC = IC // 2
        wdld = [wd_sl[e % 2].bump(32) for e in range(EC)]
        dmm = [mT.bump(1) for e in range(EC)]
        yld = [yr_sl[e % 2].bump(32) for e in range(EC)]
        oev = [mV.bump(1) for e in range(EC)]
        osv = [os_sl[e % 2].bump(16) for e in range(EC)]
        HV_LAST = hv[-1]
        with nc.Block() as bl:
            @bl.gpsimd
            def _(g):
                g.wait_ge(mV.h, HV_LAST)
                for e in range(EC):
                    if e >= 2:
                        g.wait_ge(mT.h, dmm[e - 2])
                    for piece in range(2):
                        r0 = piece * HIC * 128
                        o0 = (e % 2) * IC * 128 + piece * HIC * 128
                        g.dma_start(
                            out=wdb[:, o0:o0 + HIC * 128].rearrange(
                                "p (ic n) -> p ic n", n=128),
                            in_=wd_scr[r0:r0 + HIC * 128,
                                       e * 128:(e + 1) * 128].rearrange(
                                "(ic p) n -> p ic n", p=128)).then_inc(wd_sl[e % 2].h, 16)

            @bl.tensor
            def _(te):
                for e in range(EC):
                    te.wait_ge(wd_sl[e % 2].h, wdld[e])
                    if e >= 2:
                        te.wait_ge(mV.h, oev[e - 2])
                    pd_ = pE0 if e % 2 == 0 else pE1
                    base = (e % 2) * IC * 128
                    for ic in range(IC):
                        mm = te.matmul(pd_[:, :],
                                       wdb[:, base + ic * 128:base + (ic + 1) * 128],
                                       h_full[:, ic * 512:(ic + 1) * 512],
                                       start=(ic == 0), stop=(ic == IC - 1),
                                       skip_group_check=True)
                        if ic == IC - 1:
                            mm.then_inc(mT.h, 1)

            @bl.sync
            def _(sy):
                for e in range(EC):
                    if e >= 2:
                        sy.wait_ge(mV.h, oev[e - 2])
                    for c_ in range(2):
                        base_el = (e * 128 + c_) * TOWN + tcol
                        in_ap = bass.AP(y_scr_t, base_el,
                                        [[2 * TOWN, 4], [8 * TOWN, 16], [1, 512]])
                        sy.dma_start(out=yres[c_ * 64:(c_ + 1) * 64,
                                              (e % 2) * 512:(e % 2 + 1) * 512],
                                     in_=in_ap).then_inc(yr_sl[e % 2].h, 16)
                    sy.wait_ge(mV.h, oev[e])
                    sy.dma_start(out=out_d[e * 128:(e + 1) * 128, tcol:tcol + 512],
                                 in_=ost[:, (e % 2) * 512:(e % 2 + 1) * 512]
                                 ).then_inc(os_sl[e % 2].h, 16)

            @bl.vector
            def _(ve):
                for e in range(EC):
                    pd_ = pE0 if e % 2 == 0 else pE1
                    ve.wait_ge(mT.h, dmm[e])
                    ve.wait_ge(yr_sl[e % 2].h, yld[e])
                    if e >= 2:
                        ve.wait_ge(os_sl[e % 2].h, osv[e - 2])
                    ve.tensor_tensor(out=ost[:, (e % 2) * 512:(e % 2 + 1) * 512],
                                     in0=pd_[:, :],
                                     in1=yres[:, (e % 2) * 512:(e % 2 + 1) * 512],
                                     op=ALU.add).then_inc(mV.h, 1)
    ffn.close()

    if dbg:
        dmp = [
            (nc.dram_tensor("dbg_k", [KV * D, S], BF16, kind="ExternalOutput").ap(), k_scr),
            (nc.dram_tensor("dbg_v", [S, KV * D], BF16, kind="ExternalOutput").ap(), v_scr),
            (nc.dram_tensor("dbg_y", [E, TOWN], F32, kind="ExternalOutput").ap(), y_scr),
            (nc.dram_tensor("dbg_ln2", [E, TOWN], BF16, kind="ExternalOutput").ap(), ln2_scr),
            (nc.dram_tensor("dbg_wg", [E, 512], BF16, kind="ExternalOutput").ap(), wg_scr[:, 0:512]),
            (nc.dram_tensor("dbg_wd", [512, E], BF16, kind="ExternalOutput").ap(), wd_scr[0:512, :]),
        ]
        snap = [(s_.h, s_.n) for s_ in ALLSEMS]
        dv = ps_sl.n
        with nc.Block() as bl:
            @bl.sync
            def _(sy):
                for hh, vv in snap:
                    sy.wait_ge(hh, vv)
                for dst, srcp in dmp:
                    sy.dma_start(out=dst, in_=srcp).then_inc(ps_sl.h, 16)
                sy.wait_ge(ps_sl.h, dv + len(dmp) * 16)
        ps_sl.bump(len(dmp) * 16)
    else:
        snap = [(s_.h, s_.n) for s_ in ALLSEMS]
        with nc.Block() as bl:
            @bl.sync
            def _(sy):
                for hh, vv in snap:
                    sy.wait_ge(hh, vv)

    pes.close()
    es.close()
    return nc


_NC_CACHE = {}


def kernel(**inputs):
    P = _perm128()
    permI = np.concatenate([q * 128 + P for q in range(IC)])
    permE = np.concatenate([q * 128 + P for q in range(EC)])

    x = np.asarray(inputs["L0_attn_x"], np.float32)
    wq = np.ascontiguousarray(np.asarray(inputs["L0_attn_wq"], np.float32))
    wk = np.ascontiguousarray(np.asarray(inputs["L0_attn_wk"], np.float32))
    wv = np.ascontiguousarray(np.asarray(inputs["L0_attn_wv"], np.float32))
    bq = np.asarray(inputs["L0_attn_bq"], np.float32).reshape(1, E)
    bk = np.asarray(inputs["L0_attn_bk"], np.float32).reshape(1, KV * D)
    bv = np.asarray(inputs["L0_attn_bv"], np.float32).reshape(1, KV * D)
    gq = np.ascontiguousarray(np.asarray(inputs["L0_ffn_gate_qweight"], np.int32))
    uq = np.ascontiguousarray(np.asarray(inputs["L0_ffn_up_qweight"], np.int32))
    dq = np.ascontiguousarray(np.asarray(inputs["L0_ffn_down_qweight"], np.int32)[permI])
    gs = np.ascontiguousarray(np.asarray(inputs["L0_ffn_gate_scales"], np.float32)[:, permI])
    us = np.ascontiguousarray(np.asarray(inputs["L0_ffn_up_scales"], np.float32)[:, permI])
    ds = np.ascontiguousarray(np.asarray(inputs["L0_ffn_down_scales"], np.float32)[:, permE])
    gz = np.ascontiguousarray(np.asarray(inputs["L0_ffn_gate_qzeros"], np.int32))
    uz = np.ascontiguousarray(np.asarray(inputs["L0_ffn_up_qzeros"], np.int32))
    dz = np.ascontiguousarray(np.asarray(inputs["L0_ffn_down_qzeros"], np.int32))

    xT = [np.ascontiguousarray(x[b].T) for b in range(B)]

    if "nc" not in _NC_CACHE:
        _NC_CACHE["nc"] = build_nc()
    nc = _NC_CACHE["nc"]

    in_maps = []
    for c in range(8):
        b, t0 = c // 4, (c % 4) * TOWN
        in_maps.append({
            "xT": xT[b],
            "xq": np.ascontiguousarray(xT[b][:, t0:t0 + TOWN]),
            "wq": wq, "wk": wk, "wv": wv,
            "bq": bq, "bk": bk, "bv": bv,
            "gq": gq, "uq": uq, "dq": dq,
            "gs": gs, "us": us, "ds": ds,
            "gz": gz, "uz": uz, "dz": dz,
            "pos0": np.full((64, 1), float(t0), np.float32),
        })

    res = run_bass_kernel_spmd(nc, in_maps, core_ids=list(range(8)))

    out = np.empty((B, S, E), np.float32)
    for c in range(8):
        b, t0 = c // 4, (c % 4) * TOWN
        dev = np.asarray(res.results[c]["out"])
        nat = np.empty_like(dev)
        nat[permE] = dev
        out[b, t0:t0 + TOWN, :] = nat.T
    return out


# version 07
# speedup vs baseline: 1.0644x; 1.0644x over previous
import math
from contextlib import ExitStack

import numpy as np
import concourse.bass as bass
import concourse.mybir as mybir
from concourse.bass_utils import run_bass_kernel_spmd

F32 = mybir.dt.float32
BF16 = mybir.dt.bfloat16
I32 = mybir.dt.int32
AF = mybir.ActivationFunctionType
ALU = mybir.AluOpType

B, S, E, H, KV, D = 2, 4096, 4096, 32, 8, 128
I = 11008
EC = E // 128
IC = I // 128
GW, DW = I // 8, E // 8
TOWN = 1024
NG = S // 512
NQ = TOWN // 512
TWO_PI = float(2 * math.pi)
AWQ_INV = [0, 2, 4, 6, 1, 3, 5, 7]


def _perm128():
    P = np.zeros(128, np.int64)
    for s_ in range(128):
        j, wl = s_ // 16, s_ % 16
        P[s_] = 8 * wl + AWQ_INV[j]
    return P


class Sem:
    def __init__(self, es, nc, name):
        self.h = es.enter_context(nc.semaphore(name))
        self.n = 0

    def bump(self, k):
        self.n += k
        return self.n


def build_nc(dbg=False):
    nc = bass.Bass("TRN2", target_bir_lowering=False)

    xT = nc.dram_tensor("xT", [E, S], F32, kind="ExternalInput").ap()
    xq = nc.dram_tensor("xq", [E, TOWN], F32, kind="ExternalInput").ap()
    wq_d = nc.dram_tensor("wq", [E, E], F32, kind="ExternalInput").ap()
    wk_d = nc.dram_tensor("wk", [E, KV * D], F32, kind="ExternalInput").ap()
    wv_d = nc.dram_tensor("wv", [E, KV * D], F32, kind="ExternalInput").ap()
    bq_d = nc.dram_tensor("bq", [1, E], F32, kind="ExternalInput").ap()
    bk_d = nc.dram_tensor("bk", [1, KV * D], F32, kind="ExternalInput").ap()
    bv_d = nc.dram_tensor("bv", [1, KV * D], F32, kind="ExternalInput").ap()
    gq_d = nc.dram_tensor("gq", [E, GW], I32, kind="ExternalInput").ap()
    uq_d = nc.dram_tensor("uq", [E, GW], I32, kind="ExternalInput").ap()
    dq_d = nc.dram_tensor("dq", [I, DW], I32, kind="ExternalInput").ap()
    gs_d = nc.dram_tensor("gs", [32, I], F32, kind="ExternalInput").ap()
    us_d = nc.dram_tensor("us", [32, I], F32, kind="ExternalInput").ap()
    ds_d = nc.dram_tensor("ds", [86, E], F32, kind="ExternalInput").ap()
    gz_d = nc.dram_tensor("gz", [32, GW], I32, kind="ExternalInput").ap()
    uz_d = nc.dram_tensor("uz", [32, GW], I32, kind="ExternalInput").ap()
    dz_d = nc.dram_tensor("dz", [86, DW], I32, kind="ExternalInput").ap()
    pos0_d = nc.dram_tensor("pos0", [64, 1], F32, kind="ExternalInput").ap()
    out_d = nc.dram_tensor("out", [E, TOWN], F32, kind="ExternalOutput").ap()

    wg_scr = nc.dram_tensor("wg_scr", [E, I], BF16, kind="Internal").ap()
    wu_scr = nc.dram_tensor("wu_scr", [E, I], BF16, kind="Internal").ap()
    wd_scr = nc.dram_tensor("wd_scr", [I, E], BF16, kind="Internal").ap()
    k_scr = nc.dram_tensor("k_scr", [KV * D, S], BF16, kind="Internal").ap()
    v_scr = nc.dram_tensor("v_scr", [S, KV * D], BF16, kind="Internal").ap()
    y_scr_t = nc.dram_tensor("y_scr", [E, TOWN], F32, kind="Internal")
    y_scr = y_scr_t.ap()
    ln2_scr = nc.dram_tensor("ln2_scr", [E, TOWN], BF16, kind="Internal").ap()

    es = ExitStack()
    ones_col = es.enter_context(nc.sbuf_tensor("ones_col", [128, 1], BF16))
    ones2 = es.enter_context(nc.sbuf_tensor("ones2", [2, 128], BF16))
    onesN = es.enter_context(nc.sbuf_tensor("onesN", [2, 512], BF16))
    eps_col = es.enter_context(nc.sbuf_tensor("eps_col", [128, 1], F32))
    barrier_tile = es.enter_context(nc.sbuf_tensor("barrier_tile", [1, 2], F32))

    mT = Sem(es, nc, "mT")
    mV = Sem(es, nc, "mV")
    mS = Sem(es, nc, "mS")
    mG = Sem(es, nc, "mG")
    ALLSEMS = [mT, mV, mS, mG]

    def mksem(name):
        s_ = Sem(es, nc, name)
        ALLSEMS.append(s_)
        return s_

    def mkslots(name, k):
        return [mksem(f"{name}{i}") for i in range(k)]

    sl_ld = mkslots("sl_ld", 2)
    sl_st = mkslots("sl_st", 2)
    ps_sl = mksem("ps_sl")
    br_sl = mksem("br_sl")
    bs_sl = mksem("bs_sl")
    xsl = mkslots("xsl", 4)
    ysl = mkslots("ysl", 4)
    la_sl = mksem("la_sl")
    lb_sl = mksem("lb_sl")
    wk_sl = mkslots("wk_sl", 2)
    wv_sl = mksem("wv_sl")
    ks_sl = mksem("ks_sl")
    vs_sl = mksem("vs_sl")
    qw_sl = mkslots("qw_sl", 2)
    kv_sl = mkslots("kv_sl", 2)
    rh_sl = mksem("rh_sl")
    xq_sl = mkslots("xq_sl", 2)
    ys_sl = mkslots("ys_sl", 2)
    l2y_sl = mkslots("l2y_sl", 2)
    ls_sl = mkslots("ls_sl", 2)
    l2_sl = mksem("l2_sl")
    wg_sl = mkslots("wg_sl", 2)
    wu_sl = mkslots("wu_sl", 2)
    wd_sl = mkslots("wd_sl", 2)
    yr_sl = mkslots("yr_sl", 2)
    os_sl = mkslots("os_sl", 2)

    pes = ExitStack()
    pA = pes.enter_context(nc.psum_tensor("pA", [128, 512], F32))
    pB = pes.enter_context(nc.psum_tensor("pB", [128, 512], F32))
    pS_ = pes.enter_context(nc.psum_tensor("pS", [1, 512], F32))
    pC = pes.enter_context(nc.psum_tensor("pC", [128, 512], F32))
    pD = pes.enter_context(nc.psum_tensor("pD", [128, 512], F32))
    pE0 = pes.enter_context(nc.psum_tensor("pE0", [128, 512], F32))
    pE1 = pes.enter_context(nc.psum_tensor("pE1", [128, 512], F32))
    pF0 = pes.enter_context(nc.psum_tensor("pF0", [128, 512], F32))

    def hard_barrier():
        snap = [(s_.h, s_.n) for s_ in ALLSEMS]
        bm = mG.n + 1
        with nc.Block() as bl:
            @bl.gpsimd
            def _(g):
                for hh, vv in snap:
                    g.wait_ge(hh, vv)
                g.memset(barrier_tile[:, :], 0.0).then_inc(mG.h, 1)

            @bl.tensor
            def _(te):
                te.wait_ge(mG.h, bm)

            @bl.vector
            def _(ve):
                ve.wait_ge(mG.h, bm)

            @bl.scalar
            def _(sc):
                sc.wait_ge(mG.h, bm)

            @bl.sync
            def _(sy):
                sy.wait_ge(mG.h, bm)
        mG.bump(1)

    with nc.Block() as bl:
        @bl.gpsimd
        def _(g):
            g.memset(ones_col[:, :], 1.0)
            g.memset(ones2[:, :], 1.0)
            g.memset(onesN[:, :], 1.0)
            g.memset(eps_col[:, :], 1e-5).then_inc(mG.h, 1)
    mG.bump(1)

    deq = ExitStack()
    NSH = I // 2
    WSH = GW // 2
    q_sb = [deq.enter_context(nc.sbuf_tensor(f"q_sb{i}", [128, WSH], I32)) for i in range(2)]
    z_sb = [deq.enter_context(nc.sbuf_tensor(f"z_sb{i}", [128, WSH], I32)) for i in range(2)]
    s_sb = [deq.enter_context(nc.sbuf_tensor(f"s_sb{i}", [128, NSH], F32)) for i in range(2)]
    wn_sb = deq.enter_context(nc.sbuf_tensor("wn_sb", [128, NSH], I32))
    zn_sb = deq.enter_context(nc.sbuf_tensor("zn_sb", [128, NSH], I32))
    db_sb = deq.enter_context(nc.sbuf_tensor("db_sb", [128, NSH], BF16))
    wo_sb = [deq.enter_context(nc.sbuf_tensor(f"wo_sb{i}", [128, NSH], BF16)) for i in range(2)]

    slices = []
    for c in range(EC):
        for hf in range(2):
            slices.append((gq_d, gz_d, gs_d, wg_scr, c, hf * WSH, hf * NSH, WSH))
    for c in range(EC):
        for hf in range(2):
            slices.append((uq_d, uz_d, us_d, wu_scr, c, hf * WSH, hf * NSH, WSH))
    for c in range(IC):
        slices.append((dq_d, dz_d, ds_d, wd_scr, c, 0, 0, DW))
    nsl = len(slices)

    ld_v = [sl_ld[i % 2].bump(48) for i in range(nsl)]
    dv_v = [mV.bump(1) for i in range(nsl)]
    st_v = [sl_st[i % 2].bump(16) for i in range(nsl)]

    with nc.Block() as bl:
        @bl.sync
        def _(sy):
            for i, (qsrc, zsrc, ssrc, dst, c, woff, noff, nw) in enumerate(slices):
                bi = i % 2
                nnib = nw * 8
                if i >= 2:
                    sy.wait_ge(mV.h, dv_v[i - 2])
                sy.dma_start(out=q_sb[bi][:, 0:nw],
                             in_=qsrc[c * 128:(c + 1) * 128, woff:woff + nw]
                             ).then_inc(sl_ld[bi].h, 16)
                sy.dma_start(out=z_sb[bi][:, 0:nw],
                             in_=zsrc[c:c + 1, woff:woff + nw].broadcast_to([128, nw])
                             ).then_inc(sl_ld[bi].h, 16)
                sy.dma_start(out=s_sb[bi][:, 0:nnib],
                             in_=ssrc[c:c + 1, noff:noff + nnib].broadcast_to([128, nnib])
                             ).then_inc(sl_ld[bi].h, 16)

        @bl.vector
        def _(ve):
            for i, (qsrc, zsrc, ssrc, dst, c, woff, noff, nw) in enumerate(slices):
                bi = i % 2
                nnib = nw * 8
                ve.wait_ge(sl_ld[bi].h, ld_v[i])
                if i >= 2:
                    ve.wait_ge(sl_st[bi].h, st_v[i - 2])
                wv3 = wn_sb[:, 0:nnib].rearrange("p (q j w) -> p q j w", j=8, w=16)
                zv3 = zn_sb[:, 0:nnib].rearrange("p (q j w) -> p q j w", j=8, w=16)
                qv3 = q_sb[bi][:, 0:nw].rearrange("p (q w) -> p q w", w=16)
                zq3 = z_sb[bi][:, 0:nw].rearrange("p (q w) -> p q w", w=16)
                for j in range(8):
                    ve.tensor_scalar(out=wv3[:, :, j, :], in0=qv3, scalar1=4 * j,
                                     scalar2=15, op0=ALU.logical_shift_right,
                                     op1=ALU.bitwise_and)
                    ve.tensor_scalar(out=zv3[:, :, j, :], in0=zq3, scalar1=4 * j,
                                     scalar2=15, op0=ALU.logical_shift_right,
                                     op1=ALU.bitwise_and)
                ve.tensor_tensor(out=db_sb[:, 0:nnib], in0=wn_sb[:, 0:nnib],
                                 in1=zn_sb[:, 0:nnib], op=ALU.subtract)
                ve.tensor_tensor(out=wo_sb[bi][:, 0:nnib], in0=db_sb[:, 0:nnib],
                                 in1=s_sb[bi][:, 0:nnib], op=ALU.mult).then_inc(mV.h, 1)

        @bl.gpsimd
        def _(g):
            for i, (qsrc, zsrc, ssrc, dst, c, woff, noff, nw) in enumerate(slices):
                bi = i % 2
                nnib = nw * 8
                g.wait_ge(mV.h, dv_v[i])
                g.dma_start(out=dst[c * 128:(c + 1) * 128, noff:noff + nnib],
                            in_=wo_sb[bi][:, 0:nnib]).then_inc(sl_st[bi].h, 16)
    deq.close()

    hard_barrier()
    ap_es = ExitStack()
    kcos = ap_es.enter_context(nc.sbuf_tensor("kcos", [64, S], F32))
    ksin = ap_es.enter_context(nc.sbuf_tensor("ksin", [64, S], F32))
    qcos = ap_es.enter_context(nc.sbuf_tensor("qcos", [64, TOWN], F32))
    qsin = ap_es.enter_context(nc.sbuf_tensor("qsin", [64, TOWN], F32))
    bqhl = ap_es.enter_context(nc.sbuf_tensor("bqhl", [2, E], BF16))
    bkhl = ap_es.enter_context(nc.sbuf_tensor("bkhl", [2, KV * D], BF16))
    bvhl = ap_es.enter_context(nc.sbuf_tensor("bvhl", [2, KV * D], BF16))
    m_r = ap_es.enter_context(nc.sbuf_tensor("m_r", [1, 512], F32))
    msq_r = ap_es.enter_context(nc.sbuf_tensor("msq_r", [1, 512], F32))
    mm_r = ap_es.enter_context(nc.sbuf_tensor("mm_r", [1, 512], F32))
    rstd_r = ap_es.enter_context(nc.sbuf_tensor("rstd_r", [1, 512], F32))
    negmr_r = ap_es.enter_context(nc.sbuf_tensor("negmr_r", [1, 512], F32))
    hi_r = ap_es.enter_context(nc.sbuf_tensor("hi_r", [1, 512], BF16))
    lo_r = ap_es.enter_context(nc.sbuf_tensor("lo_r", [1, 512], BF16))
    rhl_a = ap_es.enter_context(nc.sbuf_tensor("rhl_a", [2, 512], BF16))
    rhl_b = ap_es.enter_context(nc.sbuf_tensor("rhl_b", [2, 512], BF16))

    tb_es = ExitStack()
    t_pos = tb_es.enter_context(nc.sbuf_tensor("t_pos", [64, S], F32))
    t_ang = tb_es.enter_context(nc.sbuf_tensor("t_ang", [64, S], F32))
    t_k = tb_es.enter_context(nc.sbuf_tensor("t_k", [64, S], F32))
    t_ki = tb_es.enter_context(nc.sbuf_tensor("t_ki", [64, S], I32))
    t_pidx = tb_es.enter_context(nc.sbuf_tensor("t_pidx", [64, 1], F32))
    t_invf = tb_es.enter_context(nc.sbuf_tensor("t_invf", [64, 1], F32))
    t_pos0 = tb_es.enter_context(nc.sbuf_tensor("t_pos0", [64, 1], F32))
    t_brow = tb_es.enter_context(nc.sbuf_tensor("t_brow", [1, E], F32))
    t_bhi = tb_es.enter_context(nc.sbuf_tensor("t_bhi", [1, E], BF16))
    t_blo = tb_es.enter_context(nc.sbuf_tensor("t_blo", [1, E], BF16))

    IOTA_G = mG.n + 1
    POS0_V = ps_sl.n + 16
    INVF_S = mS.n + 1
    with nc.Block() as bl:
        @bl.gpsimd
        def _(g):
            g.iota(t_pos[:, :], [[1, S]], channel_multiplier=0,
                   allow_small_or_imprecise_dtypes=True)
            g.iota(t_pidx[:, :], [[0, 1]], channel_multiplier=1,
                   allow_small_or_imprecise_dtypes=True).then_inc(mG.h, 1)

        @bl.sync
        def _(sy):
            sy.dma_start(out=t_pos0[:, :], in_=pos0_d[:, :]).then_inc(ps_sl.h, 16)

        @bl.scalar
        def _(sc):
            sc.wait_ge(mG.h, IOTA_G)
            sc.activation(out=t_invf[:, :], in_=t_pidx[:, :], func=AF.Exp,
                          scale=float(-math.log(10000.0) / 64.0)).then_inc(mS.h, 1)
    mG.bump(1)
    ps_sl.bump(16)
    mS.bump(1)

    tables = [(ksin[:, :], S, 0.0, False), (kcos[:, :], S, math.pi / 2, False),
              (qsin[:, :], TOWN, 0.0, True), (qcos[:, :], TOWN, math.pi / 2, True)]
    tb_v = [mV.bump(1) for _ in range(4)]
    with nc.Block() as bl:
        @bl.vector
        def _(ve):
            ve.wait_ge(mS.h, INVF_S)
            ve.wait_ge(ps_sl.h, POS0_V)
            for ti, (dst, width, shift, ap0) in enumerate(tables):
                if ap0:
                    ve.tensor_scalar(out=t_ang[:, 0:width], in0=t_pos[:, 0:width],
                                     scalar1=t_pos0[:, :], scalar2=None, op0=ALU.add)
                    ve.tensor_scalar(out=t_ang[:, 0:width], in0=t_ang[:, 0:width],
                                     scalar1=t_invf[:, :], scalar2=float(shift),
                                     op0=ALU.mult, op1=ALU.add)
                else:
                    ve.tensor_scalar(out=t_ang[:, 0:width], in0=t_pos[:, 0:width],
                                     scalar1=t_invf[:, :], scalar2=float(shift),
                                     op0=ALU.mult, op1=ALU.add)
                ve.tensor_scalar(out=t_k[:, 0:width], in0=t_ang[:, 0:width],
                                 scalar1=float(1.0 / TWO_PI), scalar2=None, op0=ALU.mult)
                ve.tensor_copy(out=t_ki[:, 0:width], in_=t_k[:, 0:width])
                ve.tensor_copy(out=t_k[:, 0:width], in_=t_ki[:, 0:width])
                ve.scalar_tensor_tensor(out=dst, in0=t_k[:, 0:width],
                                        scalar=float(-TWO_PI), in1=t_ang[:, 0:width],
                                        op0=ALU.mult, op1=ALU.add).then_inc(mV.h, 1)

        @bl.scalar
        def _(sc):
            for ti, (dst, width, shift, ap0) in enumerate(tables):
                sc.wait_ge(mV.h, tb_v[ti])
                sc.activation(out=dst, in_=dst, func=AF.Sin).then_inc(mS.h, 1)
    mS.bump(4)

    bias_specs = ((bq_d, bqhl, E), (bk_d, bkhl, KV * D), (bv_d, bvhl, KV * D))
    brow_v = [br_sl.bump(16) for _ in range(3)]
    bsplit_v = [mV.bump(1) for _ in range(3)]
    bpl_v = [bs_sl.bump(32) for _ in range(3)]
    with nc.Block() as bl:
        @bl.sync
        def _(sy):
            for i, (src, dsttile, wdt) in enumerate(bias_specs):
                if i > 0:
                    sy.wait_ge(bs_sl.h, bpl_v[i - 1])
                sy.dma_start(out=t_brow[:, 0:wdt], in_=src[:, :]).then_inc(br_sl.h, 16)

        @bl.vector
        def _(ve):
            for i, (src, dsttile, wdt) in enumerate(bias_specs):
                ve.wait_ge(br_sl.h, brow_v[i])
                if i > 0:
                    ve.wait_ge(bs_sl.h, bpl_v[i - 1])
                ve.tensor_copy(out=t_bhi[:, 0:wdt], in_=t_brow[:, 0:wdt])
                ve.tensor_tensor(out=t_blo[:, 0:wdt], in0=t_brow[:, 0:wdt],
                                 in1=t_bhi[:, 0:wdt], op=ALU.subtract).then_inc(mV.h, 1)

        @bl.gpsimd
        def _(g):
            for i, (src, dsttile, wdt) in enumerate(bias_specs):
                g.wait_ge(mV.h, bsplit_v[i])
                g.dma_start(out=dsttile[0:1, 0:wdt], in_=t_bhi[:, 0:wdt]).then_inc(bs_sl.h, 16)
                g.dma_start(out=dsttile[1:2, 0:wdt], in_=t_blo[:, 0:wdt]).then_inc(bs_sl.h, 16)
    tb_es.close()
    hard_barrier()

    def ln_rows_block(stats_ms):
        Va = mV.bump(1)
        Sa = mS.bump(1)
        Vb = mV.bump(1)
        Ga = la_sl.bump(32)
        Vc = mV.bump(1)
        Gb = lb_sl.bump(32)
        Ta = mT.bump(1)
        with nc.Block() as bl:
            @bl.vector
            def _(ve):
                ve.wait_ge(mT.h, stats_ms)
                ve.tensor_scalar(out=m_r[:, :], in0=pA[0:1, :], scalar1=float(1.0 / 4096.0),
                                 scalar2=None, op0=ALU.mult)
                ve.tensor_scalar(out=msq_r[:, :], in0=pB[0:1, :], scalar1=float(1.0 / 4096.0),
                                 scalar2=None, op0=ALU.mult)
                ve.tensor_tensor(out=mm_r[:, :], in0=m_r[:, :], in1=m_r[:, :], op=ALU.mult)
                ve.tensor_tensor(out=msq_r[:, :], in0=msq_r[:, :], in1=mm_r[:, :],
                                 op=ALU.subtract).then_inc(mV.h, 1)
                ve.wait_ge(mS.h, Sa)
                ve.reciprocal(out=rstd_r[:, :], in_=rstd_r[:, :])
                ve.scalar_tensor_tensor(out=negmr_r[:, :], in0=m_r[:, :], scalar=-1.0,
                                        in1=rstd_r[:, :], op0=ALU.mult, op1=ALU.mult)
                ve.tensor_copy(out=hi_r[:, :], in_=rstd_r[:, :])
                ve.tensor_tensor(out=lo_r[:, :], in0=rstd_r[:, :], in1=hi_r[:, :],
                                 op=ALU.subtract).then_inc(mV.h, 1)
                ve.wait_ge(la_sl.h, Ga)
                ve.tensor_copy(out=hi_r[:, :], in_=negmr_r[:, :])
                ve.tensor_tensor(out=lo_r[:, :], in0=negmr_r[:, :], in1=hi_r[:, :],
                                 op=ALU.subtract).then_inc(mV.h, 1)

            @bl.scalar
            def _(sc):
                sc.wait_ge(mV.h, Va)
                sc.activation(out=rstd_r[:, :], in_=msq_r[:, :], func=AF.Sqrt,
                              bias=eps_col[0:1, :]).then_inc(mS.h, 1)

            @bl.gpsimd
            def _(g):
                g.wait_ge(mV.h, Vb)
                g.dma_start(out=rhl_a[0:1, :], in_=hi_r[:, :]).then_inc(la_sl.h, 16)
                g.dma_start(out=rhl_a[1:2, :], in_=lo_r[:, :]).then_inc(la_sl.h, 16)
                g.wait_ge(mV.h, Vc)
                g.dma_start(out=rhl_b[0:1, :], in_=hi_r[:, :]).then_inc(lb_sl.h, 16)
                g.dma_start(out=rhl_b[1:2, :], in_=lo_r[:, :]).then_inc(lb_sl.h, 16)

            @bl.tensor
            def _(te):
                te.wait_ge(la_sl.h, Ga)
                te.wait_ge(lb_sl.h, Gb)
                te.matmul(pC[:, :], ones2[:, :], rhl_a[:, :], start=True, stop=True,
                          skip_group_check=True)
                te.matmul(pD[:, :], ones2[:, :], rhl_b[:, :], start=True, stop=True,
                          skip_group_check=True).then_inc(mT.h, 1)
        return Ta

    def stats_wave(src_ap, col0, xbf_grp, x2_t, dma_guard=None, act_guard=None):
        xld = [xsl[c % 4].bump(16) for c in range(EC)]
        sqv = [mS.bump(1) for c in range(EC)]
        stv = [mT.bump(1) for c in range(EC)]
        with nc.Block() as bl:
            @bl.gpsimd
            def _(g):
                for c in range(EC):
                    if c == 0 and dma_guard is not None:
                        g.wait_ge(dma_guard[0], dma_guard[1])
                    if c >= 4:
                        g.wait_ge(mS.h, sqv[c - 4])
                    g.dma_start(out=xbf_grp[:, c * 512:(c + 1) * 512],
                                in_=src_ap[c * 128:(c + 1) * 128, col0:col0 + 512]
                                ).then_inc(xsl[c % 4].h, 16)

            @bl.scalar
            def _(sc):
                for c in range(EC):
                    sc.wait_ge(xsl[c % 4].h, xld[c])
                    if c == 0 and act_guard is not None:
                        sc.wait_ge(act_guard[0], act_guard[1])
                    if c >= 4:
                        sc.wait_ge(mT.h, stv[c - 4])
                    sc.activation(out=x2_t[:, (c % 4) * 512:(c % 4 + 1) * 512],
                                  in_=xbf_grp[:, c * 512:(c + 1) * 512],
                                  func=AF.Square).then_inc(mS.h, 1)

            @bl.tensor
            def _(te):
                for c in range(EC):
                    te.wait_ge(mS.h, sqv[c])
                    te.matmul(pA[0:1, :], ones_col[:, :],
                              xbf_grp[:, c * 512:(c + 1) * 512],
                              start=(c == 0), stop=(c == EC - 1), skip_group_check=True)
                    te.matmul(pB[0:1, :], ones_col[:, :],
                              x2_t[:, (c % 4) * 512:(c % 4 + 1) * 512],
                              start=(c == 0), stop=(c == EC - 1),
                              skip_group_check=True).then_inc(mT.h, 1)
        return stv[-1]

    def norm_wave(xbf_grp, dst, bcast_ms, tn_t):
        nv = [mV.bump(1) for c in range(EC)]
        with nc.Block() as bl:
            @bl.vector
            def _(ve):
                ve.wait_ge(mT.h, bcast_ms)
                for c in range(EC):
                    ve.tensor_tensor(out=tn_t[:, :],
                                     in0=xbf_grp[:, c * 512:(c + 1) * 512],
                                     in1=pC[:, :], op=ALU.mult)
                    ve.tensor_tensor(out=dst[:, c * 512:(c + 1) * 512], in0=tn_t[:, :],
                                     in1=pD[:, :], op=ALU.add).then_inc(mV.h, 1)
        return nv[-1]

    def rope_ops(ve, psrc, cos_t, sin_t, col0, dst, dcol, rp):
        ve.tensor_tensor(out=rp[0][:, :], in0=psrc[0:64, :],
                         in1=cos_t[:, col0:col0 + 512], op=ALU.mult)
        ve.tensor_tensor(out=rp[1][:, :], in0=psrc[64:128, :],
                         in1=sin_t[:, col0:col0 + 512], op=ALU.mult)
        ve.tensor_tensor(out=dst[0:64, dcol:dcol + 512], in0=rp[0][:, :],
                         in1=rp[1][:, :], op=ALU.subtract)
        ve.tensor_tensor(out=rp[0][:, :], in0=psrc[0:64, :],
                         in1=sin_t[:, col0:col0 + 512], op=ALU.mult)
        ve.tensor_tensor(out=rp[1][:, :], in0=psrc[64:128, :],
                         in1=cos_t[:, col0:col0 + 512], op=ALU.mult)
        ve.tensor_tensor(out=dst[64:128, dcol:dcol + 512], in0=rp[0][:, :],
                         in1=rp[1][:, :], op=ALU.add).then_inc(mV.h, 1)

    a1 = ExitStack()
    xln = a1.enter_context(nc.sbuf_tensor("xln", [128, EC * 512], BF16))
    xbf_grp = a1.enter_context(nc.sbuf_tensor("xbf_grp", [128, EC * 512], BF16))
    x2_rot = a1.enter_context(nc.sbuf_tensor("x2_rot", [128, 4 * 512], BF16))
    tnorm = a1.enter_context(nc.sbuf_tensor("tnorm", [128, 512], F32))
    wvb = a1.enter_context(nc.sbuf_tensor("wvb", [128, EC * 512], BF16))
    wkb = a1.enter_context(nc.sbuf_tensor("wkb", [128, 2 * EC * 128], BF16))
    kst = a1.enter_context(nc.sbuf_tensor("kst", [128, 8 * 512], BF16))
    vst = a1.enter_context(nc.sbuf_tensor("vst", [128, 4 * 1024], BF16))
    rp1 = [a1.enter_context(nc.sbuf_tensor(f"rp1_{i}", [64, 512], F32)) for i in range(2)]

    ks_done = 0
    vs_done = 0
    prev_norm = None
    prev_stats = None
    prev_vmm_last = None
    hard_barrier()
    for g_i in range(NG):
        col0 = g_i * 512
        stats_ms = stats_wave(
            xT, col0, xbf_grp, x2_rot,
            dma_guard=(mV.h, prev_norm) if prev_norm is not None else None,
            act_guard=(mT.h, prev_stats) if prev_stats is not None else None)
        bcast_ms = ln_rows_block(stats_ms)
        norm_ms = norm_wave(xbf_grp, xln, bcast_ms, tnorm)
        prev_norm = norm_ms
        prev_stats = stats_ms

        prev_ks = ks_done
        wkld = [wk_sl[h % 2].bump(16) for h in range(KV)]
        kmm = [mT.bump(1) for h in range(KV)]
        krp = [mV.bump(1) for h in range(KV)]
        with nc.Block() as bl:
            @bl.gpsimd
            def _(g):
                for h in range(KV):
                    if h >= 2:
                        g.wait_ge(mT.h, kmm[h - 2])
                    g.dma_start(
                        out=wkb[:, (h % 2) * EC * 128:(h % 2 + 1) * EC * 128].rearrange(
                            "p (c n) -> p c n", n=128),
                        in_=wk_d[:, h * 128:(h + 1) * 128].rearrange(
                            "(c p) n -> p c n", p=128)).then_inc(wk_sl[h % 2].h, 16)

            @bl.tensor
            def _(te):
                te.wait_ge(mV.h, norm_ms)
                for h in range(KV):
                    te.wait_ge(wk_sl[h % 2].h, wkld[h])
                    if h >= 2:
                        te.wait_ge(mV.h, krp[h - 2])
                    pk = pE0 if h % 2 == 0 else pE1
                    base = (h % 2) * EC * 128
                    for c in range(EC):
                        te.matmul(pk[:, :], wkb[:, base + c * 128:base + (c + 1) * 128],
                                  xln[:, c * 512:(c + 1) * 512],
                                  start=(c == 0), stop=False, skip_group_check=True)
                    te.matmul(pk[:, :], bkhl[:, h * 128:(h + 1) * 128], onesN[:, :],
                              start=False, stop=True,
                              skip_group_check=True).then_inc(mT.h, 1)

            @bl.vector
            def _(ve):
                for h in range(KV):
                    ve.wait_ge(mT.h, kmm[h])
                    if h == 0 and prev_ks:
                        ve.wait_ge(ks_sl.h, prev_ks)
                    pk = pE0 if h % 2 == 0 else pE1
                    rope_ops(ve, pk, kcos, ksin, col0, kst, h * 512, rp1)

            @bl.sync
            def _(sy):
                for h in range(KV):
                    sy.wait_ge(mV.h, krp[h])
                    sy.dma_start(out=k_scr[h * 128:(h + 1) * 128, col0:col0 + 512],
                                 in_=kst[:, h * 512:(h + 1) * 512]).then_inc(ks_sl.h, 16)
        ks_done = ks_sl.bump(KV * 16)
        ROPES_V = mV.n

        prev_vs = vs_done
        for half in range(2):
            PRE_S = mS.n
            wv_guard = prev_vmm_last
            wvld = wv_sl.bump(16)
            vmm = [mT.bump(1) for tt in range(4)]
            vev = [mS.bump(1) for tt in range(4)]
            prev_vmm_last = vmm[-1]
            with nc.Block() as bl:
                @bl.gpsimd
                def _(g):
                    if wv_guard is not None:
                        g.wait_ge(mT.h, wv_guard)
                    g.dma_start(
                        out=wvb[:, :].rearrange("p (c n) -> p c n", n=512),
                        in_=wv_d[:, half * 512:(half + 1) * 512].rearrange(
                            "(c p) n -> p c n", p=128)).then_inc(wv_sl.h, 16)

                @bl.tensor
                def _(te):
                    te.wait_ge(wv_sl.h, wvld)
                    te.wait_ge(mS.h, PRE_S)
                    if half == 0:
                        te.wait_ge(mV.h, ROPES_V)
                    for tt in range(4):
                        pv = pF0 if tt % 2 == 0 else pE0
                        if tt >= 2:
                            te.wait_ge(mS.h, vev[tt - 2])
                        for c in range(EC):
                            te.matmul(pv[:, :],
                                      xln[:, c * 512 + tt * 128:c * 512 + tt * 128 + 128],
                                      wvb[:, c * 512:(c + 1) * 512],
                                      start=(c == 0), stop=False, skip_group_check=True)
                        te.matmul(pv[:, :], ones2[:, :],
                                  bvhl[:, half * 512:(half + 1) * 512],
                                  start=False, stop=True,
                                  skip_group_check=True).then_inc(mT.h, 1)

                @bl.scalar
                def _(sc):
                    for tt in range(4):
                        pv = pF0 if tt % 2 == 0 else pE0
                        sc.wait_ge(mT.h, vmm[tt])
                        if tt == 0 and half == 0 and prev_vs:
                            sc.wait_ge(vs_sl.h, prev_vs)
                        sc.activation(out=vst[:, tt * 1024 + half * 512:
                                              tt * 1024 + half * 512 + 512],
                                      in_=pv[:, :], func=AF.Copy).then_inc(mS.h, 1)
        VS_ALL = mS.n
        with nc.Block() as bl:
            @bl.sync
            def _(sy):
                sy.wait_ge(mS.h, VS_ALL)
                for tt in range(4):
                    sy.dma_start(out=v_scr[col0 + tt * 128:col0 + tt * 128 + 128, :],
                                 in_=vst[:, tt * 1024:(tt + 1) * 1024]).then_inc(vs_sl.h, 16)
        vs_done = vs_sl.bump(4 * 16)
    A1_KS = ks_sl.n
    A1_VS = vs_sl.n
    a1.close()

    a2p = ExitStack()
    q_tile = a2p.enter_context(nc.sbuf_tensor("q_tile", [128, EC * 512], BF16))
    for qg in range(NQ):
        qcol = qg * 512
        hard_barrier()
        p1 = ExitStack()
        xlnq = p1.enter_context(nc.sbuf_tensor(f"xlnq{qg}", [128, EC * 512], BF16))
        wqb = p1.enter_context(nc.sbuf_tensor(f"wqb{qg}", [128, 2 * EC * 128], BF16))
        xbf2 = p1.enter_context(nc.sbuf_tensor(f"xbf2{qg}", [128, EC * 512], BF16))
        x22 = p1.enter_context(nc.sbuf_tensor(f"x22{qg}", [128, 4 * 512], BF16))
        tn2 = p1.enter_context(nc.sbuf_tensor(f"tn2{qg}", [128, 512], F32))
        rp2 = [p1.enter_context(nc.sbuf_tensor(f"rp2_{qg}_{i}", [64, 512], F32))
               for i in range(2)]

        stats2 = stats_wave(xq, qcol, xbf2, x22)
        bcast2 = ln_rows_block(stats2)
        normq = norm_wave(xbf2, xlnq, bcast2, tn2)

        qwld = [qw_sl[m % 2].bump(16) for m in range(EC)]
        qmm = [mT.bump(1) for m in range(EC)]
        qrp = [mV.bump(1) for m in range(EC)]
        with nc.Block() as bl:
            @bl.gpsimd
            def _(g):
                for m in range(EC):
                    if m >= 2:
                        g.wait_ge(mT.h, qmm[m - 2])
                    g.dma_start(
                        out=wqb[:, (m % 2) * EC * 128:(m % 2 + 1) * EC * 128].rearrange(
                            "p (c n) -> p c n", n=128),
                        in_=wq_d[:, m * 128:(m + 1) * 128].rearrange(
                            "(c p) n -> p c n", p=128)).then_inc(qw_sl[m % 2].h, 16)

            @bl.tensor
            def _(te):
                te.wait_ge(mV.h, normq)
                for m in range(EC):
                    te.wait_ge(qw_sl[m % 2].h, qwld[m])
                    if m >= 2:
                        te.wait_ge(mV.h, qrp[m - 2])
                    pq = pE0 if m % 2 == 0 else pE1
                    base = (m % 2) * EC * 128
                    for c in range(EC):
                        te.matmul(pq[:, :], wqb[:, base + c * 128:base + (c + 1) * 128],
                                  xlnq[:, c * 512:(c + 1) * 512],
                                  start=(c == 0), stop=False, skip_group_check=True)
                    te.matmul(pq[:, :], bqhl[:, m * 128:(m + 1) * 128], onesN[:, :],
                              start=False, stop=True,
                              skip_group_check=True).then_inc(mT.h, 1)

            @bl.vector
            def _(ve):
                for m in range(EC):
                    ve.wait_ge(mT.h, qmm[m])
                    pq = pE0 if m % 2 == 0 else pE1
                    rope_ops(ve, pq, qcos, qsin, qcol, q_tile, m * 512, rp2)
        p1.close()
        hard_barrier()

        p2 = ExitStack()
        khb = p2.enter_context(nc.sbuf_tensor(f"khb{qg}", [128, 2 * S], BF16))
        vhb = p2.enter_context(nc.sbuf_tensor(f"vhb{qg}", [128, 2 * EC * 128], BF16))
        expb = p2.enter_context(nc.sbuf_tensor(f"expb{qg}", [128, 4 * 512], BF16))
        srecip = p2.enter_context(nc.sbuf_tensor(f"srecip{qg}", [128, 512], F32))
        attn_t = p2.enter_context(nc.sbuf_tensor(f"attn_t{qg}", [128, 512], F32))
        yst = p2.enter_context(nc.sbuf_tensor(f"yst{qg}", [128, 2 * 512], F32))
        ybf_grp = p2.enter_context(nc.sbuf_tensor(f"ybfg{qg}", [128, EC * 512], BF16))
        y22_rot = p2.enter_context(nc.sbuf_tensor(f"y22r{qg}", [128, 4 * 512], BF16))
        sacc = p2.enter_context(nc.sbuf_tensor(f"sacc{qg}", [128, 512], F32))
        saccbf = p2.enter_context(nc.sbuf_tensor(f"saccbf{qg}", [128, 512], BF16))
        xqr = p2.enter_context(nc.sbuf_tensor(f"xqr{qg}", [128, 2 * 512], F32))
        ln2st = p2.enter_context(nc.sbuf_tensor(f"ln2st{qg}", [128, 2 * 512], BF16))
        rrow = p2.enter_context(nc.sbuf_tensor(f"rrow{qg}", [1, 512], F32))
        rhi = p2.enter_context(nc.sbuf_tensor(f"rhi{qg}", [1, 512], BF16))
        rlo = p2.enter_context(nc.sbuf_tensor(f"rlo{qg}", [1, 512], BF16))
        rhl = p2.enter_context(nc.sbuf_tensor(f"rhl{qg}", [2, 512], BF16))

        evac_ms = {}
        ystore_v = {}
        h_mm_done = {}
        last_exv = None
        prev_Vr = None
        A2S_T = mT.n
        A2S_V = mV.n
        QRP_LAST = qrp[-1]
        for h in range(KV):
            kvld = kv_sl[h % 2].bump(32)
            with nc.Block() as bl:
                @bl.sync
                def _(sy):
                    if h == 0:
                        sy.wait_ge(mT.h, A2S_T)
                        sy.wait_ge(mV.h, A2S_V)
                        if qg == 0:
                            sy.wait_ge(ks_sl.h, A1_KS)
                            sy.wait_ge(vs_sl.h, A1_VS)
                    if h >= 2:
                        sy.wait_ge(mT.h, h_mm_done[h - 2])
                    sy.dma_start(out=khb[:, (h % 2) * S:(h % 2 + 1) * S],
                                 in_=k_scr[h * 128:(h + 1) * 128, :]
                                 ).then_inc(kv_sl[h % 2].h, 16)
                    sy.dma_start(
                        out=vhb[:, (h % 2) * EC * 128:(h % 2 + 1) * EC * 128].rearrange(
                            "p (kc n) -> p kc n", n=128),
                        in_=v_scr[:, h * 128:(h + 1) * 128].rearrange(
                            "(kc kp) n -> kp kc n", kp=128)).then_inc(kv_sl[h % 2].h, 16)
            kb = (h % 2) * S
            vb = (h % 2) * EC * 128

            for sub in range(4):
                qh = h * 4 + sub
                pAT = pF0 if qh % 2 == 0 else pD
                SCB = [pE0, pE1, pA, pB]
                exv = [mS.n + kc + 1 for kc in range(32)]
                svv = [mG.n + kc + 1 for kc in range(32)]
                scv = {}
                pvv = {}
                t_ = mT.n
                for kc in range(4):
                    t_ += 1
                    scv[kc] = t_
                for kc in range(32):
                    t_ += 1
                    pvv[kc] = t_
                    if kc + 4 < 32:
                        t_ += 1
                        scv[kc + 4] = t_
                mT.n = t_

                with nc.Block() as bl:
                    @bl.tensor
                    def _(te):
                        te.wait_ge(kv_sl[h % 2].h, kvld)
                        if qh == 0:
                            te.wait_ge(mV.h, QRP_LAST)

                        def sc_mm(kc):
                            psc = SCB[kc % 4]
                            if kc >= 4:
                                te.wait_ge(mS.h, exv[kc - 4])
                            elif last_exv is not None:
                                te.wait_ge(mS.h, last_exv[28 + kc])
                            te.matmul(psc[:, :],
                                      khb[:, kb + kc * 128:kb + (kc + 1) * 128],
                                      q_tile[:, qh * 512:(qh + 1) * 512],
                                      start=True, stop=True,
                                      skip_group_check=True).then_inc(mT.h, 1)
                        for kc in range(4):
                            sc_mm(kc)
                        for kc in range(32):
                            te.wait_ge(mS.h, exv[kc])
                            if kc == 0 and qh >= 2:
                                te.wait_ge(mV.h, evac_ms[qh - 2])
                            esl = slice((kc % 4) * 512, (kc % 4 + 1) * 512)
                            te.matmul(pAT[:, :],
                                      vhb[:, vb + kc * 128:vb + (kc + 1) * 128],
                                      expb[:, esl], start=(kc == 0), stop=(kc == 31),
                                      skip_group_check=True).then_inc(mT.h, 1)
                            if kc + 4 < 32:
                                sc_mm(kc + 4)

                    @bl.scalar
                    def _(sc):
                        for kc in range(32):
                            psc = SCB[kc % 4]
                            sc.wait_ge(mT.h, scv[kc])
                            if kc >= 4:
                                sc.wait_ge(mT.h, pvv[kc - 4])
                                sc.wait_ge(mG.h, svv[kc - 4])
                            sc.activation(out=expb[:, (kc % 4) * 512:(kc % 4 + 1) * 512],
                                          in_=psc[:, :], func=AF.Exp,
                                          scale=float(1.0 / math.sqrt(D))
                                          ).then_inc(mS.h, 1)

                    @bl.gpsimd
                    def _(g):
                        for kc in range(32):
                            g.wait_ge(mS.h, exv[kc])
                            esl = slice((kc % 4) * 512, (kc % 4 + 1) * 512)
                            if kc == 0:
                                g.tensor_copy(out=sacc[:, :],
                                              in_=expb[:, esl]).then_inc(mG.h, 1)
                            else:
                                g.tensor_tensor(out=sacc[:, :], in0=sacc[:, :],
                                                in1=expb[:, esl],
                                                op=ALU.add).then_inc(mG.h, 1)
                        g.tensor_copy(out=saccbf[:, :], in_=sacc[:, :]).then_inc(mG.h, 1)
                mS.bump(32)
                mG.bump(32)
                Vcast = mG.bump(1)

                Tsm = mT.bump(1)
                Vr = mV.bump(1)
                Ghl = rh_sl.bump(32)
                Tb = mT.bump(1)
                Lq = xq_sl[qh % 2].bump(16)
                Ve = mV.bump(1)
                Qs = ys_sl[qh % 2].bump(16)
                with nc.Block() as bl:
                    @bl.tensor
                    def _(te):
                        te.wait_ge(mG.h, Vcast)
                        if prev_Vr is not None:
                            te.wait_ge(mV.h, prev_Vr)
                        te.matmul(pS_[:, :], ones_col[:, :], saccbf[:, :],
                                  start=True, stop=True,
                                  skip_group_check=True).then_inc(mT.h, 1)
                        te.wait_ge(rh_sl.h, Ghl)
                        te.matmul(pC[:, :], ones2[:, :], rhl[:, :], start=True,
                                  stop=True, skip_group_check=True).then_inc(mT.h, 1)

                    @bl.vector
                    def _(ve):
                        ve.wait_ge(mT.h, Tsm)
                        ve.reciprocal(out=rrow[:, :], in_=pS_[:, :])
                        ve.tensor_copy(out=rhi[:, :], in_=rrow[:, :])
                        ve.tensor_tensor(out=rlo[:, :], in0=rrow[:, :], in1=rhi[:, :],
                                         op=ALU.subtract).then_inc(mV.h, 1)
                        ve.wait_ge(mT.h, Tb)
                        ve.tensor_copy(out=srecip[:, :], in_=pC[:, :])
                        ve.wait_ge(xq_sl[qh % 2].h, Lq)
                        if qh >= 2:
                            ve.wait_ge(ys_sl[qh % 2].h, ystore_v[qh - 2])
                        ve.tensor_tensor(out=attn_t[:, :], in0=pAT[:, :],
                                         in1=srecip[:, :], op=ALU.mult)
                        ve.tensor_tensor(out=yst[:, (qh % 2) * 512:(qh % 2 + 1) * 512],
                                         in0=attn_t[:, :],
                                         in1=xqr[:, (qh % 2) * 512:(qh % 2 + 1) * 512],
                                         op=ALU.add).then_inc(mV.h, 1)

                    @bl.gpsimd
                    def _(g):
                        g.wait_ge(mV.h, Vr)
                        g.dma_start(out=rhl[0:1, :], in_=rhi[:, :]).then_inc(rh_sl.h, 16)
                        g.dma_start(out=rhl[1:2, :], in_=rlo[:, :]).then_inc(rh_sl.h, 16)

                    @bl.sync
                    def _(sy):
                        if qh >= 2:
                            sy.wait_ge(mV.h, evac_ms[qh - 2])
                        sy.dma_start(out=xqr[:, (qh % 2) * 512:(qh % 2 + 1) * 512],
                                     in_=xq[qh * 128:(qh + 1) * 128, qcol:qcol + 512]
                                     ).then_inc(xq_sl[qh % 2].h, 16)
                        sy.wait_ge(mV.h, Ve)
                        sy.dma_start(out=y_scr[qh * 128:(qh + 1) * 128, qcol:qcol + 512],
                                     in_=yst[:, (qh % 2) * 512:(qh % 2 + 1) * 512]
                                     ).then_inc(ys_sl[qh % 2].h, 16)
                evac_ms[qh] = Ve
                ystore_v[qh] = Qs
                last_exv = exv
                prev_Vr = Vr
            h_mm_done[h] = mT.n

        YALL0 = [ys_sl[0].n, ys_sl[1].n]
        with nc.Block() as bl:
            @bl.gpsimd
            def _(g):
                g.wait_ge(ys_sl[0].h, YALL0[0])
                g.wait_ge(ys_sl[1].h, YALL0[1])
                g.memset(barrier_tile[:, :], 0.0).then_inc(mG.h, 1)
        YB = mG.bump(1)
        with nc.Block() as bl:
            @bl.gpsimd
            def _(g):
                g.wait_ge(mG.h, YB)
        stats3 = stats_wave(y_scr, qcol, ybf_grp, y22_rot)
        bcast3 = ln_rows_block(stats3)

        l2v = [mV.bump(1) for c in range(EC)]
        l2s = [ls_sl[c % 2].bump(16) for c in range(EC)]
        with nc.Block() as bl:
            @bl.vector
            def _(ve):
                ve.wait_ge(mT.h, bcast3)
                for c in range(EC):
                    if c >= 2:
                        ve.wait_ge(ls_sl[c % 2].h, l2s[c - 2])
                    ve.tensor_tensor(out=attn_t[:, :],
                                     in0=ybf_grp[:, c * 512:(c + 1) * 512],
                                     in1=pC[:, :], op=ALU.mult)
                    ve.tensor_tensor(out=ln2st[:, (c % 2) * 512:(c % 2 + 1) * 512],
                                     in0=attn_t[:, :], in1=pD[:, :],
                                     op=ALU.add).then_inc(mV.h, 1)

            @bl.gpsimd
            def _(g):
                for c in range(EC):
                    g.wait_ge(mV.h, l2v[c])
                    g.dma_start(out=ln2_scr[c * 128:(c + 1) * 128, qcol:qcol + 512],
                                in_=ln2st[:, (c % 2) * 512:(c % 2 + 1) * 512]
                                ).then_inc(ls_sl[c % 2].h, 16)
        p2.close()
    a2p.close()
    ap_es.close()

    hard_barrier()
    ffn = ExitStack()
    ln2b = ffn.enter_context(nc.sbuf_tensor("ln2b", [128, EC * 512], BF16))
    h_full = ffn.enter_context(nc.sbuf_tensor("h_full", [128, IC * 512], BF16))
    wgb = ffn.enter_context(nc.sbuf_tensor("wgb", [128, 2 * EC * 128], BF16))
    wub = ffn.enter_context(nc.sbuf_tensor("wub", [128, 2 * EC * 128], BF16))
    silu_t = ffn.enter_context(nc.sbuf_tensor("silu_t", [128, 512], F32))
    wdb = ffn.enter_context(nc.sbuf_tensor("wdb", [128, 2 * IC * 128], BF16))
    yres = ffn.enter_context(nc.sbuf_tensor("yres", [128, 2 * 512], F32))
    ost = ffn.enter_context(nc.sbuf_tensor("ost", [128, 2 * 512], F32))

    for hf in range(2):
        tcol = hf * 512
        PRE_T = mT.n
        l2bv = l2_sl.bump(16)
        with nc.Block() as bl:
            @bl.sync
            def _(sy):
                sy.wait_ge(mT.h, PRE_T)
                sy.dma_start(out=ln2b[:, :].rearrange("p (c t) -> p c t", t=512),
                             in_=ln2_scr[:, tcol:tcol + 512].rearrange(
                                 "(c p) t -> p c t", p=128)).then_inc(l2_sl.h, 16)

        wgld = [wg_sl[p % 2].bump(16) for p in range(IC)]
        wuld = [wu_sl[p % 2].bump(16) for p in range(IC)]
        gmm = [mT.bump(2) for p in range(IC)]
        sqv = [mS.bump(1) for p in range(IC)]
        hv = [mV.bump(1) for p in range(IC)]
        with nc.Block() as bl:
            @bl.gpsimd
            def _(g):
                for p in range(IC):
                    if p == 0 and hf == 1:
                        g.wait_ge(mT.h, PRE_T)
                    if p >= 2:
                        g.wait_ge(mT.h, gmm[p - 2])
                    g.dma_start(
                        out=wgb[:, (p % 2) * EC * 128:(p % 2 + 1) * EC * 128].rearrange(
                            "p (c n) -> p c n", n=128),
                        in_=wg_scr[:, p * 128:(p + 1) * 128].rearrange(
                            "(c kp) n -> kp c n", kp=128)).then_inc(wg_sl[p % 2].h, 16)
                    g.dma_start(
                        out=wub[:, (p % 2) * EC * 128:(p % 2 + 1) * EC * 128].rearrange(
                            "p (c n) -> p c n", n=128),
                        in_=wu_scr[:, p * 128:(p + 1) * 128].rearrange(
                            "(c kp) n -> kp c n", kp=128)).then_inc(wu_sl[p % 2].h, 16)

            @bl.tensor
            def _(te):
                te.wait_ge(l2_sl.h, l2bv)
                for p in range(IC):
                    te.wait_ge(wg_sl[p % 2].h, wgld[p])
                    te.wait_ge(wu_sl[p % 2].h, wuld[p])
                    if p >= 2:
                        te.wait_ge(mS.h, sqv[p - 2])
                        te.wait_ge(mV.h, hv[p - 2])
                    pg = pE0 if p % 2 == 0 else pE1
                    pu = pC if p % 2 == 0 else pD
                    base = (p % 2) * EC * 128
                    for c in range(EC):
                        mm = te.matmul(pg[:, :],
                                       wgb[:, base + c * 128:base + (c + 1) * 128],
                                       ln2b[:, c * 512:(c + 1) * 512],
                                       start=(c == 0), stop=(c == EC - 1),
                                       skip_group_check=True)
                        if c == EC - 1:
                            mm.then_inc(mT.h, 1)
                    for c in range(EC):
                        mm = te.matmul(pu[:, :],
                                       wub[:, base + c * 128:base + (c + 1) * 128],
                                       ln2b[:, c * 512:(c + 1) * 512],
                                       start=(c == 0), stop=(c == EC - 1),
                                       skip_group_check=True)
                        if c == EC - 1:
                            mm.then_inc(mT.h, 1)

            @bl.scalar
            def _(sc):
                for p in range(IC):
                    pg = pE0 if p % 2 == 0 else pE1
                    sc.wait_ge(mT.h, gmm[p] - 1)
                    sc.activation(out=silu_t[:, :], in_=pg[:, :],
                                  func=AF.Silu).then_inc(mS.h, 1)

            @bl.vector
            def _(ve):
                for p in range(IC):
                    pu = pC if p % 2 == 0 else pD
                    ve.wait_ge(mS.h, sqv[p])
                    ve.wait_ge(mT.h, gmm[p])
                    ve.tensor_tensor(out=h_full[:, p * 512:(p + 1) * 512],
                                     in0=silu_t[:, :], in1=pu[:, :],
                                     op=ALU.mult).then_inc(mV.h, 1)

        HIC = IC // 2
        wdld = [wd_sl[e % 2].bump(32) for e in range(EC)]
        dmm = [mT.bump(1) for e in range(EC)]
        yld = [yr_sl[e % 2].bump(32) for e in range(EC)]
        oev = [mV.bump(1) for e in range(EC)]
        osv = [os_sl[e % 2].bump(16) for e in range(EC)]
        HV_LAST = hv[-1]
        with nc.Block() as bl:
            @bl.gpsimd
            def _(g):
                g.wait_ge(mV.h, HV_LAST)
                for e in range(EC):
                    if e >= 2:
                        g.wait_ge(mT.h, dmm[e - 2])
                    for piece in range(2):
                        r0 = piece * HIC * 128
                        o0 = (e % 2) * IC * 128 + piece * HIC * 128
                        g.dma_start(
                            out=wdb[:, o0:o0 + HIC * 128].rearrange(
                                "p (ic n) -> p ic n", n=128),
                            in_=wd_scr[r0:r0 + HIC * 128,
                                       e * 128:(e + 1) * 128].rearrange(
                                "(ic p) n -> p ic n", p=128)).then_inc(wd_sl[e % 2].h, 16)

            @bl.tensor
            def _(te):
                for e in range(EC):
                    te.wait_ge(wd_sl[e % 2].h, wdld[e])
                    if e >= 2:
                        te.wait_ge(mV.h, oev[e - 2])
                    pd_ = pE0 if e % 2 == 0 else pE1
                    base = (e % 2) * IC * 128
                    for ic in range(IC):
                        mm = te.matmul(pd_[:, :],
                                       wdb[:, base + ic * 128:base + (ic + 1) * 128],
                                       h_full[:, ic * 512:(ic + 1) * 512],
                                       start=(ic == 0), stop=(ic == IC - 1),
                                       skip_group_check=True)
                        if ic == IC - 1:
                            mm.then_inc(mT.h, 1)

            @bl.sync
            def _(sy):
                for e in range(EC):
                    if e >= 2:
                        sy.wait_ge(mV.h, oev[e - 2])
                    for c_ in range(2):
                        base_el = (e * 128 + c_) * TOWN + tcol
                        in_ap = bass.AP(y_scr_t, base_el,
                                        [[2 * TOWN, 4], [8 * TOWN, 16], [1, 512]])
                        sy.dma_start(out=yres[c_ * 64:(c_ + 1) * 64,
                                              (e % 2) * 512:(e % 2 + 1) * 512],
                                     in_=in_ap).then_inc(yr_sl[e % 2].h, 16)
                    sy.wait_ge(mV.h, oev[e])
                    sy.dma_start(out=out_d[e * 128:(e + 1) * 128, tcol:tcol + 512],
                                 in_=ost[:, (e % 2) * 512:(e % 2 + 1) * 512]
                                 ).then_inc(os_sl[e % 2].h, 16)

            @bl.vector
            def _(ve):
                for e in range(EC):
                    pd_ = pE0 if e % 2 == 0 else pE1
                    ve.wait_ge(mT.h, dmm[e])
                    ve.wait_ge(yr_sl[e % 2].h, yld[e])
                    if e >= 2:
                        ve.wait_ge(os_sl[e % 2].h, osv[e - 2])
                    ve.tensor_tensor(out=ost[:, (e % 2) * 512:(e % 2 + 1) * 512],
                                     in0=pd_[:, :],
                                     in1=yres[:, (e % 2) * 512:(e % 2 + 1) * 512],
                                     op=ALU.add).then_inc(mV.h, 1)
    ffn.close()

    if dbg:
        dmp = [
            (nc.dram_tensor("dbg_k", [KV * D, S], BF16, kind="ExternalOutput").ap(), k_scr),
            (nc.dram_tensor("dbg_v", [S, KV * D], BF16, kind="ExternalOutput").ap(), v_scr),
            (nc.dram_tensor("dbg_y", [E, TOWN], F32, kind="ExternalOutput").ap(), y_scr),
            (nc.dram_tensor("dbg_ln2", [E, TOWN], BF16, kind="ExternalOutput").ap(), ln2_scr),
            (nc.dram_tensor("dbg_wg", [E, 512], BF16, kind="ExternalOutput").ap(), wg_scr[:, 0:512]),
            (nc.dram_tensor("dbg_wd", [512, E], BF16, kind="ExternalOutput").ap(), wd_scr[0:512, :]),
        ]
        snap = [(s_.h, s_.n) for s_ in ALLSEMS]
        dv = ps_sl.n
        with nc.Block() as bl:
            @bl.sync
            def _(sy):
                for hh, vv in snap:
                    sy.wait_ge(hh, vv)
                for dst, srcp in dmp:
                    sy.dma_start(out=dst, in_=srcp).then_inc(ps_sl.h, 16)
                sy.wait_ge(ps_sl.h, dv + len(dmp) * 16)
        ps_sl.bump(len(dmp) * 16)
    else:
        snap = [(s_.h, s_.n) for s_ in ALLSEMS]
        with nc.Block() as bl:
            @bl.sync
            def _(sy):
                for hh, vv in snap:
                    sy.wait_ge(hh, vv)

    pes.close()
    es.close()
    return nc


_NC_CACHE = {}


def kernel(**inputs):
    P = _perm128()
    permI = np.concatenate([q * 128 + P for q in range(IC)])
    permE = np.concatenate([q * 128 + P for q in range(EC)])

    x = np.asarray(inputs["L0_attn_x"], np.float32)
    wq = np.ascontiguousarray(np.asarray(inputs["L0_attn_wq"], np.float32))
    wk = np.ascontiguousarray(np.asarray(inputs["L0_attn_wk"], np.float32))
    wv = np.ascontiguousarray(np.asarray(inputs["L0_attn_wv"], np.float32))
    bq = np.asarray(inputs["L0_attn_bq"], np.float32).reshape(1, E)
    bk = np.asarray(inputs["L0_attn_bk"], np.float32).reshape(1, KV * D)
    bv = np.asarray(inputs["L0_attn_bv"], np.float32).reshape(1, KV * D)
    gq = np.ascontiguousarray(np.asarray(inputs["L0_ffn_gate_qweight"], np.int32))
    uq = np.ascontiguousarray(np.asarray(inputs["L0_ffn_up_qweight"], np.int32))
    dq = np.ascontiguousarray(np.asarray(inputs["L0_ffn_down_qweight"], np.int32)[permI])
    gs = np.ascontiguousarray(np.asarray(inputs["L0_ffn_gate_scales"], np.float32)[:, permI])
    us = np.ascontiguousarray(np.asarray(inputs["L0_ffn_up_scales"], np.float32)[:, permI])
    ds = np.ascontiguousarray(np.asarray(inputs["L0_ffn_down_scales"], np.float32)[:, permE])
    gz = np.ascontiguousarray(np.asarray(inputs["L0_ffn_gate_qzeros"], np.int32))
    uz = np.ascontiguousarray(np.asarray(inputs["L0_ffn_up_qzeros"], np.int32))
    dz = np.ascontiguousarray(np.asarray(inputs["L0_ffn_down_qzeros"], np.int32))

    xT = [np.ascontiguousarray(x[b].T) for b in range(B)]

    if "nc" not in _NC_CACHE:
        _NC_CACHE["nc"] = build_nc()
    nc = _NC_CACHE["nc"]

    in_maps = []
    for c in range(8):
        b, t0 = c // 4, (c % 4) * TOWN
        in_maps.append({
            "xT": xT[b],
            "xq": np.ascontiguousarray(xT[b][:, t0:t0 + TOWN]),
            "wq": wq, "wk": wk, "wv": wv,
            "bq": bq, "bk": bk, "bv": bv,
            "gq": gq, "uq": uq, "dq": dq,
            "gs": gs, "us": us, "ds": ds,
            "gz": gz, "uz": uz, "dz": dz,
            "pos0": np.full((64, 1), float(t0), np.float32),
        })

    res = run_bass_kernel_spmd(nc, in_maps, core_ids=list(range(8)))

    out = np.empty((B, S, E), np.float32)
    for c in range(8):
        b, t0 = c // 4, (c % 4) * TOWN
        dev = np.asarray(res.results[c]["out"])
        nat = np.empty_like(dev)
        nat[permE] = dev
        out[b, t0:t0 + TOWN, :] = nat.T
    return out


# version 08
# speedup vs baseline: 1.1138x; 1.0464x over previous
import math
from contextlib import ExitStack

import numpy as np
import concourse.bass as bass
import concourse.mybir as mybir
from concourse.bass_utils import run_bass_kernel_spmd

F32 = mybir.dt.float32
BF16 = mybir.dt.bfloat16
I32 = mybir.dt.int32
AF = mybir.ActivationFunctionType
ALU = mybir.AluOpType

B, S, E, H, KV, D = 2, 4096, 4096, 32, 8, 128
I = 11008
EC = E // 128
IC = I // 128
GW, DW = I // 8, E // 8
TOWN = 1024
NG = S // 512
NQ = TOWN // 512
TWO_PI = float(2 * math.pi)
AWQ_INV = [0, 2, 4, 6, 1, 3, 5, 7]


def _perm128():
    P = np.zeros(128, np.int64)
    for s_ in range(128):
        j, wl = s_ // 16, s_ % 16
        P[s_] = 8 * wl + AWQ_INV[j]
    return P


class Sem:
    def __init__(self, es, nc, name):
        self.h = es.enter_context(nc.semaphore(name))
        self.n = 0

    def bump(self, k):
        self.n += k
        return self.n


def build_nc(dbg=False):
    nc = bass.Bass("TRN2", target_bir_lowering=False)

    xT = nc.dram_tensor("xT", [E, S], F32, kind="ExternalInput").ap()
    xq = nc.dram_tensor("xq", [E, TOWN], F32, kind="ExternalInput").ap()
    wq_d = nc.dram_tensor("wq", [E, E], F32, kind="ExternalInput").ap()
    wk_d = nc.dram_tensor("wk", [E, KV * D], F32, kind="ExternalInput").ap()
    wv_d = nc.dram_tensor("wv", [E, KV * D], F32, kind="ExternalInput").ap()
    bq_d = nc.dram_tensor("bq", [1, E], F32, kind="ExternalInput").ap()
    bk_d = nc.dram_tensor("bk", [1, KV * D], F32, kind="ExternalInput").ap()
    bv_d = nc.dram_tensor("bv", [1, KV * D], F32, kind="ExternalInput").ap()
    gq_d = nc.dram_tensor("gq", [E, GW], I32, kind="ExternalInput").ap()
    uq_d = nc.dram_tensor("uq", [E, GW], I32, kind="ExternalInput").ap()
    dq_d = nc.dram_tensor("dq", [I, DW], I32, kind="ExternalInput").ap()
    gs_d = nc.dram_tensor("gs", [32, I], F32, kind="ExternalInput").ap()
    us_d = nc.dram_tensor("us", [32, I], F32, kind="ExternalInput").ap()
    ds_d = nc.dram_tensor("ds", [86, E], F32, kind="ExternalInput").ap()
    gz_d = nc.dram_tensor("gz", [32, GW], I32, kind="ExternalInput").ap()
    uz_d = nc.dram_tensor("uz", [32, GW], I32, kind="ExternalInput").ap()
    dz_d = nc.dram_tensor("dz", [86, DW], I32, kind="ExternalInput").ap()
    pos0_d = nc.dram_tensor("pos0", [64, 1], F32, kind="ExternalInput").ap()
    out_d = nc.dram_tensor("out", [E, TOWN], F32, kind="ExternalOutput").ap()

    wg_scr = nc.dram_tensor("wg_scr", [E, I], BF16, kind="Internal").ap()
    wu_scr = nc.dram_tensor("wu_scr", [E, I], BF16, kind="Internal").ap()
    wd_scr = nc.dram_tensor("wd_scr", [I, E], BF16, kind="Internal").ap()
    k_scr = nc.dram_tensor("k_scr", [KV * D, S], BF16, kind="Internal").ap()
    v_scr = nc.dram_tensor("v_scr", [S, KV * D], BF16, kind="Internal").ap()
    y_scr_t = nc.dram_tensor("y_scr", [E, TOWN], F32, kind="Internal")
    y_scr = y_scr_t.ap()
    ln2_scr = nc.dram_tensor("ln2_scr", [E, TOWN], BF16, kind="Internal").ap()
    row_scr = nc.dram_tensor("row_scr", [4, 512], F32, kind="Internal").ap()

    es = ExitStack()
    ones_col = es.enter_context(nc.sbuf_tensor("ones_col", [128, 1], BF16))
    ones2 = es.enter_context(nc.sbuf_tensor("ones2", [2, 128], BF16))
    onesN = es.enter_context(nc.sbuf_tensor("onesN", [2, 512], BF16))
    eps_col = es.enter_context(nc.sbuf_tensor("eps_col", [128, 1], F32))
    barrier_tile = es.enter_context(nc.sbuf_tensor("barrier_tile", [1, 2], F32))

    mT = Sem(es, nc, "mT")
    mV = Sem(es, nc, "mV")
    mS = Sem(es, nc, "mS")
    mG = Sem(es, nc, "mG")
    ALLSEMS = [mT, mV, mS, mG]

    def mksem(name):
        s_ = Sem(es, nc, name)
        ALLSEMS.append(s_)
        return s_

    def mkslots(name, k):
        return [mksem(f"{name}{i}") for i in range(k)]

    sl_ld = mkslots("sl_ld", 2)
    sl_st = mkslots("sl_st", 2)
    ps_sl = mksem("ps_sl")
    br_sl = mksem("br_sl")
    bs_sl = mksem("bs_sl")
    xsl = mkslots("xsl", 4)
    ysl = mkslots("ysl", 4)
    la_sl = mksem("la_sl")
    lb_sl = mksem("lb_sl")
    wk_sl = mkslots("wk_sl", 2)
    wv_sl = mksem("wv_sl")
    ks_sl = mksem("ks_sl")
    vs_sl = mksem("vs_sl")
    qw_sl = mkslots("qw_sl", 2)
    kv_sl = mkslots("kv_sl", 2)
    rh_sl = mksem("rh_sl")
    rh2_sl = mksem("rh2_sl")
    xq_sl = mkslots("xq_sl", 2)
    ys_sl = mkslots("ys_sl", 2)
    l2y_sl = mkslots("l2y_sl", 2)
    ls_sl = mkslots("ls_sl", 2)
    l2_sl = mksem("l2_sl")
    wg_sl = mkslots("wg_sl", 2)
    wu_sl = mkslots("wu_sl", 2)
    wd_sl = mkslots("wd_sl", 2)
    yr_sl = mkslots("yr_sl", 2)
    os_sl = mkslots("os_sl", 2)

    pes = ExitStack()
    pA = pes.enter_context(nc.psum_tensor("pA", [128, 512], F32))
    pB = pes.enter_context(nc.psum_tensor("pB", [128, 512], F32))
    pSa = pes.enter_context(nc.psum_tensor("pSa", [1, 512], F32))
    pC = pes.enter_context(nc.psum_tensor("pC", [128, 512], F32))
    pD = pes.enter_context(nc.psum_tensor("pD", [128, 512], F32))
    pE0 = pes.enter_context(nc.psum_tensor("pE0", [128, 512], F32))
    pE1 = pes.enter_context(nc.psum_tensor("pE1", [128, 512], F32))
    pF0 = pes.enter_context(nc.psum_tensor("pF0", [128, 512], F32))

    def hard_barrier():
        snap = [(s_.h, s_.n) for s_ in ALLSEMS]
        bm = mG.n + 1
        with nc.Block() as bl:
            @bl.gpsimd
            def _(g):
                for hh, vv in snap:
                    g.wait_ge(hh, vv)
                g.memset(barrier_tile[:, :], 0.0).then_inc(mG.h, 1)

            @bl.tensor
            def _(te):
                te.wait_ge(mG.h, bm)

            @bl.vector
            def _(ve):
                ve.wait_ge(mG.h, bm)

            @bl.scalar
            def _(sc):
                sc.wait_ge(mG.h, bm)

            @bl.sync
            def _(sy):
                sy.wait_ge(mG.h, bm)
        mG.bump(1)

    with nc.Block() as bl:
        @bl.gpsimd
        def _(g):
            g.memset(ones_col[:, :], 1.0)
            g.memset(ones2[:, :], 1.0)
            g.memset(onesN[:, :], 1.0)
            g.memset(eps_col[:, :], 1e-5).then_inc(mG.h, 1)
    mG.bump(1)

    deq = ExitStack()
    NSH = I // 2
    WSH = GW // 2
    q_sb = [deq.enter_context(nc.sbuf_tensor(f"q_sb{i}", [128, WSH], I32)) for i in range(2)]
    z_sb = [deq.enter_context(nc.sbuf_tensor(f"z_sb{i}", [128, WSH], I32)) for i in range(2)]
    s_sb = [deq.enter_context(nc.sbuf_tensor(f"s_sb{i}", [128, NSH], F32)) for i in range(2)]
    wn_sb = deq.enter_context(nc.sbuf_tensor("wn_sb", [128, NSH], I32))
    zn_sb = deq.enter_context(nc.sbuf_tensor("zn_sb", [128, NSH], I32))
    db_sb = deq.enter_context(nc.sbuf_tensor("db_sb", [128, NSH], BF16))
    wo_sb = [deq.enter_context(nc.sbuf_tensor(f"wo_sb{i}", [128, NSH], BF16)) for i in range(2)]

    slices = []
    for c in range(EC):
        for hf in range(2):
            slices.append((gq_d, gz_d, gs_d, wg_scr, c, hf * WSH, hf * NSH, WSH))
    for c in range(EC):
        for hf in range(2):
            slices.append((uq_d, uz_d, us_d, wu_scr, c, hf * WSH, hf * NSH, WSH))
    for c in range(IC):
        slices.append((dq_d, dz_d, ds_d, wd_scr, c, 0, 0, DW))
    nsl = len(slices)

    ld_v = [sl_ld[i % 2].bump(48) for i in range(nsl)]
    dv_v = [mV.bump(1) for i in range(nsl)]
    st_v = [sl_st[i % 2].bump(16) for i in range(nsl)]

    with nc.Block() as bl:
        @bl.sync
        def _(sy):
            for i, (qsrc, zsrc, ssrc, dst, c, woff, noff, nw) in enumerate(slices):
                bi = i % 2
                nnib = nw * 8
                if i >= 2:
                    sy.wait_ge(mV.h, dv_v[i - 2])
                sy.dma_start(out=q_sb[bi][:, 0:nw],
                             in_=qsrc[c * 128:(c + 1) * 128, woff:woff + nw]
                             ).then_inc(sl_ld[bi].h, 16)
                sy.dma_start(out=z_sb[bi][:, 0:nw],
                             in_=zsrc[c:c + 1, woff:woff + nw].broadcast_to([128, nw])
                             ).then_inc(sl_ld[bi].h, 16)
                sy.dma_start(out=s_sb[bi][:, 0:nnib],
                             in_=ssrc[c:c + 1, noff:noff + nnib].broadcast_to([128, nnib])
                             ).then_inc(sl_ld[bi].h, 16)

        @bl.vector
        def _(ve):
            for i, (qsrc, zsrc, ssrc, dst, c, woff, noff, nw) in enumerate(slices):
                bi = i % 2
                nnib = nw * 8
                ve.wait_ge(sl_ld[bi].h, ld_v[i])
                if i >= 2:
                    ve.wait_ge(sl_st[bi].h, st_v[i - 2])
                wv3 = wn_sb[:, 0:nnib].rearrange("p (q j w) -> p q j w", j=8, w=16)
                zv3 = zn_sb[:, 0:nnib].rearrange("p (q j w) -> p q j w", j=8, w=16)
                qv3 = q_sb[bi][:, 0:nw].rearrange("p (q w) -> p q w", w=16)
                zq3 = z_sb[bi][:, 0:nw].rearrange("p (q w) -> p q w", w=16)
                for j in range(8):
                    ve.tensor_scalar(out=wv3[:, :, j, :], in0=qv3, scalar1=4 * j,
                                     scalar2=15, op0=ALU.logical_shift_right,
                                     op1=ALU.bitwise_and)
                    ve.tensor_scalar(out=zv3[:, :, j, :], in0=zq3, scalar1=4 * j,
                                     scalar2=15, op0=ALU.logical_shift_right,
                                     op1=ALU.bitwise_and)
                ve.tensor_tensor(out=db_sb[:, 0:nnib], in0=wn_sb[:, 0:nnib],
                                 in1=zn_sb[:, 0:nnib], op=ALU.subtract)
                ve.tensor_tensor(out=wo_sb[bi][:, 0:nnib], in0=db_sb[:, 0:nnib],
                                 in1=s_sb[bi][:, 0:nnib], op=ALU.mult).then_inc(mV.h, 1)

        @bl.gpsimd
        def _(g):
            for i, (qsrc, zsrc, ssrc, dst, c, woff, noff, nw) in enumerate(slices):
                bi = i % 2
                nnib = nw * 8
                g.wait_ge(mV.h, dv_v[i])
                g.dma_start(out=dst[c * 128:(c + 1) * 128, noff:noff + nnib],
                            in_=wo_sb[bi][:, 0:nnib]).then_inc(sl_st[bi].h, 16)
    deq.close()

    hard_barrier()
    ap_es = ExitStack()
    kcos = ap_es.enter_context(nc.sbuf_tensor("kcos", [64, S], F32))
    ksin = ap_es.enter_context(nc.sbuf_tensor("ksin", [64, S], F32))
    qcos = ap_es.enter_context(nc.sbuf_tensor("qcos", [64, TOWN], F32))
    qsin = ap_es.enter_context(nc.sbuf_tensor("qsin", [64, TOWN], F32))
    bqhl = ap_es.enter_context(nc.sbuf_tensor("bqhl", [2, E], BF16))
    bkhl = ap_es.enter_context(nc.sbuf_tensor("bkhl", [2, KV * D], BF16))
    bvhl = ap_es.enter_context(nc.sbuf_tensor("bvhl", [2, KV * D], BF16))
    m_r = ap_es.enter_context(nc.sbuf_tensor("m_r", [1, 512], F32))
    msq_r = ap_es.enter_context(nc.sbuf_tensor("msq_r", [1, 512], F32))
    mm_r = ap_es.enter_context(nc.sbuf_tensor("mm_r", [1, 512], F32))
    rstd_r = ap_es.enter_context(nc.sbuf_tensor("rstd_r", [1, 512], F32))
    negmr_r = ap_es.enter_context(nc.sbuf_tensor("negmr_r", [1, 512], F32))
    rstd_bt = ap_es.enter_context(nc.sbuf_tensor("rstd_bt", [128, 512], F32))
    negmr_bt = ap_es.enter_context(nc.sbuf_tensor("negmr_bt", [128, 512], F32))

    tb_es = ExitStack()
    t_pos = tb_es.enter_context(nc.sbuf_tensor("t_pos", [64, S], F32))
    t_ang = tb_es.enter_context(nc.sbuf_tensor("t_ang", [64, S], F32))
    t_k = tb_es.enter_context(nc.sbuf_tensor("t_k", [64, S], F32))
    t_ki = tb_es.enter_context(nc.sbuf_tensor("t_ki", [64, S], I32))
    t_pidx = tb_es.enter_context(nc.sbuf_tensor("t_pidx", [64, 1], F32))
    t_invf = tb_es.enter_context(nc.sbuf_tensor("t_invf", [64, 1], F32))
    t_pos0 = tb_es.enter_context(nc.sbuf_tensor("t_pos0", [64, 1], F32))
    t_brow = tb_es.enter_context(nc.sbuf_tensor("t_brow", [1, E], F32))
    t_bhi = tb_es.enter_context(nc.sbuf_tensor("t_bhi", [1, E], BF16))
    t_blo = tb_es.enter_context(nc.sbuf_tensor("t_blo", [1, E], BF16))

    IOTA_G = mG.n + 1
    POS0_V = ps_sl.n + 16
    INVF_S = mS.n + 1
    with nc.Block() as bl:
        @bl.gpsimd
        def _(g):
            g.iota(t_pos[:, :], [[1, S]], channel_multiplier=0,
                   allow_small_or_imprecise_dtypes=True)
            g.iota(t_pidx[:, :], [[0, 1]], channel_multiplier=1,
                   allow_small_or_imprecise_dtypes=True).then_inc(mG.h, 1)

        @bl.sync
        def _(sy):
            sy.dma_start(out=t_pos0[:, :], in_=pos0_d[:, :]).then_inc(ps_sl.h, 16)

        @bl.scalar
        def _(sc):
            sc.wait_ge(mG.h, IOTA_G)
            sc.activation(out=t_invf[:, :], in_=t_pidx[:, :], func=AF.Exp,
                          scale=float(-math.log(10000.0) / 64.0)).then_inc(mS.h, 1)
    mG.bump(1)
    ps_sl.bump(16)
    mS.bump(1)

    tables = [(ksin[:, :], S, 0.0, False), (kcos[:, :], S, math.pi / 2, False),
              (qsin[:, :], TOWN, 0.0, True), (qcos[:, :], TOWN, math.pi / 2, True)]
    tb_v = [mV.bump(1) for _ in range(4)]
    with nc.Block() as bl:
        @bl.vector
        def _(ve):
            ve.wait_ge(mS.h, INVF_S)
            ve.wait_ge(ps_sl.h, POS0_V)
            for ti, (dst, width, shift, ap0) in enumerate(tables):
                if ap0:
                    ve.tensor_scalar(out=t_ang[:, 0:width], in0=t_pos[:, 0:width],
                                     scalar1=t_pos0[:, :], scalar2=None, op0=ALU.add)
                    ve.tensor_scalar(out=t_ang[:, 0:width], in0=t_ang[:, 0:width],
                                     scalar1=t_invf[:, :], scalar2=float(shift),
                                     op0=ALU.mult, op1=ALU.add)
                else:
                    ve.tensor_scalar(out=t_ang[:, 0:width], in0=t_pos[:, 0:width],
                                     scalar1=t_invf[:, :], scalar2=float(shift),
                                     op0=ALU.mult, op1=ALU.add)
                ve.tensor_scalar(out=t_k[:, 0:width], in0=t_ang[:, 0:width],
                                 scalar1=float(1.0 / TWO_PI), scalar2=None, op0=ALU.mult)
                ve.tensor_copy(out=t_ki[:, 0:width], in_=t_k[:, 0:width])
                ve.tensor_copy(out=t_k[:, 0:width], in_=t_ki[:, 0:width])
                ve.scalar_tensor_tensor(out=dst, in0=t_k[:, 0:width],
                                        scalar=float(-TWO_PI), in1=t_ang[:, 0:width],
                                        op0=ALU.mult, op1=ALU.add).then_inc(mV.h, 1)

        @bl.scalar
        def _(sc):
            for ti, (dst, width, shift, ap0) in enumerate(tables):
                sc.wait_ge(mV.h, tb_v[ti])
                sc.activation(out=dst, in_=dst, func=AF.Sin).then_inc(mS.h, 1)
    mS.bump(4)

    bias_specs = ((bq_d, bqhl, E), (bk_d, bkhl, KV * D), (bv_d, bvhl, KV * D))
    brow_v = [br_sl.bump(16) for _ in range(3)]
    bsplit_v = [mV.bump(1) for _ in range(3)]
    bpl_v = [bs_sl.bump(32) for _ in range(3)]
    with nc.Block() as bl:
        @bl.sync
        def _(sy):
            for i, (src, dsttile, wdt) in enumerate(bias_specs):
                if i > 0:
                    sy.wait_ge(bs_sl.h, bpl_v[i - 1])
                sy.dma_start(out=t_brow[:, 0:wdt], in_=src[:, :]).then_inc(br_sl.h, 16)

        @bl.vector
        def _(ve):
            for i, (src, dsttile, wdt) in enumerate(bias_specs):
                ve.wait_ge(br_sl.h, brow_v[i])
                if i > 0:
                    ve.wait_ge(bs_sl.h, bpl_v[i - 1])
                ve.tensor_copy(out=t_bhi[:, 0:wdt], in_=t_brow[:, 0:wdt])
                ve.tensor_tensor(out=t_blo[:, 0:wdt], in0=t_brow[:, 0:wdt],
                                 in1=t_bhi[:, 0:wdt], op=ALU.subtract).then_inc(mV.h, 1)

        @bl.gpsimd
        def _(g):
            for i, (src, dsttile, wdt) in enumerate(bias_specs):
                g.wait_ge(mV.h, bsplit_v[i])
                g.dma_start(out=dsttile[0:1, 0:wdt], in_=t_bhi[:, 0:wdt]).then_inc(bs_sl.h, 16)
                g.dma_start(out=dsttile[1:2, 0:wdt], in_=t_blo[:, 0:wdt]).then_inc(bs_sl.h, 16)
    tb_es.close()
    hard_barrier()

    def ln_rows_block(stats_ms, norm_done=None):
        Va = mV.bump(1)
        Sa = mS.bump(1)
        Vb = mV.bump(1)
        Ga = la_sl.bump(32)
        Gb = lb_sl.bump(32)
        with nc.Block() as bl:
            @bl.vector
            def _(ve):
                ve.wait_ge(mT.h, stats_ms)
                ve.tensor_scalar(out=m_r[:, :], in0=pA[0:1, :], scalar1=float(1.0 / 4096.0),
                                 scalar2=None, op0=ALU.mult)
                ve.tensor_scalar(out=msq_r[:, :], in0=pB[0:1, :], scalar1=float(1.0 / 4096.0),
                                 scalar2=None, op0=ALU.mult)
                ve.tensor_tensor(out=mm_r[:, :], in0=m_r[:, :], in1=m_r[:, :], op=ALU.mult)
                ve.tensor_tensor(out=msq_r[:, :], in0=msq_r[:, :], in1=mm_r[:, :],
                                 op=ALU.subtract).then_inc(mV.h, 1)
                ve.wait_ge(mS.h, Sa)
                ve.reciprocal(out=rstd_r[:, :], in_=rstd_r[:, :])
                ve.scalar_tensor_tensor(out=negmr_r[:, :], in0=m_r[:, :], scalar=-1.0,
                                        in1=rstd_r[:, :], op0=ALU.mult,
                                        op1=ALU.mult).then_inc(mV.h, 1)

            @bl.scalar
            def _(sc):
                sc.wait_ge(mV.h, Va)
                sc.activation(out=rstd_r[:, :], in_=msq_r[:, :], func=AF.Sqrt,
                              bias=eps_col[0:1, :]).then_inc(mS.h, 1)

            @bl.gpsimd
            def _(g):
                g.wait_ge(mV.h, Vb)
                if norm_done is not None:
                    g.wait_ge(mV.h, norm_done)
                g.dma_start(out=row_scr[0:1, :], in_=rstd_r[:, :]).then_inc(la_sl.h, 16)
                g.dma_start(out=row_scr[1:2, :], in_=negmr_r[:, :]).then_inc(lb_sl.h, 16)
                g.wait_ge(la_sl.h, Ga - 16)
                g.dma_start(out=rstd_bt[:, :],
                            in_=row_scr[0:1, :].broadcast_to([128, 512])
                            ).then_inc(la_sl.h, 16)
                g.wait_ge(lb_sl.h, Gb - 16)
                g.dma_start(out=negmr_bt[:, :],
                            in_=row_scr[1:2, :].broadcast_to([128, 512])
                            ).then_inc(lb_sl.h, 16)
        return (Ga, Gb)

    def stats_wave(src_ap, col0, xbf_grp, x2_t, dma_guard=None, act_guard=None):
        xld = [xsl[c % 4].bump(16) for c in range(EC)]
        sqv = [mS.bump(1) for c in range(EC)]
        stv = [mT.bump(1) for c in range(EC)]
        with nc.Block() as bl:
            @bl.gpsimd
            def _(g):
                for c in range(EC):
                    if c == 0 and dma_guard is not None:
                        g.wait_ge(dma_guard[0], dma_guard[1])
                    if c >= 4:
                        g.wait_ge(mS.h, sqv[c - 4])
                    g.dma_start(out=xbf_grp[:, c * 512:(c + 1) * 512],
                                in_=src_ap[c * 128:(c + 1) * 128, col0:col0 + 512]
                                ).then_inc(xsl[c % 4].h, 16)

            @bl.scalar
            def _(sc):
                for c in range(EC):
                    sc.wait_ge(xsl[c % 4].h, xld[c])
                    if c == 0 and act_guard is not None:
                        sc.wait_ge(act_guard[0], act_guard[1])
                    if c >= 4:
                        sc.wait_ge(mT.h, stv[c - 4])
                    sc.activation(out=x2_t[:, (c % 4) * 512:(c % 4 + 1) * 512],
                                  in_=xbf_grp[:, c * 512:(c + 1) * 512],
                                  func=AF.Square).then_inc(mS.h, 1)

            @bl.tensor
            def _(te):
                for c in range(EC):
                    te.wait_ge(mS.h, sqv[c])
                    te.matmul(pA[0:1, :], ones_col[:, :],
                              xbf_grp[:, c * 512:(c + 1) * 512],
                              start=(c == 0), stop=(c == EC - 1), skip_group_check=True)
                    te.matmul(pB[0:1, :], ones_col[:, :],
                              x2_t[:, (c % 4) * 512:(c % 4 + 1) * 512],
                              start=(c == 0), stop=(c == EC - 1),
                              skip_group_check=True).then_inc(mT.h, 1)
        return stv[-1]

    def norm_wave(xbf_grp, dst, bcast_ms, tn_t):
        Ga, Gb = bcast_ms
        nv = [mV.bump(1) for c in range(EC)]
        with nc.Block() as bl:
            @bl.vector
            def _(ve):
                ve.wait_ge(la_sl.h, Ga)
                ve.wait_ge(lb_sl.h, Gb)
                for c in range(EC):
                    ve.tensor_tensor(out=tn_t[:, :],
                                     in0=xbf_grp[:, c * 512:(c + 1) * 512],
                                     in1=rstd_bt[:, :], op=ALU.mult)
                    ve.tensor_tensor(out=dst[:, c * 512:(c + 1) * 512], in0=tn_t[:, :],
                                     in1=negmr_bt[:, :], op=ALU.add).then_inc(mV.h, 1)
        return nv[-1]

    def rope_ops(ve, psrc, cos_t, sin_t, col0, dst, dcol, rp):
        ve.tensor_tensor(out=rp[0][:, :], in0=psrc[0:64, :],
                         in1=cos_t[:, col0:col0 + 512], op=ALU.mult)
        ve.tensor_tensor(out=rp[1][:, :], in0=psrc[64:128, :],
                         in1=sin_t[:, col0:col0 + 512], op=ALU.mult)
        ve.tensor_tensor(out=dst[0:64, dcol:dcol + 512], in0=rp[0][:, :],
                         in1=rp[1][:, :], op=ALU.subtract)
        ve.tensor_tensor(out=rp[0][:, :], in0=psrc[0:64, :],
                         in1=sin_t[:, col0:col0 + 512], op=ALU.mult)
        ve.tensor_tensor(out=rp[1][:, :], in0=psrc[64:128, :],
                         in1=cos_t[:, col0:col0 + 512], op=ALU.mult)
        ve.tensor_tensor(out=dst[64:128, dcol:dcol + 512], in0=rp[0][:, :],
                         in1=rp[1][:, :], op=ALU.add).then_inc(mV.h, 1)

    a1 = ExitStack()
    xln = a1.enter_context(nc.sbuf_tensor("xln", [128, EC * 512], BF16))
    xbf_grp = a1.enter_context(nc.sbuf_tensor("xbf_grp", [128, EC * 512], BF16))
    x2_rot = a1.enter_context(nc.sbuf_tensor("x2_rot", [128, 4 * 512], BF16))
    tnorm = a1.enter_context(nc.sbuf_tensor("tnorm", [128, 512], F32))
    wvb = a1.enter_context(nc.sbuf_tensor("wvb", [128, EC * 512], BF16))
    wkb = a1.enter_context(nc.sbuf_tensor("wkb", [128, 2 * EC * 128], BF16))
    kst = a1.enter_context(nc.sbuf_tensor("kst", [128, 8 * 512], BF16))
    vst = a1.enter_context(nc.sbuf_tensor("vst", [128, 4 * 1024], BF16))
    rp1 = [a1.enter_context(nc.sbuf_tensor(f"rp1_{i}", [64, 512], F32)) for i in range(2)]

    ks_done = 0
    vs_done = 0
    prev_norm = None
    prev_stats = None
    prev_vmm_last = None
    hard_barrier()
    for g_i in range(NG):
        col0 = g_i * 512
        stats_ms = stats_wave(
            xT, col0, xbf_grp, x2_rot,
            dma_guard=(mV.h, prev_norm) if prev_norm is not None else None,
            act_guard=(mT.h, prev_stats) if prev_stats is not None else None)
        bcast_ms = ln_rows_block(stats_ms, norm_done=prev_norm)
        norm_ms = norm_wave(xbf_grp, xln, bcast_ms, tnorm)
        prev_norm = norm_ms
        prev_stats = stats_ms

        prev_ks = ks_done
        wkld = [wk_sl[h % 2].bump(16) for h in range(KV)]
        kmm = [mT.bump(1) for h in range(KV)]
        krp = [mV.bump(1) for h in range(KV)]
        with nc.Block() as bl:
            @bl.gpsimd
            def _(g):
                for h in range(KV):
                    if h >= 2:
                        g.wait_ge(mT.h, kmm[h - 2])
                    g.dma_start(
                        out=wkb[:, (h % 2) * EC * 128:(h % 2 + 1) * EC * 128].rearrange(
                            "p (c n) -> p c n", n=128),
                        in_=wk_d[:, h * 128:(h + 1) * 128].rearrange(
                            "(c p) n -> p c n", p=128)).then_inc(wk_sl[h % 2].h, 16)

            @bl.tensor
            def _(te):
                te.wait_ge(mV.h, norm_ms)
                for h in range(KV):
                    te.wait_ge(wk_sl[h % 2].h, wkld[h])
                    if h >= 2:
                        te.wait_ge(mV.h, krp[h - 2])
                    pk = pE0 if h % 2 == 0 else pE1
                    base = (h % 2) * EC * 128
                    for c in range(EC):
                        te.matmul(pk[:, :], wkb[:, base + c * 128:base + (c + 1) * 128],
                                  xln[:, c * 512:(c + 1) * 512],
                                  start=(c == 0), stop=False, skip_group_check=True)
                    te.matmul(pk[:, :], bkhl[:, h * 128:(h + 1) * 128], onesN[:, :],
                              start=False, stop=True,
                              skip_group_check=True).then_inc(mT.h, 1)

            @bl.vector
            def _(ve):
                for h in range(KV):
                    ve.wait_ge(mT.h, kmm[h])
                    if h == 0 and prev_ks:
                        ve.wait_ge(ks_sl.h, prev_ks)
                    pk = pE0 if h % 2 == 0 else pE1
                    rope_ops(ve, pk, kcos, ksin, col0, kst, h * 512, rp1)

            @bl.sync
            def _(sy):
                for h in range(KV):
                    sy.wait_ge(mV.h, krp[h])
                    sy.dma_start(out=k_scr[h * 128:(h + 1) * 128, col0:col0 + 512],
                                 in_=kst[:, h * 512:(h + 1) * 512]).then_inc(ks_sl.h, 16)
        ks_done = ks_sl.bump(KV * 16)
        ROPES_V = mV.n

        prev_vs = vs_done
        for half in range(2):
            PRE_S = mS.n
            wv_guard = prev_vmm_last
            wvld = wv_sl.bump(16)
            vmm = [mT.bump(1) for tt in range(4)]
            vev = [mS.bump(1) for tt in range(4)]
            prev_vmm_last = vmm[-1]
            with nc.Block() as bl:
                @bl.gpsimd
                def _(g):
                    if wv_guard is not None:
                        g.wait_ge(mT.h, wv_guard)
                    g.dma_start(
                        out=wvb[:, :].rearrange("p (c n) -> p c n", n=512),
                        in_=wv_d[:, half * 512:(half + 1) * 512].rearrange(
                            "(c p) n -> p c n", p=128)).then_inc(wv_sl.h, 16)

                @bl.tensor
                def _(te):
                    te.wait_ge(wv_sl.h, wvld)
                    te.wait_ge(mS.h, PRE_S)
                    if half == 0:
                        te.wait_ge(mV.h, ROPES_V)
                    for tt in range(4):
                        pv = pF0 if tt % 2 == 0 else pE0
                        if tt >= 2:
                            te.wait_ge(mS.h, vev[tt - 2])
                        for c in range(EC):
                            te.matmul(pv[:, :],
                                      xln[:, c * 512 + tt * 128:c * 512 + tt * 128 + 128],
                                      wvb[:, c * 512:(c + 1) * 512],
                                      start=(c == 0), stop=False, skip_group_check=True)
                        te.matmul(pv[:, :], ones2[:, :],
                                  bvhl[:, half * 512:(half + 1) * 512],
                                  start=False, stop=True,
                                  skip_group_check=True).then_inc(mT.h, 1)

                @bl.scalar
                def _(sc):
                    for tt in range(4):
                        pv = pF0 if tt % 2 == 0 else pE0
                        sc.wait_ge(mT.h, vmm[tt])
                        if tt == 0 and half == 0 and prev_vs:
                            sc.wait_ge(vs_sl.h, prev_vs)
                        sc.activation(out=vst[:, tt * 1024 + half * 512:
                                              tt * 1024 + half * 512 + 512],
                                      in_=pv[:, :], func=AF.Copy).then_inc(mS.h, 1)
        VS_ALL = mS.n
        with nc.Block() as bl:
            @bl.sync
            def _(sy):
                sy.wait_ge(mS.h, VS_ALL)
                for tt in range(4):
                    sy.dma_start(out=v_scr[col0 + tt * 128:col0 + tt * 128 + 128, :],
                                 in_=vst[:, tt * 1024:(tt + 1) * 1024]).then_inc(vs_sl.h, 16)
        vs_done = vs_sl.bump(4 * 16)
    A1_KS = ks_sl.n
    A1_VS = vs_sl.n
    a1.close()

    a2p = ExitStack()
    q_tile = a2p.enter_context(nc.sbuf_tensor("q_tile", [128, EC * 512], BF16))
    for qg in range(NQ):
        qcol = qg * 512
        hard_barrier()
        p1 = ExitStack()
        xlnq = p1.enter_context(nc.sbuf_tensor(f"xlnq{qg}", [128, EC * 512], BF16))
        wqb = p1.enter_context(nc.sbuf_tensor(f"wqb{qg}", [128, 2 * EC * 128], BF16))
        xbf2 = p1.enter_context(nc.sbuf_tensor(f"xbf2{qg}", [128, EC * 512], BF16))
        x22 = p1.enter_context(nc.sbuf_tensor(f"x22{qg}", [128, 4 * 512], BF16))
        tn2 = p1.enter_context(nc.sbuf_tensor(f"tn2{qg}", [128, 512], F32))
        rp2 = [p1.enter_context(nc.sbuf_tensor(f"rp2_{qg}_{i}", [64, 512], F32))
               for i in range(2)]

        stats2 = stats_wave(xq, qcol, xbf2, x22)
        bcast2 = ln_rows_block(stats2, norm_done=mV.n)
        normq = norm_wave(xbf2, xlnq, bcast2, tn2)

        qwld = [qw_sl[m % 2].bump(16) for m in range(EC)]
        qmm = [mT.bump(1) for m in range(EC)]
        qrp = [mV.bump(1) for m in range(EC)]
        with nc.Block() as bl:
            @bl.gpsimd
            def _(g):
                for m in range(EC):
                    if m >= 2:
                        g.wait_ge(mT.h, qmm[m - 2])
                    g.dma_start(
                        out=wqb[:, (m % 2) * EC * 128:(m % 2 + 1) * EC * 128].rearrange(
                            "p (c n) -> p c n", n=128),
                        in_=wq_d[:, m * 128:(m + 1) * 128].rearrange(
                            "(c p) n -> p c n", p=128)).then_inc(qw_sl[m % 2].h, 16)

            @bl.tensor
            def _(te):
                te.wait_ge(mV.h, normq)
                for m in range(EC):
                    te.wait_ge(qw_sl[m % 2].h, qwld[m])
                    if m >= 2:
                        te.wait_ge(mV.h, qrp[m - 2])
                    pq = pE0 if m % 2 == 0 else pE1
                    base = (m % 2) * EC * 128
                    for c in range(EC):
                        te.matmul(pq[:, :], wqb[:, base + c * 128:base + (c + 1) * 128],
                                  xlnq[:, c * 512:(c + 1) * 512],
                                  start=(c == 0), stop=False, skip_group_check=True)
                    te.matmul(pq[:, :], bqhl[:, m * 128:(m + 1) * 128], onesN[:, :],
                              start=False, stop=True,
                              skip_group_check=True).then_inc(mT.h, 1)

            @bl.vector
            def _(ve):
                for m in range(EC):
                    ve.wait_ge(mT.h, qmm[m])
                    pq = pE0 if m % 2 == 0 else pE1
                    rope_ops(ve, pq, qcos, qsin, qcol, q_tile, m * 512, rp2)
        p1.close()
        hard_barrier()

        p2 = ExitStack()
        khb = p2.enter_context(nc.sbuf_tensor(f"khb{qg}", [128, 2 * S], BF16))
        vhb = p2.enter_context(nc.sbuf_tensor(f"vhb{qg}", [128, 2 * EC * 128], BF16))
        expb = p2.enter_context(nc.sbuf_tensor(f"expb{qg}", [128, 4 * 512], BF16))
        srecip = p2.enter_context(nc.sbuf_tensor(f"srecip{qg}", [128, 512], F32))
        attn_t = p2.enter_context(nc.sbuf_tensor(f"attn_t{qg}", [128, 512], F32))
        yst = p2.enter_context(nc.sbuf_tensor(f"yst{qg}", [128, 2 * 512], F32))
        ybf_grp = p2.enter_context(nc.sbuf_tensor(f"ybfg{qg}", [128, EC * 512], BF16))
        y22_rot = p2.enter_context(nc.sbuf_tensor(f"y22r{qg}", [128, 4 * 512], BF16))
        xqr = p2.enter_context(nc.sbuf_tensor(f"xqr{qg}", [128, 2 * 512], F32))
        ln2st = p2.enter_context(nc.sbuf_tensor(f"ln2st{qg}", [128, 2 * 512], BF16))
        rrow = p2.enter_context(nc.sbuf_tensor(f"rrow{qg}", [1, 512], F32))
        rrow_b = p2.enter_context(nc.sbuf_tensor(f"rrowb{qg}", [1, 512], F32))
        srecip_b = p2.enter_context(nc.sbuf_tensor(f"srecipb{qg}", [128, 512], F32))

        evac_ms = {}
        ystore_v = {}
        h_mm_done = {}
        last_exv_a = None
        last_exv_b = None
        prev_Vr_a = None
        prev_Vr_b = None
        A2S_T = mT.n
        A2S_V = mV.n
        QRP_LAST = qrp[-1]
        for h in range(KV):
            kvld = kv_sl[h % 2].bump(32)
            with nc.Block() as bl:
                @bl.sync
                def _(sy):
                    if h == 0:
                        sy.wait_ge(mT.h, A2S_T)
                        sy.wait_ge(mV.h, A2S_V)
                        if qg == 0:
                            sy.wait_ge(ks_sl.h, A1_KS)
                            sy.wait_ge(vs_sl.h, A1_VS)
                    if h >= 2:
                        sy.wait_ge(mT.h, h_mm_done[h - 2])
                    sy.dma_start(out=khb[:, (h % 2) * S:(h % 2 + 1) * S],
                                 in_=k_scr[h * 128:(h + 1) * 128, :]
                                 ).then_inc(kv_sl[h % 2].h, 16)
                    sy.dma_start(
                        out=vhb[:, (h % 2) * EC * 128:(h % 2 + 1) * EC * 128].rearrange(
                            "p (kc n) -> p kc n", n=128),
                        in_=v_scr[:, h * 128:(h + 1) * 128].rearrange(
                            "(kc kp) n -> kp kc n", kp=128)).then_inc(kv_sl[h % 2].h, 16)
            kb = (h % 2) * S
            vb = (h % 2) * EC * 128

            for sub in (0, 2):
                qa = h * 4 + sub
                qb = qa + 1
                SCA = [pE0, pE1]
                SCB2 = [pA, pB]
                S0 = mS.n
                exv_a = [S0 + 2 * kc + 1 for kc in range(32)]
                exv_b = [S0 + 2 * kc + 2 for kc in range(32)]
                scv_a = {}
                scv_b = {}
                pvv_a = {}
                pvv_b = {}
                t_ = mT.n
                for kc in range(2):
                    t_ += 1
                    scv_a[kc] = t_
                    t_ += 1
                    scv_b[kc] = t_
                for kc in range(32):
                    t_ += 1
                    pvv_a[kc] = t_
                    t_ += 1
                    pvv_b[kc] = t_
                    if kc + 2 < 32:
                        t_ += 1
                        scv_a[kc + 2] = t_
                        t_ += 1
                        scv_b[kc + 2] = t_
                mT.n = t_

                with nc.Block() as bl:
                    @bl.tensor
                    def _(te):
                        te.wait_ge(kv_sl[h % 2].h, kvld)
                        if qa == 0:
                            te.wait_ge(mV.h, QRP_LAST)

                        def sc_mm(stream, kc):
                            if stream == 0:
                                psc, exv, lexv, qh_ = SCA[kc % 2], exv_a, last_exv_a, qa
                            else:
                                psc, exv, lexv, qh_ = SCB2[kc % 2], exv_b, last_exv_b, qb
                            if kc >= 2:
                                te.wait_ge(mS.h, exv[kc - 2])
                            elif lexv is not None:
                                te.wait_ge(mS.h, lexv[30 + kc])
                            te.matmul(psc[:, :],
                                      khb[:, kb + kc * 128:kb + (kc + 1) * 128],
                                      q_tile[:, qh_ * 512:(qh_ + 1) * 512],
                                      start=True, stop=True,
                                      skip_group_check=True).then_inc(mT.h, 1)
                        for kc in range(2):
                            sc_mm(0, kc)
                            sc_mm(1, kc)
                        for kc in range(32):
                            te.wait_ge(mS.h, exv_a[kc])
                            if kc == 0:
                                if prev_Vr_a is not None:
                                    te.wait_ge(mV.h, prev_Vr_a)
                                if qa >= 4:
                                    te.wait_ge(mV.h, evac_ms[qa - 2])
                            esl = slice((kc % 2) * 512, (kc % 2 + 1) * 512)
                            te.matmul(pSa[:, :], ones_col[:, :], expb[:, esl],
                                      start=(kc == 0), stop=(kc == 31),
                                      skip_group_check=True)
                            te.matmul(pF0[:, :],
                                      vhb[:, vb + kc * 128:vb + (kc + 1) * 128],
                                      expb[:, esl], start=(kc == 0), stop=(kc == 31),
                                      skip_group_check=True).then_inc(mT.h, 1)
                            te.wait_ge(mS.h, exv_b[kc])
                            if kc == 0:
                                if prev_Vr_b is not None:
                                    te.wait_ge(mV.h, prev_Vr_b)
                                if qb >= 4:
                                    te.wait_ge(mV.h, evac_ms[qb - 2])
                            esl2 = slice((2 + kc % 2) * 512, (2 + kc % 2 + 1) * 512)
                            te.matmul(pC[0:1, :], ones_col[:, :], expb[:, esl2],
                                      start=(kc == 0), stop=(kc == 31),
                                      skip_group_check=True)
                            te.matmul(pD[:, :],
                                      vhb[:, vb + kc * 128:vb + (kc + 1) * 128],
                                      expb[:, esl2], start=(kc == 0), stop=(kc == 31),
                                      skip_group_check=True).then_inc(mT.h, 1)
                            if kc + 2 < 32:
                                sc_mm(0, kc + 2)
                                sc_mm(1, kc + 2)

                    @bl.scalar
                    def _(sc):
                        for kc in range(32):
                            sc.wait_ge(mT.h, scv_a[kc])
                            if kc >= 2:
                                sc.wait_ge(mT.h, pvv_a[kc - 2])
                            sc.activation(out=expb[:, (kc % 2) * 512:(kc % 2 + 1) * 512],
                                          in_=SCA[kc % 2][:, :], func=AF.Exp,
                                          scale=float(1.0 / math.sqrt(D))
                                          ).then_inc(mS.h, 1)
                            sc.wait_ge(mT.h, scv_b[kc])
                            if kc >= 2:
                                sc.wait_ge(mT.h, pvv_b[kc - 2])
                            sc.activation(out=expb[:, (2 + kc % 2) * 512:(2 + kc % 2 + 1) * 512],
                                          in_=SCB2[kc % 2][:, :], func=AF.Exp,
                                          scale=float(1.0 / math.sqrt(D))
                                          ).then_inc(mS.h, 1)
                mS.bump(64)

                Vra = mV.bump(1)
                Vrb = mV.bump(1)
                Ga_ = rh_sl.bump(32)
                Gb_ = rh2_sl.bump(32)
                Lqa = xq_sl[0].bump(16)
                Lqb = xq_sl[1].bump(16)
                Vea = mV.bump(1)
                Veb = mV.bump(1)
                Qsa = ys_sl[0].bump(16)
                Qsb = ys_sl[1].bump(16)
                with nc.Block() as bl:
                    @bl.vector
                    def _(ve):
                        ve.wait_ge(mT.h, pvv_a[31])
                        ve.reciprocal(out=rrow[:, :], in_=pSa[:, :]).then_inc(mV.h, 1)
                        ve.wait_ge(mT.h, pvv_b[31])
                        ve.reciprocal(out=rrow_b[:, :], in_=pC[0:1, :]).then_inc(mV.h, 1)
                        ve.wait_ge(rh_sl.h, Ga_)
                        ve.wait_ge(xq_sl[0].h, Lqa)
                        if qa >= 4:
                            ve.wait_ge(ys_sl[0].h, ystore_v[qa - 2])
                        ve.tensor_tensor(out=attn_t[:, :], in0=pF0[:, :],
                                         in1=srecip[:, :], op=ALU.mult)
                        ve.tensor_tensor(out=yst[:, 0:512], in0=attn_t[:, :],
                                         in1=xqr[:, 0:512],
                                         op=ALU.add).then_inc(mV.h, 1)
                        ve.wait_ge(rh2_sl.h, Gb_)
                        ve.wait_ge(xq_sl[1].h, Lqb)
                        if qb >= 4:
                            ve.wait_ge(ys_sl[1].h, ystore_v[qb - 2])
                        ve.tensor_tensor(out=attn_t[:, :], in0=pD[:, :],
                                         in1=srecip_b[:, :], op=ALU.mult)
                        ve.tensor_tensor(out=yst[:, 512:1024], in0=attn_t[:, :],
                                         in1=xqr[:, 512:1024],
                                         op=ALU.add).then_inc(mV.h, 1)

                    @bl.gpsimd
                    def _(g):
                        g.wait_ge(mV.h, Vra)
                        g.dma_start(out=row_scr[2:3, :], in_=rrow[:, :]).then_inc(rh_sl.h, 16)
                        g.wait_ge(mV.h, Vrb)
                        g.dma_start(out=row_scr[3:4, :], in_=rrow_b[:, :]).then_inc(rh2_sl.h, 16)
                        g.wait_ge(rh_sl.h, Ga_ - 16)
                        g.dma_start(out=srecip[:, :],
                                    in_=row_scr[2:3, :].broadcast_to([128, 512])
                                    ).then_inc(rh_sl.h, 16)
                        g.wait_ge(rh2_sl.h, Gb_ - 16)
                        g.dma_start(out=srecip_b[:, :],
                                    in_=row_scr[3:4, :].broadcast_to([128, 512])
                                    ).then_inc(rh2_sl.h, 16)

                    @bl.sync
                    def _(sy):
                        if qa >= 4:
                            sy.wait_ge(mV.h, evac_ms[qa - 2])
                        sy.dma_start(out=xqr[:, 0:512],
                                     in_=xq[qa * 128:(qa + 1) * 128, qcol:qcol + 512]
                                     ).then_inc(xq_sl[0].h, 16)
                        if qb >= 4:
                            sy.wait_ge(mV.h, evac_ms[qb - 2])
                        sy.dma_start(out=xqr[:, 512:1024],
                                     in_=xq[qb * 128:(qb + 1) * 128, qcol:qcol + 512]
                                     ).then_inc(xq_sl[1].h, 16)
                        sy.wait_ge(mV.h, Vea)
                        sy.dma_start(out=y_scr[qa * 128:(qa + 1) * 128, qcol:qcol + 512],
                                     in_=yst[:, 0:512]).then_inc(ys_sl[0].h, 16)
                        sy.wait_ge(mV.h, Veb)
                        sy.dma_start(out=y_scr[qb * 128:(qb + 1) * 128, qcol:qcol + 512],
                                     in_=yst[:, 512:1024]).then_inc(ys_sl[1].h, 16)
                evac_ms[qa] = Vea
                evac_ms[qb] = Veb
                ystore_v[qa] = Qsa
                ystore_v[qb] = Qsb
                last_exv_a = exv_a
                last_exv_b = exv_b
                prev_Vr_a = Vra
                prev_Vr_b = Vrb
            h_mm_done[h] = mT.n

        YALL0 = [ys_sl[0].n, ys_sl[1].n]
        with nc.Block() as bl:
            @bl.gpsimd
            def _(g):
                g.wait_ge(ys_sl[0].h, YALL0[0])
                g.wait_ge(ys_sl[1].h, YALL0[1])
                g.memset(barrier_tile[:, :], 0.0).then_inc(mG.h, 1)
        YB = mG.bump(1)
        with nc.Block() as bl:
            @bl.gpsimd
            def _(g):
                g.wait_ge(mG.h, YB)
        stats3 = stats_wave(y_scr, qcol, ybf_grp, y22_rot)
        bcast3 = ln_rows_block(stats3, norm_done=mV.n)

        l2v = [mV.bump(1) for c in range(EC)]
        l2s = [ls_sl[c % 2].bump(16) for c in range(EC)]
        with nc.Block() as bl:
            @bl.vector
            def _(ve):
                ve.wait_ge(la_sl.h, bcast3[0])
                ve.wait_ge(lb_sl.h, bcast3[1])
                for c in range(EC):
                    if c >= 2:
                        ve.wait_ge(ls_sl[c % 2].h, l2s[c - 2])
                    ve.tensor_tensor(out=attn_t[:, :],
                                     in0=ybf_grp[:, c * 512:(c + 1) * 512],
                                     in1=rstd_bt[:, :], op=ALU.mult)
                    ve.tensor_tensor(out=ln2st[:, (c % 2) * 512:(c % 2 + 1) * 512],
                                     in0=attn_t[:, :], in1=negmr_bt[:, :],
                                     op=ALU.add).then_inc(mV.h, 1)

            @bl.gpsimd
            def _(g):
                for c in range(EC):
                    g.wait_ge(mV.h, l2v[c])
                    g.dma_start(out=ln2_scr[c * 128:(c + 1) * 128, qcol:qcol + 512],
                                in_=ln2st[:, (c % 2) * 512:(c % 2 + 1) * 512]
                                ).then_inc(ls_sl[c % 2].h, 16)
        p2.close()
    a2p.close()
    ap_es.close()

    hard_barrier()
    ffn = ExitStack()
    ln2b = ffn.enter_context(nc.sbuf_tensor("ln2b", [128, EC * 512], BF16))
    h_full = ffn.enter_context(nc.sbuf_tensor("h_full", [128, IC * 512], BF16))
    wgb = ffn.enter_context(nc.sbuf_tensor("wgb", [128, 2 * EC * 128], BF16))
    wub = ffn.enter_context(nc.sbuf_tensor("wub", [128, 2 * EC * 128], BF16))
    silu_t = ffn.enter_context(nc.sbuf_tensor("silu_t", [128, 512], F32))
    wdb = ffn.enter_context(nc.sbuf_tensor("wdb", [128, 2 * IC * 128], BF16))
    yres = ffn.enter_context(nc.sbuf_tensor("yres", [128, 2 * 512], F32))
    ost = ffn.enter_context(nc.sbuf_tensor("ost", [128, 2 * 512], F32))

    for hf in range(2):
        tcol = hf * 512
        PRE_T = mT.n
        l2bv = l2_sl.bump(16)
        with nc.Block() as bl:
            @bl.sync
            def _(sy):
                sy.wait_ge(mT.h, PRE_T)
                sy.dma_start(out=ln2b[:, :].rearrange("p (c t) -> p c t", t=512),
                             in_=ln2_scr[:, tcol:tcol + 512].rearrange(
                                 "(c p) t -> p c t", p=128)).then_inc(l2_sl.h, 16)

        wgld = [wg_sl[p % 2].bump(16) for p in range(IC)]
        wuld = [wu_sl[p % 2].bump(16) for p in range(IC)]
        gmm = [mT.bump(2) for p in range(IC)]
        sqv = [mS.bump(1) for p in range(IC)]
        hv = [mV.bump(1) for p in range(IC)]
        with nc.Block() as bl:
            @bl.gpsimd
            def _(g):
                for p in range(IC):
                    if p == 0 and hf == 1:
                        g.wait_ge(mT.h, PRE_T)
                    if p >= 2:
                        g.wait_ge(mT.h, gmm[p - 2])
                    g.dma_start(
                        out=wgb[:, (p % 2) * EC * 128:(p % 2 + 1) * EC * 128].rearrange(
                            "p (c n) -> p c n", n=128),
                        in_=wg_scr[:, p * 128:(p + 1) * 128].rearrange(
                            "(c kp) n -> kp c n", kp=128)).then_inc(wg_sl[p % 2].h, 16)
                    g.dma_start(
                        out=wub[:, (p % 2) * EC * 128:(p % 2 + 1) * EC * 128].rearrange(
                            "p (c n) -> p c n", n=128),
                        in_=wu_scr[:, p * 128:(p + 1) * 128].rearrange(
                            "(c kp) n -> kp c n", kp=128)).then_inc(wu_sl[p % 2].h, 16)

            @bl.tensor
            def _(te):
                te.wait_ge(l2_sl.h, l2bv)
                for p in range(IC):
                    te.wait_ge(wg_sl[p % 2].h, wgld[p])
                    te.wait_ge(wu_sl[p % 2].h, wuld[p])
                    if p >= 2:
                        te.wait_ge(mS.h, sqv[p - 2])
                        te.wait_ge(mV.h, hv[p - 2])
                    pg = pE0 if p % 2 == 0 else pE1
                    pu = pC if p % 2 == 0 else pD
                    base = (p % 2) * EC * 128
                    for c in range(EC):
                        mm = te.matmul(pg[:, :],
                                       wgb[:, base + c * 128:base + (c + 1) * 128],
                                       ln2b[:, c * 512:(c + 1) * 512],
                                       start=(c == 0), stop=(c == EC - 1),
                                       skip_group_check=True)
                        if c == EC - 1:
                            mm.then_inc(mT.h, 1)
                    for c in range(EC):
                        mm = te.matmul(pu[:, :],
                                       wub[:, base + c * 128:base + (c + 1) * 128],
                                       ln2b[:, c * 512:(c + 1) * 512],
                                       start=(c == 0), stop=(c == EC - 1),
                                       skip_group_check=True)
                        if c == EC - 1:
                            mm.then_inc(mT.h, 1)

            @bl.scalar
            def _(sc):
                for p in range(IC):
                    pg = pE0 if p % 2 == 0 else pE1
                    sc.wait_ge(mT.h, gmm[p] - 1)
                    sc.activation(out=silu_t[:, :], in_=pg[:, :],
                                  func=AF.Silu).then_inc(mS.h, 1)

            @bl.vector
            def _(ve):
                for p in range(IC):
                    pu = pC if p % 2 == 0 else pD
                    ve.wait_ge(mS.h, sqv[p])
                    ve.wait_ge(mT.h, gmm[p])
                    ve.tensor_tensor(out=h_full[:, p * 512:(p + 1) * 512],
                                     in0=silu_t[:, :], in1=pu[:, :],
                                     op=ALU.mult).then_inc(mV.h, 1)

        HIC = IC // 2
        wdld = [wd_sl[e % 2].bump(32) for e in range(EC)]
        dmm = [mT.bump(1) for e in range(EC)]
        yld = [yr_sl[e % 2].bump(32) for e in range(EC)]
        oev = [mV.bump(1) for e in range(EC)]
        osv = [os_sl[e % 2].bump(16) for e in range(EC)]
        HV_LAST = hv[-1]
        with nc.Block() as bl:
            @bl.gpsimd
            def _(g):
                g.wait_ge(mV.h, HV_LAST)
                for e in range(EC):
                    if e >= 2:
                        g.wait_ge(mT.h, dmm[e - 2])
                    for piece in range(2):
                        r0 = piece * HIC * 128
                        o0 = (e % 2) * IC * 128 + piece * HIC * 128
                        g.dma_start(
                            out=wdb[:, o0:o0 + HIC * 128].rearrange(
                                "p (ic n) -> p ic n", n=128),
                            in_=wd_scr[r0:r0 + HIC * 128,
                                       e * 128:(e + 1) * 128].rearrange(
                                "(ic p) n -> p ic n", p=128)).then_inc(wd_sl[e % 2].h, 16)

            @bl.tensor
            def _(te):
                for e in range(EC):
                    te.wait_ge(wd_sl[e % 2].h, wdld[e])
                    if e >= 2:
                        te.wait_ge(mV.h, oev[e - 2])
                    pd_ = pE0 if e % 2 == 0 else pE1
                    base = (e % 2) * IC * 128
                    for ic in range(IC):
                        mm = te.matmul(pd_[:, :],
                                       wdb[:, base + ic * 128:base + (ic + 1) * 128],
                                       h_full[:, ic * 512:(ic + 1) * 512],
                                       start=(ic == 0), stop=(ic == IC - 1),
                                       skip_group_check=True)
                        if ic == IC - 1:
                            mm.then_inc(mT.h, 1)

            @bl.sync
            def _(sy):
                for e in range(EC):
                    if e >= 2:
                        sy.wait_ge(mV.h, oev[e - 2])
                    for c_ in range(2):
                        base_el = (e * 128 + c_) * TOWN + tcol
                        in_ap = bass.AP(y_scr_t, base_el,
                                        [[2 * TOWN, 4], [8 * TOWN, 16], [1, 512]])
                        sy.dma_start(out=yres[c_ * 64:(c_ + 1) * 64,
                                              (e % 2) * 512:(e % 2 + 1) * 512],
                                     in_=in_ap).then_inc(yr_sl[e % 2].h, 16)
                    sy.wait_ge(mV.h, oev[e])
                    sy.dma_start(out=out_d[e * 128:(e + 1) * 128, tcol:tcol + 512],
                                 in_=ost[:, (e % 2) * 512:(e % 2 + 1) * 512]
                                 ).then_inc(os_sl[e % 2].h, 16)

            @bl.vector
            def _(ve):
                for e in range(EC):
                    pd_ = pE0 if e % 2 == 0 else pE1
                    ve.wait_ge(mT.h, dmm[e])
                    ve.wait_ge(yr_sl[e % 2].h, yld[e])
                    if e >= 2:
                        ve.wait_ge(os_sl[e % 2].h, osv[e - 2])
                    ve.tensor_tensor(out=ost[:, (e % 2) * 512:(e % 2 + 1) * 512],
                                     in0=pd_[:, :],
                                     in1=yres[:, (e % 2) * 512:(e % 2 + 1) * 512],
                                     op=ALU.add).then_inc(mV.h, 1)
    ffn.close()

    if dbg:
        dmp = [
            (nc.dram_tensor("dbg_k", [KV * D, S], BF16, kind="ExternalOutput").ap(), k_scr),
            (nc.dram_tensor("dbg_v", [S, KV * D], BF16, kind="ExternalOutput").ap(), v_scr),
            (nc.dram_tensor("dbg_y", [E, TOWN], F32, kind="ExternalOutput").ap(), y_scr),
            (nc.dram_tensor("dbg_ln2", [E, TOWN], BF16, kind="ExternalOutput").ap(), ln2_scr),
            (nc.dram_tensor("dbg_wg", [E, 512], BF16, kind="ExternalOutput").ap(), wg_scr[:, 0:512]),
            (nc.dram_tensor("dbg_wd", [512, E], BF16, kind="ExternalOutput").ap(), wd_scr[0:512, :]),
        ]
        snap = [(s_.h, s_.n) for s_ in ALLSEMS]
        dv = ps_sl.n
        with nc.Block() as bl:
            @bl.sync
            def _(sy):
                for hh, vv in snap:
                    sy.wait_ge(hh, vv)
                for dst, srcp in dmp:
                    sy.dma_start(out=dst, in_=srcp).then_inc(ps_sl.h, 16)
                sy.wait_ge(ps_sl.h, dv + len(dmp) * 16)
        ps_sl.bump(len(dmp) * 16)
    else:
        snap = [(s_.h, s_.n) for s_ in ALLSEMS]
        with nc.Block() as bl:
            @bl.sync
            def _(sy):
                for hh, vv in snap:
                    sy.wait_ge(hh, vv)

    pes.close()
    es.close()
    return nc


_NC_CACHE = {}


def kernel(**inputs):
    P = _perm128()
    permI = np.concatenate([q * 128 + P for q in range(IC)])
    permE = np.concatenate([q * 128 + P for q in range(EC)])

    x = np.asarray(inputs["L0_attn_x"], np.float32)
    wq = np.ascontiguousarray(np.asarray(inputs["L0_attn_wq"], np.float32))
    wk = np.ascontiguousarray(np.asarray(inputs["L0_attn_wk"], np.float32))
    wv = np.ascontiguousarray(np.asarray(inputs["L0_attn_wv"], np.float32))
    bq = np.asarray(inputs["L0_attn_bq"], np.float32).reshape(1, E)
    bk = np.asarray(inputs["L0_attn_bk"], np.float32).reshape(1, KV * D)
    bv = np.asarray(inputs["L0_attn_bv"], np.float32).reshape(1, KV * D)
    gq = np.ascontiguousarray(np.asarray(inputs["L0_ffn_gate_qweight"], np.int32))
    uq = np.ascontiguousarray(np.asarray(inputs["L0_ffn_up_qweight"], np.int32))
    dq = np.ascontiguousarray(np.asarray(inputs["L0_ffn_down_qweight"], np.int32)[permI])
    gs = np.ascontiguousarray(np.asarray(inputs["L0_ffn_gate_scales"], np.float32)[:, permI])
    us = np.ascontiguousarray(np.asarray(inputs["L0_ffn_up_scales"], np.float32)[:, permI])
    ds = np.ascontiguousarray(np.asarray(inputs["L0_ffn_down_scales"], np.float32)[:, permE])
    gz = np.ascontiguousarray(np.asarray(inputs["L0_ffn_gate_qzeros"], np.int32))
    uz = np.ascontiguousarray(np.asarray(inputs["L0_ffn_up_qzeros"], np.int32))
    dz = np.ascontiguousarray(np.asarray(inputs["L0_ffn_down_qzeros"], np.int32))

    xT = [np.ascontiguousarray(x[b].T) for b in range(B)]

    if "nc" not in _NC_CACHE:
        _NC_CACHE["nc"] = build_nc()
    nc = _NC_CACHE["nc"]

    in_maps = []
    for c in range(8):
        b, t0 = c // 4, (c % 4) * TOWN
        in_maps.append({
            "xT": xT[b],
            "xq": np.ascontiguousarray(xT[b][:, t0:t0 + TOWN]),
            "wq": wq, "wk": wk, "wv": wv,
            "bq": bq, "bk": bk, "bv": bv,
            "gq": gq, "uq": uq, "dq": dq,
            "gs": gs, "us": us, "ds": ds,
            "gz": gz, "uz": uz, "dz": dz,
            "pos0": np.full((64, 1), float(t0), np.float32),
        })

    res = run_bass_kernel_spmd(nc, in_maps, core_ids=list(range(8)))

    out = np.empty((B, S, E), np.float32)
    for c in range(8):
        b, t0 = c // 4, (c % 4) * TOWN
        dev = np.asarray(res.results[c]["out"])
        nat = np.empty_like(dev)
        nat[permE] = dev
        out[b, t0:t0 + TOWN, :] = nat.T
    return out


# version 09
# speedup vs baseline: 1.2268x; 1.1014x over previous
import math
from contextlib import ExitStack

import numpy as np
import concourse.bass as bass
import concourse.mybir as mybir
from concourse.bass_utils import run_bass_kernel_spmd

F32 = mybir.dt.float32
BF16 = mybir.dt.bfloat16
I32 = mybir.dt.int32
AF = mybir.ActivationFunctionType
ALU = mybir.AluOpType

B, S, E, H, KV, D = 2, 4096, 4096, 32, 8, 128
I = 11008
EC = E // 128
IC = I // 128
GW, DW = I // 8, E // 8
TOWN = 1024
NG = S // 512
NQ = TOWN // 512
TWO_PI = float(2 * math.pi)
AWQ_INV = [0, 2, 4, 6, 1, 3, 5, 7]


def _perm128():
    P = np.zeros(128, np.int64)
    for s_ in range(128):
        j, wl = s_ // 16, s_ % 16
        P[s_] = 8 * wl + AWQ_INV[j]
    return P


class Sem:
    def __init__(self, es, nc, name):
        self.h = es.enter_context(nc.semaphore(name))
        self.n = 0

    def bump(self, k):
        self.n += k
        return self.n


def build_nc(dbg=False):
    nc = bass.Bass("TRN2", target_bir_lowering=False)

    xT = nc.dram_tensor("xT", [E, S], F32, kind="ExternalInput").ap()
    xq = nc.dram_tensor("xq", [E, TOWN], F32, kind="ExternalInput").ap()
    wq_d = nc.dram_tensor("wq", [E, E], F32, kind="ExternalInput").ap()
    wk_d = nc.dram_tensor("wk", [E, KV * D], F32, kind="ExternalInput").ap()
    wv_d = nc.dram_tensor("wv", [E, KV * D], F32, kind="ExternalInput").ap()
    bq_d = nc.dram_tensor("bq", [1, E], F32, kind="ExternalInput").ap()
    bk_d = nc.dram_tensor("bk", [1, KV * D], F32, kind="ExternalInput").ap()
    bv_d = nc.dram_tensor("bv", [1, KV * D], F32, kind="ExternalInput").ap()
    gq_d = nc.dram_tensor("gq", [512, GW], I32, kind="ExternalInput").ap()
    uq_d = nc.dram_tensor("uq", [512, GW], I32, kind="ExternalInput").ap()
    dq_d = nc.dram_tensor("dq", [I, DW // 8], I32, kind="ExternalInput").ap()
    gs_d = nc.dram_tensor("gs", [4, I], F32, kind="ExternalInput").ap()
    us_d = nc.dram_tensor("us", [4, I], F32, kind="ExternalInput").ap()
    ds_d = nc.dram_tensor("ds", [86, 512], F32, kind="ExternalInput").ap()
    gz_d = nc.dram_tensor("gz", [4, GW], I32, kind="ExternalInput").ap()
    uz_d = nc.dram_tensor("uz", [4, GW], I32, kind="ExternalInput").ap()
    dz_d = nc.dram_tensor("dz", [86, DW // 8], I32, kind="ExternalInput").ap()
    pos0_d = nc.dram_tensor("pos0", [64, 1], F32, kind="ExternalInput").ap()
    out_d = nc.dram_tensor("out", [E, TOWN], F32, kind="ExternalOutput").ap()

    wg_scr = nc.dram_tensor("wg_scr", [E, I], BF16, kind="Internal").ap()
    wu_scr = nc.dram_tensor("wu_scr", [E, I], BF16, kind="Internal").ap()
    wd_g = nc.dram_tensor("wd_g", [8 * I, 512], BF16, kind="Internal").ap()
    wg_own = nc.dram_tensor("wg_own", [512, I], BF16, kind="Internal").ap()
    wu_own = nc.dram_tensor("wu_own", [512, I], BF16, kind="Internal").ap()
    wd_own = nc.dram_tensor("wd_own", [I, 512], BF16, kind="Internal").ap()
    k_scr = nc.dram_tensor("k_scr", [KV * D, S], BF16, kind="Internal").ap()
    v_scr = nc.dram_tensor("v_scr", [S, KV * D], BF16, kind="Internal").ap()
    y_scr_t = nc.dram_tensor("y_scr", [E, TOWN], F32, kind="Internal")
    y_scr = y_scr_t.ap()
    ln2_scr = nc.dram_tensor("ln2_scr", [E, TOWN], BF16, kind="Internal").ap()
    row_scr = nc.dram_tensor("row_scr", [4, 512], F32, kind="Internal").ap()

    es = ExitStack()
    ones_col = es.enter_context(nc.sbuf_tensor("ones_col", [128, 1], BF16))
    ones2 = es.enter_context(nc.sbuf_tensor("ones2", [2, 128], BF16))
    onesN = es.enter_context(nc.sbuf_tensor("onesN", [2, 512], BF16))
    eps_col = es.enter_context(nc.sbuf_tensor("eps_col", [128, 1], F32))
    barrier_tile = es.enter_context(nc.sbuf_tensor("barrier_tile", [1, 2], F32))

    mT = Sem(es, nc, "mT")
    mV = Sem(es, nc, "mV")
    mS = Sem(es, nc, "mS")
    mG = Sem(es, nc, "mG")
    ALLSEMS = [mT, mV, mS, mG]

    def mksem(name):
        s_ = Sem(es, nc, name)
        ALLSEMS.append(s_)
        return s_

    def mkslots(name, k):
        return [mksem(f"{name}{i}") for i in range(k)]

    sl_ld = mkslots("sl_ld", 2)
    sl_st = mkslots("sl_st", 2)
    ps_sl = mksem("ps_sl")
    br_sl = mksem("br_sl")
    bs_sl = mksem("bs_sl")
    xsl = mkslots("xsl", 4)
    ysl = mkslots("ysl", 4)
    la_sl = mksem("la_sl")
    lb_sl = mksem("lb_sl")
    wk_sl = mkslots("wk_sl", 2)
    wv_sl = mksem("wv_sl")
    ks_sl = mksem("ks_sl")
    vs_sl = mksem("vs_sl")
    qw_sl = mkslots("qw_sl", 2)
    kv_sl = mkslots("kv_sl", 2)
    rh_sl = mksem("rh_sl")
    rh2_sl = mksem("rh2_sl")
    cc_sl = mksem("cc_sl")
    xq_sl = mkslots("xq_sl", 2)
    ys_sl = mkslots("ys_sl", 2)
    l2y_sl = mkslots("l2y_sl", 2)
    ls_sl = mkslots("ls_sl", 2)
    l2_sl = mksem("l2_sl")
    wg_sl = mkslots("wg_sl", 2)
    wu_sl = mkslots("wu_sl", 2)
    wd_sl = mkslots("wd_sl", 2)
    yr_sl = mkslots("yr_sl", 2)
    os_sl = mkslots("os_sl", 2)

    pes = ExitStack()
    pA = pes.enter_context(nc.psum_tensor("pA", [128, 512], F32))
    pB = pes.enter_context(nc.psum_tensor("pB", [128, 512], F32))
    pSa = pes.enter_context(nc.psum_tensor("pSa", [1, 512], F32))
    pC = pes.enter_context(nc.psum_tensor("pC", [128, 512], F32))
    pD = pes.enter_context(nc.psum_tensor("pD", [128, 512], F32))
    pE0 = pes.enter_context(nc.psum_tensor("pE0", [128, 512], F32))
    pE1 = pes.enter_context(nc.psum_tensor("pE1", [128, 512], F32))
    pF0 = pes.enter_context(nc.psum_tensor("pF0", [128, 512], F32))

    def hard_barrier():
        snap = [(s_.h, s_.n) for s_ in ALLSEMS]
        bm = mG.n + 1
        with nc.Block() as bl:
            @bl.gpsimd
            def _(g):
                for hh, vv in snap:
                    g.wait_ge(hh, vv)
                g.memset(barrier_tile[:, :], 0.0).then_inc(mG.h, 1)

            @bl.tensor
            def _(te):
                te.wait_ge(mG.h, bm)

            @bl.vector
            def _(ve):
                ve.wait_ge(mG.h, bm)

            @bl.scalar
            def _(sc):
                sc.wait_ge(mG.h, bm)

            @bl.sync
            def _(sy):
                sy.wait_ge(mG.h, bm)
        mG.bump(1)

    with nc.Block() as bl:
        @bl.gpsimd
        def _(g):
            g.memset(ones_col[:, :], 1.0)
            g.memset(ones2[:, :], 1.0)
            g.memset(onesN[:, :], 1.0)
            g.memset(eps_col[:, :], 1e-5).then_inc(mG.h, 1)
    mG.bump(1)

    deq = ExitStack()
    NSH = I // 2
    WSH = GW // 2
    q_sb = [deq.enter_context(nc.sbuf_tensor(f"q_sb{i}", [128, WSH], I32)) for i in range(2)]
    z_sb = [deq.enter_context(nc.sbuf_tensor(f"z_sb{i}", [128, WSH], I32)) for i in range(2)]
    s_sb = [deq.enter_context(nc.sbuf_tensor(f"s_sb{i}", [128, NSH], F32)) for i in range(2)]
    wn_sb = deq.enter_context(nc.sbuf_tensor("wn_sb", [128, NSH], I32))
    zn_sb = deq.enter_context(nc.sbuf_tensor("zn_sb", [128, NSH], I32))
    db_sb = deq.enter_context(nc.sbuf_tensor("db_sb", [128, NSH], BF16))
    wo_sb = [deq.enter_context(nc.sbuf_tensor(f"wo_sb{i}", [128, NSH], BF16)) for i in range(2)]

    slices = []
    for c in range(4):
        for hf in range(2):
            slices.append((gq_d, gz_d, gs_d, wg_own, c, hf * WSH, hf * NSH, WSH))
    for c in range(4):
        for hf in range(2):
            slices.append((uq_d, uz_d, us_d, wu_own, c, hf * WSH, hf * NSH, WSH))
    for c in range(IC):
        slices.append((dq_d, dz_d, ds_d, wd_own, c, 0, 0, DW // 8))
    nsl = len(slices)

    ld_v = [sl_ld[i % 2].bump(48) for i in range(nsl)]
    dv_v = [mV.bump(1) for i in range(nsl)]
    st_v = [sl_st[i % 2].bump(16) for i in range(nsl)]

    with nc.Block() as bl:
        @bl.sync
        def _(sy):
            for i, (qsrc, zsrc, ssrc, dst, c, woff, noff, nw) in enumerate(slices):
                bi = i % 2
                nnib = nw * 8
                if i >= 2:
                    sy.wait_ge(mV.h, dv_v[i - 2])
                sy.dma_start(out=q_sb[bi][:, 0:nw],
                             in_=qsrc[c * 128:(c + 1) * 128, woff:woff + nw]
                             ).then_inc(sl_ld[bi].h, 16)
                sy.dma_start(out=z_sb[bi][:, 0:nw],
                             in_=zsrc[c:c + 1, woff:woff + nw].broadcast_to([128, nw])
                             ).then_inc(sl_ld[bi].h, 16)
                sy.dma_start(out=s_sb[bi][:, 0:nnib],
                             in_=ssrc[c:c + 1, noff:noff + nnib].broadcast_to([128, nnib])
                             ).then_inc(sl_ld[bi].h, 16)

        @bl.vector
        def _(ve):
            for i, (qsrc, zsrc, ssrc, dst, c, woff, noff, nw) in enumerate(slices):
                bi = i % 2
                nnib = nw * 8
                ve.wait_ge(sl_ld[bi].h, ld_v[i])
                if i >= 2:
                    ve.wait_ge(sl_st[bi].h, st_v[i - 2])
                wv3 = wn_sb[:, 0:nnib].rearrange("p (q j w) -> p q j w", j=8, w=16)
                zv3 = zn_sb[:, 0:nnib].rearrange("p (q j w) -> p q j w", j=8, w=16)
                qv3 = q_sb[bi][:, 0:nw].rearrange("p (q w) -> p q w", w=16)
                zq3 = z_sb[bi][:, 0:nw].rearrange("p (q w) -> p q w", w=16)
                for j in range(8):
                    ve.tensor_scalar(out=wv3[:, :, j, :], in0=qv3, scalar1=4 * j,
                                     scalar2=15, op0=ALU.logical_shift_right,
                                     op1=ALU.bitwise_and)
                    ve.tensor_scalar(out=zv3[:, :, j, :], in0=zq3, scalar1=4 * j,
                                     scalar2=15, op0=ALU.logical_shift_right,
                                     op1=ALU.bitwise_and)
                ve.tensor_tensor(out=db_sb[:, 0:nnib], in0=wn_sb[:, 0:nnib],
                                 in1=zn_sb[:, 0:nnib], op=ALU.subtract)
                ve.tensor_tensor(out=wo_sb[bi][:, 0:nnib], in0=db_sb[:, 0:nnib],
                                 in1=s_sb[bi][:, 0:nnib], op=ALU.mult).then_inc(mV.h, 1)

        @bl.gpsimd
        def _(g):
            for i, (qsrc, zsrc, ssrc, dst, c, woff, noff, nw) in enumerate(slices):
                bi = i % 2
                nnib = nw * 8
                g.wait_ge(mV.h, dv_v[i])
                g.dma_start(out=dst[c * 128:(c + 1) * 128, noff:noff + nnib],
                            in_=wo_sb[bi][:, 0:nnib]).then_inc(sl_st[bi].h, 16)
    ST_ALL = [sl_st[0].n, sl_st[1].n]
    CCV = cc_sl.n + 3
    with nc.Block() as bl:
        @bl.gpsimd
        def _(g):
            g.wait_ge(sl_st[0].h, ST_ALL[0])
            g.wait_ge(sl_st[1].h, ST_ALL[1])
            g.collective_compute(
                "AllGather", ALU.bypass,
                replica_groups=[[0, 1, 2, 3, 4, 5, 6, 7]],
                ins=[wg_own[:, :].opt()],
                outs=[wg_scr[:, :].opt()],
            ).then_inc(cc_sl.h, 1)
            g.collective_compute(
                "AllGather", ALU.bypass,
                replica_groups=[[0, 1, 2, 3, 4, 5, 6, 7]],
                ins=[wu_own[:, :].opt()],
                outs=[wu_scr[:, :].opt()],
            ).then_inc(cc_sl.h, 1)
            g.collective_compute(
                "AllGather", ALU.bypass,
                replica_groups=[[0, 1, 2, 3, 4, 5, 6, 7]],
                ins=[wd_own[:, :].opt()],
                outs=[wd_g[:, :].opt()],
            ).then_inc(cc_sl.h, 1)
    cc_sl.bump(3)
    deq.close()

    hard_barrier()
    ap_es = ExitStack()
    kcos = ap_es.enter_context(nc.sbuf_tensor("kcos", [64, S], F32))
    ksin = ap_es.enter_context(nc.sbuf_tensor("ksin", [64, S], F32))
    qcos = ap_es.enter_context(nc.sbuf_tensor("qcos", [64, TOWN], F32))
    qsin = ap_es.enter_context(nc.sbuf_tensor("qsin", [64, TOWN], F32))
    bqhl = ap_es.enter_context(nc.sbuf_tensor("bqhl", [2, E], BF16))
    bkhl = ap_es.enter_context(nc.sbuf_tensor("bkhl", [2, KV * D], BF16))
    bvhl = ap_es.enter_context(nc.sbuf_tensor("bvhl", [2, KV * D], BF16))
    m_r = ap_es.enter_context(nc.sbuf_tensor("m_r", [1, 512], F32))
    msq_r = ap_es.enter_context(nc.sbuf_tensor("msq_r", [1, 512], F32))
    mm_r = ap_es.enter_context(nc.sbuf_tensor("mm_r", [1, 512], F32))
    rstd_r = ap_es.enter_context(nc.sbuf_tensor("rstd_r", [1, 512], F32))
    negmr_r = ap_es.enter_context(nc.sbuf_tensor("negmr_r", [1, 512], F32))
    rstd_bt = ap_es.enter_context(nc.sbuf_tensor("rstd_bt", [128, 512], F32))
    negmr_bt = ap_es.enter_context(nc.sbuf_tensor("negmr_bt", [128, 512], F32))

    tb_es = ExitStack()
    t_pos = tb_es.enter_context(nc.sbuf_tensor("t_pos", [64, S], F32))
    t_ang = tb_es.enter_context(nc.sbuf_tensor("t_ang", [64, S], F32))
    t_k = tb_es.enter_context(nc.sbuf_tensor("t_k", [64, S], F32))
    t_ki = tb_es.enter_context(nc.sbuf_tensor("t_ki", [64, S], I32))
    t_pidx = tb_es.enter_context(nc.sbuf_tensor("t_pidx", [64, 1], F32))
    t_invf = tb_es.enter_context(nc.sbuf_tensor("t_invf", [64, 1], F32))
    t_pos0 = tb_es.enter_context(nc.sbuf_tensor("t_pos0", [64, 1], F32))
    t_brow = tb_es.enter_context(nc.sbuf_tensor("t_brow", [1, E], F32))
    t_bhi = tb_es.enter_context(nc.sbuf_tensor("t_bhi", [1, E], BF16))
    t_blo = tb_es.enter_context(nc.sbuf_tensor("t_blo", [1, E], BF16))

    IOTA_G = mG.n + 1
    POS0_V = ps_sl.n + 16
    INVF_S = mS.n + 1
    with nc.Block() as bl:
        @bl.gpsimd
        def _(g):
            g.iota(t_pos[:, :], [[1, S]], channel_multiplier=0,
                   allow_small_or_imprecise_dtypes=True)
            g.iota(t_pidx[:, :], [[0, 1]], channel_multiplier=1,
                   allow_small_or_imprecise_dtypes=True).then_inc(mG.h, 1)

        @bl.sync
        def _(sy):
            sy.dma_start(out=t_pos0[:, :], in_=pos0_d[:, :]).then_inc(ps_sl.h, 16)

        @bl.scalar
        def _(sc):
            sc.wait_ge(mG.h, IOTA_G)
            sc.activation(out=t_invf[:, :], in_=t_pidx[:, :], func=AF.Exp,
                          scale=float(-math.log(10000.0) / 64.0)).then_inc(mS.h, 1)
    mG.bump(1)
    ps_sl.bump(16)
    mS.bump(1)

    tables = [(ksin[:, :], S, 0.0, False), (kcos[:, :], S, math.pi / 2, False),
              (qsin[:, :], TOWN, 0.0, True), (qcos[:, :], TOWN, math.pi / 2, True)]
    tb_v = [mV.bump(1) for _ in range(4)]
    with nc.Block() as bl:
        @bl.vector
        def _(ve):
            ve.wait_ge(mS.h, INVF_S)
            ve.wait_ge(ps_sl.h, POS0_V)
            for ti, (dst, width, shift, ap0) in enumerate(tables):
                if ap0:
                    ve.tensor_scalar(out=t_ang[:, 0:width], in0=t_pos[:, 0:width],
                                     scalar1=t_pos0[:, :], scalar2=None, op0=ALU.add)
                    ve.tensor_scalar(out=t_ang[:, 0:width], in0=t_ang[:, 0:width],
                                     scalar1=t_invf[:, :], scalar2=float(shift),
                                     op0=ALU.mult, op1=ALU.add)
                else:
                    ve.tensor_scalar(out=t_ang[:, 0:width], in0=t_pos[:, 0:width],
                                     scalar1=t_invf[:, :], scalar2=float(shift),
                                     op0=ALU.mult, op1=ALU.add)
                ve.tensor_scalar(out=t_k[:, 0:width], in0=t_ang[:, 0:width],
                                 scalar1=float(1.0 / TWO_PI), scalar2=None, op0=ALU.mult)
                ve.tensor_copy(out=t_ki[:, 0:width], in_=t_k[:, 0:width])
                ve.tensor_copy(out=t_k[:, 0:width], in_=t_ki[:, 0:width])
                ve.scalar_tensor_tensor(out=dst, in0=t_k[:, 0:width],
                                        scalar=float(-TWO_PI), in1=t_ang[:, 0:width],
                                        op0=ALU.mult, op1=ALU.add).then_inc(mV.h, 1)

        @bl.scalar
        def _(sc):
            for ti, (dst, width, shift, ap0) in enumerate(tables):
                sc.wait_ge(mV.h, tb_v[ti])
                sc.activation(out=dst, in_=dst, func=AF.Sin).then_inc(mS.h, 1)
    mS.bump(4)

    bias_specs = ((bq_d, bqhl, E), (bk_d, bkhl, KV * D), (bv_d, bvhl, KV * D))
    brow_v = [br_sl.bump(16) for _ in range(3)]
    bsplit_v = [mV.bump(1) for _ in range(3)]
    bpl_v = [bs_sl.bump(32) for _ in range(3)]
    with nc.Block() as bl:
        @bl.sync
        def _(sy):
            for i, (src, dsttile, wdt) in enumerate(bias_specs):
                if i > 0:
                    sy.wait_ge(bs_sl.h, bpl_v[i - 1])
                sy.dma_start(out=t_brow[:, 0:wdt], in_=src[:, :]).then_inc(br_sl.h, 16)

        @bl.vector
        def _(ve):
            for i, (src, dsttile, wdt) in enumerate(bias_specs):
                ve.wait_ge(br_sl.h, brow_v[i])
                if i > 0:
                    ve.wait_ge(bs_sl.h, bpl_v[i - 1])
                ve.tensor_copy(out=t_bhi[:, 0:wdt], in_=t_brow[:, 0:wdt])
                ve.tensor_tensor(out=t_blo[:, 0:wdt], in0=t_brow[:, 0:wdt],
                                 in1=t_bhi[:, 0:wdt], op=ALU.subtract).then_inc(mV.h, 1)

        @bl.gpsimd
        def _(g):
            for i, (src, dsttile, wdt) in enumerate(bias_specs):
                g.wait_ge(mV.h, bsplit_v[i])
                g.dma_start(out=dsttile[0:1, 0:wdt], in_=t_bhi[:, 0:wdt]).then_inc(bs_sl.h, 16)
                g.dma_start(out=dsttile[1:2, 0:wdt], in_=t_blo[:, 0:wdt]).then_inc(bs_sl.h, 16)
    tb_es.close()
    hard_barrier()

    def ln_rows_block(stats_ms, norm_done=None):
        Va = mV.bump(1)
        Sa = mS.bump(1)
        Vb = mV.bump(1)
        Ga = la_sl.bump(32)
        Gb = lb_sl.bump(32)
        with nc.Block() as bl:
            @bl.vector
            def _(ve):
                ve.wait_ge(mT.h, stats_ms)
                ve.tensor_scalar(out=m_r[:, :], in0=pA[0:1, :], scalar1=float(1.0 / 4096.0),
                                 scalar2=None, op0=ALU.mult)
                ve.tensor_scalar(out=msq_r[:, :], in0=pB[0:1, :], scalar1=float(1.0 / 4096.0),
                                 scalar2=None, op0=ALU.mult)
                ve.tensor_tensor(out=mm_r[:, :], in0=m_r[:, :], in1=m_r[:, :], op=ALU.mult)
                ve.tensor_tensor(out=msq_r[:, :], in0=msq_r[:, :], in1=mm_r[:, :],
                                 op=ALU.subtract).then_inc(mV.h, 1)
                ve.wait_ge(mS.h, Sa)
                ve.reciprocal(out=rstd_r[:, :], in_=rstd_r[:, :])
                ve.scalar_tensor_tensor(out=negmr_r[:, :], in0=m_r[:, :], scalar=-1.0,
                                        in1=rstd_r[:, :], op0=ALU.mult,
                                        op1=ALU.mult).then_inc(mV.h, 1)

            @bl.scalar
            def _(sc):
                sc.wait_ge(mV.h, Va)
                sc.activation(out=rstd_r[:, :], in_=msq_r[:, :], func=AF.Sqrt,
                              bias=eps_col[0:1, :]).then_inc(mS.h, 1)

            @bl.gpsimd
            def _(g):
                g.wait_ge(mV.h, Vb)
                if norm_done is not None:
                    g.wait_ge(mV.h, norm_done)
                g.dma_start(out=row_scr[0:1, :], in_=rstd_r[:, :]).then_inc(la_sl.h, 16)
                g.dma_start(out=row_scr[1:2, :], in_=negmr_r[:, :]).then_inc(lb_sl.h, 16)
                g.wait_ge(la_sl.h, Ga - 16)
                g.dma_start(out=rstd_bt[:, :],
                            in_=row_scr[0:1, :].broadcast_to([128, 512])
                            ).then_inc(la_sl.h, 16)
                g.wait_ge(lb_sl.h, Gb - 16)
                g.dma_start(out=negmr_bt[:, :],
                            in_=row_scr[1:2, :].broadcast_to([128, 512])
                            ).then_inc(lb_sl.h, 16)
        return (Ga, Gb)

    def stats_wave(src_ap, col0, xbf_grp, x2_t, dma_guard=None, act_guard=None):
        xld = [xsl[c % 4].bump(16) for c in range(EC)]
        sqv = [mS.bump(1) for c in range(EC)]
        stv = [mT.bump(1) for c in range(EC)]
        with nc.Block() as bl:
            @bl.gpsimd
            def _(g):
                for c in range(EC):
                    if c == 0 and dma_guard is not None:
                        g.wait_ge(dma_guard[0], dma_guard[1])
                    if c >= 4:
                        g.wait_ge(mS.h, sqv[c - 4])
                    g.dma_start(out=xbf_grp[:, c * 512:(c + 1) * 512],
                                in_=src_ap[c * 128:(c + 1) * 128, col0:col0 + 512]
                                ).then_inc(xsl[c % 4].h, 16)

            @bl.scalar
            def _(sc):
                for c in range(EC):
                    sc.wait_ge(xsl[c % 4].h, xld[c])
                    if c == 0 and act_guard is not None:
                        sc.wait_ge(act_guard[0], act_guard[1])
                    if c >= 4:
                        sc.wait_ge(mT.h, stv[c - 4])
                    sc.activation(out=x2_t[:, (c % 4) * 512:(c % 4 + 1) * 512],
                                  in_=xbf_grp[:, c * 512:(c + 1) * 512],
                                  func=AF.Square).then_inc(mS.h, 1)

            @bl.tensor
            def _(te):
                for c in range(EC):
                    te.wait_ge(mS.h, sqv[c])
                    te.matmul(pA[0:1, :], ones_col[:, :],
                              xbf_grp[:, c * 512:(c + 1) * 512],
                              start=(c == 0), stop=(c == EC - 1), skip_group_check=True)
                    te.matmul(pB[0:1, :], ones_col[:, :],
                              x2_t[:, (c % 4) * 512:(c % 4 + 1) * 512],
                              start=(c == 0), stop=(c == EC - 1),
                              skip_group_check=True).then_inc(mT.h, 1)
        return stv[-1]

    def norm_wave(xbf_grp, dst, bcast_ms, tn_t):
        Ga, Gb = bcast_ms
        nv = [mV.bump(1) for c in range(EC)]
        with nc.Block() as bl:
            @bl.vector
            def _(ve):
                ve.wait_ge(la_sl.h, Ga)
                ve.wait_ge(lb_sl.h, Gb)
                for c in range(EC):
                    ve.tensor_tensor(out=tn_t[:, :],
                                     in0=xbf_grp[:, c * 512:(c + 1) * 512],
                                     in1=rstd_bt[:, :], op=ALU.mult)
                    ve.tensor_tensor(out=dst[:, c * 512:(c + 1) * 512], in0=tn_t[:, :],
                                     in1=negmr_bt[:, :], op=ALU.add).then_inc(mV.h, 1)
        return nv[-1]

    def rope_ops(ve, psrc, cos_t, sin_t, col0, dst, dcol, rp):
        ve.tensor_tensor(out=rp[0][:, :], in0=psrc[0:64, :],
                         in1=cos_t[:, col0:col0 + 512], op=ALU.mult)
        ve.tensor_tensor(out=rp[1][:, :], in0=psrc[64:128, :],
                         in1=sin_t[:, col0:col0 + 512], op=ALU.mult)
        ve.tensor_tensor(out=dst[0:64, dcol:dcol + 512], in0=rp[0][:, :],
                         in1=rp[1][:, :], op=ALU.subtract)
        ve.tensor_tensor(out=rp[0][:, :], in0=psrc[0:64, :],
                         in1=sin_t[:, col0:col0 + 512], op=ALU.mult)
        ve.tensor_tensor(out=rp[1][:, :], in0=psrc[64:128, :],
                         in1=cos_t[:, col0:col0 + 512], op=ALU.mult)
        ve.tensor_tensor(out=dst[64:128, dcol:dcol + 512], in0=rp[0][:, :],
                         in1=rp[1][:, :], op=ALU.add).then_inc(mV.h, 1)

    a1 = ExitStack()
    xln = a1.enter_context(nc.sbuf_tensor("xln", [128, EC * 512], BF16))
    xbf_grp = a1.enter_context(nc.sbuf_tensor("xbf_grp", [128, EC * 512], BF16))
    x2_rot = a1.enter_context(nc.sbuf_tensor("x2_rot", [128, 4 * 512], BF16))
    tnorm = a1.enter_context(nc.sbuf_tensor("tnorm", [128, 512], F32))
    wvb = a1.enter_context(nc.sbuf_tensor("wvb", [128, EC * 512], BF16))
    wkb = a1.enter_context(nc.sbuf_tensor("wkb", [128, 2 * EC * 128], BF16))
    kst = a1.enter_context(nc.sbuf_tensor("kst", [128, 8 * 512], BF16))
    vst = a1.enter_context(nc.sbuf_tensor("vst", [128, 4 * 1024], BF16))
    rp1 = [a1.enter_context(nc.sbuf_tensor(f"rp1_{i}", [64, 512], F32)) for i in range(2)]

    ks_done = 0
    vs_done = 0
    prev_norm = None
    prev_stats = None
    prev_vmm_last = None
    hard_barrier()
    for g_i in range(NG):
        col0 = g_i * 512
        stats_ms = stats_wave(
            xT, col0, xbf_grp, x2_rot,
            dma_guard=(mV.h, prev_norm) if prev_norm is not None else None,
            act_guard=(mT.h, prev_stats) if prev_stats is not None else None)
        bcast_ms = ln_rows_block(stats_ms, norm_done=prev_norm)
        norm_ms = norm_wave(xbf_grp, xln, bcast_ms, tnorm)
        prev_norm = norm_ms
        prev_stats = stats_ms

        prev_ks = ks_done
        wkld = [wk_sl[h % 2].bump(16) for h in range(KV)]
        kmm = [mT.bump(1) for h in range(KV)]
        krp = [mV.bump(1) for h in range(KV)]
        with nc.Block() as bl:
            @bl.gpsimd
            def _(g):
                for h in range(KV):
                    if h >= 2:
                        g.wait_ge(mT.h, kmm[h - 2])
                    g.dma_start(
                        out=wkb[:, (h % 2) * EC * 128:(h % 2 + 1) * EC * 128].rearrange(
                            "p (c n) -> p c n", n=128),
                        in_=wk_d[:, h * 128:(h + 1) * 128].rearrange(
                            "(c p) n -> p c n", p=128)).then_inc(wk_sl[h % 2].h, 16)

            @bl.tensor
            def _(te):
                te.wait_ge(mV.h, norm_ms)
                for h in range(KV):
                    te.wait_ge(wk_sl[h % 2].h, wkld[h])
                    if h >= 2:
                        te.wait_ge(mV.h, krp[h - 2])
                    pk = pE0 if h % 2 == 0 else pE1
                    base = (h % 2) * EC * 128
                    for c in range(EC):
                        te.matmul(pk[:, :], wkb[:, base + c * 128:base + (c + 1) * 128],
                                  xln[:, c * 512:(c + 1) * 512],
                                  start=(c == 0), stop=False, skip_group_check=True)
                    te.matmul(pk[:, :], bkhl[:, h * 128:(h + 1) * 128], onesN[:, :],
                              start=False, stop=True,
                              skip_group_check=True).then_inc(mT.h, 1)

            @bl.vector
            def _(ve):
                for h in range(KV):
                    ve.wait_ge(mT.h, kmm[h])
                    if h == 0 and prev_ks:
                        ve.wait_ge(ks_sl.h, prev_ks)
                    pk = pE0 if h % 2 == 0 else pE1
                    rope_ops(ve, pk, kcos, ksin, col0, kst, h * 512, rp1)

            @bl.sync
            def _(sy):
                for h in range(KV):
                    sy.wait_ge(mV.h, krp[h])
                    sy.dma_start(out=k_scr[h * 128:(h + 1) * 128, col0:col0 + 512],
                                 in_=kst[:, h * 512:(h + 1) * 512]).then_inc(ks_sl.h, 16)
        ks_done = ks_sl.bump(KV * 16)
        ROPES_V = mV.n

        prev_vs = vs_done
        for half in range(2):
            PRE_S = mS.n
            wv_guard = prev_vmm_last
            wvld = wv_sl.bump(16)
            vmm = [mT.bump(1) for tt in range(4)]
            vev = [mS.bump(1) for tt in range(4)]
            prev_vmm_last = vmm[-1]
            with nc.Block() as bl:
                @bl.gpsimd
                def _(g):
                    if wv_guard is not None:
                        g.wait_ge(mT.h, wv_guard)
                    g.dma_start(
                        out=wvb[:, :].rearrange("p (c n) -> p c n", n=512),
                        in_=wv_d[:, half * 512:(half + 1) * 512].rearrange(
                            "(c p) n -> p c n", p=128)).then_inc(wv_sl.h, 16)

                @bl.tensor
                def _(te):
                    te.wait_ge(wv_sl.h, wvld)
                    te.wait_ge(mS.h, PRE_S)
                    if half == 0:
                        te.wait_ge(mV.h, ROPES_V)
                    for tt in range(4):
                        pv = pF0 if tt % 2 == 0 else pE0
                        if tt >= 2:
                            te.wait_ge(mS.h, vev[tt - 2])
                        for c in range(EC):
                            te.matmul(pv[:, :],
                                      xln[:, c * 512 + tt * 128:c * 512 + tt * 128 + 128],
                                      wvb[:, c * 512:(c + 1) * 512],
                                      start=(c == 0), stop=False, skip_group_check=True)
                        te.matmul(pv[:, :], ones2[:, :],
                                  bvhl[:, half * 512:(half + 1) * 512],
                                  start=False, stop=True,
                                  skip_group_check=True).then_inc(mT.h, 1)

                @bl.scalar
                def _(sc):
                    for tt in range(4):
                        pv = pF0 if tt % 2 == 0 else pE0
                        sc.wait_ge(mT.h, vmm[tt])
                        if tt == 0 and half == 0 and prev_vs:
                            sc.wait_ge(vs_sl.h, prev_vs)
                        sc.activation(out=vst[:, tt * 1024 + half * 512:
                                              tt * 1024 + half * 512 + 512],
                                      in_=pv[:, :], func=AF.Copy).then_inc(mS.h, 1)
        VS_ALL = mS.n
        with nc.Block() as bl:
            @bl.sync
            def _(sy):
                sy.wait_ge(mS.h, VS_ALL)
                for tt in range(4):
                    sy.dma_start(out=v_scr[col0 + tt * 128:col0 + tt * 128 + 128, :],
                                 in_=vst[:, tt * 1024:(tt + 1) * 1024]).then_inc(vs_sl.h, 16)
        vs_done = vs_sl.bump(4 * 16)
    A1_KS = ks_sl.n
    A1_VS = vs_sl.n
    a1.close()

    a2p = ExitStack()
    q_tile = a2p.enter_context(nc.sbuf_tensor("q_tile", [128, EC * 512], BF16))
    for qg in range(NQ):
        qcol = qg * 512
        hard_barrier()
        p1 = ExitStack()
        xlnq = p1.enter_context(nc.sbuf_tensor(f"xlnq{qg}", [128, EC * 512], BF16))
        wqb = p1.enter_context(nc.sbuf_tensor(f"wqb{qg}", [128, 2 * EC * 128], BF16))
        xbf2 = p1.enter_context(nc.sbuf_tensor(f"xbf2{qg}", [128, EC * 512], BF16))
        x22 = p1.enter_context(nc.sbuf_tensor(f"x22{qg}", [128, 4 * 512], BF16))
        tn2 = p1.enter_context(nc.sbuf_tensor(f"tn2{qg}", [128, 512], F32))
        rp2 = [p1.enter_context(nc.sbuf_tensor(f"rp2_{qg}_{i}", [64, 512], F32))
               for i in range(2)]

        stats2 = stats_wave(xq, qcol, xbf2, x22)
        bcast2 = ln_rows_block(stats2, norm_done=mV.n)
        normq = norm_wave(xbf2, xlnq, bcast2, tn2)

        qwld = [qw_sl[m % 2].bump(16) for m in range(EC)]
        qmm = [mT.bump(1) for m in range(EC)]
        qrp = [mV.bump(1) for m in range(EC)]
        with nc.Block() as bl:
            @bl.gpsimd
            def _(g):
                for m in range(EC):
                    if m >= 2:
                        g.wait_ge(mT.h, qmm[m - 2])
                    g.dma_start(
                        out=wqb[:, (m % 2) * EC * 128:(m % 2 + 1) * EC * 128].rearrange(
                            "p (c n) -> p c n", n=128),
                        in_=wq_d[:, m * 128:(m + 1) * 128].rearrange(
                            "(c p) n -> p c n", p=128)).then_inc(qw_sl[m % 2].h, 16)

            @bl.tensor
            def _(te):
                te.wait_ge(mV.h, normq)
                for m in range(EC):
                    te.wait_ge(qw_sl[m % 2].h, qwld[m])
                    if m >= 2:
                        te.wait_ge(mV.h, qrp[m - 2])
                    pq = pE0 if m % 2 == 0 else pE1
                    base = (m % 2) * EC * 128
                    for c in range(EC):
                        te.matmul(pq[:, :], wqb[:, base + c * 128:base + (c + 1) * 128],
                                  xlnq[:, c * 512:(c + 1) * 512],
                                  start=(c == 0), stop=False, skip_group_check=True)
                    te.matmul(pq[:, :], bqhl[:, m * 128:(m + 1) * 128], onesN[:, :],
                              start=False, stop=True,
                              skip_group_check=True).then_inc(mT.h, 1)

            @bl.vector
            def _(ve):
                for m in range(EC):
                    ve.wait_ge(mT.h, qmm[m])
                    pq = pE0 if m % 2 == 0 else pE1
                    rope_ops(ve, pq, qcos, qsin, qcol, q_tile, m * 512, rp2)
        p1.close()
        hard_barrier()

        p2 = ExitStack()
        khb = p2.enter_context(nc.sbuf_tensor(f"khb{qg}", [128, 2 * S], BF16))
        vhb = p2.enter_context(nc.sbuf_tensor(f"vhb{qg}", [128, 2 * EC * 128], BF16))
        expb = p2.enter_context(nc.sbuf_tensor(f"expb{qg}", [128, 4 * 512], BF16))
        srecip = p2.enter_context(nc.sbuf_tensor(f"srecip{qg}", [128, 512], F32))
        attn_t = p2.enter_context(nc.sbuf_tensor(f"attn_t{qg}", [128, 512], F32))
        yst = p2.enter_context(nc.sbuf_tensor(f"yst{qg}", [128, 2 * 512], F32))
        ybf_grp = p2.enter_context(nc.sbuf_tensor(f"ybfg{qg}", [128, EC * 512], BF16))
        y22_rot = p2.enter_context(nc.sbuf_tensor(f"y22r{qg}", [128, 4 * 512], BF16))
        xqr = p2.enter_context(nc.sbuf_tensor(f"xqr{qg}", [128, 2 * 512], F32))
        ln2st = p2.enter_context(nc.sbuf_tensor(f"ln2st{qg}", [128, 2 * 512], BF16))
        rrow = p2.enter_context(nc.sbuf_tensor(f"rrow{qg}", [1, 512], F32))
        rrow_b = p2.enter_context(nc.sbuf_tensor(f"rrowb{qg}", [1, 512], F32))
        srecip_b = p2.enter_context(nc.sbuf_tensor(f"srecipb{qg}", [128, 512], F32))

        evac_ms = {}
        ystore_v = {}
        h_mm_done = {}
        last_exv_a = None
        last_exv_b = None
        prev_Vr_a = None
        prev_Vr_b = None
        A2S_T = mT.n
        A2S_V = mV.n
        QRP_LAST = qrp[-1]
        for h in range(KV):
            kvld = kv_sl[h % 2].bump(32)
            with nc.Block() as bl:
                @bl.sync
                def _(sy):
                    if h == 0:
                        sy.wait_ge(mT.h, A2S_T)
                        sy.wait_ge(mV.h, A2S_V)
                        if qg == 0:
                            sy.wait_ge(ks_sl.h, A1_KS)
                            sy.wait_ge(vs_sl.h, A1_VS)
                    if h >= 2:
                        sy.wait_ge(mT.h, h_mm_done[h - 2])
                    sy.dma_start(out=khb[:, (h % 2) * S:(h % 2 + 1) * S],
                                 in_=k_scr[h * 128:(h + 1) * 128, :]
                                 ).then_inc(kv_sl[h % 2].h, 16)
                    sy.dma_start(
                        out=vhb[:, (h % 2) * EC * 128:(h % 2 + 1) * EC * 128].rearrange(
                            "p (kc n) -> p kc n", n=128),
                        in_=v_scr[:, h * 128:(h + 1) * 128].rearrange(
                            "(kc kp) n -> kp kc n", kp=128)).then_inc(kv_sl[h % 2].h, 16)
            kb = (h % 2) * S
            vb = (h % 2) * EC * 128

            for sub in (0, 2):
                qa = h * 4 + sub
                qb = qa + 1
                SCA = [pE0, pE1]
                SCB2 = [pA, pB]
                S0 = mS.n
                exv_a = [S0 + 2 * kc + 1 for kc in range(32)]
                exv_b = [S0 + 2 * kc + 2 for kc in range(32)]
                scv_a = {}
                scv_b = {}
                pvv_a = {}
                pvv_b = {}
                t_ = mT.n
                for kc in range(2):
                    t_ += 1
                    scv_a[kc] = t_
                    t_ += 1
                    scv_b[kc] = t_
                for kc in range(32):
                    t_ += 1
                    pvv_a[kc] = t_
                    t_ += 1
                    pvv_b[kc] = t_
                    if kc + 2 < 32:
                        t_ += 1
                        scv_a[kc + 2] = t_
                        t_ += 1
                        scv_b[kc + 2] = t_
                mT.n = t_

                with nc.Block() as bl:
                    @bl.tensor
                    def _(te):
                        te.wait_ge(kv_sl[h % 2].h, kvld)
                        if qa == 0:
                            te.wait_ge(mV.h, QRP_LAST)

                        def sc_mm(stream, kc):
                            if stream == 0:
                                psc, exv, lexv, qh_ = SCA[kc % 2], exv_a, last_exv_a, qa
                            else:
                                psc, exv, lexv, qh_ = SCB2[kc % 2], exv_b, last_exv_b, qb
                            if kc >= 2:
                                te.wait_ge(mS.h, exv[kc - 2])
                            elif lexv is not None:
                                te.wait_ge(mS.h, lexv[30 + kc])
                            te.matmul(psc[:, :],
                                      khb[:, kb + kc * 128:kb + (kc + 1) * 128],
                                      q_tile[:, qh_ * 512:(qh_ + 1) * 512],
                                      start=True, stop=True,
                                      skip_group_check=True).then_inc(mT.h, 1)
                        for kc in range(2):
                            sc_mm(0, kc)
                            sc_mm(1, kc)
                        for kc in range(32):
                            te.wait_ge(mS.h, exv_a[kc])
                            if kc == 0:
                                if prev_Vr_a is not None:
                                    te.wait_ge(mV.h, prev_Vr_a)
                                if qa >= 4:
                                    te.wait_ge(mV.h, evac_ms[qa - 2])
                            esl = slice((kc % 2) * 512, (kc % 2 + 1) * 512)
                            te.matmul(pSa[:, :], ones_col[:, :], expb[:, esl],
                                      start=(kc == 0), stop=(kc == 31),
                                      skip_group_check=True)
                            te.matmul(pF0[:, :],
                                      vhb[:, vb + kc * 128:vb + (kc + 1) * 128],
                                      expb[:, esl], start=(kc == 0), stop=(kc == 31),
                                      skip_group_check=True).then_inc(mT.h, 1)
                            te.wait_ge(mS.h, exv_b[kc])
                            if kc == 0:
                                if prev_Vr_b is not None:
                                    te.wait_ge(mV.h, prev_Vr_b)
                                if qb >= 4:
                                    te.wait_ge(mV.h, evac_ms[qb - 2])
                            esl2 = slice((2 + kc % 2) * 512, (2 + kc % 2 + 1) * 512)
                            te.matmul(pC[0:1, :], ones_col[:, :], expb[:, esl2],
                                      start=(kc == 0), stop=(kc == 31),
                                      skip_group_check=True)
                            te.matmul(pD[:, :],
                                      vhb[:, vb + kc * 128:vb + (kc + 1) * 128],
                                      expb[:, esl2], start=(kc == 0), stop=(kc == 31),
                                      skip_group_check=True).then_inc(mT.h, 1)
                            if kc + 2 < 32:
                                sc_mm(0, kc + 2)
                                sc_mm(1, kc + 2)

                    @bl.scalar
                    def _(sc):
                        for kc in range(32):
                            sc.wait_ge(mT.h, scv_a[kc])
                            if kc >= 2:
                                sc.wait_ge(mT.h, pvv_a[kc - 2])
                            sc.activation(out=expb[:, (kc % 2) * 512:(kc % 2 + 1) * 512],
                                          in_=SCA[kc % 2][:, :], func=AF.Exp,
                                          scale=float(1.0 / math.sqrt(D))
                                          ).then_inc(mS.h, 1)
                            sc.wait_ge(mT.h, scv_b[kc])
                            if kc >= 2:
                                sc.wait_ge(mT.h, pvv_b[kc - 2])
                            sc.activation(out=expb[:, (2 + kc % 2) * 512:(2 + kc % 2 + 1) * 512],
                                          in_=SCB2[kc % 2][:, :], func=AF.Exp,
                                          scale=float(1.0 / math.sqrt(D))
                                          ).then_inc(mS.h, 1)
                mS.bump(64)

                Vra = mV.bump(1)
                Vrb = mV.bump(1)
                Ga_ = rh_sl.bump(32)
                Gb_ = rh2_sl.bump(32)
                Lqa = xq_sl[0].bump(16)
                Lqb = xq_sl[1].bump(16)
                Vea = mV.bump(1)
                Veb = mV.bump(1)
                Qsa = ys_sl[0].bump(16)
                Qsb = ys_sl[1].bump(16)
                with nc.Block() as bl:
                    @bl.vector
                    def _(ve):
                        ve.wait_ge(mT.h, pvv_a[31])
                        ve.reciprocal(out=rrow[:, :], in_=pSa[:, :]).then_inc(mV.h, 1)
                        ve.wait_ge(mT.h, pvv_b[31])
                        ve.reciprocal(out=rrow_b[:, :], in_=pC[0:1, :]).then_inc(mV.h, 1)
                        ve.wait_ge(rh_sl.h, Ga_)
                        ve.wait_ge(xq_sl[0].h, Lqa)
                        if qa >= 4:
                            ve.wait_ge(ys_sl[0].h, ystore_v[qa - 2])
                        ve.tensor_tensor(out=attn_t[:, :], in0=pF0[:, :],
                                         in1=srecip[:, :], op=ALU.mult)
                        ve.tensor_tensor(out=yst[:, 0:512], in0=attn_t[:, :],
                                         in1=xqr[:, 0:512],
                                         op=ALU.add).then_inc(mV.h, 1)
                        ve.wait_ge(rh2_sl.h, Gb_)
                        ve.wait_ge(xq_sl[1].h, Lqb)
                        if qb >= 4:
                            ve.wait_ge(ys_sl[1].h, ystore_v[qb - 2])
                        ve.tensor_tensor(out=attn_t[:, :], in0=pD[:, :],
                                         in1=srecip_b[:, :], op=ALU.mult)
                        ve.tensor_tensor(out=yst[:, 512:1024], in0=attn_t[:, :],
                                         in1=xqr[:, 512:1024],
                                         op=ALU.add).then_inc(mV.h, 1)

                    @bl.gpsimd
                    def _(g):
                        g.wait_ge(mV.h, Vra)
                        g.dma_start(out=row_scr[2:3, :], in_=rrow[:, :]).then_inc(rh_sl.h, 16)
                        g.wait_ge(mV.h, Vrb)
                        g.dma_start(out=row_scr[3:4, :], in_=rrow_b[:, :]).then_inc(rh2_sl.h, 16)
                        g.wait_ge(rh_sl.h, Ga_ - 16)
                        g.dma_start(out=srecip[:, :],
                                    in_=row_scr[2:3, :].broadcast_to([128, 512])
                                    ).then_inc(rh_sl.h, 16)
                        g.wait_ge(rh2_sl.h, Gb_ - 16)
                        g.dma_start(out=srecip_b[:, :],
                                    in_=row_scr[3:4, :].broadcast_to([128, 512])
                                    ).then_inc(rh2_sl.h, 16)

                    @bl.sync
                    def _(sy):
                        if qa >= 4:
                            sy.wait_ge(mV.h, evac_ms[qa - 2])
                        sy.dma_start(out=xqr[:, 0:512],
                                     in_=xq[qa * 128:(qa + 1) * 128, qcol:qcol + 512]
                                     ).then_inc(xq_sl[0].h, 16)
                        if qb >= 4:
                            sy.wait_ge(mV.h, evac_ms[qb - 2])
                        sy.dma_start(out=xqr[:, 512:1024],
                                     in_=xq[qb * 128:(qb + 1) * 128, qcol:qcol + 512]
                                     ).then_inc(xq_sl[1].h, 16)
                        sy.wait_ge(mV.h, Vea)
                        sy.dma_start(out=y_scr[qa * 128:(qa + 1) * 128, qcol:qcol + 512],
                                     in_=yst[:, 0:512]).then_inc(ys_sl[0].h, 16)
                        sy.wait_ge(mV.h, Veb)
                        sy.dma_start(out=y_scr[qb * 128:(qb + 1) * 128, qcol:qcol + 512],
                                     in_=yst[:, 512:1024]).then_inc(ys_sl[1].h, 16)
                evac_ms[qa] = Vea
                evac_ms[qb] = Veb
                ystore_v[qa] = Qsa
                ystore_v[qb] = Qsb
                last_exv_a = exv_a
                last_exv_b = exv_b
                prev_Vr_a = Vra
                prev_Vr_b = Vrb
            h_mm_done[h] = mT.n

        YALL0 = [ys_sl[0].n, ys_sl[1].n]
        with nc.Block() as bl:
            @bl.gpsimd
            def _(g):
                g.wait_ge(ys_sl[0].h, YALL0[0])
                g.wait_ge(ys_sl[1].h, YALL0[1])
                g.memset(barrier_tile[:, :], 0.0).then_inc(mG.h, 1)
        YB = mG.bump(1)
        with nc.Block() as bl:
            @bl.gpsimd
            def _(g):
                g.wait_ge(mG.h, YB)
        stats3 = stats_wave(y_scr, qcol, ybf_grp, y22_rot)
        bcast3 = ln_rows_block(stats3, norm_done=mV.n)

        l2v = [mV.bump(1) for c in range(EC)]
        l2s = [ls_sl[c % 2].bump(16) for c in range(EC)]
        with nc.Block() as bl:
            @bl.vector
            def _(ve):
                ve.wait_ge(la_sl.h, bcast3[0])
                ve.wait_ge(lb_sl.h, bcast3[1])
                for c in range(EC):
                    if c >= 2:
                        ve.wait_ge(ls_sl[c % 2].h, l2s[c - 2])
                    ve.tensor_tensor(out=attn_t[:, :],
                                     in0=ybf_grp[:, c * 512:(c + 1) * 512],
                                     in1=rstd_bt[:, :], op=ALU.mult)
                    ve.tensor_tensor(out=ln2st[:, (c % 2) * 512:(c % 2 + 1) * 512],
                                     in0=attn_t[:, :], in1=negmr_bt[:, :],
                                     op=ALU.add).then_inc(mV.h, 1)

            @bl.gpsimd
            def _(g):
                for c in range(EC):
                    g.wait_ge(mV.h, l2v[c])
                    g.dma_start(out=ln2_scr[c * 128:(c + 1) * 128, qcol:qcol + 512],
                                in_=ln2st[:, (c % 2) * 512:(c % 2 + 1) * 512]
                                ).then_inc(ls_sl[c % 2].h, 16)
        p2.close()
    a2p.close()
    ap_es.close()

    hard_barrier()
    ffn = ExitStack()
    ln2b = ffn.enter_context(nc.sbuf_tensor("ln2b", [128, EC * 512], BF16))
    h_full = ffn.enter_context(nc.sbuf_tensor("h_full", [128, IC * 512], BF16))
    wgb = ffn.enter_context(nc.sbuf_tensor("wgb", [128, 2 * EC * 128], BF16))
    wub = ffn.enter_context(nc.sbuf_tensor("wub", [128, 2 * EC * 128], BF16))
    silu_t = ffn.enter_context(nc.sbuf_tensor("silu_t", [128, 512], F32))
    wdb = ffn.enter_context(nc.sbuf_tensor("wdb", [128, 2 * IC * 128], BF16))
    yres = ffn.enter_context(nc.sbuf_tensor("yres", [128, 2 * 512], F32))
    ost = ffn.enter_context(nc.sbuf_tensor("ost", [128, 2 * 512], F32))

    for hf in range(2):
        tcol = hf * 512
        PRE_T = mT.n
        l2bv = l2_sl.bump(16)
        with nc.Block() as bl:
            @bl.sync
            def _(sy):
                sy.wait_ge(mT.h, PRE_T)
                if hf == 0:
                    sy.wait_ge(cc_sl.h, CCV)
                sy.dma_start(out=ln2b[:, :].rearrange("p (c t) -> p c t", t=512),
                             in_=ln2_scr[:, tcol:tcol + 512].rearrange(
                                 "(c p) t -> p c t", p=128)).then_inc(l2_sl.h, 16)

        wgld = [wg_sl[p % 2].bump(16) for p in range(IC)]
        wuld = [wu_sl[p % 2].bump(16) for p in range(IC)]
        gmm = [mT.bump(2) for p in range(IC)]
        sqv = [mS.bump(1) for p in range(IC)]
        hv = [mV.bump(1) for p in range(IC)]
        with nc.Block() as bl:
            @bl.gpsimd
            def _(g):
                for p in range(IC):
                    if p == 0 and hf == 1:
                        g.wait_ge(mT.h, PRE_T)
                    if p >= 2:
                        g.wait_ge(mT.h, gmm[p - 2])
                    g.dma_start(
                        out=wgb[:, (p % 2) * EC * 128:(p % 2 + 1) * EC * 128].rearrange(
                            "p (c n) -> p c n", n=128),
                        in_=wg_scr[:, p * 128:(p + 1) * 128].rearrange(
                            "(c kp) n -> kp c n", kp=128)).then_inc(wg_sl[p % 2].h, 16)
                    g.dma_start(
                        out=wub[:, (p % 2) * EC * 128:(p % 2 + 1) * EC * 128].rearrange(
                            "p (c n) -> p c n", n=128),
                        in_=wu_scr[:, p * 128:(p + 1) * 128].rearrange(
                            "(c kp) n -> kp c n", kp=128)).then_inc(wu_sl[p % 2].h, 16)

            @bl.tensor
            def _(te):
                te.wait_ge(l2_sl.h, l2bv)
                for p in range(IC):
                    te.wait_ge(wg_sl[p % 2].h, wgld[p])
                    te.wait_ge(wu_sl[p % 2].h, wuld[p])
                    if p >= 2:
                        te.wait_ge(mS.h, sqv[p - 2])
                        te.wait_ge(mV.h, hv[p - 2])
                    pg = pE0 if p % 2 == 0 else pE1
                    pu = pC if p % 2 == 0 else pD
                    base = (p % 2) * EC * 128
                    for c in range(EC):
                        mm = te.matmul(pg[:, :],
                                       wgb[:, base + c * 128:base + (c + 1) * 128],
                                       ln2b[:, c * 512:(c + 1) * 512],
                                       start=(c == 0), stop=(c == EC - 1),
                                       skip_group_check=True)
                        if c == EC - 1:
                            mm.then_inc(mT.h, 1)
                    for c in range(EC):
                        mm = te.matmul(pu[:, :],
                                       wub[:, base + c * 128:base + (c + 1) * 128],
                                       ln2b[:, c * 512:(c + 1) * 512],
                                       start=(c == 0), stop=(c == EC - 1),
                                       skip_group_check=True)
                        if c == EC - 1:
                            mm.then_inc(mT.h, 1)

            @bl.scalar
            def _(sc):
                for p in range(IC):
                    pg = pE0 if p % 2 == 0 else pE1
                    sc.wait_ge(mT.h, gmm[p] - 1)
                    sc.activation(out=silu_t[:, :], in_=pg[:, :],
                                  func=AF.Silu).then_inc(mS.h, 1)

            @bl.vector
            def _(ve):
                for p in range(IC):
                    pu = pC if p % 2 == 0 else pD
                    ve.wait_ge(mS.h, sqv[p])
                    ve.wait_ge(mT.h, gmm[p])
                    ve.tensor_tensor(out=h_full[:, p * 512:(p + 1) * 512],
                                     in0=silu_t[:, :], in1=pu[:, :],
                                     op=ALU.mult).then_inc(mV.h, 1)

        HIC = IC // 2
        wdld = [wd_sl[e % 2].bump(32) for e in range(EC)]
        dmm = [mT.bump(1) for e in range(EC)]
        yld = [yr_sl[e % 2].bump(32) for e in range(EC)]
        oev = [mV.bump(1) for e in range(EC)]
        osv = [os_sl[e % 2].bump(16) for e in range(EC)]
        HV_LAST = hv[-1]
        with nc.Block() as bl:
            @bl.gpsimd
            def _(g):
                g.wait_ge(mV.h, HV_LAST)
                for e in range(EC):
                    if e >= 2:
                        g.wait_ge(mT.h, dmm[e - 2])
                    for piece in range(2):
                        r0 = piece * HIC * 128
                        o0 = (e % 2) * IC * 128 + piece * HIC * 128
                        sh = e // 4
                        l0 = (e % 4) * 128
                        g.dma_start(
                            out=wdb[:, o0:o0 + HIC * 128].rearrange(
                                "p (ic n) -> p ic n", n=128),
                            in_=wd_g[sh * I + r0:sh * I + r0 + HIC * 128,
                                     l0:l0 + 128].rearrange(
                                "(ic p) n -> p ic n", p=128)).then_inc(wd_sl[e % 2].h, 16)

            @bl.tensor
            def _(te):
                for e in range(EC):
                    te.wait_ge(wd_sl[e % 2].h, wdld[e])
                    if e >= 2:
                        te.wait_ge(mV.h, oev[e - 2])
                    pd_ = pE0 if e % 2 == 0 else pE1
                    base = (e % 2) * IC * 128
                    for ic in range(IC):
                        mm = te.matmul(pd_[:, :],
                                       wdb[:, base + ic * 128:base + (ic + 1) * 128],
                                       h_full[:, ic * 512:(ic + 1) * 512],
                                       start=(ic == 0), stop=(ic == IC - 1),
                                       skip_group_check=True)
                        if ic == IC - 1:
                            mm.then_inc(mT.h, 1)

            @bl.sync
            def _(sy):
                for e in range(EC):
                    if e >= 2:
                        sy.wait_ge(mV.h, oev[e - 2])
                    for c_ in range(2):
                        base_el = (e * 128 + c_) * TOWN + tcol
                        in_ap = bass.AP(y_scr_t, base_el,
                                        [[2 * TOWN, 4], [8 * TOWN, 16], [1, 512]])
                        sy.dma_start(out=yres[c_ * 64:(c_ + 1) * 64,
                                              (e % 2) * 512:(e % 2 + 1) * 512],
                                     in_=in_ap).then_inc(yr_sl[e % 2].h, 16)
                    sy.wait_ge(mV.h, oev[e])
                    sy.dma_start(out=out_d[e * 128:(e + 1) * 128, tcol:tcol + 512],
                                 in_=ost[:, (e % 2) * 512:(e % 2 + 1) * 512]
                                 ).then_inc(os_sl[e % 2].h, 16)

            @bl.vector
            def _(ve):
                for e in range(EC):
                    pd_ = pE0 if e % 2 == 0 else pE1
                    ve.wait_ge(mT.h, dmm[e])
                    ve.wait_ge(yr_sl[e % 2].h, yld[e])
                    if e >= 2:
                        ve.wait_ge(os_sl[e % 2].h, osv[e - 2])
                    ve.tensor_tensor(out=ost[:, (e % 2) * 512:(e % 2 + 1) * 512],
                                     in0=pd_[:, :],
                                     in1=yres[:, (e % 2) * 512:(e % 2 + 1) * 512],
                                     op=ALU.add).then_inc(mV.h, 1)
    ffn.close()

    if dbg:
        dmp = [
            (nc.dram_tensor("dbg_k", [KV * D, S], BF16, kind="ExternalOutput").ap(), k_scr),
            (nc.dram_tensor("dbg_v", [S, KV * D], BF16, kind="ExternalOutput").ap(), v_scr),
            (nc.dram_tensor("dbg_y", [E, TOWN], F32, kind="ExternalOutput").ap(), y_scr),
            (nc.dram_tensor("dbg_ln2", [E, TOWN], BF16, kind="ExternalOutput").ap(), ln2_scr),
            (nc.dram_tensor("dbg_wg", [E, 512], BF16, kind="ExternalOutput").ap(), wg_scr[:, 0:512]),
            (nc.dram_tensor("dbg_wd", [512, E], BF16, kind="ExternalOutput").ap(), wd_scr[0:512, :]),
        ]
        snap = [(s_.h, s_.n) for s_ in ALLSEMS]
        dv = ps_sl.n
        with nc.Block() as bl:
            @bl.sync
            def _(sy):
                for hh, vv in snap:
                    sy.wait_ge(hh, vv)
                for dst, srcp in dmp:
                    sy.dma_start(out=dst, in_=srcp).then_inc(ps_sl.h, 16)
                sy.wait_ge(ps_sl.h, dv + len(dmp) * 16)
        ps_sl.bump(len(dmp) * 16)
    else:
        snap = [(s_.h, s_.n) for s_ in ALLSEMS]
        with nc.Block() as bl:
            @bl.sync
            def _(sy):
                for hh, vv in snap:
                    sy.wait_ge(hh, vv)

    pes.close()
    es.close()
    return nc


_NC_CACHE = {}


def kernel(**inputs):
    P = _perm128()
    permI = np.concatenate([q * 128 + P for q in range(IC)])
    permE = np.concatenate([q * 128 + P for q in range(EC)])

    x = np.asarray(inputs["L0_attn_x"], np.float32)
    wq = np.ascontiguousarray(np.asarray(inputs["L0_attn_wq"], np.float32))
    wk = np.ascontiguousarray(np.asarray(inputs["L0_attn_wk"], np.float32))
    wv = np.ascontiguousarray(np.asarray(inputs["L0_attn_wv"], np.float32))
    bq = np.asarray(inputs["L0_attn_bq"], np.float32).reshape(1, E)
    bk = np.asarray(inputs["L0_attn_bk"], np.float32).reshape(1, KV * D)
    bv = np.asarray(inputs["L0_attn_bv"], np.float32).reshape(1, KV * D)
    gq = np.asarray(inputs["L0_ffn_gate_qweight"], np.int32)
    uq = np.asarray(inputs["L0_ffn_up_qweight"], np.int32)
    dq = np.asarray(inputs["L0_ffn_down_qweight"], np.int32)[permI]
    gs = np.asarray(inputs["L0_ffn_gate_scales"], np.float32)[:, permI]
    us = np.asarray(inputs["L0_ffn_up_scales"], np.float32)[:, permI]
    ds = np.asarray(inputs["L0_ffn_down_scales"], np.float32)[:, permE]
    gz = np.asarray(inputs["L0_ffn_gate_qzeros"], np.int32)
    uz = np.asarray(inputs["L0_ffn_up_qzeros"], np.int32)
    dz = np.asarray(inputs["L0_ffn_down_qzeros"], np.int32)

    xT = [np.ascontiguousarray(x[b].T) for b in range(B)]

    if "nc" not in _NC_CACHE:
        _NC_CACHE["nc"] = build_nc()
    nc = _NC_CACHE["nc"]

    in_maps = []
    for c in range(8):
        b, t0 = c // 4, (c % 4) * TOWN
        in_maps.append({
            "xT": xT[b],
            "xq": np.ascontiguousarray(xT[b][:, t0:t0 + TOWN]),
            "wq": wq, "wk": wk, "wv": wv,
            "bq": bq, "bk": bk, "bv": bv,
            "gq": np.ascontiguousarray(gq[c * 512:(c + 1) * 512]),
            "uq": np.ascontiguousarray(uq[c * 512:(c + 1) * 512]),
            "dq": np.ascontiguousarray(dq[:, c * 64:(c + 1) * 64]),
            "gs": np.ascontiguousarray(gs[4 * c:4 * c + 4]),
            "us": np.ascontiguousarray(us[4 * c:4 * c + 4]),
            "ds": np.ascontiguousarray(ds[:, c * 512:(c + 1) * 512]),
            "gz": np.ascontiguousarray(gz[4 * c:4 * c + 4]),
            "uz": np.ascontiguousarray(uz[4 * c:4 * c + 4]),
            "dz": np.ascontiguousarray(dz[:, c * 64:(c + 1) * 64]),
            "pos0": np.full((64, 1), float(t0), np.float32),
        })

    res = run_bass_kernel_spmd(nc, in_maps, core_ids=list(range(8)))

    out = np.empty((B, S, E), np.float32)
    for c in range(8):
        b, t0 = c // 4, (c % 4) * TOWN
        dev = np.asarray(res.results[c]["out"])
        nat = np.empty_like(dev)
        nat[permE] = dev
        out[b, t0:t0 + TOWN, :] = nat.T
    return out


# version 10
# speedup vs baseline: 1.2312x; 1.0036x over previous
import math
from contextlib import ExitStack

import numpy as np
import concourse.bass as bass
import concourse.mybir as mybir
from concourse.bass_utils import run_bass_kernel_spmd

F32 = mybir.dt.float32
BF16 = mybir.dt.bfloat16
I32 = mybir.dt.int32
AF = mybir.ActivationFunctionType
ALU = mybir.AluOpType

B, S, E, H, KV, D = 2, 4096, 4096, 32, 8, 128
I = 11008
EC = E // 128
IC = I // 128
GW, DW = I // 8, E // 8
TOWN = 1024
NG = S // 512
NQ = TOWN // 512
TWO_PI = float(2 * math.pi)
AWQ_INV = [0, 2, 4, 6, 1, 3, 5, 7]


def _perm128():
    P = np.zeros(128, np.int64)
    for s_ in range(128):
        j, wl = s_ // 16, s_ % 16
        P[s_] = 8 * wl + AWQ_INV[j]
    return P


class Sem:
    def __init__(self, es, nc, name):
        self.h = es.enter_context(nc.semaphore(name))
        self.n = 0

    def bump(self, k):
        self.n += k
        return self.n


def build_nc(dbg=False):
    nc = bass.Bass("TRN2", target_bir_lowering=False)

    xT = nc.dram_tensor("xT", [E, S], F32, kind="ExternalInput").ap()
    xq = nc.dram_tensor("xq", [E, TOWN], F32, kind="ExternalInput").ap()
    wq_d = nc.dram_tensor("wq", [E, E], F32, kind="ExternalInput").ap()
    wk_d = nc.dram_tensor("wk", [E, KV * D], F32, kind="ExternalInput").ap()
    wv_d = nc.dram_tensor("wv", [E, KV * D], F32, kind="ExternalInput").ap()
    bq_d = nc.dram_tensor("bq", [1, E], F32, kind="ExternalInput").ap()
    bk_d = nc.dram_tensor("bk", [1, KV * D], F32, kind="ExternalInput").ap()
    bv_d = nc.dram_tensor("bv", [1, KV * D], F32, kind="ExternalInput").ap()
    gq_d = nc.dram_tensor("gq", [512, GW], I32, kind="ExternalInput").ap()
    uq_d = nc.dram_tensor("uq", [512, GW], I32, kind="ExternalInput").ap()
    dq_d = nc.dram_tensor("dq", [I, DW // 8], I32, kind="ExternalInput").ap()
    gs_d = nc.dram_tensor("gs", [4, I], F32, kind="ExternalInput").ap()
    us_d = nc.dram_tensor("us", [4, I], F32, kind="ExternalInput").ap()
    ds_d = nc.dram_tensor("ds", [86, 512], F32, kind="ExternalInput").ap()
    gz_d = nc.dram_tensor("gz", [4, GW], I32, kind="ExternalInput").ap()
    uz_d = nc.dram_tensor("uz", [4, GW], I32, kind="ExternalInput").ap()
    dz_d = nc.dram_tensor("dz", [86, DW // 8], I32, kind="ExternalInput").ap()
    pos0_d = nc.dram_tensor("pos0", [64, 1], F32, kind="ExternalInput").ap()
    out_d = nc.dram_tensor("out", [E, TOWN], F32, kind="ExternalOutput").ap()

    wg_scr = nc.dram_tensor("wg_scr", [E, I], BF16, kind="Internal").ap()
    wu_scr = nc.dram_tensor("wu_scr", [E, I], BF16, kind="Internal").ap()
    wd_g = nc.dram_tensor("wd_g", [8 * I, 512], BF16, kind="Internal").ap()
    wg_own = nc.dram_tensor("wg_own", [512, I], BF16, kind="Internal").ap()
    wu_own = nc.dram_tensor("wu_own", [512, I], BF16, kind="Internal").ap()
    wd_own = nc.dram_tensor("wd_own", [I, 512], BF16, kind="Internal").ap()
    k_scr = nc.dram_tensor("k_scr", [KV * D, S], BF16, kind="Internal").ap()
    v_scr = nc.dram_tensor("v_scr", [S, KV * D], BF16, kind="Internal").ap()
    y_scr_t = nc.dram_tensor("y_scr", [E, TOWN], F32, kind="Internal")
    y_scr = y_scr_t.ap()
    ln2_scr = nc.dram_tensor("ln2_scr", [E, TOWN], BF16, kind="Internal").ap()
    row_scr = nc.dram_tensor("row_scr", [4, 512], F32, kind="Internal").ap()

    es = ExitStack()
    ones_col = es.enter_context(nc.sbuf_tensor("ones_col", [128, 1], BF16))
    ones2 = es.enter_context(nc.sbuf_tensor("ones2", [2, 128], BF16))
    onesN = es.enter_context(nc.sbuf_tensor("onesN", [2, 512], BF16))
    eps_col = es.enter_context(nc.sbuf_tensor("eps_col", [128, 1], F32))
    barrier_tile = es.enter_context(nc.sbuf_tensor("barrier_tile", [1, 2], F32))

    mT = Sem(es, nc, "mT")
    mV = Sem(es, nc, "mV")
    mS = Sem(es, nc, "mS")
    mG = Sem(es, nc, "mG")
    ALLSEMS = [mT, mV, mS, mG]

    def mksem(name):
        s_ = Sem(es, nc, name)
        ALLSEMS.append(s_)
        return s_

    def mkslots(name, k):
        return [mksem(f"{name}{i}") for i in range(k)]

    sl_ld = mkslots("sl_ld", 2)
    sl_st = mkslots("sl_st", 2)
    ps_sl = mksem("ps_sl")
    br_sl = mksem("br_sl")
    bs_sl = mksem("bs_sl")
    xsl = mkslots("xsl", 4)
    ysl = mkslots("ysl", 4)
    la_sl = mksem("la_sl")
    lb_sl = mksem("lb_sl")
    wk_sl = mkslots("wk_sl", 2)
    wv_sl = mksem("wv_sl")
    ks_sl = mksem("ks_sl")
    vs_sl = mksem("vs_sl")
    qw_sl = mkslots("qw_sl", 2)
    kv_sl = mkslots("kv_sl", 2)
    rh_sl = mksem("rh_sl")
    rh2_sl = mksem("rh2_sl")
    cc_sl = mksem("cc_sl")
    xq_sl = mkslots("xq_sl", 2)
    ys_sl = mkslots("ys_sl", 2)
    l2y_sl = mkslots("l2y_sl", 2)
    ls_sl = mkslots("ls_sl", 2)
    l2_sl = mksem("l2_sl")
    wg_sl = mkslots("wg_sl", 2)
    wu_sl = mkslots("wu_sl", 2)
    wd_sl = mkslots("wd_sl", 2)
    yr_sl = mkslots("yr_sl", 2)
    os_sl = mkslots("os_sl", 2)

    pes = ExitStack()
    pA = pes.enter_context(nc.psum_tensor("pA", [128, 512], F32))
    pB = pes.enter_context(nc.psum_tensor("pB", [128, 512], F32))
    pSa = pes.enter_context(nc.psum_tensor("pSa", [1, 512], F32))
    pC = pes.enter_context(nc.psum_tensor("pC", [128, 512], F32))
    pD = pes.enter_context(nc.psum_tensor("pD", [128, 512], F32))
    pE0 = pes.enter_context(nc.psum_tensor("pE0", [128, 512], F32))
    pE1 = pes.enter_context(nc.psum_tensor("pE1", [128, 512], F32))
    pF0 = pes.enter_context(nc.psum_tensor("pF0", [128, 512], F32))

    def hard_barrier():
        snap = [(s_.h, s_.n) for s_ in ALLSEMS]
        bm = mG.n + 1
        with nc.Block() as bl:
            @bl.gpsimd
            def _(g):
                for hh, vv in snap:
                    g.wait_ge(hh, vv)
                g.memset(barrier_tile[:, :], 0.0).then_inc(mG.h, 1)

            @bl.tensor
            def _(te):
                te.wait_ge(mG.h, bm)

            @bl.vector
            def _(ve):
                ve.wait_ge(mG.h, bm)

            @bl.scalar
            def _(sc):
                sc.wait_ge(mG.h, bm)

            @bl.sync
            def _(sy):
                sy.wait_ge(mG.h, bm)
        mG.bump(1)

    with nc.Block() as bl:
        @bl.gpsimd
        def _(g):
            g.memset(ones_col[:, :], 1.0)
            g.memset(ones2[:, :], 1.0)
            g.memset(onesN[:, :], 1.0)
            g.memset(eps_col[:, :], 1e-5).then_inc(mG.h, 1)
    mG.bump(1)

    deq = ExitStack()
    NSH = I // 2
    WSH = GW // 2
    q_sb = [deq.enter_context(nc.sbuf_tensor(f"q_sb{i}", [128, WSH], I32)) for i in range(2)]
    z_sb = [deq.enter_context(nc.sbuf_tensor(f"z_sb{i}", [128, WSH], I32)) for i in range(2)]
    s_sb = [deq.enter_context(nc.sbuf_tensor(f"s_sb{i}", [128, NSH], F32)) for i in range(2)]
    wn_sb = deq.enter_context(nc.sbuf_tensor("wn_sb", [128, NSH], I32))
    zn_sb = deq.enter_context(nc.sbuf_tensor("zn_sb", [128, NSH], I32))
    db_sb = deq.enter_context(nc.sbuf_tensor("db_sb", [128, NSH], BF16))
    wo_sb = [deq.enter_context(nc.sbuf_tensor(f"wo_sb{i}", [128, NSH], BF16)) for i in range(2)]

    slices = []
    for c in range(4):
        for hf in range(2):
            slices.append((gq_d, gz_d, gs_d, wg_own, c, hf * WSH, hf * NSH, WSH))
    for c in range(4):
        for hf in range(2):
            slices.append((uq_d, uz_d, us_d, wu_own, c, hf * WSH, hf * NSH, WSH))
    for c in range(IC):
        slices.append((dq_d, dz_d, ds_d, wd_own, c, 0, 0, DW // 8))
    nsl = len(slices)

    ld_v = [sl_ld[i % 2].bump(48) for i in range(nsl)]
    dv_v = [mV.bump(1) for i in range(nsl)]
    st_v = [sl_st[i % 2].bump(16) for i in range(nsl)]

    with nc.Block() as bl:
        @bl.sync
        def _(sy):
            for i, (qsrc, zsrc, ssrc, dst, c, woff, noff, nw) in enumerate(slices):
                bi = i % 2
                nnib = nw * 8
                if i >= 2:
                    sy.wait_ge(mV.h, dv_v[i - 2])
                sy.dma_start(out=q_sb[bi][:, 0:nw],
                             in_=qsrc[c * 128:(c + 1) * 128, woff:woff + nw]
                             ).then_inc(sl_ld[bi].h, 16)
                sy.dma_start(out=z_sb[bi][:, 0:nw],
                             in_=zsrc[c:c + 1, woff:woff + nw].broadcast_to([128, nw])
                             ).then_inc(sl_ld[bi].h, 16)
                sy.dma_start(out=s_sb[bi][:, 0:nnib],
                             in_=ssrc[c:c + 1, noff:noff + nnib].broadcast_to([128, nnib])
                             ).then_inc(sl_ld[bi].h, 16)

        @bl.vector
        def _(ve):
            for i, (qsrc, zsrc, ssrc, dst, c, woff, noff, nw) in enumerate(slices):
                bi = i % 2
                nnib = nw * 8
                ve.wait_ge(sl_ld[bi].h, ld_v[i])
                if i >= 2:
                    ve.wait_ge(sl_st[bi].h, st_v[i - 2])
                wv3 = wn_sb[:, 0:nnib].rearrange("p (q j w) -> p q j w", j=8, w=16)
                zv3 = zn_sb[:, 0:nnib].rearrange("p (q j w) -> p q j w", j=8, w=16)
                qv3 = q_sb[bi][:, 0:nw].rearrange("p (q w) -> p q w", w=16)
                zq3 = z_sb[bi][:, 0:nw].rearrange("p (q w) -> p q w", w=16)
                for j in range(8):
                    ve.tensor_scalar(out=wv3[:, :, j, :], in0=qv3, scalar1=4 * j,
                                     scalar2=15, op0=ALU.logical_shift_right,
                                     op1=ALU.bitwise_and)
                    ve.tensor_scalar(out=zv3[:, :, j, :], in0=zq3, scalar1=4 * j,
                                     scalar2=15, op0=ALU.logical_shift_right,
                                     op1=ALU.bitwise_and)
                ve.tensor_tensor(out=db_sb[:, 0:nnib], in0=wn_sb[:, 0:nnib],
                                 in1=zn_sb[:, 0:nnib], op=ALU.subtract)
                ve.tensor_tensor(out=wo_sb[bi][:, 0:nnib], in0=db_sb[:, 0:nnib],
                                 in1=s_sb[bi][:, 0:nnib], op=ALU.mult).then_inc(mV.h, 1)

        @bl.gpsimd
        def _(g):
            for i, (qsrc, zsrc, ssrc, dst, c, woff, noff, nw) in enumerate(slices):
                bi = i % 2
                nnib = nw * 8
                g.wait_ge(mV.h, dv_v[i])
                g.dma_start(out=dst[c * 128:(c + 1) * 128, noff:noff + nnib],
                            in_=wo_sb[bi][:, 0:nnib]).then_inc(sl_st[bi].h, 16)
    ST_ALL = [sl_st[0].n, sl_st[1].n]
    CCV = cc_sl.n + 3
    with nc.Block() as bl:
        @bl.gpsimd
        def _(g):
            g.wait_ge(sl_st[0].h, ST_ALL[0])
            g.wait_ge(sl_st[1].h, ST_ALL[1])
            g.collective_compute(
                "AllGather", ALU.bypass,
                replica_groups=[[0, 1, 2, 3, 4, 5, 6, 7]],
                ins=[wg_own[:, :].opt()],
                outs=[wg_scr[:, :].opt()],
            ).then_inc(cc_sl.h, 1)
            g.collective_compute(
                "AllGather", ALU.bypass,
                replica_groups=[[0, 1, 2, 3, 4, 5, 6, 7]],
                ins=[wu_own[:, :].opt()],
                outs=[wu_scr[:, :].opt()],
            ).then_inc(cc_sl.h, 1)
            g.collective_compute(
                "AllGather", ALU.bypass,
                replica_groups=[[0, 1, 2, 3, 4, 5, 6, 7]],
                ins=[wd_own[:, :].opt()],
                outs=[wd_g[:, :].opt()],
            ).then_inc(cc_sl.h, 1)
    cc_sl.bump(3)
    deq.close()

    hard_barrier()
    ap_es = ExitStack()
    kcos = ap_es.enter_context(nc.sbuf_tensor("kcos", [64, S], F32))
    ksin = ap_es.enter_context(nc.sbuf_tensor("ksin", [64, S], F32))
    qcos = ap_es.enter_context(nc.sbuf_tensor("qcos", [64, TOWN], F32))
    qsin = ap_es.enter_context(nc.sbuf_tensor("qsin", [64, TOWN], F32))
    bqhl = ap_es.enter_context(nc.sbuf_tensor("bqhl", [2, E], BF16))
    bkhl = ap_es.enter_context(nc.sbuf_tensor("bkhl", [2, KV * D], BF16))
    bvhl = ap_es.enter_context(nc.sbuf_tensor("bvhl", [2, KV * D], BF16))
    m_r = ap_es.enter_context(nc.sbuf_tensor("m_r", [1, 512], F32))
    msq_r = ap_es.enter_context(nc.sbuf_tensor("msq_r", [1, 512], F32))
    mm_r = ap_es.enter_context(nc.sbuf_tensor("mm_r", [1, 512], F32))
    rstd_r = ap_es.enter_context(nc.sbuf_tensor("rstd_r", [1, 512], F32))
    negmr_r = ap_es.enter_context(nc.sbuf_tensor("negmr_r", [1, 512], F32))
    rstd_bt = ap_es.enter_context(nc.sbuf_tensor("rstd_bt", [128, 512], F32))
    negmr_bt = ap_es.enter_context(nc.sbuf_tensor("negmr_bt", [128, 512], F32))

    tb_es = ExitStack()
    t_pos = tb_es.enter_context(nc.sbuf_tensor("t_pos", [64, S], F32))
    t_ang = tb_es.enter_context(nc.sbuf_tensor("t_ang", [64, S], F32))
    t_k = tb_es.enter_context(nc.sbuf_tensor("t_k", [64, S], F32))
    t_ki = tb_es.enter_context(nc.sbuf_tensor("t_ki", [64, S], I32))
    t_pidx = tb_es.enter_context(nc.sbuf_tensor("t_pidx", [64, 1], F32))
    t_invf = tb_es.enter_context(nc.sbuf_tensor("t_invf", [64, 1], F32))
    t_pos0 = tb_es.enter_context(nc.sbuf_tensor("t_pos0", [64, 1], F32))
    t_brow = tb_es.enter_context(nc.sbuf_tensor("t_brow", [1, E], F32))
    t_bhi = tb_es.enter_context(nc.sbuf_tensor("t_bhi", [1, E], BF16))
    t_blo = tb_es.enter_context(nc.sbuf_tensor("t_blo", [1, E], BF16))

    IOTA_G = mG.n + 1
    POS0_V = ps_sl.n + 16
    INVF_S = mS.n + 1
    with nc.Block() as bl:
        @bl.gpsimd
        def _(g):
            g.iota(t_pos[:, :], [[1, S]], channel_multiplier=0,
                   allow_small_or_imprecise_dtypes=True)
            g.iota(t_pidx[:, :], [[0, 1]], channel_multiplier=1,
                   allow_small_or_imprecise_dtypes=True).then_inc(mG.h, 1)

        @bl.sync
        def _(sy):
            sy.dma_start(out=t_pos0[:, :], in_=pos0_d[:, :]).then_inc(ps_sl.h, 16)

        @bl.scalar
        def _(sc):
            sc.wait_ge(mG.h, IOTA_G)
            sc.activation(out=t_invf[:, :], in_=t_pidx[:, :], func=AF.Exp,
                          scale=float(-math.log(10000.0) / 64.0)).then_inc(mS.h, 1)
    mG.bump(1)
    ps_sl.bump(16)
    mS.bump(1)

    tables = [(ksin[:, :], S, 0.0, False), (kcos[:, :], S, math.pi / 2, False),
              (qsin[:, :], TOWN, 0.0, True), (qcos[:, :], TOWN, math.pi / 2, True)]
    tb_v = [mV.bump(1) for _ in range(4)]
    with nc.Block() as bl:
        @bl.vector
        def _(ve):
            ve.wait_ge(mS.h, INVF_S)
            ve.wait_ge(ps_sl.h, POS0_V)
            for ti, (dst, width, shift, ap0) in enumerate(tables):
                if ap0:
                    ve.tensor_scalar(out=t_ang[:, 0:width], in0=t_pos[:, 0:width],
                                     scalar1=t_pos0[:, :], scalar2=None, op0=ALU.add)
                    ve.tensor_scalar(out=t_ang[:, 0:width], in0=t_ang[:, 0:width],
                                     scalar1=t_invf[:, :], scalar2=float(shift),
                                     op0=ALU.mult, op1=ALU.add)
                else:
                    ve.tensor_scalar(out=t_ang[:, 0:width], in0=t_pos[:, 0:width],
                                     scalar1=t_invf[:, :], scalar2=float(shift),
                                     op0=ALU.mult, op1=ALU.add)
                ve.tensor_scalar(out=t_k[:, 0:width], in0=t_ang[:, 0:width],
                                 scalar1=float(1.0 / TWO_PI), scalar2=None, op0=ALU.mult)
                ve.tensor_copy(out=t_ki[:, 0:width], in_=t_k[:, 0:width])
                ve.tensor_copy(out=t_k[:, 0:width], in_=t_ki[:, 0:width])
                ve.scalar_tensor_tensor(out=dst, in0=t_k[:, 0:width],
                                        scalar=float(-TWO_PI), in1=t_ang[:, 0:width],
                                        op0=ALU.mult, op1=ALU.add).then_inc(mV.h, 1)

        @bl.scalar
        def _(sc):
            for ti, (dst, width, shift, ap0) in enumerate(tables):
                sc.wait_ge(mV.h, tb_v[ti])
                sc.activation(out=dst, in_=dst, func=AF.Sin).then_inc(mS.h, 1)
    mS.bump(4)

    bias_specs = ((bq_d, bqhl, E), (bk_d, bkhl, KV * D), (bv_d, bvhl, KV * D))
    brow_v = [br_sl.bump(16) for _ in range(3)]
    bsplit_v = [mV.bump(1) for _ in range(3)]
    bpl_v = [bs_sl.bump(32) for _ in range(3)]
    with nc.Block() as bl:
        @bl.sync
        def _(sy):
            for i, (src, dsttile, wdt) in enumerate(bias_specs):
                if i > 0:
                    sy.wait_ge(bs_sl.h, bpl_v[i - 1])
                sy.dma_start(out=t_brow[:, 0:wdt], in_=src[:, :]).then_inc(br_sl.h, 16)

        @bl.vector
        def _(ve):
            for i, (src, dsttile, wdt) in enumerate(bias_specs):
                ve.wait_ge(br_sl.h, brow_v[i])
                if i > 0:
                    ve.wait_ge(bs_sl.h, bpl_v[i - 1])
                ve.tensor_copy(out=t_bhi[:, 0:wdt], in_=t_brow[:, 0:wdt])
                ve.tensor_tensor(out=t_blo[:, 0:wdt], in0=t_brow[:, 0:wdt],
                                 in1=t_bhi[:, 0:wdt], op=ALU.subtract).then_inc(mV.h, 1)

        @bl.gpsimd
        def _(g):
            for i, (src, dsttile, wdt) in enumerate(bias_specs):
                g.wait_ge(mV.h, bsplit_v[i])
                g.dma_start(out=dsttile[0:1, 0:wdt], in_=t_bhi[:, 0:wdt]).then_inc(bs_sl.h, 16)
                g.dma_start(out=dsttile[1:2, 0:wdt], in_=t_blo[:, 0:wdt]).then_inc(bs_sl.h, 16)
    tb_es.close()
    hard_barrier()

    def ln_rows_block(stats_ms, norm_done=None):
        Va = mV.bump(1)
        Sa = mS.bump(1)
        Vb = mV.bump(1)
        Ga = la_sl.bump(32)
        Gb = lb_sl.bump(32)
        with nc.Block() as bl:
            @bl.vector
            def _(ve):
                ve.wait_ge(mT.h, stats_ms)
                ve.tensor_scalar(out=m_r[:, :], in0=pA[0:1, :], scalar1=float(1.0 / 4096.0),
                                 scalar2=None, op0=ALU.mult)
                ve.tensor_scalar(out=msq_r[:, :], in0=pB[0:1, :], scalar1=float(1.0 / 4096.0),
                                 scalar2=None, op0=ALU.mult)
                ve.tensor_tensor(out=mm_r[:, :], in0=m_r[:, :], in1=m_r[:, :], op=ALU.mult)
                ve.tensor_tensor(out=msq_r[:, :], in0=msq_r[:, :], in1=mm_r[:, :],
                                 op=ALU.subtract).then_inc(mV.h, 1)
                ve.wait_ge(mS.h, Sa)
                ve.reciprocal(out=rstd_r[:, :], in_=rstd_r[:, :])
                ve.scalar_tensor_tensor(out=negmr_r[:, :], in0=m_r[:, :], scalar=-1.0,
                                        in1=rstd_r[:, :], op0=ALU.mult,
                                        op1=ALU.mult).then_inc(mV.h, 1)

            @bl.scalar
            def _(sc):
                sc.wait_ge(mV.h, Va)
                sc.activation(out=rstd_r[:, :], in_=msq_r[:, :], func=AF.Sqrt,
                              bias=eps_col[0:1, :]).then_inc(mS.h, 1)

            @bl.gpsimd
            def _(g):
                g.wait_ge(mV.h, Vb)
                if norm_done is not None:
                    g.wait_ge(mV.h, norm_done)
                g.dma_start(out=row_scr[0:1, :], in_=rstd_r[:, :]).then_inc(la_sl.h, 16)
                g.dma_start(out=row_scr[1:2, :], in_=negmr_r[:, :]).then_inc(lb_sl.h, 16)
                g.wait_ge(la_sl.h, Ga - 16)
                g.dma_start(out=rstd_bt[:, :],
                            in_=row_scr[0:1, :].broadcast_to([128, 512])
                            ).then_inc(la_sl.h, 16)
                g.wait_ge(lb_sl.h, Gb - 16)
                g.dma_start(out=negmr_bt[:, :],
                            in_=row_scr[1:2, :].broadcast_to([128, 512])
                            ).then_inc(lb_sl.h, 16)
        return (Ga, Gb)

    def stats_wave(src_ap, col0, xbf_grp, x2_t, dma_guard=None, act_guard=None):
        xld = [xsl[c % 4].bump(16) for c in range(EC)]
        sqv = [mS.bump(1) for c in range(EC)]
        stv = [mT.bump(1) for c in range(EC)]
        with nc.Block() as bl:
            @bl.gpsimd
            def _(g):
                for c in range(EC):
                    if c == 0 and dma_guard is not None:
                        g.wait_ge(dma_guard[0], dma_guard[1])
                    if c >= 4:
                        g.wait_ge(mS.h, sqv[c - 4])
                    g.dma_start(out=xbf_grp[:, c * 512:(c + 1) * 512],
                                in_=src_ap[c * 128:(c + 1) * 128, col0:col0 + 512]
                                ).then_inc(xsl[c % 4].h, 16)

            @bl.scalar
            def _(sc):
                for c in range(EC):
                    sc.wait_ge(xsl[c % 4].h, xld[c])
                    if c == 0 and act_guard is not None:
                        sc.wait_ge(act_guard[0], act_guard[1])
                    if c >= 4:
                        sc.wait_ge(mT.h, stv[c - 4])
                    sc.activation(out=x2_t[:, (c % 4) * 512:(c % 4 + 1) * 512],
                                  in_=xbf_grp[:, c * 512:(c + 1) * 512],
                                  func=AF.Square).then_inc(mS.h, 1)

            @bl.tensor
            def _(te):
                for c in range(EC):
                    te.wait_ge(mS.h, sqv[c])
                    te.matmul(pA[0:1, :], ones_col[:, :],
                              xbf_grp[:, c * 512:(c + 1) * 512],
                              start=(c == 0), stop=(c == EC - 1), skip_group_check=True)
                    te.matmul(pB[0:1, :], ones_col[:, :],
                              x2_t[:, (c % 4) * 512:(c % 4 + 1) * 512],
                              start=(c == 0), stop=(c == EC - 1),
                              skip_group_check=True).then_inc(mT.h, 1)
        return stv[-1]

    def norm_wave(xbf_grp, dst, bcast_ms, tn_t):
        Ga, Gb = bcast_ms
        nv = [mV.bump(1) for c in range(EC)]
        with nc.Block() as bl:
            @bl.vector
            def _(ve):
                ve.wait_ge(la_sl.h, Ga)
                ve.wait_ge(lb_sl.h, Gb)
                for c in range(EC):
                    ve.tensor_tensor(out=tn_t[:, :],
                                     in0=xbf_grp[:, c * 512:(c + 1) * 512],
                                     in1=rstd_bt[:, :], op=ALU.mult)
                    ve.tensor_tensor(out=dst[:, c * 512:(c + 1) * 512], in0=tn_t[:, :],
                                     in1=negmr_bt[:, :], op=ALU.add).then_inc(mV.h, 1)
        return nv[-1]

    def rope_ops(ve, psrc, cos_t, sin_t, col0, dst, dcol, rp):
        ve.tensor_tensor(out=rp[0][:, :], in0=psrc[0:64, :],
                         in1=cos_t[:, col0:col0 + 512], op=ALU.mult)
        ve.tensor_tensor(out=rp[1][:, :], in0=psrc[64:128, :],
                         in1=sin_t[:, col0:col0 + 512], op=ALU.mult)
        ve.tensor_tensor(out=dst[0:64, dcol:dcol + 512], in0=rp[0][:, :],
                         in1=rp[1][:, :], op=ALU.subtract)
        ve.tensor_tensor(out=rp[0][:, :], in0=psrc[0:64, :],
                         in1=sin_t[:, col0:col0 + 512], op=ALU.mult)
        ve.tensor_tensor(out=rp[1][:, :], in0=psrc[64:128, :],
                         in1=cos_t[:, col0:col0 + 512], op=ALU.mult)
        ve.tensor_tensor(out=dst[64:128, dcol:dcol + 512], in0=rp[0][:, :],
                         in1=rp[1][:, :], op=ALU.add).then_inc(mV.h, 1)

    a1 = ExitStack()
    xln = a1.enter_context(nc.sbuf_tensor("xln", [128, EC * 512], BF16))
    xbf_grp = a1.enter_context(nc.sbuf_tensor("xbf_grp", [128, EC * 512], BF16))
    x2_rot = a1.enter_context(nc.sbuf_tensor("x2_rot", [128, 4 * 512], BF16))
    tnorm = a1.enter_context(nc.sbuf_tensor("tnorm", [128, 512], F32))
    wvb = a1.enter_context(nc.sbuf_tensor("wvb", [128, EC * 512], BF16))
    wkb = a1.enter_context(nc.sbuf_tensor("wkb", [128, 2 * EC * 128], BF16))
    kst = a1.enter_context(nc.sbuf_tensor("kst", [128, 8 * 512], BF16))
    vst = a1.enter_context(nc.sbuf_tensor("vst", [128, 4 * 1024], BF16))
    rp1 = [a1.enter_context(nc.sbuf_tensor(f"rp1_{i}", [64, 512], F32)) for i in range(2)]

    ks_done = 0
    vs_done = 0
    prev_norm = None
    prev_stats = None
    prev_vmm_last = None
    hard_barrier()
    for g_i in range(NG):
        col0 = g_i * 512
        stats_ms = stats_wave(
            xT, col0, xbf_grp, x2_rot,
            dma_guard=(mV.h, prev_norm) if prev_norm is not None else None,
            act_guard=(mT.h, prev_stats) if prev_stats is not None else None)
        bcast_ms = ln_rows_block(stats_ms, norm_done=prev_norm)
        norm_ms = norm_wave(xbf_grp, xln, bcast_ms, tnorm)
        prev_norm = norm_ms
        prev_stats = stats_ms

        prev_ks = ks_done
        wkld = [wk_sl[h % 2].bump(16) for h in range(KV)]
        kmm = [mT.bump(1) for h in range(KV)]
        krp = [mV.bump(1) for h in range(KV)]
        with nc.Block() as bl:
            @bl.gpsimd
            def _(g):
                for h in range(KV):
                    if h >= 2:
                        g.wait_ge(mT.h, kmm[h - 2])
                    g.dma_start(
                        out=wkb[:, (h % 2) * EC * 128:(h % 2 + 1) * EC * 128].rearrange(
                            "p (c n) -> p c n", n=128),
                        in_=wk_d[:, h * 128:(h + 1) * 128].rearrange(
                            "(c p) n -> p c n", p=128)).then_inc(wk_sl[h % 2].h, 16)

            @bl.tensor
            def _(te):
                te.wait_ge(mV.h, norm_ms)
                for h in range(KV):
                    te.wait_ge(wk_sl[h % 2].h, wkld[h])
                    if h >= 2:
                        te.wait_ge(mV.h, krp[h - 2])
                    pk = pE0 if h % 2 == 0 else pE1
                    base = (h % 2) * EC * 128
                    for c in range(EC):
                        te.matmul(pk[:, :], wkb[:, base + c * 128:base + (c + 1) * 128],
                                  xln[:, c * 512:(c + 1) * 512],
                                  start=(c == 0), stop=False, skip_group_check=True)
                    te.matmul(pk[:, :], bkhl[:, h * 128:(h + 1) * 128], onesN[:, :],
                              start=False, stop=True,
                              skip_group_check=True).then_inc(mT.h, 1)

            @bl.vector
            def _(ve):
                for h in range(KV):
                    ve.wait_ge(mT.h, kmm[h])
                    if h == 0 and prev_ks:
                        ve.wait_ge(ks_sl.h, prev_ks)
                    pk = pE0 if h % 2 == 0 else pE1
                    rope_ops(ve, pk, kcos, ksin, col0, kst, h * 512, rp1)

            @bl.sync
            def _(sy):
                for h in range(KV):
                    sy.wait_ge(mV.h, krp[h])
                    sy.dma_start(out=k_scr[h * 128:(h + 1) * 128, col0:col0 + 512],
                                 in_=kst[:, h * 512:(h + 1) * 512]).then_inc(ks_sl.h, 16)
        ks_done = ks_sl.bump(KV * 16)
        ROPES_V = mV.n

        prev_vs = vs_done
        for half in range(2):
            PRE_S = mS.n
            wv_guard = prev_vmm_last
            wvld = wv_sl.bump(16)
            vmm = [mT.bump(1) for tt in range(4)]
            vev = [mS.bump(1) for tt in range(4)]
            prev_vmm_last = vmm[-1]
            with nc.Block() as bl:
                @bl.gpsimd
                def _(g):
                    if wv_guard is not None:
                        g.wait_ge(mT.h, wv_guard)
                    g.dma_start(
                        out=wvb[:, :].rearrange("p (c n) -> p c n", n=512),
                        in_=wv_d[:, half * 512:(half + 1) * 512].rearrange(
                            "(c p) n -> p c n", p=128)).then_inc(wv_sl.h, 16)

                @bl.tensor
                def _(te):
                    te.wait_ge(wv_sl.h, wvld)
                    te.wait_ge(mS.h, PRE_S)
                    if half == 0:
                        te.wait_ge(mV.h, ROPES_V)
                    for tt in range(4):
                        pv = pF0 if tt % 2 == 0 else pE0
                        if tt >= 2:
                            te.wait_ge(mS.h, vev[tt - 2])
                        for c in range(EC):
                            te.matmul(pv[:, :],
                                      xln[:, c * 512 + tt * 128:c * 512 + tt * 128 + 128],
                                      wvb[:, c * 512:(c + 1) * 512],
                                      start=(c == 0), stop=False, skip_group_check=True)
                        te.matmul(pv[:, :], ones2[:, :],
                                  bvhl[:, half * 512:(half + 1) * 512],
                                  start=False, stop=True,
                                  skip_group_check=True).then_inc(mT.h, 1)

                @bl.scalar
                def _(sc):
                    for tt in range(4):
                        pv = pF0 if tt % 2 == 0 else pE0
                        sc.wait_ge(mT.h, vmm[tt])
                        if tt == 0 and half == 0 and prev_vs:
                            sc.wait_ge(vs_sl.h, prev_vs)
                        sc.activation(out=vst[:, tt * 1024 + half * 512:
                                              tt * 1024 + half * 512 + 512],
                                      in_=pv[:, :], func=AF.Copy).then_inc(mS.h, 1)
        VS_ALL = mS.n
        with nc.Block() as bl:
            @bl.sync
            def _(sy):
                sy.wait_ge(mS.h, VS_ALL)
                for tt in range(4):
                    sy.dma_start(out=v_scr[col0 + tt * 128:col0 + tt * 128 + 128, :],
                                 in_=vst[:, tt * 1024:(tt + 1) * 1024]).then_inc(vs_sl.h, 16)
        vs_done = vs_sl.bump(4 * 16)
    A1_KS = ks_sl.n
    A1_VS = vs_sl.n
    a1.close()

    a2p = ExitStack()
    q_tile = a2p.enter_context(nc.sbuf_tensor("q_tile", [128, EC * 512], BF16))
    for qg in range(NQ):
        qcol = qg * 512
        hard_barrier()
        p1 = ExitStack()
        xlnq = p1.enter_context(nc.sbuf_tensor(f"xlnq{qg}", [128, EC * 512], BF16))
        wqb = p1.enter_context(nc.sbuf_tensor(f"wqb{qg}", [128, 2 * EC * 128], BF16))
        xbf2 = p1.enter_context(nc.sbuf_tensor(f"xbf2{qg}", [128, EC * 512], BF16))
        x22 = p1.enter_context(nc.sbuf_tensor(f"x22{qg}", [128, 4 * 512], BF16))
        tn2 = p1.enter_context(nc.sbuf_tensor(f"tn2{qg}", [128, 512], F32))
        rp2 = [p1.enter_context(nc.sbuf_tensor(f"rp2_{qg}_{i}", [64, 512], F32))
               for i in range(2)]

        stats2 = stats_wave(xq, qcol, xbf2, x22)
        bcast2 = ln_rows_block(stats2, norm_done=mV.n)
        normq = norm_wave(xbf2, xlnq, bcast2, tn2)

        qwld = [qw_sl[m % 2].bump(16) for m in range(EC)]
        qmm = [mT.bump(1) for m in range(EC)]
        qrp = [mV.bump(1) for m in range(EC)]
        with nc.Block() as bl:
            @bl.gpsimd
            def _(g):
                for m in range(EC):
                    if m >= 2:
                        g.wait_ge(mT.h, qmm[m - 2])
                    g.dma_start(
                        out=wqb[:, (m % 2) * EC * 128:(m % 2 + 1) * EC * 128].rearrange(
                            "p (c n) -> p c n", n=128),
                        in_=wq_d[:, m * 128:(m + 1) * 128].rearrange(
                            "(c p) n -> p c n", p=128)).then_inc(qw_sl[m % 2].h, 16)

            @bl.tensor
            def _(te):
                te.wait_ge(mV.h, normq)
                for m in range(EC):
                    te.wait_ge(qw_sl[m % 2].h, qwld[m])
                    if m >= 2:
                        te.wait_ge(mV.h, qrp[m - 2])
                    pq = pE0 if m % 2 == 0 else pE1
                    base = (m % 2) * EC * 128
                    for c in range(EC):
                        te.matmul(pq[:, :], wqb[:, base + c * 128:base + (c + 1) * 128],
                                  xlnq[:, c * 512:(c + 1) * 512],
                                  start=(c == 0), stop=False, skip_group_check=True)
                    te.matmul(pq[:, :], bqhl[:, m * 128:(m + 1) * 128], onesN[:, :],
                              start=False, stop=True,
                              skip_group_check=True).then_inc(mT.h, 1)

            @bl.vector
            def _(ve):
                for m in range(EC):
                    ve.wait_ge(mT.h, qmm[m])
                    pq = pE0 if m % 2 == 0 else pE1
                    rope_ops(ve, pq, qcos, qsin, qcol, q_tile, m * 512, rp2)
        p1.close()
        hard_barrier()

        p2 = ExitStack()
        khb = p2.enter_context(nc.sbuf_tensor(f"khb{qg}", [128, 2 * S], BF16))
        vhb = p2.enter_context(nc.sbuf_tensor(f"vhb{qg}", [128, 2 * EC * 128], BF16))
        expb = p2.enter_context(nc.sbuf_tensor(f"expb{qg}", [128, 4 * 512], BF16))
        srecip = p2.enter_context(nc.sbuf_tensor(f"srecip{qg}", [128, 512], F32))
        attn_t = p2.enter_context(nc.sbuf_tensor(f"attn_t{qg}", [128, 512], F32))
        yst = p2.enter_context(nc.sbuf_tensor(f"yst{qg}", [128, 2 * 512], F32))
        ybf_grp = p2.enter_context(nc.sbuf_tensor(f"ybfg{qg}", [128, EC * 512], BF16))
        y22_rot = p2.enter_context(nc.sbuf_tensor(f"y22r{qg}", [128, 4 * 512], BF16))
        xqr = p2.enter_context(nc.sbuf_tensor(f"xqr{qg}", [128, 2 * 512], F32))
        ln2st = p2.enter_context(nc.sbuf_tensor(f"ln2st{qg}", [128, 2 * 512], BF16))
        rrow = p2.enter_context(nc.sbuf_tensor(f"rrow{qg}", [1, 512], F32))
        rrow_b = p2.enter_context(nc.sbuf_tensor(f"rrowb{qg}", [1, 512], F32))
        srecip_b = p2.enter_context(nc.sbuf_tensor(f"srecipb{qg}", [128, 512], F32))

        evac_ms = {}
        ystore_v = {}
        h_mm_done = {}
        last_exv_a = None
        last_exv_b = None
        prev_Vr_a = None
        prev_Vr_b = None
        A2S_T = mT.n
        A2S_V = mV.n
        QRP_LAST = qrp[-1]
        for h in range(KV):
            kvld = kv_sl[h % 2].bump(32)
            with nc.Block() as bl:
                @bl.sync
                def _(sy):
                    if h == 0:
                        sy.wait_ge(mT.h, A2S_T)
                        sy.wait_ge(mV.h, A2S_V)
                        if qg == 0:
                            sy.wait_ge(ks_sl.h, A1_KS)
                            sy.wait_ge(vs_sl.h, A1_VS)
                    if h >= 2:
                        sy.wait_ge(mT.h, h_mm_done[h - 2])
                    sy.dma_start(out=khb[:, (h % 2) * S:(h % 2 + 1) * S],
                                 in_=k_scr[h * 128:(h + 1) * 128, :]
                                 ).then_inc(kv_sl[h % 2].h, 16)
                    sy.dma_start(
                        out=vhb[:, (h % 2) * EC * 128:(h % 2 + 1) * EC * 128].rearrange(
                            "p (kc n) -> p kc n", n=128),
                        in_=v_scr[:, h * 128:(h + 1) * 128].rearrange(
                            "(kc kp) n -> kp kc n", kp=128)).then_inc(kv_sl[h % 2].h, 16)
            kb = (h % 2) * S
            vb = (h % 2) * EC * 128

            for sub in (0, 2):
                qa = h * 4 + sub
                qb = qa + 1
                SCA = [pE0, pE1]
                SCB2 = [pA, pB]
                S0 = mS.n
                exv_a = [S0 + 2 * kc + 1 for kc in range(32)]
                exv_b = [S0 + 2 * kc + 2 for kc in range(32)]
                scv_a = {}
                scv_b = {}
                pvv_a = {}
                pvv_b = {}
                t_ = mT.n
                for kc in range(2):
                    t_ += 1
                    scv_a[kc] = t_
                    t_ += 1
                    scv_b[kc] = t_
                for kc in range(32):
                    t_ += 1
                    pvv_a[kc] = t_
                    t_ += 1
                    pvv_b[kc] = t_
                    if kc + 2 < 32:
                        t_ += 1
                        scv_a[kc + 2] = t_
                        t_ += 1
                        scv_b[kc + 2] = t_
                mT.n = t_

                with nc.Block() as bl:
                    @bl.tensor
                    def _(te):
                        te.wait_ge(kv_sl[h % 2].h, kvld)
                        if qa == 0:
                            te.wait_ge(mV.h, QRP_LAST)

                        def sc_mm(stream, kc):
                            if stream == 0:
                                psc, exv, lexv, qh_ = SCA[kc % 2], exv_a, last_exv_a, qa
                            else:
                                psc, exv, lexv, qh_ = SCB2[kc % 2], exv_b, last_exv_b, qb
                            if kc >= 2:
                                te.wait_ge(mS.h, exv[kc - 2])
                            elif lexv is not None:
                                te.wait_ge(mS.h, lexv[30 + kc])
                            te.matmul(psc[:, :],
                                      khb[:, kb + kc * 128:kb + (kc + 1) * 128],
                                      q_tile[:, qh_ * 512:(qh_ + 1) * 512],
                                      start=True, stop=True,
                                      skip_group_check=True).then_inc(mT.h, 1)
                        for kc in range(2):
                            sc_mm(0, kc)
                            sc_mm(1, kc)
                        for kc in range(32):
                            te.wait_ge(mS.h, exv_a[kc])
                            if kc == 0:
                                if prev_Vr_a is not None:
                                    te.wait_ge(mV.h, prev_Vr_a)
                                if qa >= 4:
                                    te.wait_ge(mV.h, evac_ms[qa - 2])
                            esl = slice((kc % 2) * 512, (kc % 2 + 1) * 512)
                            te.matmul(pSa[:, :], ones_col[:, :], expb[:, esl],
                                      start=(kc == 0), stop=(kc == 31),
                                      skip_group_check=True)
                            te.matmul(pF0[:, :],
                                      vhb[:, vb + kc * 128:vb + (kc + 1) * 128],
                                      expb[:, esl], start=(kc == 0), stop=(kc == 31),
                                      skip_group_check=True).then_inc(mT.h, 1)
                            te.wait_ge(mS.h, exv_b[kc])
                            if kc == 0:
                                if prev_Vr_b is not None:
                                    te.wait_ge(mV.h, prev_Vr_b)
                                if qb >= 4:
                                    te.wait_ge(mV.h, evac_ms[qb - 2])
                            esl2 = slice((2 + kc % 2) * 512, (2 + kc % 2 + 1) * 512)
                            te.matmul(pC[0:1, :], ones_col[:, :], expb[:, esl2],
                                      start=(kc == 0), stop=(kc == 31),
                                      skip_group_check=True)
                            te.matmul(pD[:, :],
                                      vhb[:, vb + kc * 128:vb + (kc + 1) * 128],
                                      expb[:, esl2], start=(kc == 0), stop=(kc == 31),
                                      skip_group_check=True).then_inc(mT.h, 1)
                            if kc + 2 < 32:
                                sc_mm(0, kc + 2)
                                sc_mm(1, kc + 2)

                    @bl.scalar
                    def _(sc):
                        for kc in range(32):
                            sc.wait_ge(mT.h, scv_a[kc])
                            if kc >= 2:
                                sc.wait_ge(mT.h, pvv_a[kc - 2])
                            sc.activation(out=expb[:, (kc % 2) * 512:(kc % 2 + 1) * 512],
                                          in_=SCA[kc % 2][:, :], func=AF.Exp,
                                          scale=float(1.0 / math.sqrt(D))
                                          ).then_inc(mS.h, 1)
                            sc.wait_ge(mT.h, scv_b[kc])
                            if kc >= 2:
                                sc.wait_ge(mT.h, pvv_b[kc - 2])
                            sc.activation(out=expb[:, (2 + kc % 2) * 512:(2 + kc % 2 + 1) * 512],
                                          in_=SCB2[kc % 2][:, :], func=AF.Exp,
                                          scale=float(1.0 / math.sqrt(D))
                                          ).then_inc(mS.h, 1)
                mS.bump(64)

                Vra = mV.bump(1)
                Vrb = mV.bump(1)
                Ga_ = rh_sl.bump(32)
                Gb_ = rh2_sl.bump(32)
                Lqa = xq_sl[0].bump(16)
                Lqb = xq_sl[1].bump(16)
                Vea = mV.bump(1)
                Veb = mV.bump(1)
                Qsa = ys_sl[0].bump(16)
                Qsb = ys_sl[1].bump(16)
                with nc.Block() as bl:
                    @bl.vector
                    def _(ve):
                        ve.wait_ge(mT.h, pvv_a[31])
                        ve.reciprocal(out=rrow[:, :], in_=pSa[:, :]).then_inc(mV.h, 1)
                        ve.wait_ge(mT.h, pvv_b[31])
                        ve.reciprocal(out=rrow_b[:, :], in_=pC[0:1, :]).then_inc(mV.h, 1)
                        ve.wait_ge(rh_sl.h, Ga_)
                        ve.wait_ge(xq_sl[0].h, Lqa)
                        if qa >= 4:
                            ve.wait_ge(ys_sl[0].h, ystore_v[qa - 2])
                        ve.tensor_tensor(out=attn_t[:, :], in0=pF0[:, :],
                                         in1=srecip[:, :], op=ALU.mult)
                        ve.tensor_tensor(out=yst[:, 0:512], in0=attn_t[:, :],
                                         in1=xqr[:, 0:512],
                                         op=ALU.add).then_inc(mV.h, 1)
                        ve.wait_ge(rh2_sl.h, Gb_)
                        ve.wait_ge(xq_sl[1].h, Lqb)
                        if qb >= 4:
                            ve.wait_ge(ys_sl[1].h, ystore_v[qb - 2])
                        ve.tensor_tensor(out=attn_t[:, :], in0=pD[:, :],
                                         in1=srecip_b[:, :], op=ALU.mult)
                        ve.tensor_tensor(out=yst[:, 512:1024], in0=attn_t[:, :],
                                         in1=xqr[:, 512:1024],
                                         op=ALU.add).then_inc(mV.h, 1)

                    @bl.gpsimd
                    def _(g):
                        g.wait_ge(mV.h, Vra)
                        g.dma_start(out=row_scr[2:3, :], in_=rrow[:, :]).then_inc(rh_sl.h, 16)
                        g.wait_ge(mV.h, Vrb)
                        g.dma_start(out=row_scr[3:4, :], in_=rrow_b[:, :]).then_inc(rh2_sl.h, 16)
                        g.wait_ge(rh_sl.h, Ga_ - 16)
                        g.dma_start(out=srecip[:, :],
                                    in_=row_scr[2:3, :].broadcast_to([128, 512])
                                    ).then_inc(rh_sl.h, 16)
                        g.wait_ge(rh2_sl.h, Gb_ - 16)
                        g.dma_start(out=srecip_b[:, :],
                                    in_=row_scr[3:4, :].broadcast_to([128, 512])
                                    ).then_inc(rh2_sl.h, 16)

                    @bl.sync
                    def _(sy):
                        if qa >= 4:
                            sy.wait_ge(mV.h, evac_ms[qa - 2])
                        sy.dma_start(out=xqr[:, 0:512],
                                     in_=xq[qa * 128:(qa + 1) * 128, qcol:qcol + 512]
                                     ).then_inc(xq_sl[0].h, 16)
                        if qb >= 4:
                            sy.wait_ge(mV.h, evac_ms[qb - 2])
                        sy.dma_start(out=xqr[:, 512:1024],
                                     in_=xq[qb * 128:(qb + 1) * 128, qcol:qcol + 512]
                                     ).then_inc(xq_sl[1].h, 16)
                        sy.wait_ge(mV.h, Vea)
                        sy.dma_start(out=y_scr[qa * 128:(qa + 1) * 128, qcol:qcol + 512],
                                     in_=yst[:, 0:512]).then_inc(ys_sl[0].h, 16)
                        sy.wait_ge(mV.h, Veb)
                        sy.dma_start(out=y_scr[qb * 128:(qb + 1) * 128, qcol:qcol + 512],
                                     in_=yst[:, 512:1024]).then_inc(ys_sl[1].h, 16)
                evac_ms[qa] = Vea
                evac_ms[qb] = Veb
                ystore_v[qa] = Qsa
                ystore_v[qb] = Qsb
                last_exv_a = exv_a
                last_exv_b = exv_b
                prev_Vr_a = Vra
                prev_Vr_b = Vrb
            h_mm_done[h] = mT.n

        YALL0 = [ys_sl[0].n, ys_sl[1].n]
        with nc.Block() as bl:
            @bl.gpsimd
            def _(g):
                g.wait_ge(ys_sl[0].h, YALL0[0])
                g.wait_ge(ys_sl[1].h, YALL0[1])
                g.memset(barrier_tile[:, :], 0.0).then_inc(mG.h, 1)
        YB = mG.bump(1)
        with nc.Block() as bl:
            @bl.gpsimd
            def _(g):
                g.wait_ge(mG.h, YB)
        stats3 = stats_wave(y_scr, qcol, ybf_grp, y22_rot)
        bcast3 = ln_rows_block(stats3, norm_done=mV.n)

        l2v = [mV.bump(1) for c in range(EC)]
        l2s = [ls_sl[c % 2].bump(16) for c in range(EC)]
        with nc.Block() as bl:
            @bl.vector
            def _(ve):
                ve.wait_ge(la_sl.h, bcast3[0])
                ve.wait_ge(lb_sl.h, bcast3[1])
                for c in range(EC):
                    if c >= 2:
                        ve.wait_ge(ls_sl[c % 2].h, l2s[c - 2])
                    ve.tensor_tensor(out=attn_t[:, :],
                                     in0=ybf_grp[:, c * 512:(c + 1) * 512],
                                     in1=rstd_bt[:, :], op=ALU.mult)
                    ve.tensor_tensor(out=ln2st[:, (c % 2) * 512:(c % 2 + 1) * 512],
                                     in0=attn_t[:, :], in1=negmr_bt[:, :],
                                     op=ALU.add).then_inc(mV.h, 1)

            @bl.gpsimd
            def _(g):
                for c in range(EC):
                    g.wait_ge(mV.h, l2v[c])
                    g.dma_start(out=ln2_scr[c * 128:(c + 1) * 128, qcol:qcol + 512],
                                in_=ln2st[:, (c % 2) * 512:(c % 2 + 1) * 512]
                                ).then_inc(ls_sl[c % 2].h, 16)
        p2.close()
    a2p.close()
    ap_es.close()

    hard_barrier()
    ffn = ExitStack()
    ln2b = ffn.enter_context(nc.sbuf_tensor("ln2b", [128, EC * 512], BF16))
    h_full = ffn.enter_context(nc.sbuf_tensor("h_full", [128, IC * 512], BF16))
    wgb = ffn.enter_context(nc.sbuf_tensor("wgb", [128, 2 * EC * 128], BF16))
    wub = ffn.enter_context(nc.sbuf_tensor("wub", [128, 2 * EC * 128], BF16))
    silu_t = ffn.enter_context(nc.sbuf_tensor("silu_t", [128, 512], F32))
    wdb = ffn.enter_context(nc.sbuf_tensor("wdb", [128, 2 * IC * 128], BF16))
    yres = ffn.enter_context(nc.sbuf_tensor("yres", [128, 2 * 512], F32))
    ost = ffn.enter_context(nc.sbuf_tensor("ost", [128, 2 * 512], F32))

    for hf in range(2):
        tcol = hf * 512
        PRE_T = mT.n
        l2bv = l2_sl.bump(16)
        with nc.Block() as bl:
            @bl.sync
            def _(sy):
                sy.wait_ge(mT.h, PRE_T)
                if hf == 0:
                    sy.wait_ge(cc_sl.h, CCV)
                sy.dma_start(out=ln2b[:, :].rearrange("p (c t) -> p c t", t=512),
                             in_=ln2_scr[:, tcol:tcol + 512].rearrange(
                                 "(c p) t -> p c t", p=128)).then_inc(l2_sl.h, 16)

        wgld = [wg_sl[p % 2].bump(16) for p in range(IC)]
        wuld = [wu_sl[p % 2].bump(16) for p in range(IC)]
        gmm = [mT.bump(2) for p in range(IC)]
        sqv = [mS.bump(1) for p in range(IC)]
        hv = [mV.bump(1) for p in range(IC)]
        with nc.Block() as bl:
            @bl.gpsimd
            def _(g):
                for p in range(IC):
                    if p == 0 and hf == 1:
                        g.wait_ge(mT.h, PRE_T)
                    if p >= 2:
                        g.wait_ge(mT.h, gmm[p - 2])
                    g.dma_start(
                        out=wgb[:, (p % 2) * EC * 128:(p % 2 + 1) * EC * 128].rearrange(
                            "p (c n) -> p c n", n=128),
                        in_=wg_scr[:, p * 128:(p + 1) * 128].rearrange(
                            "(c kp) n -> kp c n", kp=128)).then_inc(wg_sl[p % 2].h, 16)
                    g.dma_start(
                        out=wub[:, (p % 2) * EC * 128:(p % 2 + 1) * EC * 128].rearrange(
                            "p (c n) -> p c n", n=128),
                        in_=wu_scr[:, p * 128:(p + 1) * 128].rearrange(
                            "(c kp) n -> kp c n", kp=128)).then_inc(wu_sl[p % 2].h, 16)

            @bl.tensor
            def _(te):
                te.wait_ge(l2_sl.h, l2bv)
                for p in range(IC):
                    te.wait_ge(wg_sl[p % 2].h, wgld[p])
                    te.wait_ge(wu_sl[p % 2].h, wuld[p])
                    if p >= 2:
                        te.wait_ge(mS.h, sqv[p - 2])
                        te.wait_ge(mV.h, hv[p - 2])
                    pg = pE0 if p % 2 == 0 else pE1
                    pu = pC if p % 2 == 0 else pD
                    base = (p % 2) * EC * 128
                    for c in range(EC):
                        mm = te.matmul(pg[:, :],
                                       wgb[:, base + c * 128:base + (c + 1) * 128],
                                       ln2b[:, c * 512:(c + 1) * 512],
                                       start=(c == 0), stop=(c == EC - 1),
                                       skip_group_check=True)
                        if c == EC - 1:
                            mm.then_inc(mT.h, 1)
                    for c in range(EC):
                        mm = te.matmul(pu[:, :],
                                       wub[:, base + c * 128:base + (c + 1) * 128],
                                       ln2b[:, c * 512:(c + 1) * 512],
                                       start=(c == 0), stop=(c == EC - 1),
                                       skip_group_check=True)
                        if c == EC - 1:
                            mm.then_inc(mT.h, 1)

            @bl.scalar
            def _(sc):
                for p in range(IC):
                    pg = pE0 if p % 2 == 0 else pE1
                    sc.wait_ge(mT.h, gmm[p] - 1)
                    sc.activation(out=silu_t[:, :], in_=pg[:, :],
                                  func=AF.Silu).then_inc(mS.h, 1)

            @bl.vector
            def _(ve):
                for p in range(IC):
                    pu = pC if p % 2 == 0 else pD
                    ve.wait_ge(mS.h, sqv[p])
                    ve.wait_ge(mT.h, gmm[p])
                    ve.tensor_tensor(out=h_full[:, p * 512:(p + 1) * 512],
                                     in0=silu_t[:, :], in1=pu[:, :],
                                     op=ALU.mult).then_inc(mV.h, 1)

        HIC = IC // 2
        wdld = [wd_sl[e % 2].bump(32) for e in range(EC)]
        dmm = [mT.bump(1) for e in range(EC)]
        yld = [yr_sl[e % 2].bump(32) for e in range(EC)]
        oev = [mV.bump(1) for e in range(EC)]
        osv = [os_sl[e % 2].bump(16) for e in range(EC)]
        HV_LAST = hv[-1]
        with nc.Block() as bl:
            @bl.gpsimd
            def _(g):
                g.wait_ge(mV.h, HV_LAST)
                for e in range(EC):
                    if e >= 2:
                        g.wait_ge(mT.h, dmm[e - 2])
                    for piece in range(2):
                        r0 = piece * HIC * 128
                        o0 = (e % 2) * IC * 128 + piece * HIC * 128
                        sh = e // 4
                        l0 = (e % 4) * 128
                        g.dma_start(
                            out=wdb[:, o0:o0 + HIC * 128].rearrange(
                                "p (ic n) -> p ic n", n=128),
                            in_=wd_g[sh * I + r0:sh * I + r0 + HIC * 128,
                                     l0:l0 + 128].rearrange(
                                "(ic p) n -> p ic n", p=128)).then_inc(wd_sl[e % 2].h, 16)

            @bl.tensor
            def _(te):
                for e in range(EC):
                    te.wait_ge(wd_sl[e % 2].h, wdld[e])
                    if e >= 2:
                        te.wait_ge(mV.h, oev[e - 2])
                    pd_ = pE0 if e % 2 == 0 else pE1
                    base = (e % 2) * IC * 128
                    for ic in range(IC):
                        mm = te.matmul(pd_[:, :],
                                       wdb[:, base + ic * 128:base + (ic + 1) * 128],
                                       h_full[:, ic * 512:(ic + 1) * 512],
                                       start=(ic == 0), stop=(ic == IC - 1),
                                       skip_group_check=True)
                        if ic == IC - 1:
                            mm.then_inc(mT.h, 1)

            @bl.sync
            def _(sy):
                for e in range(EC):
                    if e >= 2:
                        sy.wait_ge(mV.h, oev[e - 2])
                    for c_ in range(2):
                        base_el = (e * 128 + c_) * TOWN + tcol
                        in_ap = bass.AP(y_scr_t, base_el,
                                        [[2 * TOWN, 4], [8 * TOWN, 16], [1, 512]])
                        sy.dma_start(out=yres[c_ * 64:(c_ + 1) * 64,
                                              (e % 2) * 512:(e % 2 + 1) * 512],
                                     in_=in_ap).then_inc(yr_sl[e % 2].h, 16)
                    sy.wait_ge(mV.h, oev[e])
                    sy.dma_start(out=out_d[e * 128:(e + 1) * 128, tcol:tcol + 512],
                                 in_=ost[:, (e % 2) * 512:(e % 2 + 1) * 512]
                                 ).then_inc(os_sl[e % 2].h, 16)

            @bl.vector
            def _(ve):
                for e in range(EC):
                    pd_ = pE0 if e % 2 == 0 else pE1
                    ve.wait_ge(mT.h, dmm[e])
                    ve.wait_ge(yr_sl[e % 2].h, yld[e])
                    if e >= 2:
                        ve.wait_ge(os_sl[e % 2].h, osv[e - 2])
                    ve.tensor_tensor(out=ost[:, (e % 2) * 512:(e % 2 + 1) * 512],
                                     in0=pd_[:, :],
                                     in1=yres[:, (e % 2) * 512:(e % 2 + 1) * 512],
                                     op=ALU.add).then_inc(mV.h, 1)
    ffn.close()

    if dbg:
        dmp = [
            (nc.dram_tensor("dbg_k", [KV * D, S], BF16, kind="ExternalOutput").ap(), k_scr),
            (nc.dram_tensor("dbg_v", [S, KV * D], BF16, kind="ExternalOutput").ap(), v_scr),
            (nc.dram_tensor("dbg_y", [E, TOWN], F32, kind="ExternalOutput").ap(), y_scr),
            (nc.dram_tensor("dbg_ln2", [E, TOWN], BF16, kind="ExternalOutput").ap(), ln2_scr),
            (nc.dram_tensor("dbg_wg", [E, 512], BF16, kind="ExternalOutput").ap(), wg_scr[:, 0:512]),
            (nc.dram_tensor("dbg_wd", [512, 512], BF16, kind="ExternalOutput").ap(), wd_g[0:512, :]),
        ]
        snap = [(s_.h, s_.n) for s_ in ALLSEMS]
        dv = ps_sl.n
        with nc.Block() as bl:
            @bl.sync
            def _(sy):
                for hh, vv in snap:
                    sy.wait_ge(hh, vv)
                for dst, srcp in dmp:
                    sy.dma_start(out=dst, in_=srcp).then_inc(ps_sl.h, 16)
                sy.wait_ge(ps_sl.h, dv + len(dmp) * 16)
        ps_sl.bump(len(dmp) * 16)
    else:
        snap = [(s_.h, s_.n) for s_ in ALLSEMS]
        with nc.Block() as bl:
            @bl.sync
            def _(sy):
                for hh, vv in snap:
                    sy.wait_ge(hh, vv)

    pes.close()
    es.close()
    return nc


_NC_CACHE = {}


def kernel(**inputs):
    P = _perm128()
    permI = np.concatenate([q * 128 + P for q in range(IC)])
    permE = np.concatenate([q * 128 + P for q in range(EC)])

    x = np.asarray(inputs["L0_attn_x"], np.float32)
    wq = np.ascontiguousarray(np.asarray(inputs["L0_attn_wq"], np.float32))
    wk = np.ascontiguousarray(np.asarray(inputs["L0_attn_wk"], np.float32))
    wv = np.ascontiguousarray(np.asarray(inputs["L0_attn_wv"], np.float32))
    bq = np.asarray(inputs["L0_attn_bq"], np.float32).reshape(1, E)
    bk = np.asarray(inputs["L0_attn_bk"], np.float32).reshape(1, KV * D)
    bv = np.asarray(inputs["L0_attn_bv"], np.float32).reshape(1, KV * D)
    gq = np.asarray(inputs["L0_ffn_gate_qweight"], np.int32)
    uq = np.asarray(inputs["L0_ffn_up_qweight"], np.int32)
    dq = np.asarray(inputs["L0_ffn_down_qweight"], np.int32)[permI]
    gs = np.asarray(inputs["L0_ffn_gate_scales"], np.float32)[:, permI]
    us = np.asarray(inputs["L0_ffn_up_scales"], np.float32)[:, permI]
    ds = np.asarray(inputs["L0_ffn_down_scales"], np.float32)[:, permE]
    gz = np.asarray(inputs["L0_ffn_gate_qzeros"], np.int32)
    uz = np.asarray(inputs["L0_ffn_up_qzeros"], np.int32)
    dz = np.asarray(inputs["L0_ffn_down_qzeros"], np.int32)

    xT = [np.ascontiguousarray(x[b].T) for b in range(B)]

    if "nc" not in _NC_CACHE:
        _NC_CACHE["nc"] = build_nc()
    nc = _NC_CACHE["nc"]

    in_maps = []
    for c in range(8):
        b, t0 = c // 4, (c % 4) * TOWN
        in_maps.append({
            "xT": xT[b],
            "xq": np.ascontiguousarray(xT[b][:, t0:t0 + TOWN]),
            "wq": wq, "wk": wk, "wv": wv,
            "bq": bq, "bk": bk, "bv": bv,
            "gq": np.ascontiguousarray(gq[c * 512:(c + 1) * 512]),
            "uq": np.ascontiguousarray(uq[c * 512:(c + 1) * 512]),
            "dq": np.ascontiguousarray(dq[:, c * 64:(c + 1) * 64]),
            "gs": np.ascontiguousarray(gs[4 * c:4 * c + 4]),
            "us": np.ascontiguousarray(us[4 * c:4 * c + 4]),
            "ds": np.ascontiguousarray(ds[:, c * 512:(c + 1) * 512]),
            "gz": np.ascontiguousarray(gz[4 * c:4 * c + 4]),
            "uz": np.ascontiguousarray(uz[4 * c:4 * c + 4]),
            "dz": np.ascontiguousarray(dz[:, c * 64:(c + 1) * 64]),
            "pos0": np.full((64, 1), float(t0), np.float32),
        })

    res = run_bass_kernel_spmd(nc, in_maps, core_ids=list(range(8)))

    out = np.empty((B, S, E), np.float32)
    for c in range(8):
        b, t0 = c // 4, (c % 4) * TOWN
        dev = np.asarray(res.results[c]["out"])
        nat = np.empty_like(dev)
        nat[permE] = dev
        out[b, t0:t0 + TOWN, :] = nat.T
    return out


# version 11
# speedup vs baseline: 1.2661x; 1.0284x over previous
import math
from contextlib import ExitStack

import numpy as np
import concourse.bass as bass
import concourse.mybir as mybir
from concourse.bass_utils import run_bass_kernel_spmd

F32 = mybir.dt.float32
BF16 = mybir.dt.bfloat16
I32 = mybir.dt.int32
AF = mybir.ActivationFunctionType
ALU = mybir.AluOpType

B, S, E, H, KV, D = 2, 4096, 4096, 32, 8, 128
I = 11008
EC = E // 128
IC = I // 128
GW, DW = I // 8, E // 8
TOWN = 1024
NG = S // 512
NQ = TOWN // 512
TWO_PI = float(2 * math.pi)
AWQ_INV = [0, 2, 4, 6, 1, 3, 5, 7]


def _perm128():
    P = np.zeros(128, np.int64)
    for s_ in range(128):
        j, wl = s_ // 16, s_ % 16
        P[s_] = 8 * wl + AWQ_INV[j]
    return P


class Sem:
    def __init__(self, es, nc, name):
        self.h = es.enter_context(nc.semaphore(name))
        self.n = 0

    def bump(self, k):
        self.n += k
        return self.n


def build_nc(dbg=False):
    nc = bass.Bass("TRN2", target_bir_lowering=False)

    xT = nc.dram_tensor("xT", [E, S], F32, kind="ExternalInput").ap()
    xq = nc.dram_tensor("xq", [E, TOWN], F32, kind="ExternalInput").ap()
    wq_d = nc.dram_tensor("wq", [E, E], F32, kind="ExternalInput").ap()
    wk_d = nc.dram_tensor("wk", [E, KV * D], F32, kind="ExternalInput").ap()
    wv_d = nc.dram_tensor("wv", [E, KV * D], F32, kind="ExternalInput").ap()
    bq_d = nc.dram_tensor("bq", [1, E], F32, kind="ExternalInput").ap()
    bk_d = nc.dram_tensor("bk", [1, KV * D], F32, kind="ExternalInput").ap()
    bv_d = nc.dram_tensor("bv", [1, KV * D], F32, kind="ExternalInput").ap()
    gq_d = nc.dram_tensor("gq", [512, GW], I32, kind="ExternalInput").ap()
    uq_d = nc.dram_tensor("uq", [512, GW], I32, kind="ExternalInput").ap()
    dq_d = nc.dram_tensor("dq", [I, DW // 8], I32, kind="ExternalInput").ap()
    gs_d = nc.dram_tensor("gs", [4, I], F32, kind="ExternalInput").ap()
    us_d = nc.dram_tensor("us", [4, I], F32, kind="ExternalInput").ap()
    ds_d = nc.dram_tensor("ds", [86, 512], F32, kind="ExternalInput").ap()
    gz_d = nc.dram_tensor("gz", [4, GW], I32, kind="ExternalInput").ap()
    uz_d = nc.dram_tensor("uz", [4, GW], I32, kind="ExternalInput").ap()
    dz_d = nc.dram_tensor("dz", [86, DW // 8], I32, kind="ExternalInput").ap()
    pos0_d = nc.dram_tensor("pos0", [64, 1], F32, kind="ExternalInput").ap()
    out_d = nc.dram_tensor("out", [E, TOWN], F32, kind="ExternalOutput").ap()

    wg_scr = nc.dram_tensor("wg_scr", [E, I], BF16, kind="Internal").ap()
    wu_scr = nc.dram_tensor("wu_scr", [E, I], BF16, kind="Internal").ap()
    wd_g = nc.dram_tensor("wd_g", [8 * I, 512], BF16, kind="Internal").ap()
    wg_own = nc.dram_tensor("wg_own", [512, I], BF16, kind="Internal").ap()
    wu_own = nc.dram_tensor("wu_own", [512, I], BF16, kind="Internal").ap()
    wd_own = nc.dram_tensor("wd_own", [I, 512], BF16, kind="Internal").ap()
    k_scr = nc.dram_tensor("k_scr", [KV * D, S], BF16, kind="Internal").ap()
    v_scr = nc.dram_tensor("v_scr", [S, KV * D], BF16, kind="Internal").ap()
    y_scr_t = nc.dram_tensor("y_scr", [E, TOWN], F32, kind="Internal")
    y_scr = y_scr_t.ap()
    ln2_scr = nc.dram_tensor("ln2_scr", [E, TOWN], BF16, kind="Internal").ap()
    row_scr = nc.dram_tensor("row_scr", [4, 512], F32, kind="Internal").ap()

    es = ExitStack()
    ones_col = es.enter_context(nc.sbuf_tensor("ones_col", [128, 1], BF16))
    ones2 = es.enter_context(nc.sbuf_tensor("ones2", [2, 128], BF16))
    onesN = es.enter_context(nc.sbuf_tensor("onesN", [2, 512], BF16))
    eps_col = es.enter_context(nc.sbuf_tensor("eps_col", [128, 1], F32))
    barrier_tile = es.enter_context(nc.sbuf_tensor("barrier_tile", [1, 2], F32))

    mT = Sem(es, nc, "mT")
    mV = Sem(es, nc, "mV")
    mS = Sem(es, nc, "mS")
    mG = Sem(es, nc, "mG")
    ALLSEMS = [mT, mV, mS, mG]

    def mksem(name):
        s_ = Sem(es, nc, name)
        ALLSEMS.append(s_)
        return s_

    def mkslots(name, k):
        return [mksem(f"{name}{i}") for i in range(k)]

    sl_ld = mkslots("sl_ld", 2)
    sl_st = mkslots("sl_st", 2)
    ps_sl = mksem("ps_sl")
    br_sl = mksem("br_sl")
    bs_sl = mksem("bs_sl")
    xsl = mkslots("xsl", 4)
    ysl = mkslots("ysl", 4)
    la_sl = mksem("la_sl")
    lb_sl = mksem("lb_sl")
    wk_sl = mkslots("wk_sl", 2)
    wv_sl = mksem("wv_sl")
    ks_sl = mksem("ks_sl")
    vs_sl = mksem("vs_sl")
    qw_sl = mkslots("qw_sl", 2)
    kv_sl = mkslots("kv_sl", 2)
    rh_sl = mksem("rh_sl")
    rh2_sl = mksem("rh2_sl")
    cc_sl = Sem(es, nc, "cc_sl")
    xq_sl = mkslots("xq_sl", 2)
    ys_sl = mkslots("ys_sl", 2)
    l2y_sl = mkslots("l2y_sl", 2)
    ls_sl = mkslots("ls_sl", 2)
    l2_sl = mksem("l2_sl")
    wg_sl = mkslots("wg_sl", 2)
    wu_sl = mkslots("wu_sl", 2)
    wd_sl = mkslots("wd_sl", 2)
    yr_sl = mkslots("yr_sl", 2)
    os_sl = mkslots("os_sl", 2)

    pes = ExitStack()
    pA = pes.enter_context(nc.psum_tensor("pA", [128, 512], F32))
    pB = pes.enter_context(nc.psum_tensor("pB", [128, 512], F32))
    pSa = pes.enter_context(nc.psum_tensor("pSa", [1, 512], F32))
    pC = pes.enter_context(nc.psum_tensor("pC", [128, 512], F32))
    pD = pes.enter_context(nc.psum_tensor("pD", [128, 512], F32))
    pE0 = pes.enter_context(nc.psum_tensor("pE0", [128, 512], F32))
    pE1 = pes.enter_context(nc.psum_tensor("pE1", [128, 512], F32))
    pF0 = pes.enter_context(nc.psum_tensor("pF0", [128, 512], F32))

    def hard_barrier():
        snap = [(s_.h, s_.n) for s_ in ALLSEMS]
        bm = mG.n + 1
        with nc.Block() as bl:
            @bl.gpsimd
            def _(g):
                for hh, vv in snap:
                    g.wait_ge(hh, vv)
                g.memset(barrier_tile[:, :], 0.0).then_inc(mG.h, 1)

            @bl.tensor
            def _(te):
                te.wait_ge(mG.h, bm)

            @bl.vector
            def _(ve):
                ve.wait_ge(mG.h, bm)

            @bl.scalar
            def _(sc):
                sc.wait_ge(mG.h, bm)

            @bl.sync
            def _(sy):
                sy.wait_ge(mG.h, bm)
        mG.bump(1)

    with nc.Block() as bl:
        @bl.gpsimd
        def _(g):
            g.memset(ones_col[:, :], 1.0)
            g.memset(ones2[:, :], 1.0)
            g.memset(onesN[:, :], 1.0)
            g.memset(eps_col[:, :], 1e-5).then_inc(mG.h, 1)
    mG.bump(1)

    deq = ExitStack()
    NSH = I // 2
    WSH = GW // 2
    q_sb = [deq.enter_context(nc.sbuf_tensor(f"q_sb{i}", [128, WSH], I32)) for i in range(2)]
    z_sb = [deq.enter_context(nc.sbuf_tensor(f"z_sb{i}", [128, WSH], I32)) for i in range(2)]
    s_sb = [deq.enter_context(nc.sbuf_tensor(f"s_sb{i}", [128, NSH], F32)) for i in range(2)]
    wn_sb = deq.enter_context(nc.sbuf_tensor("wn_sb", [128, NSH], I32))
    zn_sb = deq.enter_context(nc.sbuf_tensor("zn_sb", [128, NSH], I32))
    db_sb = deq.enter_context(nc.sbuf_tensor("db_sb", [128, NSH], BF16))
    wo_sb = [deq.enter_context(nc.sbuf_tensor(f"wo_sb{i}", [128, NSH], BF16)) for i in range(2)]

    slices = []
    for c in range(4):
        for hf in range(2):
            slices.append((gq_d, gz_d, gs_d, wg_own, c, hf * WSH, hf * NSH, WSH))
    for c in range(4):
        for hf in range(2):
            slices.append((uq_d, uz_d, us_d, wu_own, c, hf * WSH, hf * NSH, WSH))
    for c in range(IC):
        slices.append((dq_d, dz_d, ds_d, wd_own, c, 0, 0, DW // 8))
    nsl = len(slices)

    ld_v = [sl_ld[i % 2].bump(48) for i in range(nsl)]
    dv_v = [mV.bump(1) for i in range(nsl)]
    st_v = [sl_st[i % 2].bump(16) for i in range(nsl)]

    with nc.Block() as bl:
        @bl.sync
        def _(sy):
            for i, (qsrc, zsrc, ssrc, dst, c, woff, noff, nw) in enumerate(slices):
                bi = i % 2
                nnib = nw * 8
                if i >= 2:
                    sy.wait_ge(mV.h, dv_v[i - 2])
                sy.dma_start(out=q_sb[bi][:, 0:nw],
                             in_=qsrc[c * 128:(c + 1) * 128, woff:woff + nw]
                             ).then_inc(sl_ld[bi].h, 16)
                sy.dma_start(out=z_sb[bi][:, 0:nw],
                             in_=zsrc[c:c + 1, woff:woff + nw].broadcast_to([128, nw])
                             ).then_inc(sl_ld[bi].h, 16)
                sy.dma_start(out=s_sb[bi][:, 0:nnib],
                             in_=ssrc[c:c + 1, noff:noff + nnib].broadcast_to([128, nnib])
                             ).then_inc(sl_ld[bi].h, 16)

        @bl.vector
        def _(ve):
            for i, (qsrc, zsrc, ssrc, dst, c, woff, noff, nw) in enumerate(slices):
                bi = i % 2
                nnib = nw * 8
                ve.wait_ge(sl_ld[bi].h, ld_v[i])
                if i >= 2:
                    ve.wait_ge(sl_st[bi].h, st_v[i - 2])
                wv3 = wn_sb[:, 0:nnib].rearrange("p (q j w) -> p q j w", j=8, w=16)
                zv3 = zn_sb[:, 0:nnib].rearrange("p (q j w) -> p q j w", j=8, w=16)
                qv3 = q_sb[bi][:, 0:nw].rearrange("p (q w) -> p q w", w=16)
                zq3 = z_sb[bi][:, 0:nw].rearrange("p (q w) -> p q w", w=16)
                for j in range(8):
                    ve.tensor_scalar(out=wv3[:, :, j, :], in0=qv3, scalar1=4 * j,
                                     scalar2=15, op0=ALU.logical_shift_right,
                                     op1=ALU.bitwise_and)
                    ve.tensor_scalar(out=zv3[:, :, j, :], in0=zq3, scalar1=4 * j,
                                     scalar2=15, op0=ALU.logical_shift_right,
                                     op1=ALU.bitwise_and)
                ve.tensor_tensor(out=db_sb[:, 0:nnib], in0=wn_sb[:, 0:nnib],
                                 in1=zn_sb[:, 0:nnib], op=ALU.subtract)
                ve.tensor_tensor(out=wo_sb[bi][:, 0:nnib], in0=db_sb[:, 0:nnib],
                                 in1=s_sb[bi][:, 0:nnib], op=ALU.mult).then_inc(mV.h, 1)

        @bl.gpsimd
        def _(g):
            for i, (qsrc, zsrc, ssrc, dst, c, woff, noff, nw) in enumerate(slices):
                bi = i % 2
                nnib = nw * 8
                g.wait_ge(mV.h, dv_v[i])
                g.dma_start(out=dst[c * 128:(c + 1) * 128, noff:noff + nnib],
                            in_=wo_sb[bi][:, 0:nnib]).then_inc(sl_st[bi].h, 16)
    ST_ALL = [sl_st[0].n, sl_st[1].n]
    CCV = cc_sl.n + 3
    with nc.Block() as bl:
        @bl.gpsimd
        def _(g):
            g.wait_ge(sl_st[0].h, ST_ALL[0])
            g.wait_ge(sl_st[1].h, ST_ALL[1])
            g.collective_compute(
                "AllGather", ALU.bypass,
                replica_groups=[[0, 1, 2, 3, 4, 5, 6, 7]],
                ins=[wg_own[:, :].opt()],
                outs=[wg_scr[:, :].opt()],
            ).then_inc(cc_sl.h, 1)
            g.collective_compute(
                "AllGather", ALU.bypass,
                replica_groups=[[0, 1, 2, 3, 4, 5, 6, 7]],
                ins=[wu_own[:, :].opt()],
                outs=[wu_scr[:, :].opt()],
            ).then_inc(cc_sl.h, 1)
            g.collective_compute(
                "AllGather", ALU.bypass,
                replica_groups=[[0, 1, 2, 3, 4, 5, 6, 7]],
                ins=[wd_own[:, :].opt()],
                outs=[wd_g[:, :].opt()],
            ).then_inc(cc_sl.h, 1)
    cc_sl.bump(3)
    deq.close()

    hard_barrier()
    ap_es = ExitStack()
    kcos = ap_es.enter_context(nc.sbuf_tensor("kcos", [64, S], F32))
    ksin = ap_es.enter_context(nc.sbuf_tensor("ksin", [64, S], F32))
    qcos = ap_es.enter_context(nc.sbuf_tensor("qcos", [64, TOWN], F32))
    qsin = ap_es.enter_context(nc.sbuf_tensor("qsin", [64, TOWN], F32))
    bqhl = ap_es.enter_context(nc.sbuf_tensor("bqhl", [2, E], BF16))
    bkhl = ap_es.enter_context(nc.sbuf_tensor("bkhl", [2, KV * D], BF16))
    bvhl = ap_es.enter_context(nc.sbuf_tensor("bvhl", [2, KV * D], BF16))
    m_r = ap_es.enter_context(nc.sbuf_tensor("m_r", [1, 512], F32))
    msq_r = ap_es.enter_context(nc.sbuf_tensor("msq_r", [1, 512], F32))
    mm_r = ap_es.enter_context(nc.sbuf_tensor("mm_r", [1, 512], F32))
    rstd_r = ap_es.enter_context(nc.sbuf_tensor("rstd_r", [1, 512], F32))
    negmr_r = ap_es.enter_context(nc.sbuf_tensor("negmr_r", [1, 512], F32))
    rstd_bt = ap_es.enter_context(nc.sbuf_tensor("rstd_bt", [128, 512], F32))
    negmr_bt = ap_es.enter_context(nc.sbuf_tensor("negmr_bt", [128, 512], F32))

    tb_es = ExitStack()
    t_pos = tb_es.enter_context(nc.sbuf_tensor("t_pos", [64, S], F32))
    t_ang = tb_es.enter_context(nc.sbuf_tensor("t_ang", [64, S], F32))
    t_k = tb_es.enter_context(nc.sbuf_tensor("t_k", [64, S], F32))
    t_ki = tb_es.enter_context(nc.sbuf_tensor("t_ki", [64, S], I32))
    t_pidx = tb_es.enter_context(nc.sbuf_tensor("t_pidx", [64, 1], F32))
    t_invf = tb_es.enter_context(nc.sbuf_tensor("t_invf", [64, 1], F32))
    t_pos0 = tb_es.enter_context(nc.sbuf_tensor("t_pos0", [64, 1], F32))
    t_brow = tb_es.enter_context(nc.sbuf_tensor("t_brow", [1, E], F32))
    t_bhi = tb_es.enter_context(nc.sbuf_tensor("t_bhi", [1, E], BF16))
    t_blo = tb_es.enter_context(nc.sbuf_tensor("t_blo", [1, E], BF16))

    IOTA_G = mG.n + 1
    POS0_V = ps_sl.n + 16
    INVF_S = mS.n + 1
    with nc.Block() as bl:
        @bl.gpsimd
        def _(g):
            g.iota(t_pos[:, :], [[1, S]], channel_multiplier=0,
                   allow_small_or_imprecise_dtypes=True)
            g.iota(t_pidx[:, :], [[0, 1]], channel_multiplier=1,
                   allow_small_or_imprecise_dtypes=True).then_inc(mG.h, 1)

        @bl.sync
        def _(sy):
            sy.dma_start(out=t_pos0[:, :], in_=pos0_d[:, :]).then_inc(ps_sl.h, 16)

        @bl.scalar
        def _(sc):
            sc.wait_ge(mG.h, IOTA_G)
            sc.activation(out=t_invf[:, :], in_=t_pidx[:, :], func=AF.Exp,
                          scale=float(-math.log(10000.0) / 64.0)).then_inc(mS.h, 1)
    mG.bump(1)
    ps_sl.bump(16)
    mS.bump(1)

    tables = [(ksin[:, :], S, 0.0, False), (kcos[:, :], S, math.pi / 2, False),
              (qsin[:, :], TOWN, 0.0, True), (qcos[:, :], TOWN, math.pi / 2, True)]
    tb_v = [mV.bump(1) for _ in range(4)]
    with nc.Block() as bl:
        @bl.vector
        def _(ve):
            ve.wait_ge(mS.h, INVF_S)
            ve.wait_ge(ps_sl.h, POS0_V)
            for ti, (dst, width, shift, ap0) in enumerate(tables):
                if ap0:
                    ve.tensor_scalar(out=t_ang[:, 0:width], in0=t_pos[:, 0:width],
                                     scalar1=t_pos0[:, :], scalar2=None, op0=ALU.add)
                    ve.tensor_scalar(out=t_ang[:, 0:width], in0=t_ang[:, 0:width],
                                     scalar1=t_invf[:, :], scalar2=float(shift),
                                     op0=ALU.mult, op1=ALU.add)
                else:
                    ve.tensor_scalar(out=t_ang[:, 0:width], in0=t_pos[:, 0:width],
                                     scalar1=t_invf[:, :], scalar2=float(shift),
                                     op0=ALU.mult, op1=ALU.add)
                ve.tensor_scalar(out=t_k[:, 0:width], in0=t_ang[:, 0:width],
                                 scalar1=float(1.0 / TWO_PI), scalar2=None, op0=ALU.mult)
                ve.tensor_copy(out=t_ki[:, 0:width], in_=t_k[:, 0:width])
                ve.tensor_copy(out=t_k[:, 0:width], in_=t_ki[:, 0:width])
                ve.scalar_tensor_tensor(out=dst, in0=t_k[:, 0:width],
                                        scalar=float(-TWO_PI), in1=t_ang[:, 0:width],
                                        op0=ALU.mult, op1=ALU.add).then_inc(mV.h, 1)

        @bl.scalar
        def _(sc):
            for ti, (dst, width, shift, ap0) in enumerate(tables):
                sc.wait_ge(mV.h, tb_v[ti])
                sc.activation(out=dst, in_=dst, func=AF.Sin).then_inc(mS.h, 1)
    mS.bump(4)

    bias_specs = ((bq_d, bqhl, E), (bk_d, bkhl, KV * D), (bv_d, bvhl, KV * D))
    brow_v = [br_sl.bump(16) for _ in range(3)]
    bsplit_v = [mV.bump(1) for _ in range(3)]
    bpl_v = [bs_sl.bump(32) for _ in range(3)]
    with nc.Block() as bl:
        @bl.sync
        def _(sy):
            for i, (src, dsttile, wdt) in enumerate(bias_specs):
                if i > 0:
                    sy.wait_ge(bs_sl.h, bpl_v[i - 1])
                sy.dma_start(out=t_brow[:, 0:wdt], in_=src[:, :]).then_inc(br_sl.h, 16)

        @bl.vector
        def _(ve):
            for i, (src, dsttile, wdt) in enumerate(bias_specs):
                ve.wait_ge(br_sl.h, brow_v[i])
                if i > 0:
                    ve.wait_ge(bs_sl.h, bpl_v[i - 1])
                ve.tensor_copy(out=t_bhi[:, 0:wdt], in_=t_brow[:, 0:wdt])
                ve.tensor_tensor(out=t_blo[:, 0:wdt], in0=t_brow[:, 0:wdt],
                                 in1=t_bhi[:, 0:wdt], op=ALU.subtract).then_inc(mV.h, 1)

        @bl.gpsimd
        def _(g):
            for i, (src, dsttile, wdt) in enumerate(bias_specs):
                g.wait_ge(mV.h, bsplit_v[i])
                g.dma_start(out=dsttile[0:1, 0:wdt], in_=t_bhi[:, 0:wdt]).then_inc(bs_sl.h, 16)
                g.dma_start(out=dsttile[1:2, 0:wdt], in_=t_blo[:, 0:wdt]).then_inc(bs_sl.h, 16)
    tb_es.close()
    hard_barrier()

    def ln_rows_block(stats_ms, norm_done=None):
        Va = mV.bump(1)
        Sa = mS.bump(1)
        Vb = mV.bump(1)
        Ga = la_sl.bump(32)
        Gb = lb_sl.bump(32)
        with nc.Block() as bl:
            @bl.vector
            def _(ve):
                ve.wait_ge(mT.h, stats_ms)
                ve.tensor_scalar(out=m_r[:, :], in0=pA[0:1, :], scalar1=float(1.0 / 4096.0),
                                 scalar2=None, op0=ALU.mult)
                ve.tensor_scalar(out=msq_r[:, :], in0=pB[0:1, :], scalar1=float(1.0 / 4096.0),
                                 scalar2=None, op0=ALU.mult)
                ve.tensor_tensor(out=mm_r[:, :], in0=m_r[:, :], in1=m_r[:, :], op=ALU.mult)
                ve.tensor_tensor(out=msq_r[:, :], in0=msq_r[:, :], in1=mm_r[:, :],
                                 op=ALU.subtract).then_inc(mV.h, 1)
                ve.wait_ge(mS.h, Sa)
                ve.reciprocal(out=rstd_r[:, :], in_=rstd_r[:, :])
                ve.scalar_tensor_tensor(out=negmr_r[:, :], in0=m_r[:, :], scalar=-1.0,
                                        in1=rstd_r[:, :], op0=ALU.mult,
                                        op1=ALU.mult).then_inc(mV.h, 1)

            @bl.scalar
            def _(sc):
                sc.wait_ge(mV.h, Va)
                sc.activation(out=rstd_r[:, :], in_=msq_r[:, :], func=AF.Sqrt,
                              bias=eps_col[0:1, :]).then_inc(mS.h, 1)

            @bl.gpsimd
            def _(g):
                g.wait_ge(mV.h, Vb)
                if norm_done is not None:
                    g.wait_ge(mV.h, norm_done)
                g.dma_start(out=row_scr[0:1, :], in_=rstd_r[:, :]).then_inc(la_sl.h, 16)
                g.dma_start(out=row_scr[1:2, :], in_=negmr_r[:, :]).then_inc(lb_sl.h, 16)
                g.wait_ge(la_sl.h, Ga - 16)
                g.dma_start(out=rstd_bt[:, :],
                            in_=row_scr[0:1, :].broadcast_to([128, 512])
                            ).then_inc(la_sl.h, 16)
                g.wait_ge(lb_sl.h, Gb - 16)
                g.dma_start(out=negmr_bt[:, :],
                            in_=row_scr[1:2, :].broadcast_to([128, 512])
                            ).then_inc(lb_sl.h, 16)
        return (Ga, Gb)

    def stats_wave(src_ap, col0, xbf_grp, x2_t, dma_guard=None, act_guard=None):
        xld = [xsl[c % 4].bump(16) for c in range(EC)]
        sqv = [mS.bump(1) for c in range(EC)]
        stv = [mT.bump(1) for c in range(EC)]
        with nc.Block() as bl:
            @bl.gpsimd
            def _(g):
                for c in range(EC):
                    if c == 0 and dma_guard is not None:
                        g.wait_ge(dma_guard[0], dma_guard[1])
                    if c >= 4:
                        g.wait_ge(mS.h, sqv[c - 4])
                    g.dma_start(out=xbf_grp[:, c * 512:(c + 1) * 512],
                                in_=src_ap[c * 128:(c + 1) * 128, col0:col0 + 512]
                                ).then_inc(xsl[c % 4].h, 16)

            @bl.scalar
            def _(sc):
                for c in range(EC):
                    sc.wait_ge(xsl[c % 4].h, xld[c])
                    if c == 0 and act_guard is not None:
                        sc.wait_ge(act_guard[0], act_guard[1])
                    if c >= 4:
                        sc.wait_ge(mT.h, stv[c - 4])
                    sc.activation(out=x2_t[:, (c % 4) * 512:(c % 4 + 1) * 512],
                                  in_=xbf_grp[:, c * 512:(c + 1) * 512],
                                  func=AF.Square).then_inc(mS.h, 1)

            @bl.tensor
            def _(te):
                for c in range(EC):
                    te.wait_ge(mS.h, sqv[c])
                    te.matmul(pA[0:1, :], ones_col[:, :],
                              xbf_grp[:, c * 512:(c + 1) * 512],
                              start=(c == 0), stop=(c == EC - 1), skip_group_check=True)
                    te.matmul(pB[0:1, :], ones_col[:, :],
                              x2_t[:, (c % 4) * 512:(c % 4 + 1) * 512],
                              start=(c == 0), stop=(c == EC - 1),
                              skip_group_check=True).then_inc(mT.h, 1)
        return stv[-1]

    def norm_wave(xbf_grp, dst, bcast_ms, tn_t):
        Ga, Gb = bcast_ms
        nv = [mV.bump(1) for c in range(EC)]
        with nc.Block() as bl:
            @bl.vector
            def _(ve):
                ve.wait_ge(la_sl.h, Ga)
                ve.wait_ge(lb_sl.h, Gb)
                for c in range(EC):
                    ve.tensor_tensor(out=tn_t[:, :],
                                     in0=xbf_grp[:, c * 512:(c + 1) * 512],
                                     in1=rstd_bt[:, :], op=ALU.mult)
                    ve.tensor_tensor(out=dst[:, c * 512:(c + 1) * 512], in0=tn_t[:, :],
                                     in1=negmr_bt[:, :], op=ALU.add).then_inc(mV.h, 1)
        return nv[-1]

    def rope_ops(ve, psrc, cos_t, sin_t, col0, dst, dcol, rp):
        ve.tensor_tensor(out=rp[0][:, :], in0=psrc[0:64, :],
                         in1=cos_t[:, col0:col0 + 512], op=ALU.mult)
        ve.tensor_tensor(out=rp[1][:, :], in0=psrc[64:128, :],
                         in1=sin_t[:, col0:col0 + 512], op=ALU.mult)
        ve.tensor_tensor(out=dst[0:64, dcol:dcol + 512], in0=rp[0][:, :],
                         in1=rp[1][:, :], op=ALU.subtract)
        ve.tensor_tensor(out=rp[0][:, :], in0=psrc[0:64, :],
                         in1=sin_t[:, col0:col0 + 512], op=ALU.mult)
        ve.tensor_tensor(out=rp[1][:, :], in0=psrc[64:128, :],
                         in1=cos_t[:, col0:col0 + 512], op=ALU.mult)
        ve.tensor_tensor(out=dst[64:128, dcol:dcol + 512], in0=rp[0][:, :],
                         in1=rp[1][:, :], op=ALU.add).then_inc(mV.h, 1)

    a1 = ExitStack()
    xln = a1.enter_context(nc.sbuf_tensor("xln", [128, EC * 512], BF16))
    xbf_grp = a1.enter_context(nc.sbuf_tensor("xbf_grp", [128, EC * 512], BF16))
    x2_rot = a1.enter_context(nc.sbuf_tensor("x2_rot", [128, 4 * 512], BF16))
    tnorm = a1.enter_context(nc.sbuf_tensor("tnorm", [128, 512], F32))
    wvb = a1.enter_context(nc.sbuf_tensor("wvb", [128, EC * 512], BF16))
    wkb = a1.enter_context(nc.sbuf_tensor("wkb", [128, 2 * EC * 128], BF16))
    kst = a1.enter_context(nc.sbuf_tensor("kst", [128, 8 * 512], BF16))
    vst = a1.enter_context(nc.sbuf_tensor("vst", [128, 4 * 1024], BF16))
    rp1 = [a1.enter_context(nc.sbuf_tensor(f"rp1_{i}", [64, 512], F32)) for i in range(2)]

    ks_done = 0
    vs_done = 0
    prev_norm = None
    prev_stats = None
    prev_vmm_last = None
    hard_barrier()
    for g_i in range(NG):
        col0 = g_i * 512
        stats_ms = stats_wave(
            xT, col0, xbf_grp, x2_rot,
            dma_guard=(mV.h, prev_norm) if prev_norm is not None else None,
            act_guard=(mT.h, prev_stats) if prev_stats is not None else None)
        bcast_ms = ln_rows_block(stats_ms, norm_done=prev_norm)
        norm_ms = norm_wave(xbf_grp, xln, bcast_ms, tnorm)
        prev_norm = norm_ms
        prev_stats = stats_ms

        prev_ks = ks_done
        wkld = [wk_sl[h % 2].bump(16) for h in range(KV)]
        kmm = [mT.bump(1) for h in range(KV)]
        krp = [mV.bump(1) for h in range(KV)]
        with nc.Block() as bl:
            @bl.gpsimd
            def _(g):
                for h in range(KV):
                    if h >= 2:
                        g.wait_ge(mT.h, kmm[h - 2])
                    g.dma_start(
                        out=wkb[:, (h % 2) * EC * 128:(h % 2 + 1) * EC * 128].rearrange(
                            "p (c n) -> p c n", n=128),
                        in_=wk_d[:, h * 128:(h + 1) * 128].rearrange(
                            "(c p) n -> p c n", p=128)).then_inc(wk_sl[h % 2].h, 16)

            @bl.tensor
            def _(te):
                te.wait_ge(mV.h, norm_ms)
                for h in range(KV):
                    te.wait_ge(wk_sl[h % 2].h, wkld[h])
                    if h >= 2:
                        te.wait_ge(mV.h, krp[h - 2])
                    pk = pE0 if h % 2 == 0 else pE1
                    base = (h % 2) * EC * 128
                    for c in range(EC):
                        te.matmul(pk[:, :], wkb[:, base + c * 128:base + (c + 1) * 128],
                                  xln[:, c * 512:(c + 1) * 512],
                                  start=(c == 0), stop=False, skip_group_check=True)
                    te.matmul(pk[:, :], bkhl[:, h * 128:(h + 1) * 128], onesN[:, :],
                              start=False, stop=True,
                              skip_group_check=True).then_inc(mT.h, 1)

            @bl.vector
            def _(ve):
                for h in range(KV):
                    ve.wait_ge(mT.h, kmm[h])
                    if h == 0 and prev_ks:
                        ve.wait_ge(ks_sl.h, prev_ks)
                    pk = pE0 if h % 2 == 0 else pE1
                    rope_ops(ve, pk, kcos, ksin, col0, kst, h * 512, rp1)

            @bl.sync
            def _(sy):
                for h in range(KV):
                    sy.wait_ge(mV.h, krp[h])
                    sy.dma_start(out=k_scr[h * 128:(h + 1) * 128, col0:col0 + 512],
                                 in_=kst[:, h * 512:(h + 1) * 512]).then_inc(ks_sl.h, 16)
        ks_done = ks_sl.bump(KV * 16)
        ROPES_V = mV.n

        prev_vs = vs_done
        for half in range(2):
            PRE_S = mS.n
            wv_guard = prev_vmm_last
            wvld = wv_sl.bump(16)
            vmm = [mT.bump(1) for tt in range(4)]
            vev = [mS.bump(1) for tt in range(4)]
            prev_vmm_last = vmm[-1]
            with nc.Block() as bl:
                @bl.gpsimd
                def _(g):
                    if wv_guard is not None:
                        g.wait_ge(mT.h, wv_guard)
                    g.dma_start(
                        out=wvb[:, :].rearrange("p (c n) -> p c n", n=512),
                        in_=wv_d[:, half * 512:(half + 1) * 512].rearrange(
                            "(c p) n -> p c n", p=128)).then_inc(wv_sl.h, 16)

                @bl.tensor
                def _(te):
                    te.wait_ge(wv_sl.h, wvld)
                    te.wait_ge(mS.h, PRE_S)
                    if half == 0:
                        te.wait_ge(mV.h, ROPES_V)
                    for tt in range(4):
                        pv = pF0 if tt % 2 == 0 else pE0
                        if tt >= 2:
                            te.wait_ge(mS.h, vev[tt - 2])
                        for c in range(EC):
                            te.matmul(pv[:, :],
                                      xln[:, c * 512 + tt * 128:c * 512 + tt * 128 + 128],
                                      wvb[:, c * 512:(c + 1) * 512],
                                      start=(c == 0), stop=False, skip_group_check=True)
                        te.matmul(pv[:, :], ones2[:, :],
                                  bvhl[:, half * 512:(half + 1) * 512],
                                  start=False, stop=True,
                                  skip_group_check=True).then_inc(mT.h, 1)

                @bl.scalar
                def _(sc):
                    for tt in range(4):
                        pv = pF0 if tt % 2 == 0 else pE0
                        sc.wait_ge(mT.h, vmm[tt])
                        if tt == 0 and half == 0 and prev_vs:
                            sc.wait_ge(vs_sl.h, prev_vs)
                        sc.activation(out=vst[:, tt * 1024 + half * 512:
                                              tt * 1024 + half * 512 + 512],
                                      in_=pv[:, :], func=AF.Copy).then_inc(mS.h, 1)
        VS_ALL = mS.n
        with nc.Block() as bl:
            @bl.sync
            def _(sy):
                sy.wait_ge(mS.h, VS_ALL)
                for tt in range(4):
                    sy.dma_start(out=v_scr[col0 + tt * 128:col0 + tt * 128 + 128, :],
                                 in_=vst[:, tt * 1024:(tt + 1) * 1024]).then_inc(vs_sl.h, 16)
        vs_done = vs_sl.bump(4 * 16)
    A1_KS = ks_sl.n
    A1_VS = vs_sl.n
    a1.close()

    a2p = ExitStack()
    q_tile = a2p.enter_context(nc.sbuf_tensor("q_tile", [128, EC * 512], BF16))
    for qg in range(NQ):
        qcol = qg * 512
        hard_barrier()
        p1 = ExitStack()
        xlnq = p1.enter_context(nc.sbuf_tensor(f"xlnq{qg}", [128, EC * 512], BF16))
        wqb = p1.enter_context(nc.sbuf_tensor(f"wqb{qg}", [128, 2 * EC * 128], BF16))
        xbf2 = p1.enter_context(nc.sbuf_tensor(f"xbf2{qg}", [128, EC * 512], BF16))
        x22 = p1.enter_context(nc.sbuf_tensor(f"x22{qg}", [128, 4 * 512], BF16))
        tn2 = p1.enter_context(nc.sbuf_tensor(f"tn2{qg}", [128, 512], F32))
        rp2 = [p1.enter_context(nc.sbuf_tensor(f"rp2_{qg}_{i}", [64, 512], F32))
               for i in range(2)]

        stats2 = stats_wave(xq, qcol, xbf2, x22)
        bcast2 = ln_rows_block(stats2, norm_done=mV.n)
        normq = norm_wave(xbf2, xlnq, bcast2, tn2)

        qwld = [qw_sl[m % 2].bump(16) for m in range(EC)]
        qmm = [mT.bump(1) for m in range(EC)]
        qrp = [mV.bump(1) for m in range(EC)]
        with nc.Block() as bl:
            @bl.gpsimd
            def _(g):
                for m in range(EC):
                    if m >= 2:
                        g.wait_ge(mT.h, qmm[m - 2])
                    g.dma_start(
                        out=wqb[:, (m % 2) * EC * 128:(m % 2 + 1) * EC * 128].rearrange(
                            "p (c n) -> p c n", n=128),
                        in_=wq_d[:, m * 128:(m + 1) * 128].rearrange(
                            "(c p) n -> p c n", p=128)).then_inc(qw_sl[m % 2].h, 16)

            @bl.tensor
            def _(te):
                te.wait_ge(mV.h, normq)
                for m in range(EC):
                    te.wait_ge(qw_sl[m % 2].h, qwld[m])
                    if m >= 2:
                        te.wait_ge(mV.h, qrp[m - 2])
                    pq = pE0 if m % 2 == 0 else pE1
                    base = (m % 2) * EC * 128
                    for c in range(EC):
                        te.matmul(pq[:, :], wqb[:, base + c * 128:base + (c + 1) * 128],
                                  xlnq[:, c * 512:(c + 1) * 512],
                                  start=(c == 0), stop=False, skip_group_check=True)
                    te.matmul(pq[:, :], bqhl[:, m * 128:(m + 1) * 128], onesN[:, :],
                              start=False, stop=True,
                              skip_group_check=True).then_inc(mT.h, 1)

            @bl.vector
            def _(ve):
                for m in range(EC):
                    ve.wait_ge(mT.h, qmm[m])
                    pq = pE0 if m % 2 == 0 else pE1
                    rope_ops(ve, pq, qcos, qsin, qcol, q_tile, m * 512, rp2)
        p1.close()
        hard_barrier()

        p2 = ExitStack()
        khb = p2.enter_context(nc.sbuf_tensor(f"khb{qg}", [128, 2 * S], BF16))
        vhb = p2.enter_context(nc.sbuf_tensor(f"vhb{qg}", [128, 2 * EC * 128], BF16))
        expb = p2.enter_context(nc.sbuf_tensor(f"expb{qg}", [128, 4 * 512], BF16))
        srecip = p2.enter_context(nc.sbuf_tensor(f"srecip{qg}", [128, 512], F32))
        attn_t = p2.enter_context(nc.sbuf_tensor(f"attn_t{qg}", [128, 512], F32))
        yst = p2.enter_context(nc.sbuf_tensor(f"yst{qg}", [128, 2 * 512], F32))
        ybf_grp = p2.enter_context(nc.sbuf_tensor(f"ybfg{qg}", [128, EC * 512], BF16))
        y22_rot = p2.enter_context(nc.sbuf_tensor(f"y22r{qg}", [128, 4 * 512], BF16))
        xqr = p2.enter_context(nc.sbuf_tensor(f"xqr{qg}", [128, 2 * 512], F32))
        ln2st = p2.enter_context(nc.sbuf_tensor(f"ln2st{qg}", [128, 2 * 512], BF16))
        rrow = p2.enter_context(nc.sbuf_tensor(f"rrow{qg}", [1, 512], F32))
        rrow_b = p2.enter_context(nc.sbuf_tensor(f"rrowb{qg}", [1, 512], F32))
        srecip_b = p2.enter_context(nc.sbuf_tensor(f"srecipb{qg}", [128, 512], F32))

        evac_ms = {}
        ystore_v = {}
        h_mm_done = {}
        last_exv_a = None
        last_exv_b = None
        prev_Vr_a = None
        prev_Vr_b = None
        A2S_T = mT.n
        A2S_V = mV.n
        QRP_LAST = qrp[-1]
        for h in range(KV):
            kvld = kv_sl[h % 2].bump(32)
            with nc.Block() as bl:
                @bl.sync
                def _(sy):
                    if h == 0:
                        sy.wait_ge(mT.h, A2S_T)
                        sy.wait_ge(mV.h, A2S_V)
                        if qg == 0:
                            sy.wait_ge(ks_sl.h, A1_KS)
                            sy.wait_ge(vs_sl.h, A1_VS)
                    if h >= 2:
                        sy.wait_ge(mT.h, h_mm_done[h - 2])
                    sy.dma_start(out=khb[:, (h % 2) * S:(h % 2 + 1) * S],
                                 in_=k_scr[h * 128:(h + 1) * 128, :]
                                 ).then_inc(kv_sl[h % 2].h, 16)
                    sy.dma_start(
                        out=vhb[:, (h % 2) * EC * 128:(h % 2 + 1) * EC * 128].rearrange(
                            "p (kc n) -> p kc n", n=128),
                        in_=v_scr[:, h * 128:(h + 1) * 128].rearrange(
                            "(kc kp) n -> kp kc n", kp=128)).then_inc(kv_sl[h % 2].h, 16)
            kb = (h % 2) * S
            vb = (h % 2) * EC * 128

            for sub in (0, 2):
                qa = h * 4 + sub
                qb = qa + 1
                SCA = [pE0, pE1]
                SCB2 = [pA, pB]
                S0 = mS.n
                exv_a = [S0 + 2 * kc + 1 for kc in range(32)]
                exv_b = [S0 + 2 * kc + 2 for kc in range(32)]
                scv_a = {}
                scv_b = {}
                pvv_a = {}
                pvv_b = {}
                t_ = mT.n
                for kc in range(2):
                    t_ += 1
                    scv_a[kc] = t_
                    t_ += 1
                    scv_b[kc] = t_
                for kc in range(32):
                    t_ += 1
                    pvv_a[kc] = t_
                    t_ += 1
                    pvv_b[kc] = t_
                    if kc + 2 < 32:
                        t_ += 1
                        scv_a[kc + 2] = t_
                        t_ += 1
                        scv_b[kc + 2] = t_
                mT.n = t_

                with nc.Block() as bl:
                    @bl.tensor
                    def _(te):
                        te.wait_ge(kv_sl[h % 2].h, kvld)
                        if qa == 0:
                            te.wait_ge(mV.h, QRP_LAST)

                        def sc_mm(stream, kc):
                            if stream == 0:
                                psc, exv, lexv, qh_ = SCA[kc % 2], exv_a, last_exv_a, qa
                            else:
                                psc, exv, lexv, qh_ = SCB2[kc % 2], exv_b, last_exv_b, qb
                            if kc >= 2:
                                te.wait_ge(mS.h, exv[kc - 2])
                            elif lexv is not None:
                                te.wait_ge(mS.h, lexv[30 + kc])
                            te.matmul(psc[:, :],
                                      khb[:, kb + kc * 128:kb + (kc + 1) * 128],
                                      q_tile[:, qh_ * 512:(qh_ + 1) * 512],
                                      start=True, stop=True,
                                      skip_group_check=True).then_inc(mT.h, 1)
                        for kc in range(2):
                            sc_mm(0, kc)
                            sc_mm(1, kc)
                        for kc in range(32):
                            te.wait_ge(mS.h, exv_a[kc])
                            if kc == 0:
                                if prev_Vr_a is not None:
                                    te.wait_ge(mV.h, prev_Vr_a)
                                if qa >= 4:
                                    te.wait_ge(mV.h, evac_ms[qa - 2])
                            esl = slice((kc % 2) * 512, (kc % 2 + 1) * 512)
                            te.matmul(pSa[:, :], ones_col[:, :], expb[:, esl],
                                      start=(kc == 0), stop=(kc == 31),
                                      skip_group_check=True)
                            te.matmul(pF0[:, :],
                                      vhb[:, vb + kc * 128:vb + (kc + 1) * 128],
                                      expb[:, esl], start=(kc == 0), stop=(kc == 31),
                                      skip_group_check=True).then_inc(mT.h, 1)
                            te.wait_ge(mS.h, exv_b[kc])
                            if kc == 0:
                                if prev_Vr_b is not None:
                                    te.wait_ge(mV.h, prev_Vr_b)
                                if qb >= 4:
                                    te.wait_ge(mV.h, evac_ms[qb - 2])
                            esl2 = slice((2 + kc % 2) * 512, (2 + kc % 2 + 1) * 512)
                            te.matmul(pC[0:1, :], ones_col[:, :], expb[:, esl2],
                                      start=(kc == 0), stop=(kc == 31),
                                      skip_group_check=True)
                            te.matmul(pD[:, :],
                                      vhb[:, vb + kc * 128:vb + (kc + 1) * 128],
                                      expb[:, esl2], start=(kc == 0), stop=(kc == 31),
                                      skip_group_check=True).then_inc(mT.h, 1)
                            if kc + 2 < 32:
                                sc_mm(0, kc + 2)
                                sc_mm(1, kc + 2)

                    @bl.scalar
                    def _(sc):
                        for kc in range(32):
                            sc.wait_ge(mT.h, scv_a[kc])
                            if kc >= 2:
                                sc.wait_ge(mT.h, pvv_a[kc - 2])
                            sc.activation(out=expb[:, (kc % 2) * 512:(kc % 2 + 1) * 512],
                                          in_=SCA[kc % 2][:, :], func=AF.Exp,
                                          scale=float(1.0 / math.sqrt(D))
                                          ).then_inc(mS.h, 1)
                            sc.wait_ge(mT.h, scv_b[kc])
                            if kc >= 2:
                                sc.wait_ge(mT.h, pvv_b[kc - 2])
                            sc.activation(out=expb[:, (2 + kc % 2) * 512:(2 + kc % 2 + 1) * 512],
                                          in_=SCB2[kc % 2][:, :], func=AF.Exp,
                                          scale=float(1.0 / math.sqrt(D))
                                          ).then_inc(mS.h, 1)
                mS.bump(64)

                Vra = mV.bump(1)
                Vrb = mV.bump(1)
                Ga_ = rh_sl.bump(32)
                Gb_ = rh2_sl.bump(32)
                Lqa = xq_sl[0].bump(16)
                Lqb = xq_sl[1].bump(16)
                Vea = mV.bump(1)
                Veb = mV.bump(1)
                Qsa = ys_sl[0].bump(16)
                Qsb = ys_sl[1].bump(16)
                with nc.Block() as bl:
                    @bl.vector
                    def _(ve):
                        ve.wait_ge(mT.h, pvv_a[31])
                        ve.reciprocal(out=rrow[:, :], in_=pSa[:, :]).then_inc(mV.h, 1)
                        ve.wait_ge(mT.h, pvv_b[31])
                        ve.reciprocal(out=rrow_b[:, :], in_=pC[0:1, :]).then_inc(mV.h, 1)
                        ve.wait_ge(rh_sl.h, Ga_)
                        ve.wait_ge(xq_sl[0].h, Lqa)
                        if qa >= 4:
                            ve.wait_ge(ys_sl[0].h, ystore_v[qa - 2])
                        ve.tensor_tensor(out=attn_t[:, :], in0=pF0[:, :],
                                         in1=srecip[:, :], op=ALU.mult)
                        ve.tensor_tensor(out=yst[:, 0:512], in0=attn_t[:, :],
                                         in1=xqr[:, 0:512],
                                         op=ALU.add).then_inc(mV.h, 1)
                        ve.wait_ge(rh2_sl.h, Gb_)
                        ve.wait_ge(xq_sl[1].h, Lqb)
                        if qb >= 4:
                            ve.wait_ge(ys_sl[1].h, ystore_v[qb - 2])
                        ve.tensor_tensor(out=attn_t[:, :], in0=pD[:, :],
                                         in1=srecip_b[:, :], op=ALU.mult)
                        ve.tensor_tensor(out=yst[:, 512:1024], in0=attn_t[:, :],
                                         in1=xqr[:, 512:1024],
                                         op=ALU.add).then_inc(mV.h, 1)

                    @bl.gpsimd
                    def _(g):
                        g.wait_ge(mV.h, Vra)
                        g.dma_start(out=row_scr[2:3, :], in_=rrow[:, :]).then_inc(rh_sl.h, 16)
                        g.wait_ge(mV.h, Vrb)
                        g.dma_start(out=row_scr[3:4, :], in_=rrow_b[:, :]).then_inc(rh2_sl.h, 16)
                        g.wait_ge(rh_sl.h, Ga_ - 16)
                        g.dma_start(out=srecip[:, :],
                                    in_=row_scr[2:3, :].broadcast_to([128, 512])
                                    ).then_inc(rh_sl.h, 16)
                        g.wait_ge(rh2_sl.h, Gb_ - 16)
                        g.dma_start(out=srecip_b[:, :],
                                    in_=row_scr[3:4, :].broadcast_to([128, 512])
                                    ).then_inc(rh2_sl.h, 16)

                    @bl.sync
                    def _(sy):
                        if qa >= 4:
                            sy.wait_ge(mV.h, evac_ms[qa - 2])
                        sy.dma_start(out=xqr[:, 0:512],
                                     in_=xq[qa * 128:(qa + 1) * 128, qcol:qcol + 512]
                                     ).then_inc(xq_sl[0].h, 16)
                        if qb >= 4:
                            sy.wait_ge(mV.h, evac_ms[qb - 2])
                        sy.dma_start(out=xqr[:, 512:1024],
                                     in_=xq[qb * 128:(qb + 1) * 128, qcol:qcol + 512]
                                     ).then_inc(xq_sl[1].h, 16)
                        sy.wait_ge(mV.h, Vea)
                        sy.dma_start(out=y_scr[qa * 128:(qa + 1) * 128, qcol:qcol + 512],
                                     in_=yst[:, 0:512]).then_inc(ys_sl[0].h, 16)
                        sy.wait_ge(mV.h, Veb)
                        sy.dma_start(out=y_scr[qb * 128:(qb + 1) * 128, qcol:qcol + 512],
                                     in_=yst[:, 512:1024]).then_inc(ys_sl[1].h, 16)
                evac_ms[qa] = Vea
                evac_ms[qb] = Veb
                ystore_v[qa] = Qsa
                ystore_v[qb] = Qsb
                last_exv_a = exv_a
                last_exv_b = exv_b
                prev_Vr_a = Vra
                prev_Vr_b = Vrb
            h_mm_done[h] = mT.n

        YALL0 = [ys_sl[0].n, ys_sl[1].n]
        with nc.Block() as bl:
            @bl.gpsimd
            def _(g):
                g.wait_ge(ys_sl[0].h, YALL0[0])
                g.wait_ge(ys_sl[1].h, YALL0[1])
                g.memset(barrier_tile[:, :], 0.0).then_inc(mG.h, 1)
        YB = mG.bump(1)
        with nc.Block() as bl:
            @bl.gpsimd
            def _(g):
                g.wait_ge(mG.h, YB)
        stats3 = stats_wave(y_scr, qcol, ybf_grp, y22_rot)
        bcast3 = ln_rows_block(stats3, norm_done=mV.n)

        l2v = [mV.bump(1) for c in range(EC)]
        l2s = [ls_sl[c % 2].bump(16) for c in range(EC)]
        with nc.Block() as bl:
            @bl.vector
            def _(ve):
                ve.wait_ge(la_sl.h, bcast3[0])
                ve.wait_ge(lb_sl.h, bcast3[1])
                for c in range(EC):
                    if c >= 2:
                        ve.wait_ge(ls_sl[c % 2].h, l2s[c - 2])
                    ve.tensor_tensor(out=attn_t[:, :],
                                     in0=ybf_grp[:, c * 512:(c + 1) * 512],
                                     in1=rstd_bt[:, :], op=ALU.mult)
                    ve.tensor_tensor(out=ln2st[:, (c % 2) * 512:(c % 2 + 1) * 512],
                                     in0=attn_t[:, :], in1=negmr_bt[:, :],
                                     op=ALU.add).then_inc(mV.h, 1)

            @bl.gpsimd
            def _(g):
                for c in range(EC):
                    g.wait_ge(mV.h, l2v[c])
                    g.dma_start(out=ln2_scr[c * 128:(c + 1) * 128, qcol:qcol + 512],
                                in_=ln2st[:, (c % 2) * 512:(c % 2 + 1) * 512]
                                ).then_inc(ls_sl[c % 2].h, 16)
        p2.close()
    a2p.close()
    ap_es.close()

    hard_barrier()
    ffn = ExitStack()
    ln2b = ffn.enter_context(nc.sbuf_tensor("ln2b", [128, EC * 512], BF16))
    h_full = ffn.enter_context(nc.sbuf_tensor("h_full", [128, IC * 512], BF16))
    wgb = ffn.enter_context(nc.sbuf_tensor("wgb", [128, 2 * EC * 128], BF16))
    wub = ffn.enter_context(nc.sbuf_tensor("wub", [128, 2 * EC * 128], BF16))
    silu_t = ffn.enter_context(nc.sbuf_tensor("silu_t", [128, 512], F32))
    wdb = ffn.enter_context(nc.sbuf_tensor("wdb", [128, 2 * IC * 128], BF16))
    yres = ffn.enter_context(nc.sbuf_tensor("yres", [128, 2 * 512], F32))
    ost = ffn.enter_context(nc.sbuf_tensor("ost", [128, 2 * 512], F32))

    for hf in range(2):
        tcol = hf * 512
        PRE_T = mT.n
        l2bv = l2_sl.bump(16)
        with nc.Block() as bl:
            @bl.sync
            def _(sy):
                sy.wait_ge(mT.h, PRE_T)
                if hf == 0:
                    sy.wait_ge(cc_sl.h, CCV)
                sy.dma_start(out=ln2b[:, :].rearrange("p (c t) -> p c t", t=512),
                             in_=ln2_scr[:, tcol:tcol + 512].rearrange(
                                 "(c p) t -> p c t", p=128)).then_inc(l2_sl.h, 16)

        wgld = [wg_sl[p % 2].bump(16) for p in range(IC)]
        wuld = [wu_sl[p % 2].bump(16) for p in range(IC)]
        gmm = [mT.bump(2) for p in range(IC)]
        sqv = [mS.bump(1) for p in range(IC)]
        hv = [mV.bump(1) for p in range(IC)]
        with nc.Block() as bl:
            @bl.gpsimd
            def _(g):
                for p in range(IC):
                    if p == 0 and hf == 1:
                        g.wait_ge(mT.h, PRE_T)
                    if p >= 2:
                        g.wait_ge(mT.h, gmm[p - 2])
                    g.dma_start(
                        out=wgb[:, (p % 2) * EC * 128:(p % 2 + 1) * EC * 128].rearrange(
                            "p (c n) -> p c n", n=128),
                        in_=wg_scr[:, p * 128:(p + 1) * 128].rearrange(
                            "(c kp) n -> kp c n", kp=128)).then_inc(wg_sl[p % 2].h, 16)
                    g.dma_start(
                        out=wub[:, (p % 2) * EC * 128:(p % 2 + 1) * EC * 128].rearrange(
                            "p (c n) -> p c n", n=128),
                        in_=wu_scr[:, p * 128:(p + 1) * 128].rearrange(
                            "(c kp) n -> kp c n", kp=128)).then_inc(wu_sl[p % 2].h, 16)

            @bl.tensor
            def _(te):
                te.wait_ge(l2_sl.h, l2bv)
                for p in range(IC):
                    te.wait_ge(wg_sl[p % 2].h, wgld[p])
                    te.wait_ge(wu_sl[p % 2].h, wuld[p])
                    if p >= 2:
                        te.wait_ge(mS.h, sqv[p - 2])
                        te.wait_ge(mV.h, hv[p - 2])
                    pg = pE0 if p % 2 == 0 else pE1
                    pu = pC if p % 2 == 0 else pD
                    base = (p % 2) * EC * 128
                    for c in range(EC):
                        mm = te.matmul(pg[:, :],
                                       wgb[:, base + c * 128:base + (c + 1) * 128],
                                       ln2b[:, c * 512:(c + 1) * 512],
                                       start=(c == 0), stop=(c == EC - 1),
                                       skip_group_check=True)
                        if c == EC - 1:
                            mm.then_inc(mT.h, 1)
                    for c in range(EC):
                        mm = te.matmul(pu[:, :],
                                       wub[:, base + c * 128:base + (c + 1) * 128],
                                       ln2b[:, c * 512:(c + 1) * 512],
                                       start=(c == 0), stop=(c == EC - 1),
                                       skip_group_check=True)
                        if c == EC - 1:
                            mm.then_inc(mT.h, 1)

            @bl.scalar
            def _(sc):
                for p in range(IC):
                    pg = pE0 if p % 2 == 0 else pE1
                    sc.wait_ge(mT.h, gmm[p] - 1)
                    sc.activation(out=silu_t[:, :], in_=pg[:, :],
                                  func=AF.Silu).then_inc(mS.h, 1)

            @bl.vector
            def _(ve):
                for p in range(IC):
                    pu = pC if p % 2 == 0 else pD
                    ve.wait_ge(mS.h, sqv[p])
                    ve.wait_ge(mT.h, gmm[p])
                    ve.tensor_tensor(out=h_full[:, p * 512:(p + 1) * 512],
                                     in0=silu_t[:, :], in1=pu[:, :],
                                     op=ALU.mult).then_inc(mV.h, 1)

        HIC = IC // 2
        wdld = [wd_sl[e % 2].bump(32) for e in range(EC)]
        dmm = [mT.bump(1) for e in range(EC)]
        yld = [yr_sl[e % 2].bump(32) for e in range(EC)]
        oev = [mV.bump(1) for e in range(EC)]
        osv = [os_sl[e % 2].bump(16) for e in range(EC)]
        HV_LAST = hv[-1]
        with nc.Block() as bl:
            @bl.gpsimd
            def _(g):
                g.wait_ge(mV.h, HV_LAST)
                for e in range(EC):
                    if e >= 2:
                        g.wait_ge(mT.h, dmm[e - 2])
                    for piece in range(2):
                        r0 = piece * HIC * 128
                        o0 = (e % 2) * IC * 128 + piece * HIC * 128
                        sh = e // 4
                        l0 = (e % 4) * 128
                        g.dma_start(
                            out=wdb[:, o0:o0 + HIC * 128].rearrange(
                                "p (ic n) -> p ic n", n=128),
                            in_=wd_g[sh * I + r0:sh * I + r0 + HIC * 128,
                                     l0:l0 + 128].rearrange(
                                "(ic p) n -> p ic n", p=128)).then_inc(wd_sl[e % 2].h, 16)

            @bl.tensor
            def _(te):
                for e in range(EC):
                    te.wait_ge(wd_sl[e % 2].h, wdld[e])
                    if e >= 2:
                        te.wait_ge(mV.h, oev[e - 2])
                    pd_ = pE0 if e % 2 == 0 else pE1
                    base = (e % 2) * IC * 128
                    for ic in range(IC):
                        mm = te.matmul(pd_[:, :],
                                       wdb[:, base + ic * 128:base + (ic + 1) * 128],
                                       h_full[:, ic * 512:(ic + 1) * 512],
                                       start=(ic == 0), stop=(ic == IC - 1),
                                       skip_group_check=True)
                        if ic == IC - 1:
                            mm.then_inc(mT.h, 1)

            @bl.sync
            def _(sy):
                for e in range(EC):
                    if e >= 2:
                        sy.wait_ge(mV.h, oev[e - 2])
                    for c_ in range(2):
                        base_el = (e * 128 + c_) * TOWN + tcol
                        in_ap = bass.AP(y_scr_t, base_el,
                                        [[2 * TOWN, 4], [8 * TOWN, 16], [1, 512]])
                        sy.dma_start(out=yres[c_ * 64:(c_ + 1) * 64,
                                              (e % 2) * 512:(e % 2 + 1) * 512],
                                     in_=in_ap).then_inc(yr_sl[e % 2].h, 16)
                    sy.wait_ge(mV.h, oev[e])
                    sy.dma_start(out=out_d[e * 128:(e + 1) * 128, tcol:tcol + 512],
                                 in_=ost[:, (e % 2) * 512:(e % 2 + 1) * 512]
                                 ).then_inc(os_sl[e % 2].h, 16)

            @bl.vector
            def _(ve):
                for e in range(EC):
                    pd_ = pE0 if e % 2 == 0 else pE1
                    ve.wait_ge(mT.h, dmm[e])
                    ve.wait_ge(yr_sl[e % 2].h, yld[e])
                    if e >= 2:
                        ve.wait_ge(os_sl[e % 2].h, osv[e - 2])
                    ve.tensor_tensor(out=ost[:, (e % 2) * 512:(e % 2 + 1) * 512],
                                     in0=pd_[:, :],
                                     in1=yres[:, (e % 2) * 512:(e % 2 + 1) * 512],
                                     op=ALU.add).then_inc(mV.h, 1)
    ffn.close()

    if dbg:
        dmp = [
            (nc.dram_tensor("dbg_k", [KV * D, S], BF16, kind="ExternalOutput").ap(), k_scr),
            (nc.dram_tensor("dbg_v", [S, KV * D], BF16, kind="ExternalOutput").ap(), v_scr),
            (nc.dram_tensor("dbg_y", [E, TOWN], F32, kind="ExternalOutput").ap(), y_scr),
            (nc.dram_tensor("dbg_ln2", [E, TOWN], BF16, kind="ExternalOutput").ap(), ln2_scr),
            (nc.dram_tensor("dbg_wg", [E, 512], BF16, kind="ExternalOutput").ap(), wg_scr[:, 0:512]),
            (nc.dram_tensor("dbg_wd", [512, 512], BF16, kind="ExternalOutput").ap(), wd_g[0:512, :]),
        ]
        snap = [(s_.h, s_.n) for s_ in ALLSEMS]
        dv = ps_sl.n
        with nc.Block() as bl:
            @bl.sync
            def _(sy):
                for hh, vv in snap:
                    sy.wait_ge(hh, vv)
                for dst, srcp in dmp:
                    sy.dma_start(out=dst, in_=srcp).then_inc(ps_sl.h, 16)
                sy.wait_ge(ps_sl.h, dv + len(dmp) * 16)
        ps_sl.bump(len(dmp) * 16)
    else:
        snap = [(s_.h, s_.n) for s_ in ALLSEMS] + [(cc_sl.h, cc_sl.n)]
        with nc.Block() as bl:
            @bl.sync
            def _(sy):
                for hh, vv in snap:
                    sy.wait_ge(hh, vv)

    pes.close()
    es.close()
    return nc


_NC_CACHE = {}


def kernel(**inputs):
    P = _perm128()
    permI = np.concatenate([q * 128 + P for q in range(IC)])
    permE = np.concatenate([q * 128 + P for q in range(EC)])

    x = np.asarray(inputs["L0_attn_x"], np.float32)
    wq = np.ascontiguousarray(np.asarray(inputs["L0_attn_wq"], np.float32))
    wk = np.ascontiguousarray(np.asarray(inputs["L0_attn_wk"], np.float32))
    wv = np.ascontiguousarray(np.asarray(inputs["L0_attn_wv"], np.float32))
    bq = np.asarray(inputs["L0_attn_bq"], np.float32).reshape(1, E)
    bk = np.asarray(inputs["L0_attn_bk"], np.float32).reshape(1, KV * D)
    bv = np.asarray(inputs["L0_attn_bv"], np.float32).reshape(1, KV * D)
    gq = np.asarray(inputs["L0_ffn_gate_qweight"], np.int32)
    uq = np.asarray(inputs["L0_ffn_up_qweight"], np.int32)
    dq = np.asarray(inputs["L0_ffn_down_qweight"], np.int32)[permI]
    gs = np.asarray(inputs["L0_ffn_gate_scales"], np.float32)[:, permI]
    us = np.asarray(inputs["L0_ffn_up_scales"], np.float32)[:, permI]
    ds = np.asarray(inputs["L0_ffn_down_scales"], np.float32)[:, permE]
    gz = np.asarray(inputs["L0_ffn_gate_qzeros"], np.int32)
    uz = np.asarray(inputs["L0_ffn_up_qzeros"], np.int32)
    dz = np.asarray(inputs["L0_ffn_down_qzeros"], np.int32)

    xT = [np.ascontiguousarray(x[b].T) for b in range(B)]

    if "nc" not in _NC_CACHE:
        _NC_CACHE["nc"] = build_nc()
    nc = _NC_CACHE["nc"]

    in_maps = []
    for c in range(8):
        b, t0 = c // 4, (c % 4) * TOWN
        in_maps.append({
            "xT": xT[b],
            "xq": np.ascontiguousarray(xT[b][:, t0:t0 + TOWN]),
            "wq": wq, "wk": wk, "wv": wv,
            "bq": bq, "bk": bk, "bv": bv,
            "gq": np.ascontiguousarray(gq[c * 512:(c + 1) * 512]),
            "uq": np.ascontiguousarray(uq[c * 512:(c + 1) * 512]),
            "dq": np.ascontiguousarray(dq[:, c * 64:(c + 1) * 64]),
            "gs": np.ascontiguousarray(gs[4 * c:4 * c + 4]),
            "us": np.ascontiguousarray(us[4 * c:4 * c + 4]),
            "ds": np.ascontiguousarray(ds[:, c * 512:(c + 1) * 512]),
            "gz": np.ascontiguousarray(gz[4 * c:4 * c + 4]),
            "uz": np.ascontiguousarray(uz[4 * c:4 * c + 4]),
            "dz": np.ascontiguousarray(dz[:, c * 64:(c + 1) * 64]),
            "pos0": np.full((64, 1), float(t0), np.float32),
        })

    res = run_bass_kernel_spmd(nc, in_maps, core_ids=list(range(8)))

    out = np.empty((B, S, E), np.float32)
    for c in range(8):
        b, t0 = c // 4, (c % 4) * TOWN
        dev = np.asarray(res.results[c]["out"])
        nat = np.empty_like(dev)
        nat[permE] = dev
        out[b, t0:t0 + TOWN, :] = nat.T
    return out


# version 12
# speedup vs baseline: 1.2954x; 1.0231x over previous
import math
from contextlib import ExitStack

import numpy as np
import concourse.bass as bass
import concourse.mybir as mybir
from concourse.bass_utils import run_bass_kernel_spmd

F32 = mybir.dt.float32
BF16 = mybir.dt.bfloat16
I32 = mybir.dt.int32
AF = mybir.ActivationFunctionType
ALU = mybir.AluOpType

B, S, E, H, KV, D = 2, 4096, 4096, 32, 8, 128
I = 11008
EC = E // 128
IC = I // 128
GW, DW = I // 8, E // 8
TOWN = 1024
NG = S // 512
NQ = TOWN // 512
TWO_PI = float(2 * math.pi)
AWQ_INV = [0, 2, 4, 6, 1, 3, 5, 7]


def _perm128():
    P = np.zeros(128, np.int64)
    for s_ in range(128):
        j, wl = s_ // 16, s_ % 16
        P[s_] = 8 * wl + AWQ_INV[j]
    return P


class Sem:
    def __init__(self, es, nc, name):
        self.h = es.enter_context(nc.semaphore(name))
        self.n = 0

    def bump(self, k):
        self.n += k
        return self.n


def build_nc(dbg=False):
    nc = bass.Bass("TRN2", target_bir_lowering=False)

    xT = nc.dram_tensor("xT", [E, S], F32, kind="ExternalInput").ap()
    xq = nc.dram_tensor("xq", [E, TOWN], F32, kind="ExternalInput").ap()
    wq_d = nc.dram_tensor("wq", [E, E], F32, kind="ExternalInput").ap()
    wk_d = nc.dram_tensor("wk", [E, KV * D], F32, kind="ExternalInput").ap()
    wv_d = nc.dram_tensor("wv", [E, KV * D], F32, kind="ExternalInput").ap()
    bq_d = nc.dram_tensor("bq", [1, E], F32, kind="ExternalInput").ap()
    bk_d = nc.dram_tensor("bk", [1, KV * D], F32, kind="ExternalInput").ap()
    bv_d = nc.dram_tensor("bv", [1, KV * D], F32, kind="ExternalInput").ap()
    gq_d = nc.dram_tensor("gq", [512, GW], I32, kind="ExternalInput").ap()
    uq_d = nc.dram_tensor("uq", [512, GW], I32, kind="ExternalInput").ap()
    dq_d = nc.dram_tensor("dq", [I, DW // 8], I32, kind="ExternalInput").ap()
    gs_d = nc.dram_tensor("gs", [4, I], F32, kind="ExternalInput").ap()
    us_d = nc.dram_tensor("us", [4, I], F32, kind="ExternalInput").ap()
    ds_d = nc.dram_tensor("ds", [86, 512], F32, kind="ExternalInput").ap()
    gz_d = nc.dram_tensor("gz", [4, GW], I32, kind="ExternalInput").ap()
    uz_d = nc.dram_tensor("uz", [4, GW], I32, kind="ExternalInput").ap()
    dz_d = nc.dram_tensor("dz", [86, DW // 8], I32, kind="ExternalInput").ap()
    pos0_d = nc.dram_tensor("pos0", [64, 1], F32, kind="ExternalInput").ap()
    out_d = nc.dram_tensor("out", [E, TOWN], F32, kind="ExternalOutput").ap()

    wg_scr = nc.dram_tensor("wg_scr", [E, I], BF16, kind="Internal").ap()
    wu_scr = nc.dram_tensor("wu_scr", [E, I], BF16, kind="Internal").ap()
    wd_g = nc.dram_tensor("wd_g", [8 * 512, I], BF16, kind="Internal").ap()
    wg_own = nc.dram_tensor("wg_own", [512, I], BF16, kind="Internal").ap()
    wkb_scr = nc.dram_tensor("wkb_scr", [E, KV * D], BF16, kind="Internal").ap()
    wvb_scr = nc.dram_tensor("wvb_scr", [E, KV * D], BF16, kind="Internal").ap()
    wqb_scr = nc.dram_tensor("wqb_scr", [E, E], BF16, kind="Internal").ap()
    wu_own = nc.dram_tensor("wu_own", [512, I], BF16, kind="Internal").ap()
    wd_own = nc.dram_tensor("wd_own", [512, I], BF16, kind="Internal").ap()
    k_scr = nc.dram_tensor("k_scr", [KV * D, S], BF16, kind="Internal").ap()
    v_scr = nc.dram_tensor("v_scr", [S, KV * D], BF16, kind="Internal").ap()
    y_scr_t = nc.dram_tensor("y_scr", [E, TOWN], F32, kind="Internal")
    y_scr = y_scr_t.ap()
    ln2_scr = nc.dram_tensor("ln2_scr", [E, TOWN], BF16, kind="Internal").ap()
    row_scr = nc.dram_tensor("row_scr", [4, 512], F32, kind="Internal").ap()

    es = ExitStack()
    ones_col = es.enter_context(nc.sbuf_tensor("ones_col", [128, 1], BF16))
    ones2 = es.enter_context(nc.sbuf_tensor("ones2", [2, 128], BF16))
    onesN = es.enter_context(nc.sbuf_tensor("onesN", [2, 512], BF16))
    eps_col = es.enter_context(nc.sbuf_tensor("eps_col", [128, 1], F32))
    barrier_tile = es.enter_context(nc.sbuf_tensor("barrier_tile", [1, 2], F32))

    mT = Sem(es, nc, "mT")
    mV = Sem(es, nc, "mV")
    mS = Sem(es, nc, "mS")
    mG = Sem(es, nc, "mG")
    ALLSEMS = [mT, mV, mS, mG]

    def mksem(name):
        s_ = Sem(es, nc, name)
        ALLSEMS.append(s_)
        return s_

    def mkslots(name, k):
        return [mksem(f"{name}{i}") for i in range(k)]

    sl_ld = mkslots("sl_ld", 2)
    sl_st = mkslots("sl_st", 2)
    ps_sl = mksem("ps_sl")
    br_sl = mksem("br_sl")
    bs_sl = mksem("bs_sl")
    xsl = mkslots("xsl", 4)
    ysl = mkslots("ysl", 4)
    la_sl = mksem("la_sl")
    lb_sl = mksem("lb_sl")
    wk_sl = mkslots("wk_sl", 2)
    wv_sl = mksem("wv_sl")
    ks_sl = mksem("ks_sl")
    vs_sl = mksem("vs_sl")
    qw_sl = mkslots("qw_sl", 2)
    kv_sl = mkslots("kv_sl", 2)
    rh_sl = mksem("rh_sl")
    rh2_sl = mksem("rh2_sl")
    cc_sl = Sem(es, nc, "cc_sl")
    wsc_sl = mksem("wsc_sl")
    xq_sl = mkslots("xq_sl", 2)
    ys_sl = mkslots("ys_sl", 2)
    l2y_sl = mkslots("l2y_sl", 2)
    ls_sl = mkslots("ls_sl", 2)
    l2_sl = mksem("l2_sl")
    wg_sl = mkslots("wg_sl", 2)
    wu_sl = mkslots("wu_sl", 2)
    wd_sl = mkslots("wd_sl", 2)
    yr_sl = mkslots("yr_sl", 2)
    os_sl = mkslots("os_sl", 2)

    pes = ExitStack()
    pA = pes.enter_context(nc.psum_tensor("pA", [128, 512], F32))
    pB = pes.enter_context(nc.psum_tensor("pB", [128, 512], F32))
    pSa = pes.enter_context(nc.psum_tensor("pSa", [1, 512], F32))
    pC = pes.enter_context(nc.psum_tensor("pC", [128, 512], F32))
    pD = pes.enter_context(nc.psum_tensor("pD", [128, 512], F32))
    pE0 = pes.enter_context(nc.psum_tensor("pE0", [128, 512], F32))
    pE1 = pes.enter_context(nc.psum_tensor("pE1", [128, 512], F32))
    pF0 = pes.enter_context(nc.psum_tensor("pF0", [128, 512], F32))

    def hard_barrier():
        snap = [(s_.h, s_.n) for s_ in ALLSEMS]
        bm = mG.n + 1
        with nc.Block() as bl:
            @bl.gpsimd
            def _(g):
                for hh, vv in snap:
                    g.wait_ge(hh, vv)
                g.memset(barrier_tile[:, :], 0.0).then_inc(mG.h, 1)

            @bl.tensor
            def _(te):
                te.wait_ge(mG.h, bm)

            @bl.vector
            def _(ve):
                ve.wait_ge(mG.h, bm)

            @bl.scalar
            def _(sc):
                sc.wait_ge(mG.h, bm)

            @bl.sync
            def _(sy):
                sy.wait_ge(mG.h, bm)
        mG.bump(1)

    with nc.Block() as bl:
        @bl.gpsimd
        def _(g):
            g.memset(ones_col[:, :], 1.0)
            g.memset(ones2[:, :], 1.0)
            g.memset(onesN[:, :], 1.0)
            g.memset(eps_col[:, :], 1e-5).then_inc(mG.h, 1)
    mG.bump(1)

    WSC = wsc_sl.n + 48
    with nc.Block() as bl:
        @bl.gpsimd
        def _(g):
            g.dma_start(out=wkb_scr[:, :], in_=wk_d[:, :]).then_inc(wsc_sl.h, 16)
            g.dma_start(out=wvb_scr[:, :], in_=wv_d[:, :]).then_inc(wsc_sl.h, 16)
            g.dma_start(out=wqb_scr[:, :], in_=wq_d[:, :]).then_inc(wsc_sl.h, 16)
    wsc_sl.bump(48)

    deq = ExitStack()
    NSH = I // 2
    WSH = GW // 2
    q_sb = [deq.enter_context(nc.sbuf_tensor(f"q_sb{i}", [128, WSH], I32)) for i in range(2)]
    z_sb = [deq.enter_context(nc.sbuf_tensor(f"z_sb{i}", [128, WSH], I32)) for i in range(2)]
    s_sb = [deq.enter_context(nc.sbuf_tensor(f"s_sb{i}", [128, NSH], F32)) for i in range(2)]
    wn_sb = deq.enter_context(nc.sbuf_tensor("wn_sb", [128, NSH], I32))
    zn_sb = deq.enter_context(nc.sbuf_tensor("zn_sb", [128, NSH], I32))
    db_sb = deq.enter_context(nc.sbuf_tensor("db_sb", [128, NSH], BF16))
    wo_sb = [deq.enter_context(nc.sbuf_tensor(f"wo_sb{i}", [128, NSH], BF16)) for i in range(2)]

    slices = []
    for c in range(4):
        for hf in range(2):
            slices.append((gq_d, gz_d, gs_d, wg_own, c, hf * WSH, hf * NSH, WSH))
    for c in range(4):
        for hf in range(2):
            slices.append((uq_d, uz_d, us_d, wu_own, c, hf * WSH, hf * NSH, WSH))
    for c in range(IC):
        slices.append((dq_d, dz_d, ds_d, wd_own, c, 0, 0, DW // 8))
    nsl = len(slices)

    ld_v = [sl_ld[i % 2].bump(48) for i in range(nsl)]
    dv_v = [mV.bump(1) for i in range(nsl)]
    st_v = [sl_st[i % 2].bump(16) for i in range(nsl)]

    with nc.Block() as bl:
        @bl.sync
        def _(sy):
            for i, (qsrc, zsrc, ssrc, dst, c, woff, noff, nw) in enumerate(slices):
                bi = i % 2
                nnib = nw * 8
                if i >= 2:
                    sy.wait_ge(mV.h, dv_v[i - 2])
                sy.dma_start(out=q_sb[bi][:, 0:nw],
                             in_=qsrc[c * 128:(c + 1) * 128, woff:woff + nw]
                             ).then_inc(sl_ld[bi].h, 16)
                sy.dma_start(out=z_sb[bi][:, 0:nw],
                             in_=zsrc[c:c + 1, woff:woff + nw].broadcast_to([128, nw])
                             ).then_inc(sl_ld[bi].h, 16)
                sy.dma_start(out=s_sb[bi][:, 0:nnib],
                             in_=ssrc[c:c + 1, noff:noff + nnib].broadcast_to([128, nnib])
                             ).then_inc(sl_ld[bi].h, 16)

        @bl.vector
        def _(ve):
            for i, (qsrc, zsrc, ssrc, dst, c, woff, noff, nw) in enumerate(slices):
                bi = i % 2
                nnib = nw * 8
                ve.wait_ge(sl_ld[bi].h, ld_v[i])
                if i >= 2:
                    ve.wait_ge(sl_st[bi].h, st_v[i - 2])
                wv3 = wn_sb[:, 0:nnib].rearrange("p (q j w) -> p q j w", j=8, w=16)
                zv3 = zn_sb[:, 0:nnib].rearrange("p (q j w) -> p q j w", j=8, w=16)
                qv3 = q_sb[bi][:, 0:nw].rearrange("p (q w) -> p q w", w=16)
                zq3 = z_sb[bi][:, 0:nw].rearrange("p (q w) -> p q w", w=16)
                for j in range(8):
                    ve.tensor_scalar(out=wv3[:, :, j, :], in0=qv3, scalar1=4 * j,
                                     scalar2=15, op0=ALU.logical_shift_right,
                                     op1=ALU.bitwise_and)
                    ve.tensor_scalar(out=zv3[:, :, j, :], in0=zq3, scalar1=4 * j,
                                     scalar2=15, op0=ALU.logical_shift_right,
                                     op1=ALU.bitwise_and)
                ve.tensor_tensor(out=db_sb[:, 0:nnib], in0=wn_sb[:, 0:nnib],
                                 in1=zn_sb[:, 0:nnib], op=ALU.subtract)
                ve.tensor_tensor(out=wo_sb[bi][:, 0:nnib], in0=db_sb[:, 0:nnib],
                                 in1=s_sb[bi][:, 0:nnib], op=ALU.mult).then_inc(mV.h, 1)

        @bl.gpsimd
        def _(g):
            for i, (qsrc, zsrc, ssrc, dst, c, woff, noff, nw) in enumerate(slices):
                bi = i % 2
                nnib = nw * 8
                g.wait_ge(mV.h, dv_v[i])
                if dst is wd_own:
                    out_ap = bass.AP(nc.find_tensor("wd_own") if False else wd_own.tensor,
                                     c * 128,
                                     [[I, 128], [128 * I, 4], [1, 128]])
                    g.dma_start(out=out_ap,
                                in_=wo_sb[bi][:, 0:nnib].rearrange(
                                    "p (e n) -> p e n", n=128)
                                ).then_inc(sl_st[bi].h, 16)
                else:
                    g.dma_start(out=dst[c * 128:(c + 1) * 128, noff:noff + nnib],
                                in_=wo_sb[bi][:, 0:nnib]).then_inc(sl_st[bi].h, 16)
    ST_ALL = [sl_st[0].n, sl_st[1].n]
    CCV = cc_sl.n + 3
    with nc.Block() as bl:
        @bl.gpsimd
        def _(g):
            g.wait_ge(sl_st[0].h, ST_ALL[0])
            g.wait_ge(sl_st[1].h, ST_ALL[1])
            g.collective_compute(
                "AllGather", ALU.bypass,
                replica_groups=[[0, 1, 2, 3, 4, 5, 6, 7]],
                ins=[wg_own[:, :].opt()],
                outs=[wg_scr[:, :].opt()],
            ).then_inc(cc_sl.h, 1)
            g.collective_compute(
                "AllGather", ALU.bypass,
                replica_groups=[[0, 1, 2, 3, 4, 5, 6, 7]],
                ins=[wu_own[:, :].opt()],
                outs=[wu_scr[:, :].opt()],
            ).then_inc(cc_sl.h, 1)
            g.collective_compute(
                "AllGather", ALU.bypass,
                replica_groups=[[0, 1, 2, 3, 4, 5, 6, 7]],
                ins=[wd_own[:, :].opt()],
                outs=[wd_g[:, :].opt()],
            ).then_inc(cc_sl.h, 1)
    cc_sl.bump(3)
    deq.close()

    hard_barrier()
    ap_es = ExitStack()
    kcos = ap_es.enter_context(nc.sbuf_tensor("kcos", [64, S], F32))
    ksin = ap_es.enter_context(nc.sbuf_tensor("ksin", [64, S], F32))
    qcos = ap_es.enter_context(nc.sbuf_tensor("qcos", [64, TOWN], F32))
    qsin = ap_es.enter_context(nc.sbuf_tensor("qsin", [64, TOWN], F32))
    bqhl = ap_es.enter_context(nc.sbuf_tensor("bqhl", [2, E], BF16))
    bkhl = ap_es.enter_context(nc.sbuf_tensor("bkhl", [2, KV * D], BF16))
    bvhl = ap_es.enter_context(nc.sbuf_tensor("bvhl", [2, KV * D], BF16))
    m_r = ap_es.enter_context(nc.sbuf_tensor("m_r", [1, 512], F32))
    msq_r = ap_es.enter_context(nc.sbuf_tensor("msq_r", [1, 512], F32))
    mm_r = ap_es.enter_context(nc.sbuf_tensor("mm_r", [1, 512], F32))
    rstd_r = ap_es.enter_context(nc.sbuf_tensor("rstd_r", [1, 512], F32))
    negmr_r = ap_es.enter_context(nc.sbuf_tensor("negmr_r", [1, 512], F32))
    rstd_bt = ap_es.enter_context(nc.sbuf_tensor("rstd_bt", [128, 512], F32))
    negmr_bt = ap_es.enter_context(nc.sbuf_tensor("negmr_bt", [128, 512], F32))

    tb_es = ExitStack()
    t_pos = tb_es.enter_context(nc.sbuf_tensor("t_pos", [64, S], F32))
    t_ang = tb_es.enter_context(nc.sbuf_tensor("t_ang", [64, S], F32))
    t_k = tb_es.enter_context(nc.sbuf_tensor("t_k", [64, S], F32))
    t_ki = tb_es.enter_context(nc.sbuf_tensor("t_ki", [64, S], I32))
    t_pidx = tb_es.enter_context(nc.sbuf_tensor("t_pidx", [64, 1], F32))
    t_invf = tb_es.enter_context(nc.sbuf_tensor("t_invf", [64, 1], F32))
    t_pos0 = tb_es.enter_context(nc.sbuf_tensor("t_pos0", [64, 1], F32))
    t_brow = tb_es.enter_context(nc.sbuf_tensor("t_brow", [1, E], F32))
    t_bhi = tb_es.enter_context(nc.sbuf_tensor("t_bhi", [1, E], BF16))
    t_blo = tb_es.enter_context(nc.sbuf_tensor("t_blo", [1, E], BF16))

    IOTA_G = mG.n + 1
    POS0_V = ps_sl.n + 16
    INVF_S = mS.n + 1
    with nc.Block() as bl:
        @bl.gpsimd
        def _(g):
            g.iota(t_pos[:, :], [[1, S]], channel_multiplier=0,
                   allow_small_or_imprecise_dtypes=True)
            g.iota(t_pidx[:, :], [[0, 1]], channel_multiplier=1,
                   allow_small_or_imprecise_dtypes=True).then_inc(mG.h, 1)

        @bl.sync
        def _(sy):
            sy.dma_start(out=t_pos0[:, :], in_=pos0_d[:, :]).then_inc(ps_sl.h, 16)

        @bl.scalar
        def _(sc):
            sc.wait_ge(mG.h, IOTA_G)
            sc.activation(out=t_invf[:, :], in_=t_pidx[:, :], func=AF.Exp,
                          scale=float(-math.log(10000.0) / 64.0)).then_inc(mS.h, 1)
    mG.bump(1)
    ps_sl.bump(16)
    mS.bump(1)

    tables = [(ksin[:, :], S, 0.0, False), (kcos[:, :], S, math.pi / 2, False),
              (qsin[:, :], TOWN, 0.0, True), (qcos[:, :], TOWN, math.pi / 2, True)]
    tb_v = [mV.bump(1) for _ in range(4)]
    with nc.Block() as bl:
        @bl.vector
        def _(ve):
            ve.wait_ge(mS.h, INVF_S)
            ve.wait_ge(ps_sl.h, POS0_V)
            for ti, (dst, width, shift, ap0) in enumerate(tables):
                if ap0:
                    ve.tensor_scalar(out=t_ang[:, 0:width], in0=t_pos[:, 0:width],
                                     scalar1=t_pos0[:, :], scalar2=None, op0=ALU.add)
                    ve.tensor_scalar(out=t_ang[:, 0:width], in0=t_ang[:, 0:width],
                                     scalar1=t_invf[:, :], scalar2=float(shift),
                                     op0=ALU.mult, op1=ALU.add)
                else:
                    ve.tensor_scalar(out=t_ang[:, 0:width], in0=t_pos[:, 0:width],
                                     scalar1=t_invf[:, :], scalar2=float(shift),
                                     op0=ALU.mult, op1=ALU.add)
                ve.tensor_scalar(out=t_k[:, 0:width], in0=t_ang[:, 0:width],
                                 scalar1=float(1.0 / TWO_PI), scalar2=None, op0=ALU.mult)
                ve.tensor_copy(out=t_ki[:, 0:width], in_=t_k[:, 0:width])
                ve.tensor_copy(out=t_k[:, 0:width], in_=t_ki[:, 0:width])
                ve.scalar_tensor_tensor(out=dst, in0=t_k[:, 0:width],
                                        scalar=float(-TWO_PI), in1=t_ang[:, 0:width],
                                        op0=ALU.mult, op1=ALU.add).then_inc(mV.h, 1)

        @bl.scalar
        def _(sc):
            for ti, (dst, width, shift, ap0) in enumerate(tables):
                sc.wait_ge(mV.h, tb_v[ti])
                sc.activation(out=dst, in_=dst, func=AF.Sin).then_inc(mS.h, 1)
    mS.bump(4)

    bias_specs = ((bq_d, bqhl, E), (bk_d, bkhl, KV * D), (bv_d, bvhl, KV * D))
    brow_v = [br_sl.bump(16) for _ in range(3)]
    bsplit_v = [mV.bump(1) for _ in range(3)]
    bpl_v = [bs_sl.bump(32) for _ in range(3)]
    with nc.Block() as bl:
        @bl.sync
        def _(sy):
            for i, (src, dsttile, wdt) in enumerate(bias_specs):
                if i > 0:
                    sy.wait_ge(bs_sl.h, bpl_v[i - 1])
                sy.dma_start(out=t_brow[:, 0:wdt], in_=src[:, :]).then_inc(br_sl.h, 16)

        @bl.vector
        def _(ve):
            for i, (src, dsttile, wdt) in enumerate(bias_specs):
                ve.wait_ge(br_sl.h, brow_v[i])
                if i > 0:
                    ve.wait_ge(bs_sl.h, bpl_v[i - 1])
                ve.tensor_copy(out=t_bhi[:, 0:wdt], in_=t_brow[:, 0:wdt])
                ve.tensor_tensor(out=t_blo[:, 0:wdt], in0=t_brow[:, 0:wdt],
                                 in1=t_bhi[:, 0:wdt], op=ALU.subtract).then_inc(mV.h, 1)

        @bl.gpsimd
        def _(g):
            for i, (src, dsttile, wdt) in enumerate(bias_specs):
                g.wait_ge(mV.h, bsplit_v[i])
                g.dma_start(out=dsttile[0:1, 0:wdt], in_=t_bhi[:, 0:wdt]).then_inc(bs_sl.h, 16)
                g.dma_start(out=dsttile[1:2, 0:wdt], in_=t_blo[:, 0:wdt]).then_inc(bs_sl.h, 16)
    tb_es.close()
    hard_barrier()

    def ln_rows_block(stats_ms, norm_done=None):
        Va = mV.bump(1)
        Sa = mS.bump(1)
        Vb = mV.bump(1)
        Ga = la_sl.bump(32)
        Gb = lb_sl.bump(32)
        with nc.Block() as bl:
            @bl.vector
            def _(ve):
                ve.wait_ge(mT.h, stats_ms)
                ve.tensor_scalar(out=m_r[:, :], in0=pA[0:1, :], scalar1=float(1.0 / 4096.0),
                                 scalar2=None, op0=ALU.mult)
                ve.tensor_scalar(out=msq_r[:, :], in0=pB[0:1, :], scalar1=float(1.0 / 4096.0),
                                 scalar2=None, op0=ALU.mult)
                ve.tensor_tensor(out=mm_r[:, :], in0=m_r[:, :], in1=m_r[:, :], op=ALU.mult)
                ve.tensor_tensor(out=msq_r[:, :], in0=msq_r[:, :], in1=mm_r[:, :],
                                 op=ALU.subtract).then_inc(mV.h, 1)
                ve.wait_ge(mS.h, Sa)
                ve.reciprocal(out=rstd_r[:, :], in_=rstd_r[:, :])
                ve.scalar_tensor_tensor(out=negmr_r[:, :], in0=m_r[:, :], scalar=-1.0,
                                        in1=rstd_r[:, :], op0=ALU.mult,
                                        op1=ALU.mult).then_inc(mV.h, 1)

            @bl.scalar
            def _(sc):
                sc.wait_ge(mV.h, Va)
                sc.activation(out=rstd_r[:, :], in_=msq_r[:, :], func=AF.Sqrt,
                              bias=eps_col[0:1, :]).then_inc(mS.h, 1)

            @bl.gpsimd
            def _(g):
                g.wait_ge(mV.h, Vb)
                if norm_done is not None:
                    g.wait_ge(mV.h, norm_done)
                g.dma_start(out=row_scr[0:1, :], in_=rstd_r[:, :]).then_inc(la_sl.h, 16)
                g.dma_start(out=row_scr[1:2, :], in_=negmr_r[:, :]).then_inc(lb_sl.h, 16)
                g.wait_ge(la_sl.h, Ga - 16)
                g.dma_start(out=rstd_bt[:, :],
                            in_=row_scr[0:1, :].broadcast_to([128, 512])
                            ).then_inc(la_sl.h, 16)
                g.wait_ge(lb_sl.h, Gb - 16)
                g.dma_start(out=negmr_bt[:, :],
                            in_=row_scr[1:2, :].broadcast_to([128, 512])
                            ).then_inc(lb_sl.h, 16)
        return (Ga, Gb)

    def stats_wave(src_ap, col0, xbf_grp, x2_t, dma_guard=None, act_guard=None):
        xld = [xsl[c % 4].bump(16) for c in range(EC)]
        sqv = [mS.bump(1) for c in range(EC)]
        stv = [mT.bump(1) for c in range(EC)]
        with nc.Block() as bl:
            @bl.gpsimd
            def _(g):
                for c in range(EC):
                    if c == 0 and dma_guard is not None:
                        g.wait_ge(dma_guard[0], dma_guard[1])
                    if c >= 4:
                        g.wait_ge(mS.h, sqv[c - 4])
                    g.dma_start(out=xbf_grp[:, c * 512:(c + 1) * 512],
                                in_=src_ap[c * 128:(c + 1) * 128, col0:col0 + 512]
                                ).then_inc(xsl[c % 4].h, 16)

            @bl.scalar
            def _(sc):
                for c in range(EC):
                    sc.wait_ge(xsl[c % 4].h, xld[c])
                    if c == 0 and act_guard is not None:
                        sc.wait_ge(act_guard[0], act_guard[1])
                    if c >= 4:
                        sc.wait_ge(mT.h, stv[c - 4])
                    sc.activation(out=x2_t[:, (c % 4) * 512:(c % 4 + 1) * 512],
                                  in_=xbf_grp[:, c * 512:(c + 1) * 512],
                                  func=AF.Square).then_inc(mS.h, 1)

            @bl.tensor
            def _(te):
                for c in range(EC):
                    te.wait_ge(mS.h, sqv[c])
                    te.matmul(pA[0:1, :], ones_col[:, :],
                              xbf_grp[:, c * 512:(c + 1) * 512],
                              start=(c == 0), stop=(c == EC - 1), skip_group_check=True)
                    te.matmul(pB[0:1, :], ones_col[:, :],
                              x2_t[:, (c % 4) * 512:(c % 4 + 1) * 512],
                              start=(c == 0), stop=(c == EC - 1),
                              skip_group_check=True).then_inc(mT.h, 1)
        return stv[-1]

    def norm_wave(xbf_grp, dst, bcast_ms, tn_t):
        Ga, Gb = bcast_ms
        nv = [mV.bump(1) for c in range(EC)]
        with nc.Block() as bl:
            @bl.vector
            def _(ve):
                ve.wait_ge(la_sl.h, Ga)
                ve.wait_ge(lb_sl.h, Gb)
                for c in range(EC):
                    ve.tensor_tensor(out=tn_t[:, :],
                                     in0=xbf_grp[:, c * 512:(c + 1) * 512],
                                     in1=rstd_bt[:, :], op=ALU.mult)
                    ve.tensor_tensor(out=dst[:, c * 512:(c + 1) * 512], in0=tn_t[:, :],
                                     in1=negmr_bt[:, :], op=ALU.add).then_inc(mV.h, 1)
        return nv[-1]

    def rope_ops(ve, psrc, cos_t, sin_t, col0, dst, dcol, rp):
        ve.tensor_tensor(out=rp[0][:, :], in0=psrc[0:64, :],
                         in1=cos_t[:, col0:col0 + 512], op=ALU.mult)
        ve.tensor_tensor(out=rp[1][:, :], in0=psrc[64:128, :],
                         in1=sin_t[:, col0:col0 + 512], op=ALU.mult)
        ve.tensor_tensor(out=dst[0:64, dcol:dcol + 512], in0=rp[0][:, :],
                         in1=rp[1][:, :], op=ALU.subtract)
        ve.tensor_tensor(out=rp[0][:, :], in0=psrc[0:64, :],
                         in1=sin_t[:, col0:col0 + 512], op=ALU.mult)
        ve.tensor_tensor(out=rp[1][:, :], in0=psrc[64:128, :],
                         in1=cos_t[:, col0:col0 + 512], op=ALU.mult)
        ve.tensor_tensor(out=dst[64:128, dcol:dcol + 512], in0=rp[0][:, :],
                         in1=rp[1][:, :], op=ALU.add).then_inc(mV.h, 1)

    a1 = ExitStack()
    xln = a1.enter_context(nc.sbuf_tensor("xln", [128, EC * 512], BF16))
    xbf_grp = a1.enter_context(nc.sbuf_tensor("xbf_grp", [128, EC * 512], BF16))
    x2_rot = a1.enter_context(nc.sbuf_tensor("x2_rot", [128, 4 * 512], BF16))
    tnorm = a1.enter_context(nc.sbuf_tensor("tnorm", [128, 512], F32))
    wvb = a1.enter_context(nc.sbuf_tensor("wvb", [128, EC * 512], BF16))
    wkb = a1.enter_context(nc.sbuf_tensor("wkb", [128, 2 * EC * 128], BF16))
    kst = a1.enter_context(nc.sbuf_tensor("kst", [128, 8 * 512], BF16))
    vst = a1.enter_context(nc.sbuf_tensor("vst", [128, 4 * 1024], BF16))
    rp1 = [a1.enter_context(nc.sbuf_tensor(f"rp1_{i}", [64, 512], F32)) for i in range(2)]

    ks_done = 0
    vs_done = 0
    prev_norm = None
    prev_stats = None
    prev_vmm_last = None
    hard_barrier()
    for g_i in range(NG):
        col0 = g_i * 512
        stats_ms = stats_wave(
            xT, col0, xbf_grp, x2_rot,
            dma_guard=(mV.h, prev_norm) if prev_norm is not None else None,
            act_guard=(mT.h, prev_stats) if prev_stats is not None else None)
        bcast_ms = ln_rows_block(stats_ms, norm_done=prev_norm)
        norm_ms = norm_wave(xbf_grp, xln, bcast_ms, tnorm)
        prev_norm = norm_ms
        prev_stats = stats_ms

        prev_ks = ks_done
        wkld = [wk_sl[h % 2].bump(16) for h in range(KV)]
        kmm = [mT.bump(1) for h in range(KV)]
        krp = [mV.bump(1) for h in range(KV)]
        with nc.Block() as bl:
            @bl.sync
            def _(sy):
                for h in range(KV):
                    if h == 0 and g_i == 0:
                        sy.wait_ge(wsc_sl.h, WSC)
                    if h >= 2:
                        sy.wait_ge(mT.h, kmm[h - 2])
                    sy.dma_start(
                        out=wkb[:, (h % 2) * EC * 128:(h % 2 + 1) * EC * 128].rearrange(
                            "p (c n) -> p c n", n=128),
                        in_=wkb_scr[:, h * 128:(h + 1) * 128].rearrange(
                            "(c p) n -> p c n", p=128)).then_inc(wk_sl[h % 2].h, 16)

            @bl.tensor
            def _(te):
                te.wait_ge(mV.h, norm_ms)
                for h in range(KV):
                    te.wait_ge(wk_sl[h % 2].h, wkld[h])
                    if h >= 2:
                        te.wait_ge(mV.h, krp[h - 2])
                    pk = pE0 if h % 2 == 0 else pE1
                    base = (h % 2) * EC * 128
                    for c in range(EC):
                        te.matmul(pk[:, :], wkb[:, base + c * 128:base + (c + 1) * 128],
                                  xln[:, c * 512:(c + 1) * 512],
                                  start=(c == 0), stop=False, skip_group_check=True)
                    te.matmul(pk[:, :], bkhl[:, h * 128:(h + 1) * 128], onesN[:, :],
                              start=False, stop=True,
                              skip_group_check=True).then_inc(mT.h, 1)

            @bl.vector
            def _(ve):
                for h in range(KV):
                    ve.wait_ge(mT.h, kmm[h])
                    if h == 0 and prev_ks:
                        ve.wait_ge(ks_sl.h, prev_ks)
                    pk = pE0 if h % 2 == 0 else pE1
                    rope_ops(ve, pk, kcos, ksin, col0, kst, h * 512, rp1)

            @bl.sync
            def _(sy):
                for h in range(KV):
                    sy.wait_ge(mV.h, krp[h])
                    sy.dma_start(out=k_scr[h * 128:(h + 1) * 128, col0:col0 + 512],
                                 in_=kst[:, h * 512:(h + 1) * 512]).then_inc(ks_sl.h, 16)
        ks_done = ks_sl.bump(KV * 16)
        ROPES_V = mV.n

        prev_vs = vs_done
        for half in range(2):
            PRE_S = mS.n
            wv_guard = prev_vmm_last
            wvld = wv_sl.bump(16)
            vmm = [mT.bump(1) for tt in range(4)]
            vev = [mS.bump(1) for tt in range(4)]
            prev_vmm_last = vmm[-1]
            with nc.Block() as bl:
                @bl.sync
                def _(sy):
                    if wv_guard is not None:
                        sy.wait_ge(mT.h, wv_guard)
                    sy.dma_start(
                        out=wvb[:, :].rearrange("p (c n) -> p c n", n=512),
                        in_=wvb_scr[:, half * 512:(half + 1) * 512].rearrange(
                            "(c p) n -> p c n", p=128)).then_inc(wv_sl.h, 16)

                @bl.tensor
                def _(te):
                    te.wait_ge(wv_sl.h, wvld)
                    te.wait_ge(mS.h, PRE_S)
                    if half == 0:
                        te.wait_ge(mV.h, ROPES_V)
                    for tt in range(4):
                        pv = pF0 if tt % 2 == 0 else pE0
                        if tt >= 2:
                            te.wait_ge(mS.h, vev[tt - 2])
                        for c in range(EC):
                            te.matmul(pv[:, :],
                                      xln[:, c * 512 + tt * 128:c * 512 + tt * 128 + 128],
                                      wvb[:, c * 512:(c + 1) * 512],
                                      start=(c == 0), stop=False, skip_group_check=True)
                        te.matmul(pv[:, :], ones2[:, :],
                                  bvhl[:, half * 512:(half + 1) * 512],
                                  start=False, stop=True,
                                  skip_group_check=True).then_inc(mT.h, 1)

                @bl.scalar
                def _(sc):
                    for tt in range(4):
                        pv = pF0 if tt % 2 == 0 else pE0
                        sc.wait_ge(mT.h, vmm[tt])
                        if tt == 0 and half == 0 and prev_vs:
                            sc.wait_ge(vs_sl.h, prev_vs)
                        sc.activation(out=vst[:, tt * 1024 + half * 512:
                                              tt * 1024 + half * 512 + 512],
                                      in_=pv[:, :], func=AF.Copy).then_inc(mS.h, 1)
        VS_ALL = mS.n
        with nc.Block() as bl:
            @bl.sync
            def _(sy):
                sy.wait_ge(mS.h, VS_ALL)
                for tt in range(4):
                    sy.dma_start(out=v_scr[col0 + tt * 128:col0 + tt * 128 + 128, :],
                                 in_=vst[:, tt * 1024:(tt + 1) * 1024]).then_inc(vs_sl.h, 16)
        vs_done = vs_sl.bump(4 * 16)
    A1_KS = ks_sl.n
    A1_VS = vs_sl.n
    a1.close()

    a2p = ExitStack()
    q_tile = a2p.enter_context(nc.sbuf_tensor("q_tile", [128, EC * 512], BF16))
    for qg in range(NQ):
        qcol = qg * 512
        hard_barrier()
        p1 = ExitStack()
        xlnq = p1.enter_context(nc.sbuf_tensor(f"xlnq{qg}", [128, EC * 512], BF16))
        wqb = p1.enter_context(nc.sbuf_tensor(f"wqb{qg}", [128, 2 * EC * 128], BF16))
        xbf2 = p1.enter_context(nc.sbuf_tensor(f"xbf2{qg}", [128, EC * 512], BF16))
        x22 = p1.enter_context(nc.sbuf_tensor(f"x22{qg}", [128, 4 * 512], BF16))
        tn2 = p1.enter_context(nc.sbuf_tensor(f"tn2{qg}", [128, 512], F32))
        rp2 = [p1.enter_context(nc.sbuf_tensor(f"rp2_{qg}_{i}", [64, 512], F32))
               for i in range(2)]

        stats2 = stats_wave(xq, qcol, xbf2, x22)
        bcast2 = ln_rows_block(stats2, norm_done=mV.n)
        normq = norm_wave(xbf2, xlnq, bcast2, tn2)

        qwld = [qw_sl[m % 2].bump(16) for m in range(EC)]
        qmm = [mT.bump(1) for m in range(EC)]
        qrp = [mV.bump(1) for m in range(EC)]
        with nc.Block() as bl:
            @bl.sync
            def _(sy):
                for m in range(EC):
                    if m >= 2:
                        sy.wait_ge(mT.h, qmm[m - 2])
                    sy.dma_start(
                        out=wqb[:, (m % 2) * EC * 128:(m % 2 + 1) * EC * 128].rearrange(
                            "p (c n) -> p c n", n=128),
                        in_=wqb_scr[:, m * 128:(m + 1) * 128].rearrange(
                            "(c p) n -> p c n", p=128)).then_inc(qw_sl[m % 2].h, 16)

            @bl.tensor
            def _(te):
                te.wait_ge(mV.h, normq)
                for m in range(EC):
                    te.wait_ge(qw_sl[m % 2].h, qwld[m])
                    if m >= 2:
                        te.wait_ge(mV.h, qrp[m - 2])
                    pq = pE0 if m % 2 == 0 else pE1
                    base = (m % 2) * EC * 128
                    for c in range(EC):
                        te.matmul(pq[:, :], wqb[:, base + c * 128:base + (c + 1) * 128],
                                  xlnq[:, c * 512:(c + 1) * 512],
                                  start=(c == 0), stop=False, skip_group_check=True)
                    te.matmul(pq[:, :], bqhl[:, m * 128:(m + 1) * 128], onesN[:, :],
                              start=False, stop=True,
                              skip_group_check=True).then_inc(mT.h, 1)

            @bl.vector
            def _(ve):
                for m in range(EC):
                    ve.wait_ge(mT.h, qmm[m])
                    pq = pE0 if m % 2 == 0 else pE1
                    rope_ops(ve, pq, qcos, qsin, qcol, q_tile, m * 512, rp2)
        p1.close()
        hard_barrier()

        p2 = ExitStack()
        khb = p2.enter_context(nc.sbuf_tensor(f"khb{qg}", [128, 2 * S], BF16))
        vhb = p2.enter_context(nc.sbuf_tensor(f"vhb{qg}", [128, 2 * EC * 128], BF16))
        expb = p2.enter_context(nc.sbuf_tensor(f"expb{qg}", [128, 4 * 512], BF16))
        srecip = p2.enter_context(nc.sbuf_tensor(f"srecip{qg}", [128, 512], F32))
        attn_t = p2.enter_context(nc.sbuf_tensor(f"attn_t{qg}", [128, 512], F32))
        yst = p2.enter_context(nc.sbuf_tensor(f"yst{qg}", [128, 2 * 512], F32))
        ybf_grp = p2.enter_context(nc.sbuf_tensor(f"ybfg{qg}", [128, EC * 512], BF16))
        y22_rot = p2.enter_context(nc.sbuf_tensor(f"y22r{qg}", [128, 4 * 512], BF16))
        xqr = p2.enter_context(nc.sbuf_tensor(f"xqr{qg}", [128, 2 * 512], F32))
        ln2st = p2.enter_context(nc.sbuf_tensor(f"ln2st{qg}", [128, 2 * 512], BF16))
        rrow = p2.enter_context(nc.sbuf_tensor(f"rrow{qg}", [1, 512], F32))
        rrow_b = p2.enter_context(nc.sbuf_tensor(f"rrowb{qg}", [1, 512], F32))
        srecip_b = p2.enter_context(nc.sbuf_tensor(f"srecipb{qg}", [128, 512], F32))

        evac_ms = {}
        ystore_v = {}
        h_mm_done = {}
        last_exv_a = None
        last_exv_b = None
        prev_Vr_a = None
        prev_Vr_b = None
        A2S_T = mT.n
        A2S_V = mV.n
        QRP_LAST = qrp[-1]
        for h in range(KV):
            kvld = kv_sl[h % 2].bump(32)
            with nc.Block() as bl:
                @bl.sync
                def _(sy):
                    if h == 0:
                        sy.wait_ge(mT.h, A2S_T)
                        sy.wait_ge(mV.h, A2S_V)
                        if qg == 0:
                            sy.wait_ge(ks_sl.h, A1_KS)
                            sy.wait_ge(vs_sl.h, A1_VS)
                    if h >= 2:
                        sy.wait_ge(mT.h, h_mm_done[h - 2])
                    sy.dma_start(out=khb[:, (h % 2) * S:(h % 2 + 1) * S],
                                 in_=k_scr[h * 128:(h + 1) * 128, :]
                                 ).then_inc(kv_sl[h % 2].h, 16)
                    sy.dma_start(
                        out=vhb[:, (h % 2) * EC * 128:(h % 2 + 1) * EC * 128].rearrange(
                            "p (kc n) -> p kc n", n=128),
                        in_=v_scr[:, h * 128:(h + 1) * 128].rearrange(
                            "(kc kp) n -> kp kc n", kp=128)).then_inc(kv_sl[h % 2].h, 16)
            kb = (h % 2) * S
            vb = (h % 2) * EC * 128

            for sub in (0, 2):
                qa = h * 4 + sub
                qb = qa + 1
                SCA = [pE0, pE1]
                SCB2 = [pA, pB]
                S0 = mS.n
                exv_a = [S0 + 2 * kc + 1 for kc in range(32)]
                exv_b = [S0 + 2 * kc + 2 for kc in range(32)]
                scv_a = {}
                scv_b = {}
                pvv_a = {}
                pvv_b = {}
                t_ = mT.n
                for kc in range(2):
                    t_ += 1
                    scv_a[kc] = t_
                    t_ += 1
                    scv_b[kc] = t_
                for kc in range(32):
                    t_ += 1
                    pvv_a[kc] = t_
                    t_ += 1
                    pvv_b[kc] = t_
                    if kc + 2 < 32:
                        t_ += 1
                        scv_a[kc + 2] = t_
                        t_ += 1
                        scv_b[kc + 2] = t_
                mT.n = t_

                with nc.Block() as bl:
                    @bl.tensor
                    def _(te):
                        te.wait_ge(kv_sl[h % 2].h, kvld)
                        if qa == 0:
                            te.wait_ge(mV.h, QRP_LAST)

                        def sc_mm(stream, kc):
                            if stream == 0:
                                psc, exv, lexv, qh_ = SCA[kc % 2], exv_a, last_exv_a, qa
                            else:
                                psc, exv, lexv, qh_ = SCB2[kc % 2], exv_b, last_exv_b, qb
                            if kc >= 2:
                                te.wait_ge(mS.h, exv[kc - 2])
                            elif lexv is not None:
                                te.wait_ge(mS.h, lexv[30 + kc])
                            te.matmul(psc[:, :],
                                      khb[:, kb + kc * 128:kb + (kc + 1) * 128],
                                      q_tile[:, qh_ * 512:(qh_ + 1) * 512],
                                      start=True, stop=True,
                                      skip_group_check=True).then_inc(mT.h, 1)
                        for kc in range(2):
                            sc_mm(0, kc)
                            sc_mm(1, kc)
                        for kc in range(32):
                            te.wait_ge(mS.h, exv_a[kc])
                            if kc == 0:
                                if prev_Vr_a is not None:
                                    te.wait_ge(mV.h, prev_Vr_a)
                                if qa >= 4:
                                    te.wait_ge(mV.h, evac_ms[qa - 2])
                            esl = slice((kc % 2) * 512, (kc % 2 + 1) * 512)
                            te.matmul(pSa[:, :], ones_col[:, :], expb[:, esl],
                                      start=(kc == 0), stop=(kc == 31),
                                      skip_group_check=True)
                            te.matmul(pF0[:, :],
                                      vhb[:, vb + kc * 128:vb + (kc + 1) * 128],
                                      expb[:, esl], start=(kc == 0), stop=(kc == 31),
                                      skip_group_check=True).then_inc(mT.h, 1)
                            te.wait_ge(mS.h, exv_b[kc])
                            if kc == 0:
                                if prev_Vr_b is not None:
                                    te.wait_ge(mV.h, prev_Vr_b)
                                if qb >= 4:
                                    te.wait_ge(mV.h, evac_ms[qb - 2])
                            esl2 = slice((2 + kc % 2) * 512, (2 + kc % 2 + 1) * 512)
                            te.matmul(pC[0:1, :], ones_col[:, :], expb[:, esl2],
                                      start=(kc == 0), stop=(kc == 31),
                                      skip_group_check=True)
                            te.matmul(pD[:, :],
                                      vhb[:, vb + kc * 128:vb + (kc + 1) * 128],
                                      expb[:, esl2], start=(kc == 0), stop=(kc == 31),
                                      skip_group_check=True).then_inc(mT.h, 1)
                            if kc + 2 < 32:
                                sc_mm(0, kc + 2)
                                sc_mm(1, kc + 2)

                    @bl.scalar
                    def _(sc):
                        for kc in range(32):
                            sc.wait_ge(mT.h, scv_a[kc])
                            if kc >= 2:
                                sc.wait_ge(mT.h, pvv_a[kc - 2])
                            sc.activation(out=expb[:, (kc % 2) * 512:(kc % 2 + 1) * 512],
                                          in_=SCA[kc % 2][:, :], func=AF.Exp,
                                          scale=float(1.0 / math.sqrt(D))
                                          ).then_inc(mS.h, 1)
                            sc.wait_ge(mT.h, scv_b[kc])
                            if kc >= 2:
                                sc.wait_ge(mT.h, pvv_b[kc - 2])
                            sc.activation(out=expb[:, (2 + kc % 2) * 512:(2 + kc % 2 + 1) * 512],
                                          in_=SCB2[kc % 2][:, :], func=AF.Exp,
                                          scale=float(1.0 / math.sqrt(D))
                                          ).then_inc(mS.h, 1)
                mS.bump(64)

                Vra = mV.bump(1)
                Vrb = mV.bump(1)
                Ga_ = rh_sl.bump(32)
                Gb_ = rh2_sl.bump(32)
                Lqa = xq_sl[0].bump(16)
                Lqb = xq_sl[1].bump(16)
                Vea = mV.bump(1)
                Veb = mV.bump(1)
                Qsa = ys_sl[0].bump(16)
                Qsb = ys_sl[1].bump(16)
                with nc.Block() as bl:
                    @bl.vector
                    def _(ve):
                        ve.wait_ge(mT.h, pvv_a[31])
                        ve.reciprocal(out=rrow[:, :], in_=pSa[:, :]).then_inc(mV.h, 1)
                        ve.wait_ge(mT.h, pvv_b[31])
                        ve.reciprocal(out=rrow_b[:, :], in_=pC[0:1, :]).then_inc(mV.h, 1)
                        ve.wait_ge(rh_sl.h, Ga_)
                        ve.wait_ge(xq_sl[0].h, Lqa)
                        if qa >= 4:
                            ve.wait_ge(ys_sl[0].h, ystore_v[qa - 2])
                        ve.tensor_tensor(out=attn_t[:, :], in0=pF0[:, :],
                                         in1=srecip[:, :], op=ALU.mult)
                        ve.tensor_tensor(out=yst[:, 0:512], in0=attn_t[:, :],
                                         in1=xqr[:, 0:512],
                                         op=ALU.add).then_inc(mV.h, 1)
                        ve.wait_ge(rh2_sl.h, Gb_)
                        ve.wait_ge(xq_sl[1].h, Lqb)
                        if qb >= 4:
                            ve.wait_ge(ys_sl[1].h, ystore_v[qb - 2])
                        ve.tensor_tensor(out=attn_t[:, :], in0=pD[:, :],
                                         in1=srecip_b[:, :], op=ALU.mult)
                        ve.tensor_tensor(out=yst[:, 512:1024], in0=attn_t[:, :],
                                         in1=xqr[:, 512:1024],
                                         op=ALU.add).then_inc(mV.h, 1)

                    @bl.gpsimd
                    def _(g):
                        g.wait_ge(mV.h, Vra)
                        g.dma_start(out=row_scr[2:3, :], in_=rrow[:, :]).then_inc(rh_sl.h, 16)
                        g.wait_ge(mV.h, Vrb)
                        g.dma_start(out=row_scr[3:4, :], in_=rrow_b[:, :]).then_inc(rh2_sl.h, 16)
                        g.wait_ge(rh_sl.h, Ga_ - 16)
                        g.dma_start(out=srecip[:, :],
                                    in_=row_scr[2:3, :].broadcast_to([128, 512])
                                    ).then_inc(rh_sl.h, 16)
                        g.wait_ge(rh2_sl.h, Gb_ - 16)
                        g.dma_start(out=srecip_b[:, :],
                                    in_=row_scr[3:4, :].broadcast_to([128, 512])
                                    ).then_inc(rh2_sl.h, 16)

                    @bl.sync
                    def _(sy):
                        if qa >= 4:
                            sy.wait_ge(mV.h, evac_ms[qa - 2])
                        sy.dma_start(out=xqr[:, 0:512],
                                     in_=xq[qa * 128:(qa + 1) * 128, qcol:qcol + 512]
                                     ).then_inc(xq_sl[0].h, 16)
                        if qb >= 4:
                            sy.wait_ge(mV.h, evac_ms[qb - 2])
                        sy.dma_start(out=xqr[:, 512:1024],
                                     in_=xq[qb * 128:(qb + 1) * 128, qcol:qcol + 512]
                                     ).then_inc(xq_sl[1].h, 16)
                        sy.wait_ge(mV.h, Vea)
                        sy.dma_start(out=y_scr[qa * 128:(qa + 1) * 128, qcol:qcol + 512],
                                     in_=yst[:, 0:512]).then_inc(ys_sl[0].h, 16)
                        sy.wait_ge(mV.h, Veb)
                        sy.dma_start(out=y_scr[qb * 128:(qb + 1) * 128, qcol:qcol + 512],
                                     in_=yst[:, 512:1024]).then_inc(ys_sl[1].h, 16)
                evac_ms[qa] = Vea
                evac_ms[qb] = Veb
                ystore_v[qa] = Qsa
                ystore_v[qb] = Qsb
                last_exv_a = exv_a
                last_exv_b = exv_b
                prev_Vr_a = Vra
                prev_Vr_b = Vrb
            h_mm_done[h] = mT.n

        YALL0 = [ys_sl[0].n, ys_sl[1].n]
        with nc.Block() as bl:
            @bl.gpsimd
            def _(g):
                g.wait_ge(ys_sl[0].h, YALL0[0])
                g.wait_ge(ys_sl[1].h, YALL0[1])
                g.memset(barrier_tile[:, :], 0.0).then_inc(mG.h, 1)
        YB = mG.bump(1)
        with nc.Block() as bl:
            @bl.gpsimd
            def _(g):
                g.wait_ge(mG.h, YB)
        stats3 = stats_wave(y_scr, qcol, ybf_grp, y22_rot)
        bcast3 = ln_rows_block(stats3, norm_done=mV.n)

        l2v = [mV.bump(1) for c in range(EC)]
        l2s = [ls_sl[c % 2].bump(16) for c in range(EC)]
        with nc.Block() as bl:
            @bl.vector
            def _(ve):
                ve.wait_ge(la_sl.h, bcast3[0])
                ve.wait_ge(lb_sl.h, bcast3[1])
                for c in range(EC):
                    if c >= 2:
                        ve.wait_ge(ls_sl[c % 2].h, l2s[c - 2])
                    ve.tensor_tensor(out=attn_t[:, :],
                                     in0=ybf_grp[:, c * 512:(c + 1) * 512],
                                     in1=rstd_bt[:, :], op=ALU.mult)
                    ve.tensor_tensor(out=ln2st[:, (c % 2) * 512:(c % 2 + 1) * 512],
                                     in0=attn_t[:, :], in1=negmr_bt[:, :],
                                     op=ALU.add).then_inc(mV.h, 1)

            @bl.gpsimd
            def _(g):
                for c in range(EC):
                    g.wait_ge(mV.h, l2v[c])
                    g.dma_start(out=ln2_scr[c * 128:(c + 1) * 128, qcol:qcol + 512],
                                in_=ln2st[:, (c % 2) * 512:(c % 2 + 1) * 512]
                                ).then_inc(ls_sl[c % 2].h, 16)
        p2.close()
    a2p.close()
    ap_es.close()

    hard_barrier()
    ffn = ExitStack()
    ln2b = ffn.enter_context(nc.sbuf_tensor("ln2b", [128, EC * 512], BF16))
    h_full = ffn.enter_context(nc.sbuf_tensor("h_full", [128, IC * 512], BF16))
    wgb = ffn.enter_context(nc.sbuf_tensor("wgb", [128, 2 * EC * 128], BF16))
    wub = ffn.enter_context(nc.sbuf_tensor("wub", [128, 2 * EC * 128], BF16))
    silu_t = ffn.enter_context(nc.sbuf_tensor("silu_t", [128, 512], F32))
    wdb = ffn.enter_context(nc.sbuf_tensor("wdb", [128, 2 * IC * 128], BF16))
    yres = ffn.enter_context(nc.sbuf_tensor("yres", [128, 2 * 512], F32))
    ost = ffn.enter_context(nc.sbuf_tensor("ost", [128, 2 * 512], F32))

    for hf in range(2):
        tcol = hf * 512
        PRE_T = mT.n
        l2bv = l2_sl.bump(16)
        with nc.Block() as bl:
            @bl.sync
            def _(sy):
                sy.wait_ge(mT.h, PRE_T)
                if hf == 0:
                    sy.wait_ge(cc_sl.h, CCV)
                sy.dma_start(out=ln2b[:, :].rearrange("p (c t) -> p c t", t=512),
                             in_=ln2_scr[:, tcol:tcol + 512].rearrange(
                                 "(c p) t -> p c t", p=128)).then_inc(l2_sl.h, 16)

        wgld = [wg_sl[p % 2].bump(16) for p in range(IC)]
        wuld = [wu_sl[p % 2].bump(16) for p in range(IC)]
        gmm = [mT.bump(2) for p in range(IC)]
        sqv = [mS.bump(1) for p in range(IC)]
        hv = [mV.bump(1) for p in range(IC)]
        with nc.Block() as bl:
            @bl.gpsimd
            def _(g):
                for p in range(IC):
                    if p == 0 and hf == 1:
                        g.wait_ge(mT.h, PRE_T)
                    if p >= 2:
                        g.wait_ge(mT.h, gmm[p - 2])
                    g.dma_start(
                        out=wgb[:, (p % 2) * EC * 128:(p % 2 + 1) * EC * 128].rearrange(
                            "p (c n) -> p c n", n=128),
                        in_=wg_scr[:, p * 128:(p + 1) * 128].rearrange(
                            "(c kp) n -> kp c n", kp=128)).then_inc(wg_sl[p % 2].h, 16)
                    g.dma_start(
                        out=wub[:, (p % 2) * EC * 128:(p % 2 + 1) * EC * 128].rearrange(
                            "p (c n) -> p c n", n=128),
                        in_=wu_scr[:, p * 128:(p + 1) * 128].rearrange(
                            "(c kp) n -> kp c n", kp=128)).then_inc(wu_sl[p % 2].h, 16)

            @bl.tensor
            def _(te):
                te.wait_ge(l2_sl.h, l2bv)
                for p in range(IC):
                    te.wait_ge(wg_sl[p % 2].h, wgld[p])
                    te.wait_ge(wu_sl[p % 2].h, wuld[p])
                    if p >= 2:
                        te.wait_ge(mS.h, sqv[p - 2])
                        te.wait_ge(mV.h, hv[p - 2])
                    pg = pE0 if p % 2 == 0 else pE1
                    pu = pC if p % 2 == 0 else pD
                    base = (p % 2) * EC * 128
                    for c in range(EC):
                        mm = te.matmul(pg[:, :],
                                       wgb[:, base + c * 128:base + (c + 1) * 128],
                                       ln2b[:, c * 512:(c + 1) * 512],
                                       start=(c == 0), stop=(c == EC - 1),
                                       skip_group_check=True)
                        if c == EC - 1:
                            mm.then_inc(mT.h, 1)
                    for c in range(EC):
                        mm = te.matmul(pu[:, :],
                                       wub[:, base + c * 128:base + (c + 1) * 128],
                                       ln2b[:, c * 512:(c + 1) * 512],
                                       start=(c == 0), stop=(c == EC - 1),
                                       skip_group_check=True)
                        if c == EC - 1:
                            mm.then_inc(mT.h, 1)

            @bl.scalar
            def _(sc):
                for p in range(IC):
                    pg = pE0 if p % 2 == 0 else pE1
                    sc.wait_ge(mT.h, gmm[p] - 1)
                    sc.activation(out=silu_t[:, :], in_=pg[:, :],
                                  func=AF.Silu).then_inc(mS.h, 1)

            @bl.vector
            def _(ve):
                for p in range(IC):
                    pu = pC if p % 2 == 0 else pD
                    ve.wait_ge(mS.h, sqv[p])
                    ve.wait_ge(mT.h, gmm[p])
                    ve.tensor_tensor(out=h_full[:, p * 512:(p + 1) * 512],
                                     in0=silu_t[:, :], in1=pu[:, :],
                                     op=ALU.mult).then_inc(mV.h, 1)

        wdld = [wd_sl[e % 2].bump(16) for e in range(EC)]
        dmm = [mT.bump(1) for e in range(EC)]
        yld = [yr_sl[e % 2].bump(32) for e in range(EC)]
        oev = [mV.bump(1) for e in range(EC)]
        osv = [os_sl[e % 2].bump(16) for e in range(EC)]
        HV_LAST = hv[-1]
        with nc.Block() as bl:
            @bl.gpsimd
            def _(g):
                g.wait_ge(mV.h, HV_LAST)
                for e in range(EC):
                    if e >= 2:
                        g.wait_ge(mT.h, dmm[e - 2])
                    o0 = (e % 2) * IC * 128
                    g.dma_start(
                        out=wdb[:, o0:o0 + IC * 128],
                        in_=wd_g[e * 128:(e + 1) * 128, :]).then_inc(wd_sl[e % 2].h, 16)

            @bl.tensor
            def _(te):
                for e in range(EC):
                    te.wait_ge(wd_sl[e % 2].h, wdld[e])
                    if e >= 2:
                        te.wait_ge(mV.h, oev[e - 2])
                    pd_ = pE0 if e % 2 == 0 else pE1
                    base = (e % 2) * IC * 128
                    for ic in range(IC):
                        mm = te.matmul(pd_[:, :],
                                       wdb[:, base + ic * 128:base + (ic + 1) * 128],
                                       h_full[:, ic * 512:(ic + 1) * 512],
                                       start=(ic == 0), stop=(ic == IC - 1),
                                       skip_group_check=True)
                        if ic == IC - 1:
                            mm.then_inc(mT.h, 1)

            @bl.sync
            def _(sy):
                for e in range(EC):
                    if e >= 2:
                        sy.wait_ge(mV.h, oev[e - 2])
                    for c_ in range(2):
                        base_el = (e * 128 + c_) * TOWN + tcol
                        in_ap = bass.AP(y_scr_t, base_el,
                                        [[2 * TOWN, 4], [8 * TOWN, 16], [1, 512]])
                        sy.dma_start(out=yres[c_ * 64:(c_ + 1) * 64,
                                              (e % 2) * 512:(e % 2 + 1) * 512],
                                     in_=in_ap).then_inc(yr_sl[e % 2].h, 16)
                    sy.wait_ge(mV.h, oev[e])
                    sy.dma_start(out=out_d[e * 128:(e + 1) * 128, tcol:tcol + 512],
                                 in_=ost[:, (e % 2) * 512:(e % 2 + 1) * 512]
                                 ).then_inc(os_sl[e % 2].h, 16)

            @bl.vector
            def _(ve):
                for e in range(EC):
                    pd_ = pE0 if e % 2 == 0 else pE1
                    ve.wait_ge(mT.h, dmm[e])
                    ve.wait_ge(yr_sl[e % 2].h, yld[e])
                    if e >= 2:
                        ve.wait_ge(os_sl[e % 2].h, osv[e - 2])
                    ve.tensor_tensor(out=ost[:, (e % 2) * 512:(e % 2 + 1) * 512],
                                     in0=pd_[:, :],
                                     in1=yres[:, (e % 2) * 512:(e % 2 + 1) * 512],
                                     op=ALU.add).then_inc(mV.h, 1)
    ffn.close()

    if dbg:
        dmp = [
            (nc.dram_tensor("dbg_k", [KV * D, S], BF16, kind="ExternalOutput").ap(), k_scr),
            (nc.dram_tensor("dbg_v", [S, KV * D], BF16, kind="ExternalOutput").ap(), v_scr),
            (nc.dram_tensor("dbg_y", [E, TOWN], F32, kind="ExternalOutput").ap(), y_scr),
            (nc.dram_tensor("dbg_ln2", [E, TOWN], BF16, kind="ExternalOutput").ap(), ln2_scr),
            (nc.dram_tensor("dbg_wg", [E, 512], BF16, kind="ExternalOutput").ap(), wg_scr[:, 0:512]),
            (nc.dram_tensor("dbg_wd", [512, I], BF16, kind="ExternalOutput").ap(), wd_g[0:512, :]),
        ]
        snap = [(s_.h, s_.n) for s_ in ALLSEMS]
        dv = ps_sl.n
        with nc.Block() as bl:
            @bl.sync
            def _(sy):
                for hh, vv in snap:
                    sy.wait_ge(hh, vv)
                for dst, srcp in dmp:
                    sy.dma_start(out=dst, in_=srcp).then_inc(ps_sl.h, 16)
                sy.wait_ge(ps_sl.h, dv + len(dmp) * 16)
        ps_sl.bump(len(dmp) * 16)
    else:
        snap = [(s_.h, s_.n) for s_ in ALLSEMS] + [(cc_sl.h, cc_sl.n)]
        with nc.Block() as bl:
            @bl.sync
            def _(sy):
                for hh, vv in snap:
                    sy.wait_ge(hh, vv)

    pes.close()
    es.close()
    return nc


_NC_CACHE = {}


def kernel(**inputs):
    P = _perm128()
    permI = np.concatenate([q * 128 + P for q in range(IC)])
    permE = np.concatenate([q * 128 + P for q in range(EC)])

    x = np.asarray(inputs["L0_attn_x"], np.float32)
    wq = np.ascontiguousarray(np.asarray(inputs["L0_attn_wq"], np.float32))
    wk = np.ascontiguousarray(np.asarray(inputs["L0_attn_wk"], np.float32))
    wv = np.ascontiguousarray(np.asarray(inputs["L0_attn_wv"], np.float32))
    bq = np.asarray(inputs["L0_attn_bq"], np.float32).reshape(1, E)
    bk = np.asarray(inputs["L0_attn_bk"], np.float32).reshape(1, KV * D)
    bv = np.asarray(inputs["L0_attn_bv"], np.float32).reshape(1, KV * D)
    gq = np.asarray(inputs["L0_ffn_gate_qweight"], np.int32)
    uq = np.asarray(inputs["L0_ffn_up_qweight"], np.int32)
    dq = np.asarray(inputs["L0_ffn_down_qweight"], np.int32)[permI]
    gs = np.asarray(inputs["L0_ffn_gate_scales"], np.float32)[:, permI]
    us = np.asarray(inputs["L0_ffn_up_scales"], np.float32)[:, permI]
    ds = np.asarray(inputs["L0_ffn_down_scales"], np.float32)[:, permE]
    gz = np.asarray(inputs["L0_ffn_gate_qzeros"], np.int32)
    uz = np.asarray(inputs["L0_ffn_up_qzeros"], np.int32)
    dz = np.asarray(inputs["L0_ffn_down_qzeros"], np.int32)

    xT = [np.ascontiguousarray(x[b].T) for b in range(B)]

    if "nc" not in _NC_CACHE:
        _NC_CACHE["nc"] = build_nc()
    nc = _NC_CACHE["nc"]

    in_maps = []
    for c in range(8):
        b, t0 = c // 4, (c % 4) * TOWN
        in_maps.append({
            "xT": xT[b],
            "xq": np.ascontiguousarray(xT[b][:, t0:t0 + TOWN]),
            "wq": wq, "wk": wk, "wv": wv,
            "bq": bq, "bk": bk, "bv": bv,
            "gq": np.ascontiguousarray(gq[c * 512:(c + 1) * 512]),
            "uq": np.ascontiguousarray(uq[c * 512:(c + 1) * 512]),
            "dq": np.ascontiguousarray(dq[:, c * 64:(c + 1) * 64]),
            "gs": np.ascontiguousarray(gs[4 * c:4 * c + 4]),
            "us": np.ascontiguousarray(us[4 * c:4 * c + 4]),
            "ds": np.ascontiguousarray(ds[:, c * 512:(c + 1) * 512]),
            "gz": np.ascontiguousarray(gz[4 * c:4 * c + 4]),
            "uz": np.ascontiguousarray(uz[4 * c:4 * c + 4]),
            "dz": np.ascontiguousarray(dz[:, c * 64:(c + 1) * 64]),
            "pos0": np.full((64, 1), float(t0), np.float32),
        })

    res = run_bass_kernel_spmd(nc, in_maps, core_ids=list(range(8)))

    out = np.empty((B, S, E), np.float32)
    for c in range(8):
        b, t0 = c // 4, (c % 4) * TOWN
        dev = np.asarray(res.results[c]["out"])
        nat = np.empty_like(dev)
        nat[permE] = dev
        out[b, t0:t0 + TOWN, :] = nat.T
    return out
